# version 01
# baseline (speedup 1.0000x reference)
import contextlib
import numpy as np
import ml_dtypes
import concourse.bass as bass
import concourse.mybir as mybir
from concourse.bass_utils import run_bass_kernel_spmd

F32 = mybir.dt.float32
BF16 = mybir.dt.bfloat16
I32 = mybir.dt.int32
AF = mybir.ActivationFunctionType
ALU = mybir.AluOpType
AX = mybir.AxisListType
BFT = ml_dtypes.bfloat16

NCORE = 8
TOK = 2048
NCH = TOK // 128
D = 1024
DC = D // 128
LN_EPS = 1e-5

IDENT = np.eye(128, dtype=np.float32)


def _f32(a):
    return np.ascontiguousarray(a, dtype=np.float32)


def _to_host_bf16(a):
    return np.ascontiguousarray(np.asarray(a).astype(BFT))


class Prog:

    def __init__(self, name="prog"):
        self.nc = bass.Bass("TRN2", target_bir_lowering=False)
        self.es = contextlib.ExitStack()
        self.name = name
        self._started = False
        self._coll_n = 0

    def inp(self, name, shape, dt=F32):
        return self.nc.dram_tensor(name, list(shape), dt, kind="ExternalInput").ap()

    def outp(self, name, shape, dt=F32):
        return self.nc.dram_tensor(name, list(shape), dt, kind="ExternalOutput").ap()

    def dram(self, name, shape, dt=F32, **kw):
        return self.nc.dram_tensor(name, list(shape), dt, kind="Internal", **kw)

    def sb(self, name, shape, dt=BF16):
        return self.es.enter_context(self.nc.sbuf_tensor(name, list(shape), dt))

    def ps(self, name, shape, dt=F32):
        return self.es.enter_context(self.nc.psum_tensor(name, list(shape), dt))

    def begin(self):
        self.S = self.es.enter_context(self.nc.semaphore("S"))
        self.block = self.es.enter_context(self.nc.Block())
        self.c = 0
        self._pend_eng = None
        self._pend = []
        self._started = True

    def _flush(self):
        if not self._pend:
            return
        items = list(self._pend)
        self._pend = []
        S = self.S

        def f(eng):
            for emit, inc, wait_at in items:
                eng.wait_ge(S, wait_at)
                inst = emit(eng)
                if inst is not None:
                    inst.then_inc(S, inc)

        getattr(self.block, self._pend_eng)(f)

    def op(self, engname, emit, inc=1):
        if self._pend_eng != engname:
            self._flush()
            self._pend_eng = engname
        self._pend.append((emit, inc, self.c))
        self.c += inc

    def t(self, emit):
        self.op("tensor", emit, 1)

    def v(self, emit):
        self.op("vector", emit, 1)

    def s(self, emit):
        self.op("scalar", emit, 1)

    def g(self, emit):
        self.op("gpsimd", emit, 1)

    def dma(self, emit, eng="sync"):
        self.op(eng, emit, 16)

    def gdma(self, emit):
        self.op("gpsimd", emit, 16)

    def end(self):
        self._flush()
        total = self.c
        S = self.S

        def f(eng):
            eng.wait_ge(S, total)

        self.block.sync(f)
        self.es.close()
        return self.nc



def emit_ln(p, r_ap, xn_ap, scr):
    sq, s1, s2, m, rstd, negmr = (scr["sq"], scr["s1"], scr["s2"], scr["m"],
                                  scr["rstd"], scr["negmr"])
    n = r_ap.shape[-1]
    p.v(lambda e: e.reduce_sum(s1[:, :], r_ap, AX.X))
    p.s(lambda e: e.activation(sq[:, :], r_ap, AF.Square, accum_out=s2[:, :]))
    p.v(lambda e: e.tensor_scalar(m[:, :], s1[:, :], 1.0 / n, None, ALU.mult))
    p.v(lambda e: e.tensor_scalar(s2[:, :], s2[:, :], 1.0 / n, None, ALU.mult))
    p.v(lambda e: e.scalar_tensor_tensor(s2[:, :], m[:, :], m[:, :], s2[:, :], ALU.mult, ALU.subtract))
    eps = scr["eps"]
    p.s(lambda e: e.activation(rstd[:, :], s2[:, :], AF.Sqrt, bias=eps[:, :], scale=-1.0))
    p.v(lambda e: e.reciprocal(rstd[:, :], rstd[:, :]))
    p.v(lambda e: e.tensor_scalar(negmr[:, :], m[:, :], rstd[:, :], -1.0, ALU.mult, ALU.mult))
    p.v(lambda e: e.tensor_scalar(xn_ap, r_ap, rstd[:, :], negmr[:, :], ALU.mult, ALU.add))


def ln_scratch(p, tag=""):
    scr = {
        "sq": p.sb("ln_sq" + tag, [128, D], F32),
        "s1": p.sb("ln_s1" + tag, [128, 1], F32),
        "s2": p.sb("ln_s2" + tag, [128, 1], F32),
        "m": p.sb("ln_m" + tag, [128, 1], F32),
        "rstd": p.sb("ln_rstd" + tag, [128, 1], F32),
        "negmr": p.sb("ln_negmr" + tag, [128, 1], F32),
        "eps": p.sb("ln_eps" + tag, [128, 1], F32),
    }
    eps = scr["eps"]
    p.v(lambda e: e.memset(eps[:, :], LN_EPS))
    return scr


def emit_transpose_cols(p, src_bf_ap, ncol_chunks, ptr, ident, out_writes):
    for j in range(ncol_chunks):
        jj = j
        p.t(lambda e, jj=jj: e.transpose(ptr[:, jj * 128:(jj + 1) * 128],
                                         src_bf_ap[:, jj * 128:(jj + 1) * 128],
                                         ident[:, :]))
    if callable(out_writes):
        for j in range(ncol_chunks):
            jj = j
            p.v(lambda e, jj=jj: e.tensor_copy(out_writes(jj), ptr[:, jj * 128:(jj + 1) * 128]))
    else:
        p.v(lambda e: e.tensor_copy(out_writes, ptr[:, 0:ncol_chunks * 128]))



def build_ffn_int8():
    p = Prog("ffn")
    xs = p.inp("xs", [TOK, D], F32)
    wgT = p.inp("wgT", [D, 3072], I32)
    wuT = p.inp("wuT", [D, 3072], I32)
    wdT = p.inp("wdT", [3072, D], I32)
    ident_in = p.inp("ident", [128, 128], F32)
    ys = p.outp("ys", [TOK, D], F32)

    H = 3072
    HC = H // 128
    G = 256
    NG = TOK // G

    wg_sb = p.sb("wg", [128, DC * H], BF16)
    wu_sb = p.sb("wu", [128, DC * H], BF16)
    wd_sb = p.sb("wd", [128, HC * D], BF16)
    ident = p.sb("identb", [128, 128], BF16)
    r_g = p.sb("r_g", [128, 2 * D], F32)
    xn_c = p.sb("xn_c", [128, D], BF16)
    xnT_g = p.sb("xnT_g", [128, DC * G], BF16)
    gt = p.sb("gt", [128, G], BF16)
    ut = p.sb("ut", [128, G], BF16)
    hT_g = p.sb("hT_g", [128, HC * G], BF16)
    outs = p.sb("outs", [128, D], F32)
    scr = None

    ptr = p.ps("ptr", [128, 1024], BF16)
    pmm = p.ps("pmm", [128, G], F32)
    pdn = p.ps("pdn", [128, 512], F32)

    p.begin()
    scr = ln_scratch(p)
    p.gdma(lambda e: e.dma_start(out=ident[:, :], in_=ident_in))
    p.gdma(lambda e: e.dma_start(
        out=wg_sb[:, :].rearrange("p (k h) -> p k h", k=DC),
        in_=wgT.rearrange("(k p) h -> p k h", p=128)))
    p.gdma(lambda e: e.dma_start(
        out=wu_sb[:, :].rearrange("p (k h) -> p k h", k=DC),
        in_=wuT.rearrange("(k p) h -> p k h", p=128)))
    p.gdma(lambda e: e.dma_start(
        out=wd_sb[:, :].rearrange("p (m n) -> p m n", m=HC),
        in_=wdT.rearrange("(m p) n -> p m n", p=128)))

    for grp in range(NG):
        for cc in range(2):
            c = grp * 2 + cc
            rsl = r_g[:, cc * D:(cc + 1) * D]
            p.dma(lambda e, c=c, rsl=rsl: e.dma_start(out=rsl, in_=xs[c * 128:(c + 1) * 128, :]))
            emit_ln(p, rsl, xn_c[:, :], scr)
            dst = xnT_g[:, :].rearrange("p (k t) -> p k t", k=DC)[:, :, cc * 128:cc * 128 + 128]
            emit_transpose_cols(p, xn_c[:, :], DC, ptr, ident, dst)
        for m in range(HC):
            for k in range(DC):
                p.t(lambda e, m=m, k=k: e.matmul(
                    pmm[:, :], wg_sb[:, k * H + m * 128:k * H + (m + 1) * 128],
                    xnT_g[:, k * G:(k + 1) * G], start=(k == 0), stop=(k == DC - 1)))
            p.s(lambda e: e.activation(gt[:, :], pmm[:, :], AF.Silu, scale=1.0 / 127.0))
            for k in range(DC):
                p.t(lambda e, m=m, k=k: e.matmul(
                    pmm[:, :], wu_sb[:, k * H + m * 128:k * H + (m + 1) * 128],
                    xnT_g[:, k * G:(k + 1) * G], start=(k == 0), stop=(k == DC - 1)))
            p.s(lambda e: e.activation(ut[:, :], pmm[:, :], AF.Copy, scale=1.0 / 127.0))
            p.v(lambda e, m=m: e.tensor_tensor(hT_g[:, m * G:(m + 1) * G], gt[:, :], ut[:, :], ALU.mult))
        for cc in range(2):
            c = grp * 2 + cc
            for half in range(2):
                for m in range(HC):
                    p.t(lambda e, m=m, cc=cc, half=half: e.matmul(
                        pdn[:, :], hT_g[:, m * G + cc * 128:m * G + cc * 128 + 128],
                        wd_sb[:, m * D + half * 512:m * D + (half + 1) * 512],
                        start=(m == 0), stop=(m == HC - 1)))
                p.v(lambda e, cc=cc, half=half: e.scalar_tensor_tensor(
                    outs[:, half * 512:(half + 1) * 512], pdn[:, :], 1.0 / 127.0,
                    r_g[:, cc * D + half * 512:cc * D + (half + 1) * 512], ALU.mult, ALU.add))
            p.dma(lambda e, c=c: e.dma_start(out=ys[c * 128:(c + 1) * 128, :], in_=outs[:, :]))
    return p.end()


def _ffn_inmaps(x_full, wg, wu, wd):
    wgT = np.ascontiguousarray(wg.T).astype(np.int32)
    wuT = np.ascontiguousarray(wu.T).astype(np.int32)
    wdT = np.ascontiguousarray(wd.T).astype(np.int32)
    maps = []
    for cid in range(NCORE):
        maps.append({
            "xs": _f32(x_full[cid * TOK:(cid + 1) * TOK]),
            "wgT": wgT, "wuT": wuT, "wdT": wdT, "ident": IDENT,
        })
    return maps


def run_ffn(x_full, wg, wu, wd, nc_cache={}):
    if "ffn" not in nc_cache:
        nc_cache["ffn"] = build_ffn_int8()
    nc = nc_cache["ffn"]
    res = run_bass_kernel_spmd(nc, _ffn_inmaps(x_full, wg, wu, wd), core_ids=list(range(NCORE)))
    return np.concatenate([res.results[i]["ys"] for i in range(NCORE)], axis=0)



HALO_CH = 2
XCH = NCH + 2 * HALO_CH


def _l0_rope_tables():
    h = 16
    d = 64
    inv = 1.0 / 10000.0 ** (np.arange(0, d, 2, dtype=np.float64) / d)
    freqs = np.arange(h, dtype=np.float64)[:, None] * inv[None, :]
    emb = np.concatenate([freqs, freqs], axis=-1)
    C = np.cos(emb).reshape(-1).astype(np.float32)
    S = np.sin(emb).reshape(-1).astype(np.float32)
    return C, S


def build_l0a():
    p = Prog("l0a")
    xh = p.inp("xh", [XCH * 128, D], F32)
    wqT = p.inp("wqT", [D, D], F32)
    wkT = p.inp("wkT", [D, D], F32)
    wvT = p.inp("wvT", [D, D], F32)
    woT = p.inp("woT", [D, D], F32)
    bq_r = p.inp("bq", [1, D], F32)
    bk_r = p.inp("bk", [1, D], F32)
    bv_r = p.inp("bv", [1, D], F32)
    bo_r = p.inp("bo", [1, D], F32)
    ident_in = p.inp("ident", [128, 128], F32)
    ropeq_in = p.inp("ropeq", [128, 4 * 512], F32)
    ropek_in = p.inp("ropek", [128, 4 * 512], F32)
    tri_in = p.inp("tri", [128, 2 * 128], F32)
    e16_in = p.inp("e16", [128, 8 * 16], F32)
    sel16_in = p.inp("sel16", [16, 16 * 128], F32)
    sel64_in = p.inp("sel64", [16, 16 * 64], F32)
    kill01_in = p.inp("kill01", [1, NCH * 5], F32)
    ys = p.outp("ys", [TOK, D], F32)

    NH = 16
    KW = XCH * 128

    wA = p.sb("wA", [128, DC * D], BF16)
    wB = p.sb("wB", [128, DC * D], BF16)
    wC = p.sb("wC", [128, DC * D], BF16)
    ropeq = p.sb("ropeq_sb", [128, 4 * 512], BF16)
    ropek = p.sb("ropek_sb", [128, 4 * 512], BF16)
    rs1 = p.sb("rs1", [128, 512], BF16)
    rs2 = p.sb("rs2", [128, 512], BF16)
    ident = p.sb("identb", [128, 128], BF16)
    tri = p.sb("tri_sb", [128, 2 * 128], F32)
    e16 = p.sb("e16_sb", [128, 8 * 16], BF16)
    sel16 = p.sb("sel16_sb", [16, 16 * 128], BF16)
    sel64 = p.sb("sel64_sb", [16, 16 * 64], BF16)
    kill01 = p.sb("kill01_sb", [1, NCH * 5], F32)
    killrow = p.sb("killrow", [1, 5 * 128], BF16)
    neg_row = p.sb("neg_row", [1, 128], F32)
    ones_c = p.sb("ones_c", [128, 1], BF16)
    ones_r = p.sb("ones_r", [1, 128], BF16)
    bqc = p.sb("bqc", [128, DC], F32)
    bkc = p.sb("bkc", [128, DC], F32)
    brow_b = p.sb("brow_b", [1, D], BF16)
    bx_bf = p.sb("bx_bf", [1, D], BF16)
    bvB = p.sb("bvB", [128, D], F32)
    boB = p.sb("boB", [128, D], F32)

    KT = p.sb("KT", [128, DC * KW], BF16)
    V = p.sb("V", [128, XCH * D], BF16)
    QTc = p.sb("QTc", [128, DC * 128], BF16)
    r_c = p.sb("r_c", [128, D], F32)
    xn_c = p.sb("xn_c", [128, D], BF16)
    xnT_c = p.sb("xnT_c", [128, DC * 128], BF16)
    sqs = p.sb("sqs", [128, 128], BF16)
    qn2c = p.sb("qn2c", [16, 128], F32)
    kn2m = p.sb("kn2m", [16, 1], F32)
    kn2c = p.sb("kn2c", [16, 1], F32)
    qbias_c = p.sb("qbias_c", [16, 128], BF16)
    PT = p.sb("PT", [128, 5 * 128], BF16)
    rS = p.sb("rS", [128, 16], F32)
    rSb = p.sb("rSb", [128, 16], BF16)
    rT = p.sb("rT", [16, 128], BF16)
    rB = p.sb("rB", [64, 128], F32)
    ctxT = p.sb("ctxT", [128, D], BF16)
    outs = p.sb("outs", [128, D], F32)

    P1 = p.ps("P1", [128, 512], F32)
    P2 = p.ps("P2", [128, 640], F32)
    P3 = p.ps("P3", [128, 1024], F32)
    P4 = p.ps("P4", [128, 1024], BF16)
    P5 = p.ps("P5", [128, 128], F32)
    P6 = p.ps("P6", [16, 128], BF16)

    p.begin()
    scr = ln_scratch(p)
    p.gdma(lambda e: e.dma_start(out=ident[:, :], in_=ident_in))
    p.gdma(lambda e: e.dma_start(out=ropeq[:, :], in_=ropeq_in))
    p.gdma(lambda e: e.dma_start(out=ropek[:, :], in_=ropek_in))
    p.dma(lambda e: e.dma_start(out=tri[:, :], in_=tri_in))
    p.gdma(lambda e: e.dma_start(out=e16[:, :], in_=e16_in))
    p.gdma(lambda e: e.dma_start(out=sel16[:, :], in_=sel16_in))
    p.gdma(lambda e: e.dma_start(out=sel64[:, :], in_=sel64_in))
    p.dma(lambda e: e.dma_start(out=kill01[:, :], in_=kill01_in))
    p.v(lambda e: e.memset(ones_c[:, :], 1.0))
    p.v(lambda e: e.memset(ones_r[:, :], 1.0))
    p.v(lambda e: e.memset(neg_row[:, :], -1e30))
    p.v(lambda e: e.memset(kn2m[:, :], 0.0))

    def rope_fold(dst, rope_sb, src_in):
        p.gdma(lambda e: e.dma_start(
            out=wB[:, :].rearrange("p (k n) -> p k n", k=DC),
            in_=src_in.rearrange("(k p) n -> p k n", p=128)))
        for k in range(DC):
            b0 = k * D
            ev = wB[:, b0:b0 + D:2]
            od = wB[:, b0 + 1:b0 + D:2]
            dev = dst[:, b0:b0 + D:2]
            dod = dst[:, b0 + 1:b0 + D:2]
            ce = rope_sb[:, 0:512]
            se = rope_sb[:, 512:1024]
            co = rope_sb[:, 1024:1536]
            so = rope_sb[:, 1536:2048]
            p.v(lambda e, od=od, se=se: e.tensor_tensor(rs1[:, :], od, se, ALU.mult))
            p.v(lambda e, ev=ev, so=so: e.tensor_tensor(rs2[:, :], ev, so, ALU.mult))
            p.v(lambda e, ev=ev, ce=ce, dev=dev: e.tensor_tensor(dev, ev, ce, ALU.mult))
            p.v(lambda e, dev=dev: e.tensor_tensor(dev, dev, rs1[:, :], ALU.subtract))
            p.v(lambda e, od=od, co=co, dod=dod: e.tensor_tensor(dod, od, co, ALU.mult))
            p.v(lambda e, dod=dod: e.tensor_tensor(dod, dod, rs2[:, :], ALU.add))

    rope_fold(wA, ropeq, wqT)
    def rope_fold_k():
        p.gdma(lambda e: e.dma_start(
            out=wC[:, :].rearrange("p (k n) -> p k n", k=DC),
            in_=wkT.rearrange("(k p) n -> p k n", p=128)))
        for k in range(DC):
            b0 = k * D
            ev = wC[:, b0:b0 + D:2]
            od = wC[:, b0 + 1:b0 + D:2]
            dev = wB[:, b0:b0 + D:2]
            dod = wB[:, b0 + 1:b0 + D:2]
            ce = ropek[:, 0:512]
            se = ropek[:, 512:1024]
            co = ropek[:, 1024:1536]
            so = ropek[:, 1536:2048]
            p.v(lambda e, od=od, se=se: e.tensor_tensor(rs1[:, :], od, se, ALU.mult))
            p.v(lambda e, ev=ev, so=so: e.tensor_tensor(rs2[:, :], ev, so, ALU.mult))
            p.v(lambda e, ev=ev, ce=ce, dev=dev: e.tensor_tensor(dev, ev, ce, ALU.mult))
            p.v(lambda e, dev=dev: e.tensor_tensor(dev, dev, rs1[:, :], ALU.subtract))
            p.v(lambda e, od=od, co=co, dod=dod: e.tensor_tensor(dod, od, co, ALU.mult))
            p.v(lambda e, dod=dod: e.tensor_tensor(dod, dod, rs2[:, :], ALU.add))

    rope_fold_k()
    p.gdma(lambda e: e.dma_start(
        out=wC[:, :].rearrange("p (k n) -> p k n", k=DC),
        in_=wvT.rearrange("(k p) n -> p k n", p=128)))

    def rope_bias_row(rope_sb, b_in):
        p.gdma(lambda e: e.dma_start(out=bx_bf[:, :], in_=b_in))
        ev = bx_bf[:, 0:D:2]
        od = bx_bf[:, 1:D:2]
        p.v(lambda e: e.tensor_tensor(rs1[0:1, 0:512], od, rope_sb[0:1, 512:1024], ALU.mult))
        p.v(lambda e: e.tensor_tensor(rs2[0:1, 0:512], ev, rope_sb[0:1, 1536:2048], ALU.mult))
        p.v(lambda e: e.tensor_tensor(brow_b[0:1, 0:D:2], ev, rope_sb[0:1, 0:512], ALU.mult))
        p.v(lambda e: e.tensor_tensor(brow_b[0:1, 0:D:2], brow_b[0:1, 0:D:2], rs1[0:1, 0:512], ALU.subtract))
        p.v(lambda e: e.tensor_tensor(brow_b[0:1, 1:D:2], od, rope_sb[0:1, 1024:1536], ALU.mult))
        p.v(lambda e: e.tensor_tensor(brow_b[0:1, 1:D:2], brow_b[0:1, 1:D:2], rs2[0:1, 0:512], ALU.add))

    def row_to_cols(row_bf, cols_f32):
        for m in range(DC):
            p.t(lambda e, m=m: e.matmul(P1[:, m:m + 1], row_bf[0:1, m * 128:(m + 1) * 128],
                                        ones_r[0:1, 0:1], start=True, stop=True))
        p.v(lambda e: e.tensor_copy(cols_f32[:, :], P1[:, 0:DC]))

    rope_bias_row(ropeq, bq_r)
    row_to_cols(brow_b, bqc)
    rope_bias_row(ropek, bk_r)
    row_to_cols(brow_b, bkc)
    p.gdma(lambda e: e.dma_start(out=bx_bf[:, :], in_=bv_r))
    for half in range(2):
        p.t(lambda e, half=half: e.matmul(P1[:, :], ones_r[:, :],
                                          bx_bf[0:1, half * 512:(half + 1) * 512],
                                          start=True, stop=True))
        p.v(lambda e, half=half: e.tensor_copy(bvB[:, half * 512:(half + 1) * 512], P1[:, :]))
    p.gdma(lambda e: e.dma_start(out=bx_bf[:, :], in_=bo_r))
    for half in range(2):
        p.t(lambda e, half=half: e.matmul(P1[:, :], ones_r[:, :],
                                          bx_bf[0:1, half * 512:(half + 1) * 512],
                                          start=True, stop=True))
        p.v(lambda e, half=half: e.tensor_copy(boB[:, half * 512:(half + 1) * 512], P1[:, :]))

    for c in range(XCH):
        p.dma(lambda e, c=c: e.dma_start(out=r_c[:, :], in_=xh[c * 128:(c + 1) * 128, :]))
        emit_ln(p, r_c[:, :], xn_c[:, :], scr)
        emit_transpose_cols(p, xn_c[:, :], DC, P4, ident,
                            xnT_c[:, :].rearrange("p (k t) -> p k t", k=DC))
        for m in range(DC):
            for k in range(DC):
                p.t(lambda e, m=m, k=k: e.matmul(
                    P1[:, 0:128], wB[:, k * D + m * 128:k * D + (m + 1) * 128],
                    xnT_c[:, k * 128:(k + 1) * 128], start=(k == 0), stop=(k == DC - 1)))
            p.v(lambda e, m=m, c=c: e.tensor_scalar(
                KT[:, m * KW + c * 128:m * KW + (c + 1) * 128],
                P1[:, 0:128], bkc[:, m:m + 1], None, ALU.add))
            p.s(lambda e, m=m, c=c: e.activation(
                sqs[:, :], KT[:, m * KW + c * 128:m * KW + (c + 1) * 128], AF.Square))
            p.t(lambda e, m=m: e.matmul(P5[0:16, :], e16[:, m * 16:(m + 1) * 16],
                                        sqs[:, :], start=(m == 0), stop=(m == DC - 1)))
        p.v(lambda e: e.reduce_max(kn2c[:, :], P5[0:16, :], AX.X))
        p.v(lambda e: e.tensor_tensor(kn2m[:, :], kn2m[:, :], kn2c[:, :], ALU.max))
        for half in range(2):
            for k in range(DC):
                p.t(lambda e, half=half, k=k: e.matmul(
                    P1[:, :], xnT_c[:, k * 128:(k + 1) * 128],
                    wC[:, k * D + half * 512:k * D + (half + 1) * 512],
                    start=(k == 0), stop=(k == DC - 1)))
            p.v(lambda e, half=half, c=c: e.tensor_tensor(
                V[:, c * D + half * 512:c * D + (half + 1) * 512], P1[:, :],
                bvB[:, half * 512:(half + 1) * 512], ALU.add))

    p.gdma(lambda e: e.dma_start(
        out=wB[:, :].rearrange("p (k n) -> p k n", k=DC),
        in_=woT.rearrange("(k p) n -> p k n", p=128)))

    for oc in range(NCH):
        hc = oc + HALO_CH
        p.dma(lambda e, hc=hc: e.dma_start(out=r_c[:, :], in_=xh[hc * 128:(hc + 1) * 128, :]))
        emit_ln(p, r_c[:, :], xn_c[:, :], scr)
        emit_transpose_cols(p, xn_c[:, :], DC, P4, ident,
                            xnT_c[:, :].rearrange("p (k t) -> p k t", k=DC))
        for m in range(DC):
            for k in range(DC):
                p.t(lambda e, m=m, k=k: e.matmul(
                    P1[:, 0:128], wA[:, k * D + m * 128:k * D + (m + 1) * 128],
                    xnT_c[:, k * 128:(k + 1) * 128], start=(k == 0), stop=(k == DC - 1)))
            p.v(lambda e, m=m: e.tensor_scalar(
                QTc[:, m * 128:(m + 1) * 128], P1[:, 0:128], bqc[:, m:m + 1], None, ALU.add))
            p.s(lambda e, m=m: e.activation(sqs[:, :], QTc[:, m * 128:(m + 1) * 128], AF.Square))
            p.t(lambda e, m=m: e.matmul(P5[0:16, :], e16[:, m * 16:(m + 1) * 16],
                                        sqs[:, :], start=(m == 0), stop=(m == DC - 1)))
        p.v(lambda e: e.tensor_copy(qn2c[:, :], P5[0:16, :]))
        p.v(lambda e: e.tensor_scalar(qn2c[:, :], qn2c[:, :], kn2m[0:16, 0:1],
                                      1.0, ALU.mult, ALU.mult))
        p.s(lambda e: e.activation(qn2c[:, :], qn2c[:, :], AF.Sqrt))
        p.v(lambda e: e.tensor_scalar(qbias_c[:, :], qn2c[:, :], -1.0, 50.0, ALU.mult, ALU.add))
        for di in range(5):
            p.v(lambda e, di=di, oc=oc: e.tensor_scalar(
                killrow[0:1, di * 128:(di + 1) * 128], neg_row[0:1, :],
                kill01[0:1, oc * 5 + di:oc * 5 + di + 1], None, ALU.mult))
        for h in range(NH):
            sl = h // 2
            po = (h % 2) * 64
            for di in range(5):
                kc = hc + di - 2
                p.t(lambda e, di=di, kc=kc, sl=sl, po=po: e.matmul(
                    P2[:, di * 128:(di + 1) * 128],
                    KT[po:po + 64, sl * KW + kc * 128:sl * KW + (kc + 1) * 128],
                    QTc[po:po + 64, sl * 128:(sl + 1) * 128],
                    start=True, stop=False))
                p.t(lambda e, di=di, h=h: e.matmul(
                    P2[:, di * 128:(di + 1) * 128], sel16[:, h * 128:(h + 1) * 128],
                    qbias_c[:, :], start=False, stop=False))
                p.t(lambda e, di=di: e.matmul(
                    P2[:, di * 128:(di + 1) * 128],
                    killrow[0:1, di * 128:(di + 1) * 128], ones_r[:, :],
                    start=False, stop=True))
            p.v(lambda e: e.tensor_tensor(P2[:, 0:128], P2[:, 0:128], tri[:, 0:128], ALU.add))
            p.v(lambda e: e.tensor_tensor(P2[:, 512:640], P2[:, 512:640], tri[:, 128:256], ALU.add))
            p.s(lambda e: e.activation(PT[:, :], P2[:, :], AF.Exp))
            for di in range(5):
                p.t(lambda e, di=di: e.matmul(P5[:, 0:1], PT[:, di * 128:(di + 1) * 128],
                                              ones_c[:, 0:1], start=(di == 0), stop=(di == 4)))
            p.v(lambda e, h=h: e.tensor_copy(rS[:, h:h + 1], P5[:, 0:1]))
            for di in range(5):
                kc = hc + di - 2
                p.t(lambda e, di=di, kc=kc, h=h, sl=sl, po=po: e.matmul(
                    P3[po:po + 64, sl * 128:(sl + 1) * 128],
                    V[:, kc * D + h * 64:kc * D + (h + 1) * 64],
                    PT[:, di * 128:(di + 1) * 128], start=(di == 0), stop=(di == 4)))
        p.v(lambda e: e.reciprocal(rS[:, :], rS[:, :]))
        p.v(lambda e: e.tensor_copy(rSb[:, :], rS[:, :]))
        p.t(lambda e: e.transpose(P6[:, :], rSb[:, :], ident[:, :]))
        p.v(lambda e: e.tensor_copy(rT[:, :], P6[0:16, :]))
        for h in range(NH):
            sl = h // 2
            po = (h % 2) * 64
            p.t(lambda e, h=h: e.matmul(P5[0:64, :], sel64[:, h * 64:(h + 1) * 64],
                                        rT[:, :], start=True, stop=True))
            p.v(lambda e: e.tensor_copy(rB[:, :], P5[0:64, :]))
            p.v(lambda e, sl=sl, po=po: e.tensor_tensor(
                ctxT[po:po + 64, sl * 128:(sl + 1) * 128],
                P3[po:po + 64, sl * 128:(sl + 1) * 128], rB[:, :], ALU.mult))
        for half in range(2):
            for k in range(DC):
                p.t(lambda e, half=half, k=k: e.matmul(
                    P1[:, :], ctxT[:, k * 128:(k + 1) * 128],
                    wB[:, k * D + half * 512:k * D + (half + 1) * 512],
                    start=(k == 0), stop=(k == DC - 1)))
            p.v(lambda e, half=half: e.tensor_tensor(
                outs[:, half * 512:(half + 1) * 512], P1[:, :],
                boB[:, half * 512:(half + 1) * 512], ALU.add))
            p.v(lambda e, half=half: e.tensor_tensor(
                outs[:, half * 512:(half + 1) * 512], outs[:, half * 512:(half + 1) * 512],
                r_c[:, half * 512:(half + 1) * 512], ALU.add))
        p.dma(lambda e, oc=oc: e.dma_start(out=ys[oc * 128:(oc + 1) * 128, :], in_=outs[:, :]))

    return p.end()


def _l0a_host():
    C, S = _l0_rope_tables()
    def rep(v):
        return np.ascontiguousarray(np.broadcast_to(v[None, :], (128, 512)), dtype=np.float32)
    CE, SE = C[0::2], S[0::2]
    CO, SO = C[1::2], S[1::2]
    ropeq = np.concatenate([rep(CE / 8), rep(SE / 8), rep(CO / 8), rep(SO / 8)], axis=1)
    ropek = np.concatenate([rep(CE), rep(SE), rep(CO), rep(SO)], axis=1)
    pp = np.arange(128)[:, None]
    ff = np.arange(128)[None, :]
    tri_l = np.where(pp >= ff, 0.0, -1e30).astype(np.float32)
    tri_r = np.where(pp <= ff, 0.0, -1e30).astype(np.float32)
    tri = np.concatenate([tri_l, tri_r], axis=1)
    e16 = np.zeros((128, 8 * 16), dtype=np.float32)
    for m in range(8):
        e16[0:64, m * 16 + 2 * m] = 1.0
        e16[64:128, m * 16 + 2 * m + 1] = 1.0
    sel16 = np.zeros((16, 16 * 128), dtype=np.float32)
    for h in range(16):
        sel16[h, h * 128:(h + 1) * 128] = 1.0
    sel64 = np.zeros((16, 16 * 64), dtype=np.float32)
    for h in range(16):
        sel64[h, h * 64:(h + 1) * 64] = 1.0
    return dict(ropeq=ropeq, ropek=ropek, tri=tri, e16=e16, sel16=sel16,
                sel64=sel64, ident=IDENT)


def _l0a_inmaps(x_full, inputs):
    hostc = _l0a_host()
    wqT = _f32(np.asarray(inputs["L0_attn_Wq"]).T)
    wkT = _f32(np.asarray(inputs["L0_attn_Wk"]).T)
    wvT = _f32(np.asarray(inputs["L0_attn_Wv"]).T)
    woT = _f32(np.asarray(inputs["L0_attn_Wo"]).T)
    bq = _f32(inputs["L0_attn_bq"]).reshape(1, D)
    bk = _f32(inputs["L0_attn_bk"]).reshape(1, D)
    bv = _f32(inputs["L0_attn_bv"]).reshape(1, D)
    bo = _f32(inputs["L0_attn_bo"]).reshape(1, D)
    maps = []
    HALO = HALO_CH * 128
    for cid in range(NCORE):
        t0 = cid * TOK
        seq0 = (cid // 4) * 8192
        seq1 = seq0 + 8192
        lo = t0 - HALO
        hi = t0 + TOK + HALO
        xhp = np.zeros((XCH * 128, D), dtype=np.float32)
        vlo = max(lo, seq0)
        vhi = min(hi, seq1)
        xhp[vlo - lo:vhi - lo] = x_full[vlo:vhi]
        kill = np.zeros((NCH, 5), dtype=np.float32)
        for oc in range(NCH):
            for di in range(5):
                kc_lo = t0 + (oc + di - 2) * 128
                if kc_lo < seq0 or kc_lo + 128 > seq1:
                    kill[oc, di] = 1.0
        maps.append(dict(xh=xhp, wqT=wqT, wkT=wkT, wvT=wvT, woT=woT,
                         bq=bq, bk=bk, bv=bv, bo=bo,
                         kill01=kill.reshape(1, -1), **hostc))
    return maps


def run_l0a(x_full, inputs, nc_cache={}):
    if "l0a" not in nc_cache:
        nc_cache["l0a"] = build_l0a()
    res = run_bass_kernel_spmd(nc_cache["l0a"], _l0a_inmaps(x_full, inputs),
                               core_ids=list(range(NCORE)))
    return np.concatenate([res.results[i]["ys"] for i in range(NCORE)], axis=0)



SEQ = 8192
SCH = SEQ // 128
GN_EPS = 1e-5


def _l1_gammas():
    lo = float(np.log(1.0 - 0.999))
    hi = float(np.log(1.0 - 0.9))
    return 1.0 - np.exp(np.linspace(lo, hi, 16, dtype=np.float64))


def build_l1a():
    p = Prog("l1a")
    xq = p.inp("xq", [SEQ, D], F32)
    wqT = p.inp("wqT", [D, D], F32)
    wkT = p.inp("wkT", [D, D], F32)
    wvT = p.inp("wvT", [D, D], F32)
    woT = p.inp("woT", [D, D], F32)
    gnw_in = p.inp("gnw", [128, DC], F32)
    gnb_in = p.inp("gnb", [128, DC], F32)
    ident_in = p.inp("ident", [128, 128], F32)
    dt_in = p.inp("dtm", [128, 16 * 128], F32)
    lamq_in = p.inp("lamq", [128, DC * 128], F32)
    lamk_in = p.inp("lamk", [128, 16], F32)
    e2_in = p.inp("e2", [128, 2], F32)
    eb_in = p.inp("eb", [2, 128], F32)
    ys = p.outp("ys", [SEQ, D], F32)

    NH = 16
    gam = _l1_gammas()
    lam_c = [float(g ** 128) for g in gam]

    wA = p.sb("wA", [128, DC * D], BF16)
    wB = p.sb("wB", [128, DC * D], BF16)
    wC = p.sb("wC", [128, DC * D], BF16)
    ident = p.sb("identb", [128, 128], BF16)
    dtm = p.sb("dtm_sb", [128, 16 * 128], F32)
    lamq = p.sb("lamq_sb", [128, DC * 128], BF16)
    lamk = p.sb("lamk_sb", [128, 16], F32)
    e2 = p.sb("e2_sb", [128, 2], F32)
    eb = p.sb("eb_sb", [2, 128], F32)
    gnw = p.sb("gnw_sb", [128, DC], F32)
    gnb = p.sb("gnb_sb", [128, DC], F32)
    r_c = p.sb("r_c", [128, D], F32)
    xn_c = p.sb("xn_c", [128, D], BF16)
    xnT_c = p.sb("xnT_c", [128, DC * 128], BF16)
    QTc = p.sb("QTc", [128, DC * 128], BF16)
    QLc = p.sb("QLc", [128, DC * 128], BF16)
    KTc = p.sb("KTc", [128, DC * 128], BF16)
    Ktm = p.sb("Ktm", [128, D], BF16)
    Klam = p.sb("Klam", [128, 64], BF16)
    Vc = p.sb("Vc", [128, D], BF16)
    St = p.sb("St", [128, 128], BF16)
    state = p.sb("state", [128, 512], F32)
    stateb = p.sb("stateb", [128, 512], BF16)
    ysl = p.sb("ysl", [128, 128], BF16)
    ysq = p.sb("ysq", [128, 128], BF16)
    acc1 = p.sb("acc1", [128, 1], F32)
    acc2 = p.sb("acc2", [128, 1], F32)
    ysum = p.sb("ysum", [128, DC], F32)
    yssq = p.sb("yssq", [128, DC], F32)
    sst = p.sb("sst", [2, DC], F32)
    ssq2 = p.sb("ssq2", [2, DC], F32)
    smean = p.sb("smean", [2, DC], F32)
    srstd = p.sb("srstd", [2, DC], F32)
    scomb = p.sb("scomb", [2, 2 * DC], F32)
    statB = p.sb("statB", [128, 2 * DC], F32)
    scale_col = p.sb("scale_col", [128, DC], F32)
    bias_col = p.sb("bias_col", [128, DC], F32)
    geps = p.sb("geps", [128, 1], F32)
    ygn = p.sb("ygn", [128, DC * 128], BF16)
    outs = p.sb("outs", [128, D], F32)

    P1 = p.ps("P1", [128, 512], F32)
    P3 = p.ps("P3", [128, 1024], F32)
    P4 = p.ps("P4", [128, 1024], BF16)
    P5 = p.ps("P5", [128, 128], F32)
    PS2 = p.ps("PS2", [2, DC], F32)

    ybuf = p.dram("ybuf", [D, SEQ], BF16)

    p.begin()
    scr = ln_scratch(p)
    p.gdma(lambda e: e.dma_start(out=ident[:, :], in_=ident_in))
    p.dma(lambda e: e.dma_start(out=dtm[:, :], in_=dt_in))
    p.gdma(lambda e: e.dma_start(out=lamq[:, :], in_=lamq_in))
    p.dma(lambda e: e.dma_start(out=lamk[:, :], in_=lamk_in))
    p.dma(lambda e: e.dma_start(out=e2[:, :], in_=e2_in))
    p.dma(lambda e: e.dma_start(out=eb[:, :], in_=eb_in))
    p.dma(lambda e: e.dma_start(out=gnw[:, :], in_=gnw_in))
    p.dma(lambda e: e.dma_start(out=gnb[:, :], in_=gnb_in))
    p.gdma(lambda e: e.dma_start(
        out=wA[:, :].rearrange("p (k n) -> p k n", k=DC),
        in_=wqT.rearrange("(k p) n -> p k n", p=128)))
    p.gdma(lambda e: e.dma_start(
        out=wB[:, :].rearrange("p (k n) -> p k n", k=DC),
        in_=wkT.rearrange("(k p) n -> p k n", p=128)))
    p.gdma(lambda e: e.dma_start(
        out=wC[:, :].rearrange("p (k n) -> p k n", k=DC),
        in_=wvT.rearrange("(k p) n -> p k n", p=128)))
    p.v(lambda e: e.memset(state[:, :], 0.0))
    p.v(lambda e: e.memset(stateb[:, :], 0.0))
    p.v(lambda e: e.memset(ysum[:, :], 0.0))
    p.v(lambda e: e.memset(yssq[:, :], 0.0))
    p.v(lambda e: e.memset(geps[:, :], GN_EPS))

    for c in range(SCH):
        p.dma(lambda e, c=c: e.dma_start(out=r_c[:, :], in_=xq[c * 128:(c + 1) * 128, :]))
        emit_ln(p, r_c[:, :], xn_c[:, :], scr)
        emit_transpose_cols(p, xn_c[:, :], DC, P4, ident,
                            xnT_c[:, :].rearrange("p (k t) -> p k t", k=DC))
        for m in range(DC):
            for k in range(DC):
                p.t(lambda e, m=m, k=k: e.matmul(
                    P1[:, 0:128], wA[:, k * D + m * 128:k * D + (m + 1) * 128],
                    xnT_c[:, k * 128:(k + 1) * 128], start=(k == 0), stop=(k == DC - 1)))
            p.v(lambda e, m=m: e.tensor_copy(QTc[:, m * 128:(m + 1) * 128], P1[:, 0:128]))
            p.v(lambda e, m=m: e.tensor_tensor(QLc[:, m * 128:(m + 1) * 128],
                                               QTc[:, m * 128:(m + 1) * 128],
                                               lamq[:, m * 128:(m + 1) * 128], ALU.mult))
        for m in range(DC):
            for k in range(DC):
                p.t(lambda e, m=m, k=k: e.matmul(
                    P1[:, 0:128], wB[:, k * D + m * 128:k * D + (m + 1) * 128],
                    xnT_c[:, k * 128:(k + 1) * 128], start=(k == 0), stop=(k == DC - 1)))
            p.v(lambda e, m=m: e.tensor_copy(KTc[:, m * 128:(m + 1) * 128], P1[:, 0:128]))
        for half in range(2):
            for k in range(DC):
                p.t(lambda e, half=half, k=k: e.matmul(
                    P1[:, :], xnT_c[:, k * 128:(k + 1) * 128],
                    wC[:, k * D + half * 512:k * D + (half + 1) * 512],
                    start=(k == 0), stop=(k == DC - 1)))
            p.v(lambda e, half=half: e.tensor_copy(Vc[:, half * 512:(half + 1) * 512], P1[:, :]))
        emit_transpose_cols(p, KTc[:, :].rearrange("p (a b) -> p (b a)", a=DC) if False else KTc[:, :], DC, P4, ident, None) if False else None
        for j in range(DC):
            p.t(lambda e, j=j: e.transpose(P4[:, j * 128:(j + 1) * 128],
                                           KTc[:, j * 128:(j + 1) * 128], ident[:, :]))
        p.v(lambda e: e.tensor_copy(Ktm[:, :].rearrange("p (a b) -> p a b", a=DC),
                                    P4[:, :].rearrange("p (a b) -> p a b", a=DC)))
        for h in range(NH):
            sl = h // 2
            po = (h % 2) * 64
            p.t(lambda e, sl=sl, po=po: e.matmul(
                P5[:, :], KTc[po:po + 64, sl * 128:(sl + 1) * 128],
                QTc[po:po + 64, sl * 128:(sl + 1) * 128], start=True, stop=True))
            p.v(lambda e, h=h: e.tensor_tensor(St[:, :], P5[:, :],
                                               dtm[:, h * 128:(h + 1) * 128], ALU.mult))
            p.t(lambda e, h=h, sl=sl, po=po: e.matmul(
                P3[po:po + 64, sl * 128:(sl + 1) * 128],
                Vc[:, h * 64:(h + 1) * 64], St[:, :], start=True, stop=False))
            p.t(lambda e, h=h, sl=sl, po=po: e.matmul(
                P3[po:po + 64, sl * 128:(sl + 1) * 128],
                stateb[po:po + 64, sl * 64:(sl + 1) * 64],
                QLc[po:po + 64, sl * 128:(sl + 1) * 128], start=False, stop=True))
            p.v(lambda e, h=h: e.tensor_scalar(Klam[:, :], Ktm[:, h * 64:(h + 1) * 64],
                                               lamk[:, h:h + 1], None, ALU.mult))
            p.t(lambda e, h=h: e.matmul(P5[0:64, 0:64], Klam[:, :],
                                        Vc[:, h * 64:(h + 1) * 64], start=True, stop=True))
            p.v(lambda e, h=h, sl=sl, po=po, lc=lam_c[h]: e.tensor_scalar(
                state[po:po + 64, sl * 64:(sl + 1) * 64],
                state[po:po + 64, sl * 64:(sl + 1) * 64], lc, None, ALU.mult))
            p.v(lambda e, sl=sl, po=po: e.tensor_tensor(
                state[po:po + 64, sl * 64:(sl + 1) * 64],
                state[po:po + 64, sl * 64:(sl + 1) * 64], P5[0:64, 0:64], ALU.add))
            p.v(lambda e, sl=sl, po=po: e.tensor_copy(
                stateb[po:po + 64, sl * 64:(sl + 1) * 64],
                state[po:po + 64, sl * 64:(sl + 1) * 64]))
        for s in range(DC):
            p.s(lambda e, s=s: e.activation(ysl[:, :], P3[:, s * 128:(s + 1) * 128],
                                            AF.Copy, accum_out=acc1[:, :]))
            p.s(lambda e, s=s: e.activation(ysq[:, :], P3[:, s * 128:(s + 1) * 128],
                                            AF.Square, accum_out=acc2[:, :]))
            p.v(lambda e, s=s: e.tensor_tensor(ysum[:, s:s + 1], ysum[:, s:s + 1],
                                               acc1[:, :], ALU.add))
            p.v(lambda e, s=s: e.tensor_tensor(yssq[:, s:s + 1], yssq[:, s:s + 1],
                                               acc2[:, :], ALU.add))
            p.dma(lambda e, s=s, c=c: e.dma_start(
                out=ybuf.ap()[s * 128:(s + 1) * 128, c * 128:(c + 1) * 128], in_=ysl[:, :]))

    NTOT = float(SEQ * 64)
    p.t(lambda e: e.matmul(PS2[:, :], e2[:, 0:2], ysum[:, :], start=True, stop=True))
    p.v(lambda e: e.tensor_copy(sst[:, :], PS2[:, :]))
    p.t(lambda e: e.matmul(PS2[:, :], e2[:, 0:2], yssq[:, :], start=True, stop=True))
    p.v(lambda e: e.tensor_copy(ssq2[:, :], PS2[:, :]))
    p.v(lambda e: e.tensor_scalar(smean[:, :], sst[:, :], 1.0 / NTOT, None, ALU.mult))
    p.v(lambda e: e.tensor_scalar(ssq2[:, :], ssq2[:, :], 1.0 / NTOT, None, ALU.mult))
    p.v(lambda e: e.tensor_tensor(sst[:, :], smean[:, :], smean[:, :], ALU.mult))
    p.v(lambda e: e.tensor_tensor(ssq2[:, :], sst[:, :], ssq2[:, :], ALU.subtract))
    p.s(lambda e: e.activation(srstd[:, :], ssq2[:, :], AF.Sqrt, bias=geps[0:2, :], scale=-1.0))
    p.v(lambda e: e.reciprocal(srstd[:, :], srstd[:, :]))
    p.v(lambda e: e.tensor_copy(scomb[:, 0:DC], srstd[:, :]))
    p.v(lambda e: e.scalar_tensor_tensor(scomb[:, DC:2 * DC], smean[:, :], -1.0,
                                         srstd[:, :], ALU.mult, ALU.mult))
    p.t(lambda e: e.matmul(P1[:, 0:2 * DC], eb[:, :], scomb[:, :], start=True, stop=True))
    p.v(lambda e: e.tensor_copy(statB[:, :], P1[:, 0:2 * DC]))
    p.v(lambda e: e.tensor_tensor(scale_col[:, :], statB[:, 0:DC], gnw[:, :], ALU.mult))
    p.v(lambda e: e.tensor_tensor(bias_col[:, :], statB[:, DC:2 * DC], gnw[:, :], ALU.mult))
    p.v(lambda e: e.tensor_tensor(bias_col[:, :], bias_col[:, :], gnb[:, :], ALU.add))
    p.gdma(lambda e: e.dma_start(
        out=wA[:, :].rearrange("p (k n) -> p k n", k=DC),
        in_=woT.rearrange("(k p) n -> p k n", p=128)))

    for c in range(SCH):
        p.gdma(lambda e, c=c: e.dma_start(
            out=ygn[:, :].rearrange("p (s t) -> p s t", s=DC),
            in_=ybuf.ap().rearrange("(s p) t -> p s t", p=128)[:, :, c * 128:(c + 1) * 128]))
        for s in range(DC):
            p.v(lambda e, s=s: e.tensor_scalar(
                ygn[:, s * 128:(s + 1) * 128], ygn[:, s * 128:(s + 1) * 128],
                scale_col[:, s:s + 1], bias_col[:, s:s + 1], ALU.mult, ALU.add))
        p.dma(lambda e, c=c: e.dma_start(out=r_c[:, :], in_=xq[c * 128:(c + 1) * 128, :]))
        for half in range(2):
            for k in range(DC):
                p.t(lambda e, half=half, k=k: e.matmul(
                    P1[:, :], ygn[:, k * 128:(k + 1) * 128],
                    wA[:, k * D + half * 512:k * D + (half + 1) * 512],
                    start=(k == 0), stop=(k == DC - 1)))
            p.v(lambda e, half=half: e.tensor_tensor(
                outs[:, half * 512:(half + 1) * 512], P1[:, :],
                r_c[:, half * 512:(half + 1) * 512], ALU.add))
        p.dma(lambda e, c=c: e.dma_start(out=ys[c * 128:(c + 1) * 128, :], in_=outs[:, :]))

    return p.end()


def _l1a_inmaps(x_full, inputs):
    gam = _l1_gammas()
    dtm = np.zeros((128, 16 * 128), dtype=np.float32)
    ii = np.arange(128)
    for h in range(16):
        diff = ii[None, :] - ii[:, None]
        dtm[:, h * 128:(h + 1) * 128] = np.where(diff >= 0, gam[h] ** np.maximum(diff, 0), 0.0)
    lamq = np.zeros((128, DC * 128), dtype=np.float32)
    for s in range(DC):
        for half in range(2):
            h = 2 * s + half
            lamq[half * 64:(half + 1) * 64, s * 128:(s + 1) * 128] = \
                (gam[h] ** (ii + 1.0))[None, :]
    lamk = np.zeros((128, 16), dtype=np.float32)
    for h in range(16):
        lamk[:, h] = gam[h] ** (127.0 - ii)
    e2 = np.zeros((128, 2), dtype=np.float32)
    e2[0:64, 0] = 1.0
    e2[64:128, 1] = 1.0
    eb = np.zeros((2, 128), dtype=np.float32)
    eb[0, 0:64] = 1.0
    eb[1, 64:128] = 1.0
    gnw = _f32(inputs["L1_attn_gn_w"]).reshape(DC, 128).T.copy()
    gnb = _f32(inputs["L1_attn_gn_b"]).reshape(DC, 128).T.copy()
    wqT = _f32(np.asarray(inputs["L1_attn_wq"]).T)
    wkT = _f32(np.asarray(inputs["L1_attn_wk"]).T)
    wvT = _f32(np.asarray(inputs["L1_attn_wv"]).T)
    woT = _f32(np.asarray(inputs["L1_attn_wo"]).T)
    maps = []
    for cid in range(NCORE):
        b = cid // 4
        xq = _f32(x_full[b * SEQ:(b + 1) * SEQ])
        maps.append(dict(xq=xq, wqT=wqT, wkT=wkT, wvT=wvT, woT=woT,
                         gnw=gnw, gnb=gnb, ident=IDENT, dtm=dtm, lamq=lamq,
                         lamk=lamk, e2=e2, eb=eb))
    return maps


def run_l1a(x_full, inputs, nc_cache={}):
    if "l1a" not in nc_cache:
        nc_cache["l1a"] = build_l1a()
    res = run_bass_kernel_spmd(nc_cache["l1a"], _l1a_inmaps(x_full, inputs),
                               core_ids=list(range(NCORE)))
    return np.concatenate([res.results[0]["ys"], res.results[4]["ys"]], axis=0)



def _l2_slopes():
    def po2(m):
        start = 2.0 ** (-2.0 ** (-(np.log2(m) - 3)))
        return [start * start ** i for i in range(m)]
    s = np.array(po2(16), dtype=np.float64)
    s = s * (1.0 - 0.0 / 11.0 + 1e-5)
    return s


def build_l2a():
    p = Prog("l2a")
    xq = p.inp("xq", [SEQ, D], F32)
    wq_in = p.inp("wq", [D, D], F32)
    wk_in = p.inp("wk", [D, D], F32)
    wv_in = p.inp("wv", [D, D], F32)
    wo_in = p.inp("wo", [D, D], F32)
    wg1_in = p.inp("wg1", [D, 64], F32)
    wg2_in = p.inp("wg2", [64, D], F32)
    gamma_in = p.inp("gamma", [1, D], F32)
    beta_in = p.inp("beta", [1, D], F32)
    ident_in = p.inp("ident", [128, 128], F32)
    dt_in = p.inp("dtm", [128, 16 * 128], F32)
    lamq_in = p.inp("lamq", [128, DC * 128], F32)
    lamk_in = p.inp("lamk", [128, 16], F32)
    ys = p.outp("ys", [SEQ, D], F32)

    NH = 16
    s_h = _l2_slopes()
    lam_c = [float(np.exp(-s * 128.0)) for s in s_h]

    wA = p.sb("wA", [128, DC * D], BF16)
    wB = p.sb("wB", [128, DC * D], BF16)
    wC = p.sb("wC", [128, DC * D], BF16)
    wO = p.sb("wO", [128, DC * D], BF16)
    wg1 = p.sb("wg1_sb", [128, DC * 64], BF16)
    wg2 = p.sb("wg2_sb", [64, D], BF16)
    ident = p.sb("identb", [128, 128], BF16)
    identf = p.sb("identf", [128, 128], F32)
    dtm = p.sb("dtm_sb", [128, 16 * 128], F32)
    lamq = p.sb("lamq_sb", [128, DC * 128], BF16)
    lamk = p.sb("lamk_sb", [128, 16], F32)
    gamB = p.sb("gamB", [128, D], F32)
    betB = p.sb("betB", [128, D], F32)
    grow = p.sb("grow", [1, D], BF16)
    ones_r = p.sb("ones_r", [1, 128], BF16)
    r_c = p.sb("r_c", [128, D], F32)
    xn_c = p.sb("xn_c", [128, D], BF16)
    xnT_c = p.sb("xnT_c", [128, DC * 128], BF16)
    QTc = p.sb("QTc", [128, DC * 128], BF16)
    QLc = p.sb("QLc", [128, DC * 128], BF16)
    KTc = p.sb("KTc", [128, DC * 128], BF16)
    Ktm = p.sb("Ktm", [128, D], BF16)
    Klam = p.sb("Klam", [128, 64], BF16)
    Vc = p.sb("Vc", [128, D], BF16)
    St = p.sb("St", [128, 128], BF16)
    state = p.sb("state", [128, 512], F32)
    stateb = p.sb("stateb", [128, 512], BF16)
    oT = p.sb("oT", [128, D], F32)
    otm = p.sb("otm", [128, D], F32)
    g1T = p.sb("g1T", [64, 128], BF16)
    gate = p.sb("gate", [128, D], F32)
    og = p.sb("og", [128, D], BF16)
    ogT = p.sb("ogT", [128, DC * 128], BF16)
    outs = p.sb("outs", [128, D], F32)

    P1 = p.ps("P1", [128, 512], F32)
    P3 = p.ps("P3", [128, 1024], F32)
    P4 = p.ps("P4", [128, 1024], BF16)
    P4f = p.ps("P4f", [128, 512], F32)
    P5 = p.ps("P5", [128, 128], F32)

    p.begin()
    scr = ln_scratch(p)
    scr2 = ln_scratch(p, "_o")
    p.gdma(lambda e: e.dma_start(out=ident[:, :], in_=ident_in))
    p.dma(lambda e: e.dma_start(out=identf[:, :], in_=ident_in))
    p.dma(lambda e: e.dma_start(out=dtm[:, :], in_=dt_in))
    p.gdma(lambda e: e.dma_start(out=lamq[:, :], in_=lamq_in))
    p.dma(lambda e: e.dma_start(out=lamk[:, :], in_=lamk_in))
    for nm, dst, src in [("wA", wA, wq_in), ("wB", wB, wk_in), ("wC", wC, wv_in), ("wO", wO, wo_in)]:
        p.gdma(lambda e, dst=dst, src=src: e.dma_start(
            out=dst[:, :].rearrange("p (k n) -> p k n", k=DC),
            in_=src.rearrange("(k p) n -> p k n", p=128)))
    p.gdma(lambda e: e.dma_start(
        out=wg1[:, :].rearrange("p (k n) -> p k n", k=DC),
        in_=wg1_in.rearrange("(k p) n -> p k n", p=128)))
    p.gdma(lambda e: e.dma_start(out=wg2[:, :], in_=wg2_in))
    p.v(lambda e: e.memset(ones_r[:, :], 1.0))
    p.v(lambda e: e.memset(state[:, :], 0.0))
    p.v(lambda e: e.memset(stateb[:, :], 0.0))
    p.gdma(lambda e: e.dma_start(out=grow[:, :], in_=gamma_in))
    for half in range(2):
        p.t(lambda e, half=half: e.matmul(P1[:, :], ones_r[:, :],
                                          grow[0:1, half * 512:(half + 1) * 512],
                                          start=True, stop=True))
        p.v(lambda e, half=half: e.tensor_copy(gamB[:, half * 512:(half + 1) * 512], P1[:, :]))
    p.gdma(lambda e: e.dma_start(out=grow[:, :], in_=beta_in))
    for half in range(2):
        p.t(lambda e, half=half: e.matmul(P1[:, :], ones_r[:, :],
                                          grow[0:1, half * 512:(half + 1) * 512],
                                          start=True, stop=True))
        p.v(lambda e, half=half: e.tensor_copy(betB[:, half * 512:(half + 1) * 512], P1[:, :]))

    for c in range(SCH):
        p.dma(lambda e, c=c: e.dma_start(out=r_c[:, :], in_=xq[c * 128:(c + 1) * 128, :]))
        emit_ln(p, r_c[:, :], xn_c[:, :], scr)
        emit_transpose_cols(p, xn_c[:, :], DC, P4, ident,
                            xnT_c[:, :].rearrange("p (k t) -> p k t", k=DC))
        for m in range(DC):
            for k in range(DC):
                p.t(lambda e, m=m, k=k: e.matmul(
                    P1[:, 0:128], wA[:, k * D + m * 128:k * D + (m + 1) * 128],
                    xnT_c[:, k * 128:(k + 1) * 128], start=(k == 0), stop=(k == DC - 1)))
            p.s(lambda e, m=m: e.activation(QTc[:, m * 128:(m + 1) * 128], P1[:, 0:128], AF.Silu))
            p.v(lambda e, m=m: e.tensor_tensor(QLc[:, m * 128:(m + 1) * 128],
                                               QTc[:, m * 128:(m + 1) * 128],
                                               lamq[:, m * 128:(m + 1) * 128], ALU.mult))
        for m in range(DC):
            for k in range(DC):
                p.t(lambda e, m=m, k=k: e.matmul(
                    P1[:, 0:128], wB[:, k * D + m * 128:k * D + (m + 1) * 128],
                    xnT_c[:, k * 128:(k + 1) * 128], start=(k == 0), stop=(k == DC - 1)))
            p.s(lambda e, m=m: e.activation(KTc[:, m * 128:(m + 1) * 128], P1[:, 0:128], AF.Silu))
        for half in range(2):
            for k in range(DC):
                p.t(lambda e, half=half, k=k: e.matmul(
                    P1[:, :], xnT_c[:, k * 128:(k + 1) * 128],
                    wC[:, k * D + half * 512:k * D + (half + 1) * 512],
                    start=(k == 0), stop=(k == DC - 1)))
            p.s(lambda e, half=half: e.activation(Vc[:, half * 512:(half + 1) * 512],
                                                  P1[:, :], AF.Silu))
        for j in range(DC):
            p.t(lambda e, j=j: e.transpose(P4[:, j * 128:(j + 1) * 128],
                                           KTc[:, j * 128:(j + 1) * 128], ident[:, :]))
        p.v(lambda e: e.tensor_copy(Ktm[:, :], P4[:, :]))
        for h in range(NH):
            sl = h // 2
            po = (h % 2) * 64
            p.t(lambda e, sl=sl, po=po: e.matmul(
                P5[:, :], KTc[po:po + 64, sl * 128:(sl + 1) * 128],
                QTc[po:po + 64, sl * 128:(sl + 1) * 128], start=True, stop=True))
            p.v(lambda e, h=h: e.tensor_tensor(St[:, :], P5[:, :],
                                               dtm[:, h * 128:(h + 1) * 128], ALU.mult))
            p.t(lambda e, h=h, sl=sl, po=po: e.matmul(
                P3[po:po + 64, sl * 128:(sl + 1) * 128],
                Vc[:, h * 64:(h + 1) * 64], St[:, :], start=True, stop=False))
            p.t(lambda e, h=h, sl=sl, po=po: e.matmul(
                P3[po:po + 64, sl * 128:(sl + 1) * 128],
                stateb[po:po + 64, sl * 64:(sl + 1) * 64],
                QLc[po:po + 64, sl * 128:(sl + 1) * 128], start=False, stop=True))
            p.v(lambda e, h=h: e.tensor_scalar(Klam[:, :], Ktm[:, h * 64:(h + 1) * 64],
                                               lamk[:, h:h + 1], None, ALU.mult))
            p.t(lambda e, h=h: e.matmul(P5[0:64, 0:64], Klam[:, :],
                                        Vc[:, h * 64:(h + 1) * 64], start=True, stop=True))
            p.v(lambda e, h=h, sl=sl, po=po, lc=lam_c[h]: e.tensor_scalar(
                state[po:po + 64, sl * 64:(sl + 1) * 64],
                state[po:po + 64, sl * 64:(sl + 1) * 64], lc, None, ALU.mult))
            p.v(lambda e, sl=sl, po=po: e.tensor_tensor(
                state[po:po + 64, sl * 64:(sl + 1) * 64],
                state[po:po + 64, sl * 64:(sl + 1) * 64], P5[0:64, 0:64], ALU.add))
            p.v(lambda e, sl=sl, po=po: e.tensor_copy(
                stateb[po:po + 64, sl * 64:(sl + 1) * 64],
                state[po:po + 64, sl * 64:(sl + 1) * 64]))
        p.v(lambda e: e.tensor_copy(oT[:, :], P3[:, :]))
        for j in range(DC):
            half = j // 4
            jj = j % 4
            p.t(lambda e, j=j, half=half, jj=jj: e.transpose(
                P4f[:, jj * 128:(jj + 1) * 128], oT[:, j * 128:(j + 1) * 128], identf[:, :]))
            p.v(lambda e, j=j, half=half, jj=jj: e.tensor_copy(
                otm[:, :].rearrange("p (a b) -> p a b", a=DC)[:, j, :],
                P4f[:, jj * 128:(jj + 1) * 128]))
        emit_ln(p, otm[:, :], outs[:, :], scr2)
        for k in range(DC):
            p.t(lambda e, k=k: e.matmul(P5[0:64, :], wg1[:, k * 64:(k + 1) * 64],
                                        xnT_c[:, k * 128:(k + 1) * 128],
                                        start=(k == 0), stop=(k == DC - 1)))
        p.v(lambda e: e.tensor_copy(g1T[:, :], P5[0:64, :]))
        for half in range(2):
            p.t(lambda e, half=half: e.matmul(
                P1[:, :], g1T[:, :], wg2[:, half * 512:(half + 1) * 512],
                start=True, stop=True))
            p.s(lambda e, half=half: e.activation(gate[:, half * 512:(half + 1) * 512],
                                                  P1[:, :], AF.Sigmoid))
        p.v(lambda e: e.tensor_tensor(otm[:, :], outs[:, :], gamB[:, :], ALU.mult))
        p.v(lambda e: e.tensor_tensor(otm[:, :], otm[:, :], betB[:, :], ALU.add))
        p.v(lambda e: e.tensor_tensor(og[:, :], otm[:, :], gate[:, :], ALU.mult))
        for j in range(DC):
            p.t(lambda e, j=j: e.transpose(P4[:, j * 128:(j + 1) * 128],
                                           og[:, j * 128:(j + 1) * 128], ident[:, :]))
        p.v(lambda e: e.tensor_copy(ogT[:, :], P4[:, :]))
        p.dma(lambda e, c=c: e.dma_start(out=r_c[:, :], in_=xq[c * 128:(c + 1) * 128, :]))
        for half in range(2):
            for k in range(DC):
                p.t(lambda e, half=half, k=k: e.matmul(
                    P1[:, :], ogT[:, k * 128:(k + 1) * 128],
                    wO[:, k * D + half * 512:k * D + (half + 1) * 512],
                    start=(k == 0), stop=(k == DC - 1)))
            p.v(lambda e, half=half: e.tensor_tensor(
                outs[:, half * 512:(half + 1) * 512], P1[:, :],
                r_c[:, half * 512:(half + 1) * 512], ALU.add))
        p.dma(lambda e, c=c: e.dma_start(out=ys[c * 128:(c + 1) * 128, :], in_=outs[:, :]))

    return p.end()


def _l2a_inmaps(x_full, inputs):
    s_h = _l2_slopes()
    ii = np.arange(128)
    dtm = np.zeros((128, 16 * 128), dtype=np.float32)
    for h in range(16):
        diff = ii[None, :] - ii[:, None]
        dtm[:, h * 128:(h + 1) * 128] = np.where(
            diff >= 0, np.exp(-s_h[h] * np.maximum(diff, 0)), 0.0)
    lamq = np.zeros((128, DC * 128), dtype=np.float32)
    for s in range(DC):
        for half in range(2):
            h = 2 * s + half
            lamq[half * 64:(half + 1) * 64, s * 128:(s + 1) * 128] = \
                np.exp(-s_h[h] * (ii + 1.0))[None, :]
    lamk = np.zeros((128, 16), dtype=np.float32)
    for h in range(16):
        lamk[:, h] = np.exp(-s_h[h] * (127.0 - ii))
    maps = []
    for cid in range(NCORE):
        b = cid // 4
        maps.append(dict(
            xq=_f32(x_full[b * SEQ:(b + 1) * SEQ]),
            wq=_f32(inputs["L2_attn_Wq"]), wk=_f32(inputs["L2_attn_Wk"]),
            wv=_f32(inputs["L2_attn_Wv"]), wo=_f32(inputs["L2_attn_Wo"]),
            wg1=_f32(inputs["L2_attn_Wg1"]), wg2=_f32(inputs["L2_attn_Wg2"]),
            gamma=_f32(inputs["L2_attn_gamma"]).reshape(1, D),
            beta=_f32(inputs["L2_attn_beta"]).reshape(1, D),
            ident=IDENT, dtm=dtm, lamq=lamq, lamk=lamk))
    return maps


def run_l2a(x_full, inputs, nc_cache={}):
    if "l2a" not in nc_cache:
        nc_cache["l2a"] = build_l2a()
    res = run_bass_kernel_spmd(nc_cache["l2a"], _l2a_inmaps(x_full, inputs),
                               core_ids=list(range(NCORE)))
    return np.concatenate([res.results[0]["ys"], res.results[4]["ys"]], axis=0)



DH = 4096
DHC = DH // 128


def build_l2f():
    p = Prog("l2f")
    xs = p.inp("xs", [TOK, D], F32)
    waugT = p.inp("waugT", [DH, D + 1], F32)
    wcon = p.inp("wcon", [DH, D], F32)
    upclo = p.inp("upclo", [128, 7 * 4 * 2048], F32)
    upchi = p.inp("upchi", [128, 5 * 4 * 2048], F32)
    dnclo = p.inp("dnclo", [128, 7 * 4 * 2048], F32)
    dnhcols = p.inp("dnhcols", [128, 5 * 16 * 4], F32)
    upbias = p.inp("upbias", [1, DH], F32)
    bcon = p.inp("bcon", [1, D], F32)
    dbias = p.inp("dbias", [1, D], F32)
    eyeb_in = p.inp("eyeb", [128, DH], F32)
    ys = p.outp("ys", [TOK, D], F32)

    WU = p.sb("WU", [128, DC * DH], BF16)
    WC = p.sb("WC", [128, DHC * D], BF16)
    Ea = p.sb("Ea", [128, DH], BF16)
    Eb = p.sb("Eb", [128, DH], BF16)
    coefB = p.sb("coefB", [128, 4 * 2048], BF16)
    scrA = p.sb("scrA", [128, 2048], BF16)
    scrB = p.sb("scrB", [128, 2048], BF16)
    dnh = p.sb("dnh", [128, 5 * 16 * 4], F32)
    blo = p.sb("blo", [1, DH], BF16)
    becols = p.sb("becols", [128, DHC], F32)
    bcoB = p.sb("bcoB", [128, D], F32)
    ones_r = p.sb("ones_r", [1, 128], BF16)
    onef = p.sb("onef", [1, 1], BF16)
    r_c = p.sb("r_c", [128, D], F32)
    xn_c = p.sb("xn_c", [128, D], BF16)
    xnT_c = p.sb("xnT_c", [128, DC * 128], BF16)
    ident = p.sb("identb", [128, 128], BF16)
    hT = p.sb("hT", [128, DHC * 128], BF16)
    outs = p.sb("outs", [128, D], F32)
    wslab = hT[:, 1024:1024 + D + 1]
    chscr = hT[:, 0:1024]

    P1 = p.ps("P1", [128, 512], F32)
    P2 = p.ps("P2", [128, 128], F32)
    P4 = p.ps("P4", [128, 1024], BF16)
    PB = p.ps("PB", [1, 128], F32)
    PC = p.ps("PC", [128, 32], F32)

    p.begin()
    scr = ln_scratch(p)
    p.gdma(lambda e: e.dma_start(out=ident[:, :], in_=eyeb_in.ap()[0:128, 0:128] if False else eyeb_in[0:128, 0:128]))
    p.dma(lambda e: e.dma_start(out=dnh[:, :], in_=dnhcols))
    p.v(lambda e: e.memset(ones_r[:, :], 1.0))
    p.v(lambda e: e.memset(onef[:, :], 1.0))

    def apply_stages_tile(dst_pair, coefs_in, nstages, stage_sizes):
        src, dst = dst_pair
        for st in range(nstages):
            s = stage_sizes[st]
            nb = DH // (2 * s)
            base = st * 4 * 2048
            p.gdma(lambda e, base=base: e.dma_start(
                out=coefB[:, 0:4096], in_=coefs_in[:, base:base + 2 * 2048]))
            av = src[:, :].rearrange("p (nb two s) -> p nb two s", two=2, s=s)[:, :, 0, :]
            bv = src[:, :].rearrange("p (nb two s) -> p nb two s", two=2, s=s)[:, :, 1, :]
            dav = dst[:, :].rearrange("p (nb two s) -> p nb two s", two=2, s=s)[:, :, 0, :]
            dbv = dst[:, :].rearrange("p (nb two s) -> p nb two s", two=2, s=s)[:, :, 1, :]
            w00 = coefB[:, 0:2048].rearrange("p (nb s) -> p nb s", s=s)
            w01 = coefB[:, 2048:4096].rearrange("p (nb s) -> p nb s", s=s)
            scv = scrA[:, :].rearrange("p (nb s) -> p nb s", s=s)
            p.v(lambda e, av=av, w00=w00, dav=dav: e.tensor_tensor(dav, av, w00, ALU.mult))
            p.v(lambda e, bv=bv, w01=w01, scv=scv: e.tensor_tensor(scv, bv, w01, ALU.mult))
            p.v(lambda e, dav=dav, scv=scv: e.tensor_tensor(dav, dav, scv, ALU.add))
            p.gdma(lambda e, base=base: e.dma_start(
                out=coefB[:, 0:4096], in_=coefs_in[:, base + 2 * 2048:base + 4 * 2048]))
            w10 = coefB[:, 0:2048].rearrange("p (nb s) -> p nb s", s=s)
            w11 = coefB[:, 2048:4096].rearrange("p (nb s) -> p nb s", s=s)
            p.v(lambda e, av=av, w10=w10, dbv=dbv: e.tensor_tensor(dbv, av, w10, ALU.mult))
            p.v(lambda e, bv=bv, w11=w11, scv=scv: e.tensor_tensor(scv, bv, w11, ALU.mult))
            p.v(lambda e, dbv=dbv, scv=scv: e.tensor_tensor(dbv, dbv, scv, ALU.add))
            src, dst = dst, src
        return src

    p.gdma(lambda e: e.dma_start(
        out=WC[:, :].rearrange("p (c n) -> p c n", c=DHC),
        in_=wcon.rearrange("(c p) n -> p c n", p=128)))
    for si, st in enumerate(range(11, 6, -1)):
        S = 2 ** st
        dch = S // 128
        pairs = [c for c in range(DHC) if (c // dch) % 2 == 0]
        for pi, c in enumerate(pairs):
            cb = c + dch
            colbase = (si * 16 + pi) * 4
            a = WC[:, c * D:(c + 1) * D]
            b = WC[:, cb * D:(cb + 1) * D]
            w00 = dnh[:, colbase + 0:colbase + 1]
            w01 = dnh[:, colbase + 1:colbase + 2]
            w10 = dnh[:, colbase + 2:colbase + 3]
            w11 = dnh[:, colbase + 3:colbase + 4]
            p.v(lambda e, a=a: e.tensor_copy(chscr[:, :], a))
            p.v(lambda e, a=a, w00=w00: e.tensor_scalar(a, a, w00, None, ALU.mult))
            p.v(lambda e, a=a, b=b, w10=w10: e.scalar_tensor_tensor(a, b, w10, a, ALU.mult, ALU.add))
            p.v(lambda e, b=b, w11=w11: e.tensor_scalar(b, b, w11, None, ALU.mult))
            p.v(lambda e, b=b, w01=w01: e.scalar_tensor_tensor(b, chscr[:, :], w01, b, ALU.mult, ALU.add))
    p.gdma(lambda e: e.dma_start(out=Ea[:, :], in_=eyeb_in))
    e2t = apply_stages_tile((Ea, Eb), dnclo, 7, [64, 32, 16, 8, 4, 2, 1])
    for g in range(DHC):
        for half in range(2):
            p.t(lambda e, g=g, half=half: e.matmul(
                P1[:, :], e2t[:, g * 128:(g + 1) * 128],
                WC[:, g * D + half * 512:g * D + (half + 1) * 512],
                start=True, stop=True))
            p.v(lambda e, g=g, half=half: e.tensor_copy(
                WC[:, g * D + half * 512:g * D + (half + 1) * 512], P1[:, :]))

    p.gdma(lambda e: e.dma_start(out=Ea[:, :], in_=eyeb_in))
    eut = apply_stages_tile((Ea, Eb), upclo, 7, [1, 2, 4, 8, 16, 32, 64])
    for g in range(DHC):
        p.gdma(lambda e, g=g: e.dma_start(out=wslab[:, :], in_=waugT[g * 128:(g + 1) * 128, :]))
        for rc in range(DC):
            p.t(lambda e, g=g, rc=rc: e.matmul(
                P2[:, :], wslab[:, rc * 128:(rc + 1) * 128],
                eut[:, g * 128:(g + 1) * 128], start=True, stop=True))
            p.v(lambda e, g=g, rc=rc: e.tensor_copy(
                WU[:, rc * DH + g * 128:rc * DH + (g + 1) * 128], P2[:, :]))
        p.t(lambda e, g=g: e.matmul(PB[:, :], wslab[:, D:D + 1],
                                    eut[:, g * 128:(g + 1) * 128], start=True, stop=True))
        p.v(lambda e, g=g: e.tensor_copy(blo[0:1, g * 128:(g + 1) * 128], PB[0:1, :]))
    for si, st in enumerate(range(7, 12)):
        s = 2 ** st
        nb = DH // (2 * s)
        base = si * 4 * 2048
        p.gdma(lambda e, base=base: e.dma_start(
            out=coefB[:, :], in_=upchi[:, base:base + 4 * 2048]))
        w00 = coefB[:, 0:2048].rearrange("p (nb s) -> p nb s", s=s)
        w01 = coefB[:, 2048:4096].rearrange("p (nb s) -> p nb s", s=s)
        w10 = coefB[:, 4096:6144].rearrange("p (nb s) -> p nb s", s=s)
        w11 = coefB[:, 6144:8192].rearrange("p (nb s) -> p nb s", s=s)
        scv = scrA[:, :].rearrange("p (nb s) -> p nb s", s=s)
        sc2 = scrB[:, :].rearrange("p (nb s) -> p nb s", s=s)
        for rc in range(DC):
            sl = WU[:, rc * DH:(rc + 1) * DH]
            av = sl.rearrange("p (nb two s) -> p nb two s", two=2, s=s)[:, :, 0, :]
            bv = sl.rearrange("p (nb two s) -> p nb two s", two=2, s=s)[:, :, 1, :]
            p.v(lambda e, av=av, scv=scv: e.tensor_copy(scv, av))
            p.v(lambda e, av=av, w00=w00: e.tensor_tensor(av, av, w00, ALU.mult))
            p.v(lambda e, bv=bv, w01=w01, sc2=sc2: e.tensor_tensor(sc2, bv, w01, ALU.mult))
            p.v(lambda e, av=av, sc2=sc2: e.tensor_tensor(av, av, sc2, ALU.add))
            p.v(lambda e, bv=bv, w11=w11: e.tensor_tensor(bv, bv, w11, ALU.mult))
            p.v(lambda e, scv=scv, w10=w10, sc2=sc2: e.tensor_tensor(sc2, scv, w10, ALU.mult))
            p.v(lambda e, bv=bv, sc2=sc2: e.tensor_tensor(bv, bv, sc2, ALU.add))
        rav = blo[0:1, :].rearrange("p (nb two s) -> p nb two s", two=2, s=s)[:, :, 0, :]
        rbv = blo[0:1, :].rearrange("p (nb two s) -> p nb two s", two=2, s=s)[:, :, 1, :]
        rw00 = coefB[0:1, 0:2048].rearrange("p (nb s) -> p nb s", s=s)
        rw01 = coefB[0:1, 2048:4096].rearrange("p (nb s) -> p nb s", s=s)
        rw10 = coefB[0:1, 4096:6144].rearrange("p (nb s) -> p nb s", s=s)
        rw11 = coefB[0:1, 6144:8192].rearrange("p (nb s) -> p nb s", s=s)
        r1 = scrA[0:1, :].rearrange("p (nb s) -> p nb s", s=s)
        r2 = scrB[0:1, :].rearrange("p (nb s) -> p nb s", s=s)
        p.v(lambda e, rav=rav, r1=r1: e.tensor_copy(r1, rav))
        p.v(lambda e, rav=rav, rw00=rw00: e.tensor_tensor(rav, rav, rw00, ALU.mult))
        p.v(lambda e, rbv=rbv, rw01=rw01, r2=r2: e.tensor_tensor(r2, rbv, rw01, ALU.mult))
        p.v(lambda e, rav=rav, r2=r2: e.tensor_tensor(rav, rav, r2, ALU.add))
        p.v(lambda e, rbv=rbv, rw11=rw11: e.tensor_tensor(rbv, rbv, rw11, ALU.mult))
        p.v(lambda e, r1=r1, rw10=rw10, r2=r2: e.tensor_tensor(r2, r1, rw10, ALU.mult))
        p.v(lambda e, rbv=rbv, r2=r2: e.tensor_tensor(rbv, rbv, r2, ALU.add))
    upb = Ea[0:1, 0:DH]
    p.gdma(lambda e: e.dma_start(out=upb, in_=upbias))
    p.v(lambda e: e.tensor_tensor(blo[0:1, :], blo[0:1, :], upb, ALU.add))
    for g in range(DHC):
        p.t(lambda e, g=g: e.matmul(PC[:, g:g + 1], blo[0:1, g * 128:(g + 1) * 128],
                                    onef[0:1, 0:1], start=True, stop=True))
    p.v(lambda e: e.tensor_copy(becols[:, :], PC[:, :]))
    brow_d = Ea[32:33, 0:D]
    brow_d2 = Ea[32:33, D:2 * D]
    p.gdma(lambda e: e.dma_start(out=brow_d, in_=bcon))
    p.gdma(lambda e: e.dma_start(out=brow_d2, in_=dbias))
    p.v(lambda e: e.tensor_tensor(brow_d, brow_d, brow_d2, ALU.add))
    p.v(lambda e: e.tensor_copy(blo[0:1, 0:D], brow_d))
    for half in range(2):
        p.t(lambda e, half=half: e.matmul(P1[:, :], ones_r[:, :],
                                          blo[0:1, half * 512:(half + 1) * 512],
                                          start=True, stop=True))
        p.v(lambda e, half=half: e.tensor_copy(bcoB[:, half * 512:(half + 1) * 512], P1[:, :]))

    for c in range(NCH):
        p.dma(lambda e, c=c: e.dma_start(out=r_c[:, :], in_=xs[c * 128:(c + 1) * 128, :]))
        emit_ln(p, r_c[:, :], xn_c[:, :], scr)
        emit_transpose_cols(p, xn_c[:, :], DC, P4, ident,
                            xnT_c[:, :].rearrange("p (k t) -> p k t", k=DC))
        for oc in range(DHC):
            for k in range(DC):
                p.t(lambda e, oc=oc, k=k: e.matmul(
                    P2[:, :], WU[:, k * DH + oc * 128:k * DH + (oc + 1) * 128],
                    xnT_c[:, k * 128:(k + 1) * 128], start=(k == 0), stop=(k == DC - 1)))
            p.s(lambda e, oc=oc: e.activation(hT[:, oc * 128:(oc + 1) * 128], P2[:, :],
                                              AF.Gelu, bias=becols[:, oc:oc + 1]))
        for half in range(2):
            for k in range(DHC):
                p.t(lambda e, half=half, k=k: e.matmul(
                    P1[:, :], hT[:, k * 128:(k + 1) * 128],
                    WC[:, k * D + half * 512:k * D + (half + 1) * 512],
                    start=(k == 0), stop=(k == DHC - 1)))
            p.v(lambda e, half=half: e.tensor_tensor(
                outs[:, half * 512:(half + 1) * 512], P1[:, :],
                bcoB[:, half * 512:(half + 1) * 512], ALU.add))
            p.v(lambda e, half=half: e.tensor_tensor(
                outs[:, half * 512:(half + 1) * 512], outs[:, half * 512:(half + 1) * 512],
                r_c[:, half * 512:(half + 1) * 512], ALU.add))
        p.dma(lambda e, c=c: e.dma_start(out=ys[c * 128:(c + 1) * 128, :], in_=outs[:, :]))

    return p.end()


def _l2f_hostprep(inputs):
    W_exp = _f32(inputs["L2_ffn_W_exp"])
    b_exp = _f32(inputs["L2_ffn_b_exp"])
    up_w = _f32(inputs["L2_ffn_up_weights"])
    dn_w = _f32(inputs["L2_ffn_down_weights"])
    waug = np.concatenate([W_exp, b_exp.reshape(1, DH)], axis=0)
    waugT = np.ascontiguousarray(waug.T)

    def coef_tiles(w, stage, swap):
        s = 2 ** stage
        nb = DH // (2 * s)
        wr = w[stage].reshape(nb, s, 2, 2)
        w00 = wr[:, :, 0, 0].reshape(-1)
        w01 = wr[:, :, 0, 1].reshape(-1)
        w10 = wr[:, :, 1, 0].reshape(-1)
        w11 = wr[:, :, 1, 1].reshape(-1)
        if swap:
            w01, w10 = w10, w01
        out = np.concatenate([w00, w01, w10, w11])
        return np.broadcast_to(out[None, :], (128, 4 * 2048)).astype(np.float32)

    upclo = np.concatenate([coef_tiles(up_w, st, False) for st in range(7)], axis=1)
    upchi = np.concatenate([coef_tiles(up_w, st, False) for st in range(7, 12)], axis=1)
    dnclo = np.concatenate([coef_tiles(dn_w, st, True) for st in range(6, -1, -1)], axis=1)
    dnh = np.zeros((128, 5 * 16 * 4), dtype=np.float32)
    pp = np.arange(128)
    for si, st in enumerate(range(11, 6, -1)):
        S = 2 ** st
        dch = S // 128
        pairs = [cc for cc in range(DHC) if (cc // dch) % 2 == 0]
        wr = dn_w[st].reshape(DH // (2 * S), S, 2, 2)
        for pi, cc in enumerate(pairs):
            i = cc * 128 + pp
            kidx = (i // (2 * S)) * S + (i % S)
            cb = (si * 16 + pi) * 4
            q, r = kidx // S, kidx % S
            dnh[:, cb + 0] = wr[q, r, 0, 0]
            dnh[:, cb + 1] = wr[q, r, 0, 1]
            dnh[:, cb + 2] = wr[q, r, 1, 0]
            dnh[:, cb + 3] = wr[q, r, 1, 1]
        del wr
    eyeb = np.zeros((128, DH), dtype=np.float32)
    for g in range(DHC):
        eyeb[:, g * 128:(g + 1) * 128] = IDENT
    return dict(waugT=waugT, wcon=_f32(inputs["L2_ffn_W_con"]), upclo=upclo,
                upchi=upchi, dnclo=dnclo, dnhcols=dnh,
                upbias=_f32(inputs["L2_ffn_up_bias"]).reshape(1, DH),
                bcon=_f32(inputs["L2_ffn_b_con"]).reshape(1, D),
                dbias=_f32(inputs["L2_ffn_down_bias"]).reshape(1, D),
                eyeb=eyeb)


def run_l2f(x_full, inputs, nc_cache={}):
    if "l2f" not in nc_cache:
        nc_cache["l2f"] = build_l2f()
    hp = _l2f_hostprep(inputs)
    maps = [dict(xs=_f32(x_full[cid * TOK:(cid + 1) * TOK]), **hp) for cid in range(NCORE)]
    res = run_bass_kernel_spmd(nc_cache["l2f"], maps, core_ids=list(range(NCORE)))
    return np.concatenate([res.results[i]["ys"] for i in range(NCORE)], axis=0)



X3CH = NCH + 2
LNH = 8
LHD = 64


def build_l3a():
    p = Prog("l3a")
    xh = p.inp("xh", [X3CH * 128, D], F32)
    xq = p.inp("xq", [SEQ, D], F32)
    wq_in = p.inp("wq", [D, D], F32)
    wk_in = p.inp("wk", [D, D], F32)
    wv_in = p.inp("wv", [D, D], F32)
    wgq_in = p.inp("wgq", [D, 512], F32)
    wgk_in = p.inp("wgk", [D, 512], F32)
    wgv_in = p.inp("wgv", [D, 512], F32)
    rout_in = p.inp("rout", [512, D], F32)
    wout_in = p.inp("wout", [D, D], F32)
    glbd_in = p.inp("glbd", [D, 16], F32)
    grbd_in = p.inp("grbd", [D, 16], F32)
    ident_in = p.inp("ident", [128, 128], F32)
    tri3_in = p.inp("tri3", [128, 2 * 128], F32)
    e16_in = p.inp("e16", [128, 8 * 16], F32)
    e8_in = p.inp("e8", [128, 4 * 8], F32)
    sel16_in = p.inp("sel16", [16, 16 * 128], F32)
    sel64_in = p.inp("sel64", [16, 16 * 64], F32)
    sel8_in = p.inp("sel8", [8, 8 * 128], F32)
    sel8b_in = p.inp("sel8b", [8, 8 * 64], F32)
    kill01_in = p.inp("kill01", [1, NCH * 3], F32)
    lmask_in = p.inp("lmask", [64, NCH], F32)
    ys = p.outp("ys", [TOK, D], F32)

    NH = 16
    KW3 = X3CH * 128

    wBig1 = p.sb("wBig1", [128, DC * D], BF16)
    wBig2 = p.sb("wBig2", [128, DC * D], BF16)
    wSm1 = p.sb("wSm1", [128, DC * 512], BF16)
    wSm2 = p.sb("wSm2", [128, 4 * D], BF16)
    glbd = p.sb("glbd_sb", [128, DC * 16], BF16)
    grbd = p.sb("grbd_sb", [128, DC * 16], BF16)
    ident = p.sb("identb", [128, 128], BF16)
    tri3 = p.sb("tri3_sb", [128, 2 * 128], F32)
    e16 = p.sb("e16_sb", [128, 8 * 16], BF16)
    e8 = p.sb("e8_sb", [128, 4 * 8], BF16)
    sel16 = p.sb("sel16_sb", [16, 16 * 128], BF16)
    sel64 = p.sb("sel64_sb", [16, 16 * 64], BF16)
    sel8 = p.sb("sel8_sb", [8, 8 * 128], BF16)
    sel8b = p.sb("sel8b_sb", [8, 8 * 64], BF16)
    kill01 = p.sb("kill01_sb", [1, NCH * 3], F32)
    lmask = p.sb("lmask_sb", [64, NCH], F32)
    killrow = p.sb("killrow", [1, 3 * 128], BF16)
    neg_row = p.sb("neg_row", [1, 128], F32)
    ones_c = p.sb("ones_c", [128, 1], BF16)
    ones_r = p.sb("ones_r", [1, 128], BF16)

    KT = p.sb("KT", [128, DC * KW3], BF16)
    V = p.sb("V", [128, X3CH * D], BF16)
    QTc = p.sb("QTc", [128, DC * 128], BF16)
    GQTc = p.sb("GQTc", [128, 4 * 128], BF16)
    pooledT = p.sb("pooledT", [128, DC * 64], BF16)
    GKT = p.sb("GKT", [128, 4 * 64], BF16)
    GV = p.sb("GV", [64, 512], BF16)
    r_c = p.sb("r_c", [128, D], F32)
    xn_c = p.sb("xn_c", [128, D], BF16)
    xnT_c = p.sb("xnT_c", [128, DC * 128], BF16)
    sqs = p.sb("sqs", [128, 128], BF16)
    qn2c = p.sb("qn2c", [16, 128], F32)
    gqn2c = p.sb("gqn2c", [8, 128], F32)
    kn2m = p.sb("kn2m", [16, 1], F32)
    kn2c = p.sb("kn2c", [16, 1], F32)
    gkn2m = p.sb("gkn2m", [8, 1], F32)
    qbias_c = p.sb("qbias_c", [16, 128], BF16)
    gqbias_c = p.sb("gqbias_c", [8, 128], BF16)
    PT = p.sb("PT", [128, 3 * 128], BF16)
    PTl = p.sb("PTl", [64, 128], BF16)
    rS = p.sb("rS", [128, 16], F32)
    rSl = p.sb("rSl", [128, 8], F32)
    rSb = p.sb("rSb", [128, 16], BF16)
    rT = p.sb("rT", [16, 128], BF16)
    rB = p.sb("rB", [128, 128], F32)
    glT = p.sb("glT", [16, 128], BF16)
    grT = p.sb("grT", [16, 128], BF16)
    locT = p.sb("locT", [128, D], F32)
    remT = p.sb("remT", [128, 512], BF16)
    cmbT = p.sb("cmbT", [128, D], BF16)
    rem2s = p.sb("rem2s", [128, 128], F32)
    cmbF = p.sb("cmbF", [128, 128], F32)
    outs = p.sb("outs", [128, D], F32)

    P1 = p.ps("P1", [128, 512], F32)
    P2 = p.ps("P2", [128, 384], F32)
    P3 = p.ps("P3", [128, 1024], F32)
    P4 = p.ps("P4", [128, 1024], BF16)
    P5 = p.ps("P5", [128, 128], F32)
    P6 = p.ps("P6", [16, 128], BF16)

    p.begin()
    scr = ln_scratch(p)
    for dst, src in [(ident, ident_in), (e16, e16_in), (e8, e8_in), (sel16, sel16_in),
                     (sel64, sel64_in), (sel8, sel8_in), (sel8b, sel8b_in),
                     (glbd, glbd_in), (grbd, grbd_in)]:
        p.gdma(lambda e, dst=dst, src=src: e.dma_start(out=dst[:, :], in_=src))
    p.dma(lambda e: e.dma_start(out=tri3[:, :], in_=tri3_in))
    p.dma(lambda e: e.dma_start(out=kill01[:, :], in_=kill01_in))
    p.dma(lambda e: e.dma_start(out=lmask[:, :], in_=lmask_in))
    p.v(lambda e: e.memset(ones_c[:, :], 1.0))
    p.v(lambda e: e.memset(ones_r[:, :], 1.0))
    p.v(lambda e: e.memset(neg_row[:, :], -1e30))
    p.v(lambda e: e.memset(kn2m[:, :], 0.0))
    p.v(lambda e: e.memset(gkn2m[:, :], 0.0))

    p.gdma(lambda e: e.dma_start(
        out=wSm1[:, :].rearrange("p (k n) -> p k n", k=DC),
        in_=wgk_in.rearrange("(k p) n -> p k n", p=128)))
    p.gdma(lambda e: e.dma_start(
        out=wSm2[:, 0:DC * 512].rearrange("p (k n) -> p k n", k=DC),
        in_=wgv_in.rearrange("(k p) n -> p k n", p=128)))
    for c in range(SCH):
        p.dma(lambda e, c=c: e.dma_start(out=r_c[:, :], in_=xq[c * 128:(c + 1) * 128, :]))
        emit_ln(p, r_c[:, :], xn_c[:, :], scr)
        for s in range(DC):
            p.t(lambda e, s=s: e.matmul(P5[:, s:s + 1], xn_c[:, s * 128:(s + 1) * 128],
                                        ones_c[:, :], start=True, stop=True))
        p.s(lambda e, c=c: e.activation(
            pooledT[:, :].rearrange("p (s l) -> p s l", s=DC)[:, :, c % 64:c % 64 + 1],
            P5[:, 0:DC], AF.Copy, scale=1.0 / 128.0))
    for m in range(4):
        for k in range(DC):
            p.t(lambda e, m=m, k=k: e.matmul(
                P5[:, 0:64], wSm1[:, k * 512 + m * 128:k * 512 + (m + 1) * 128],
                pooledT[:, k * 64:(k + 1) * 64], start=(k == 0), stop=(k == DC - 1)))
        p.v(lambda e, m=m: e.tensor_copy(GKT[:, m * 64:(m + 1) * 64], P5[:, 0:64]))
        p.s(lambda e, m=m: e.activation(sqs[:, 0:64], GKT[:, m * 64:(m + 1) * 64], AF.Square))
        p.t(lambda e, m=m: e.matmul(P5[0:8, 64:128], e8[:, m * 8:(m + 1) * 8],
                                    sqs[:, 0:64], start=(m == 0), stop=(m == 3)))
    p.v(lambda e: e.reduce_max(gkn2m[:, :], P5[0:8, 64:128], AX.X))
    for k in range(DC):
        p.t(lambda e, k=k: e.matmul(
            P1[0:64, :], pooledT[:, k * 64:(k + 1) * 64],
            wSm2[:, k * 512:(k + 1) * 512], start=(k == 0), stop=(k == DC - 1)))
    p.v(lambda e: e.tensor_copy(GV[:, :], P1[0:64, :]))

    p.gdma(lambda e: e.dma_start(
        out=wBig1[:, :].rearrange("p (k n) -> p k n", k=DC),
        in_=wk_in.rearrange("(k p) n -> p k n", p=128)))
    p.gdma(lambda e: e.dma_start(
        out=wBig2[:, :].rearrange("p (k n) -> p k n", k=DC),
        in_=wv_in.rearrange("(k p) n -> p k n", p=128)))
    for c in range(X3CH):
        p.dma(lambda e, c=c: e.dma_start(out=r_c[:, :], in_=xh[c * 128:(c + 1) * 128, :]))
        emit_ln(p, r_c[:, :], xn_c[:, :], scr)
        emit_transpose_cols(p, xn_c[:, :], DC, P4, ident,
                            xnT_c[:, :].rearrange("p (k t) -> p k t", k=DC))
        for m in range(DC):
            for k in range(DC):
                p.t(lambda e, m=m, k=k: e.matmul(
                    P1[:, 0:128], wBig1[:, k * D + m * 128:k * D + (m + 1) * 128],
                    xnT_c[:, k * 128:(k + 1) * 128], start=(k == 0), stop=(k == DC - 1)))
            p.v(lambda e, m=m, c=c: e.tensor_copy(
                KT[:, m * KW3 + c * 128:m * KW3 + (c + 1) * 128], P1[:, 0:128]))
            p.s(lambda e, m=m, c=c: e.activation(
                sqs[:, :], KT[:, m * KW3 + c * 128:m * KW3 + (c + 1) * 128], AF.Square))
            p.t(lambda e, m=m: e.matmul(P5[0:16, :], e16[:, m * 16:(m + 1) * 16],
                                        sqs[:, :], start=(m == 0), stop=(m == DC - 1)))
        p.v(lambda e: e.reduce_max(kn2c[:, :], P5[0:16, :], AX.X))
        p.v(lambda e: e.tensor_tensor(kn2m[:, :], kn2m[:, :], kn2c[:, :], ALU.max))
        for half in range(2):
            for k in range(DC):
                p.t(lambda e, half=half, k=k: e.matmul(
                    P1[:, :], xnT_c[:, k * 128:(k + 1) * 128],
                    wBig2[:, k * D + half * 512:k * D + (half + 1) * 512],
                    start=(k == 0), stop=(k == DC - 1)))
            p.v(lambda e, half=half, c=c: e.tensor_copy(
                V[:, c * D + half * 512:c * D + (half + 1) * 512], P1[:, :]))

    p.gdma(lambda e: e.dma_start(
        out=wBig1[:, :].rearrange("p (k n) -> p k n", k=DC),
        in_=wq_in.rearrange("(k p) n -> p k n", p=128)))
    p.gdma(lambda e: e.dma_start(
        out=wBig2[:, :].rearrange("p (k n) -> p k n", k=DC),
        in_=wout_in.rearrange("(k p) n -> p k n", p=128)))
    p.gdma(lambda e: e.dma_start(
        out=wSm1[:, :].rearrange("p (k n) -> p k n", k=DC),
        in_=wgq_in.rearrange("(k p) n -> p k n", p=128)))
    p.gdma(lambda e: e.dma_start(
        out=wSm2[:, 0:4 * D].rearrange("p (k n) -> p k n", k=4),
        in_=rout_in.rearrange("(k p) n -> p k n", p=128)))

    for oc in range(NCH):
        hc = oc + 2
        p.dma(lambda e, hc=hc: e.dma_start(out=r_c[:, :], in_=xh[hc * 128:(hc + 1) * 128, :]))
        emit_ln(p, r_c[:, :], xn_c[:, :], scr)
        emit_transpose_cols(p, xn_c[:, :], DC, P4, ident,
                            xnT_c[:, :].rearrange("p (k t) -> p k t", k=DC))
        for m in range(DC):
            for k in range(DC):
                p.t(lambda e, m=m, k=k: e.matmul(
                    P1[:, 0:128], wBig1[:, k * D + m * 128:k * D + (m + 1) * 128],
                    xnT_c[:, k * 128:(k + 1) * 128], start=(k == 0), stop=(k == DC - 1)))
            p.s(lambda e, m=m: e.activation(QTc[:, m * 128:(m + 1) * 128], P1[:, 0:128],
                                            AF.Copy, scale=1.0 / 8.0))
            p.s(lambda e, m=m: e.activation(sqs[:, :], QTc[:, m * 128:(m + 1) * 128], AF.Square))
            p.t(lambda e, m=m: e.matmul(P5[0:16, :], e16[:, m * 16:(m + 1) * 16],
                                        sqs[:, :], start=(m == 0), stop=(m == DC - 1)))
        p.v(lambda e: e.tensor_copy(qn2c[:, :], P5[0:16, :]))
        p.v(lambda e: e.tensor_scalar(qn2c[:, :], qn2c[:, :], kn2m[0:16, 0:1],
                                      1.0, ALU.mult, ALU.mult))
        p.s(lambda e: e.activation(qn2c[:, :], qn2c[:, :], AF.Sqrt))
        p.v(lambda e: e.tensor_scalar(qbias_c[:, :], qn2c[:, :], -1.0, 50.0, ALU.mult, ALU.add))
        for k in range(DC):
            p.t(lambda e, k=k: e.matmul(P5[0:16, :], glbd[:, k * 16:(k + 1) * 16],
                                        QTc[:, k * 128:(k + 1) * 128],
                                        start=(k == 0), stop=(k == DC - 1)))
        p.s(lambda e: e.activation(glT[:, :], P5[0:16, :], AF.Sigmoid, scale=8.0))
        for k in range(DC):
            p.t(lambda e, k=k: e.matmul(P5[0:16, :], grbd[:, k * 16:(k + 1) * 16],
                                        QTc[:, k * 128:(k + 1) * 128],
                                        start=(k == 0), stop=(k == DC - 1)))
        p.s(lambda e: e.activation(grT[:, :], P5[0:16, :], AF.Sigmoid, scale=8.0))
        for m in range(4):
            for k in range(DC):
                p.t(lambda e, m=m, k=k: e.matmul(
                    P1[:, 0:128], wSm1[:, k * 512 + m * 128:k * 512 + (m + 1) * 128],
                    xnT_c[:, k * 128:(k + 1) * 128], start=(k == 0), stop=(k == DC - 1)))
            p.s(lambda e, m=m: e.activation(GQTc[:, m * 128:(m + 1) * 128], P1[:, 0:128],
                                            AF.Copy, scale=1.0 / 8.0))
            p.s(lambda e, m=m: e.activation(sqs[:, :], GQTc[:, m * 128:(m + 1) * 128], AF.Square))
            p.t(lambda e, m=m: e.matmul(P5[0:8, :], e8[:, m * 8:(m + 1) * 8],
                                        sqs[:, :], start=(m == 0), stop=(m == 3)))
        p.v(lambda e: e.tensor_copy(gqn2c[:, :], P5[0:8, :]))
        p.v(lambda e: e.tensor_scalar(gqn2c[:, :], gqn2c[:, :], gkn2m[0:8, 0:1],
                                      1.0, ALU.mult, ALU.mult))
        p.s(lambda e: e.activation(gqn2c[:, :], gqn2c[:, :], AF.Sqrt))
        p.v(lambda e: e.tensor_scalar(gqbias_c[:, :], gqn2c[:, :], -1.0, 50.0, ALU.mult, ALU.add))
        for di in range(3):
            p.v(lambda e, di=di, oc=oc: e.tensor_scalar(
                killrow[0:1, di * 128:(di + 1) * 128], neg_row[0:1, :],
                kill01[0:1, oc * 3 + di:oc * 3 + di + 1], None, ALU.mult))
        for h in range(NH):
            sl = h // 2
            po = (h % 2) * 64
            for di in range(3):
                kc = hc + di - 2
                p.t(lambda e, di=di, kc=kc, sl=sl, po=po: e.matmul(
                    P2[:, di * 128:(di + 1) * 128],
                    KT[po:po + 64, sl * KW3 + kc * 128:sl * KW3 + (kc + 1) * 128],
                    QTc[po:po + 64, sl * 128:(sl + 1) * 128], start=True, stop=False))
                p.t(lambda e, di=di, h=h: e.matmul(
                    P2[:, di * 128:(di + 1) * 128], sel16[:, h * 128:(h + 1) * 128],
                    qbias_c[:, :], start=False, stop=False))
                p.t(lambda e, di=di: e.matmul(
                    P2[:, di * 128:(di + 1) * 128],
                    killrow[0:1, di * 128:(di + 1) * 128], ones_r[:, :],
                    start=False, stop=True))
            p.v(lambda e: e.tensor_tensor(P2[:, 0:128], P2[:, 0:128], tri3[:, 0:128], ALU.add))
            p.v(lambda e: e.tensor_tensor(P2[:, 256:384], P2[:, 256:384], tri3[:, 128:256], ALU.add))
            p.s(lambda e: e.activation(PT[:, :], P2[:, :], AF.Exp))
            for di in range(3):
                p.t(lambda e, di=di: e.matmul(P5[:, 0:1], PT[:, di * 128:(di + 1) * 128],
                                              ones_c[:, 0:1], start=(di == 0), stop=(di == 2)))
            p.v(lambda e, h=h: e.tensor_copy(rS[:, h:h + 1], P5[:, 0:1]))
            for di in range(3):
                kc = hc + di - 2
                p.t(lambda e, di=di, kc=kc, h=h, sl=sl, po=po: e.matmul(
                    P3[po:po + 64, sl * 128:(sl + 1) * 128],
                    V[:, kc * D + h * 64:kc * D + (h + 1) * 64],
                    PT[:, di * 128:(di + 1) * 128], start=(di == 0), stop=(di == 2)))
        p.v(lambda e: e.reciprocal(rS[:, :], rS[:, :]))
        p.v(lambda e: e.tensor_copy(rSb[:, :], rS[:, :]))
        p.t(lambda e: e.transpose(P6[:, :], rSb[:, :], ident[:, :]))
        p.v(lambda e: e.tensor_copy(rT[:, :], P6[0:16, :]))
        for h in range(NH):
            sl = h // 2
            po = (h % 2) * 64
            p.t(lambda e, h=h: e.matmul(P5[0:64, :], sel64[:, h * 64:(h + 1) * 64],
                                        rT[:, :], start=True, stop=True))
            p.v(lambda e: e.tensor_copy(rB[0:64, :], P5[0:64, :]))
            p.v(lambda e, sl=sl, po=po: e.tensor_tensor(
                locT[po:po + 64, sl * 128:(sl + 1) * 128],
                P3[po:po + 64, sl * 128:(sl + 1) * 128], rB[0:64, :], ALU.mult))
        for lh in range(LNH):
            sl = lh // 2
            po = (lh % 2) * 64
            p.t(lambda e, lh=lh, sl=sl, po=po: e.matmul(
                P5[0:64, :], GKT[po:po + 64, sl * 64:(sl + 1) * 64],
                GQTc[po:po + 64, sl * 128:(sl + 1) * 128], start=True, stop=False))
            p.t(lambda e, lh=lh: e.matmul(
                P5[0:64, :], sel8b[:, lh * 64:(lh + 1) * 64],
                gqbias_c[:, :], start=False, stop=True))
            p.v(lambda e, oc=oc: e.tensor_scalar(P5[0:64, :], P5[0:64, :],
                                                 lmask[0:64, oc:oc + 1], None, ALU.add))
            p.s(lambda e: e.activation(PTl[:, :], P5[0:64, :], AF.Exp))
            p.t(lambda e: e.matmul(P5[:, 64:65], PTl[:, :], ones_c[0:64, 0:1],
                                   start=True, stop=True))
            p.v(lambda e, lh=lh: e.tensor_copy(rSl[:, lh:lh + 1], P5[:, 64:65]))
            p.t(lambda e, lh=lh, sl=sl, po=po: e.matmul(
                P3[po:po + 64, sl * 128:(sl + 1) * 128],
                GV[:, lh * 64:(lh + 1) * 64], PTl[:, :], start=True, stop=True))
        p.v(lambda e: e.reciprocal(rSl[:, :], rSl[:, :]))
        p.v(lambda e: e.tensor_copy(rSb[:, 0:8], rSl[:, :]))
        p.t(lambda e: e.transpose(P6[0:8, :], rSb[:, 0:8], ident[:, :]))
        p.v(lambda e: e.tensor_copy(rT[0:8, :], P6[0:8, :]))
        for lh in range(LNH):
            sl = lh // 2
            po = (lh % 2) * 64
            p.t(lambda e, lh=lh: e.matmul(P5[0:64, :], sel8b[:, lh * 64:(lh + 1) * 64],
                                          rT[0:8, :], start=True, stop=True))
            p.v(lambda e: e.tensor_copy(rB[0:64, :], P5[0:64, :]))
            p.v(lambda e, sl=sl, po=po: e.tensor_tensor(
                remT[po:po + 64, sl * 128:(sl + 1) * 128],
                P3[po:po + 64, sl * 128:(sl + 1) * 128], rB[0:64, :], ALU.mult))
        for m in range(DC):
            for k in range(4):
                p.t(lambda e, m=m, k=k: e.matmul(
                    P1[:, 0:128], wSm2[:, k * D + m * 128:k * D + (m + 1) * 128],
                    remT[:, k * 128:(k + 1) * 128], start=(k == 0), stop=(k == 3)))
            p.v(lambda e, m=m: e.tensor_copy(rem2s[:, :], P1[:, 0:128]))
            for half in range(2):
                h = 2 * m + half
                po = half * 64
                p.t(lambda e, h=h: e.matmul(P5[0:64, :], sel64[:, h * 64:(h + 1) * 64],
                                            glT[:, :], start=True, stop=True))
                p.v(lambda e, po=po: e.tensor_copy(rB[po:po + 64, :], P5[0:64, :]))
                p.v(lambda e, m=m, po=po: e.tensor_tensor(
                    cmbF[po:po + 64, :], locT[po:po + 64, m * 128:(m + 1) * 128],
                    rB[po:po + 64, :], ALU.mult))
                p.t(lambda e, h=h: e.matmul(P5[0:64, :], sel64[:, h * 64:(h + 1) * 64],
                                            grT[:, :], start=True, stop=True))
                p.v(lambda e, po=po: e.tensor_copy(rB[po:po + 64, :], P5[0:64, :]))
                p.v(lambda e, po=po: e.tensor_tensor(
                    rB[po:po + 64, :], rem2s[po:po + 64, :], rB[po:po + 64, :], ALU.mult))
                p.v(lambda e, m=m, po=po: e.tensor_tensor(
                    cmbT[po:po + 64, m * 128:(m + 1) * 128], cmbF[po:po + 64, :],
                    rB[po:po + 64, :], ALU.add))
        for half in range(2):
            for k in range(DC):
                p.t(lambda e, half=half, k=k: e.matmul(
                    P1[:, :], cmbT[:, k * 128:(k + 1) * 128],
                    wBig2[:, k * D + half * 512:k * D + (half + 1) * 512],
                    start=(k == 0), stop=(k == DC - 1)))
            p.v(lambda e, half=half: e.tensor_tensor(
                outs[:, half * 512:(half + 1) * 512], P1[:, :],
                r_c[:, half * 512:(half + 1) * 512], ALU.add))
        p.dma(lambda e, oc=oc: e.dma_start(out=ys[oc * 128:(oc + 1) * 128, :], in_=outs[:, :]))

    return p.end()


def _l3a_inmaps(x_full, inputs):
    pp = np.arange(128)[:, None]
    ff = np.arange(128)[None, :]
    tri_a = np.where(pp > ff, 0.0, -1e30).astype(np.float32)
    tri_b = np.where(pp <= ff, 0.0, -1e30).astype(np.float32)
    tri3 = np.concatenate([tri_a, tri_b], axis=1)
    e16 = np.zeros((128, 8 * 16), dtype=np.float32)
    for m in range(8):
        e16[0:64, m * 16 + 2 * m] = 1.0
        e16[64:128, m * 16 + 2 * m + 1] = 1.0
    e8 = np.zeros((128, 4 * 8), dtype=np.float32)
    for m in range(4):
        e8[0:64, m * 8 + 2 * m] = 1.0
        e8[64:128, m * 8 + 2 * m + 1] = 1.0
    sel16 = np.zeros((16, 16 * 128), dtype=np.float32)
    for h in range(16):
        sel16[h, h * 128:(h + 1) * 128] = 1.0
    sel64 = np.zeros((16, 16 * 64), dtype=np.float32)
    for h in range(16):
        sel64[h, h * 64:(h + 1) * 64] = 1.0
    sel8 = np.zeros((8, 8 * 128), dtype=np.float32)
    for h in range(8):
        sel8[h, h * 128:(h + 1) * 128] = 1.0
    sel8b = np.zeros((8, 8 * 64), dtype=np.float32)
    for h in range(8):
        sel8b[h, h * 64:(h + 1) * 64] = 1.0
    glw = _f32(inputs["L3_attn_gate_local_w"])
    grw = _f32(inputs["L3_attn_gate_remote_w"])
    glbd = np.zeros((D, 16), dtype=np.float32)
    grbd = np.zeros((D, 16), dtype=np.float32)
    for h in range(16):
        glbd[h * 64:(h + 1) * 64, h] = glw[h]
        grbd[h * 64:(h + 1) * 64, h] = grw[h]
    maps = []
    for cid in range(NCORE):
        b = cid // 4
        sc0 = (cid % 4) * NCH
        t0 = cid * TOK
        seq0 = b * SEQ
        lo = t0 - 256
        xhp = np.zeros((X3CH * 128, D), dtype=np.float32)
        vlo = max(lo, seq0)
        xhp[vlo - lo:] = x_full[vlo:t0 + TOK]
        kill = np.zeros((NCH, 3), dtype=np.float32)
        for oc in range(NCH):
            for di in range(3):
                if sc0 + oc + di - 2 < 0:
                    kill[oc, di] = 1.0
        lm = np.zeros((64, NCH), dtype=np.float32)
        for oc in range(NCH):
            cg = sc0 + oc
            lm[cg + 1:, oc] = -1e30
        maps.append(dict(
            xh=xhp, xq=_f32(x_full[b * SEQ:(b + 1) * SEQ]),
            wq=_f32(inputs["L3_attn_wq"]), wk=_f32(inputs["L3_attn_wk"]),
            wv=_f32(inputs["L3_attn_wv"]), wgq=_f32(inputs["L3_attn_wgq"]),
            wgk=_f32(inputs["L3_attn_wgk"]), wgv=_f32(inputs["L3_attn_wgv"]),
            rout=_f32(inputs["L3_attn_r_out_w"]), wout=_f32(inputs["L3_attn_w_out"]),
            glbd=glbd, grbd=grbd, ident=IDENT, tri3=tri3, e16=e16, e8=e8,
            sel16=sel16, sel64=sel64, sel8=sel8, sel8b=sel8b,
            kill01=kill.reshape(1, -1), lmask=lm))
    return maps


def run_l3a(x_full, inputs, nc_cache={}):
    if "l3a" not in nc_cache:
        nc_cache["l3a"] = build_l3a()
    res = run_bass_kernel_spmd(nc_cache["l3a"], _l3a_inmaps(x_full, inputs),
                               core_ids=list(range(NCORE)))
    return np.concatenate([res.results[i]["ys"] for i in range(NCORE)], axis=0)



def build_l3f():
    p = Prog("l3f")
    xs = p.inp("xs", [TOK, D], F32)
    xq = p.inp("xq", [SEQ, D], F32)
    wqT_in = p.inp("wqT", [D, D], F32)
    wkT_in = p.inp("wkT", [D, D], F32)
    wvT_in = p.inp("wvT", [D, D], F32)
    wr_in = p.inp("wr", [D, 16], F32)
    wo_in = p.inp("wo", [256, D], F32)
    bqc_in = p.inp("bqc", [128, DC], F32)
    bkc_in = p.inp("bkc", [128, DC], F32)
    bvc_in = p.inp("bvc", [128, DC], F32)
    brc_in = p.inp("brc", [16, 1], F32)
    bo_in = p.inp("bo", [1, D], F32)
    ident_in = p.inp("ident", [128, 128], F32)
    ehh_in = p.inp("ehh", [16, 128], F32)
    mh_in = p.inp("mh", [16, 8], F32)
    t64_in = p.inp("t64", [128, 64], F32)
    e4_in = p.inp("e4", [128, 2 * 4], F32)
    sel4_in = p.inp("sel4", [4, 4 * 128], F32)
    ys = p.outp("ys", [TOK, D], F32)

    KW4 = SEQ

    wT = p.sb("wT", [128, DC * D], BF16)
    SelM = p.sb("SelM", [128, DC * 256], BF16)
    SelWq = p.sb("SelWq", [128, DC * 256], BF16)
    SelWk = p.sb("SelWk", [128, DC * 256], BF16)
    SelWv = p.sb("SelWv", [128, DC * 256], BF16)
    woS = p.sb("woS", [128, 2 * D], BF16)
    KselT = p.sb("KselT", [128, 2 * SEQ], BF16)
    Vsel = p.sb("Vsel", [128, 64 * 260], BF16)
    QselT = p.sb("QselT", [128, 2 * TOK], BF16)
    ident = p.sb("identb", [128, 128], BF16)
    ehh = p.sb("ehh_sb", [16, 128], BF16)
    mh = p.sb("mh_sb", [16, 8], BF16)
    t64 = p.sb("t64_sb", [128, 64], BF16)
    e4 = p.sb("e4_sb", [128, 2 * 4], BF16)
    sel4 = p.sb("sel4_sb", [4, 4 * 128], BF16)
    bqc = p.sb("bqc_sb", [128, DC], BF16)
    bkc = p.sb("bkc_sb", [128, DC], BF16)
    bvc = p.sb("bvc_sb", [128, DC], BF16)
    brc = p.sb("brc_sb", [16, 1], F32)
    ones_r = p.sb("ones_r", [1, 128], BF16)
    ones_c = p.sb("ones_c", [128, 1], BF16)
    wr_sb = p.sb("wr_sb", [128, DC * 16], BF16)
    rsum = p.sb("rsum", [16, 1], F32)
    rsc = p.sb("rsc", [16, 1], F32)
    rmb = p.sb("rmb", [16, 1], BF16)
    drow = p.sb("drow", [1, 16], F32)
    drow2 = p.sb("drow2", [1, 16], F32)
    dmax = p.sb("dmax", [1, 1], F32)
    dsum = p.sb("dsum", [1, 1], F32)
    ohs = p.sb("ohs", [1, 4 * 16], F32)
    ohsum = p.sb("ohsum", [1, 16], F32)
    grows = p.sb("grows", [1, 4 * 16], BF16)
    growT = p.sb("growT", [16, 1], BF16)
    gcm = p.sb("gcm", [16, 8], BF16)
    colv = p.sb("colv", [128, 8], F32)
    bsel_q = p.sb("bsel_q", [1, 256], BF16)
    bsel_k = p.sb("bsel_k", [1, 256], BF16)
    bsel_v = p.sb("bsel_v", [1, 256], BF16)
    bselqc = p.sb("bselqc", [128, 2], F32)
    bselkc = p.sb("bselkc", [128, 2], F32)
    bvselB = p.sb("bvselB", [128, 256], F32)
    boB = p.sb("boB", [128, D], F32)
    bo_bf = p.sb("bo_bf", [1, D], BF16)
    r_c = p.sb("r_c", [128, D], F32)
    xn_c = p.sb("xn_c", [128, D], BF16)
    xnT_c = p.sb("xnT_c", [128, DC * 128], BF16)
    sqs = p.sb("sqs", [128, 128], BF16)
    kn2m = p.sb("kn2m", [4, 1], F32)
    kn2c = p.sb("kn2c", [4, 1], F32)
    qn2 = p.sb("qn2", [4, TOK], F32)
    qbias = p.sb("qbias", [4, TOK], BF16)
    PT = p.sb("PT", [128, 512], BF16)
    rrow = p.sb("rrow", [1, 512], F32)
    rrowb = p.sb("rrowb", [1, 512], BF16)
    rBf = p.sb("rBf", [64, 512], F32)
    ctxall = p.sb("ctxall", [128, 2 * 512], BF16)
    outs = p.sb("outs", [128, D], F32)

    P1 = p.ps("P1", [128, 512], F32)
    PS = p.ps("PS", [128, 512], F32)
    PC65 = p.ps("PC65", [65, 512], F32)
    PB = p.ps("PB", [64, 512], F32)
    P4 = p.ps("P4", [128, 1024], BF16)
    P5 = p.ps("P5", [128, 128], F32)
    P6 = p.ps("P6", [16, 16], BF16)

    p.begin()
    scr = ln_scratch(p)
    for dst, src in [(ident, ident_in), (ehh, ehh_in), (mh, mh_in), (t64, t64_in),
                     (e4, e4_in), (sel4, sel4_in), (bqc, bqc_in), (bkc, bkc_in),
                     (bvc, bvc_in), (bo_bf, bo_in)]:
        p.gdma(lambda e, dst=dst, src=src: e.dma_start(out=dst[:, :], in_=src))
    p.dma(lambda e: e.dma_start(out=brc[:, :], in_=brc_in))
    p.gdma(lambda e: e.dma_start(
        out=wr_sb[:, :].rearrange("p (k n) -> p k n", k=DC),
        in_=wr_in.rearrange("(k p) n -> p k n", p=128)))
    p.v(lambda e: e.memset(ones_r[:, :], 1.0))
    p.v(lambda e: e.memset(ones_c[:, :], 1.0))
    p.v(lambda e: e.memset(rsum[:, :], 0.0))
    p.v(lambda e: e.memset(kn2m[:, :], 0.0))

    for c in range(SCH):
        p.dma(lambda e, c=c: e.dma_start(out=r_c[:, :], in_=xq[c * 128:(c + 1) * 128, :]))
        emit_ln(p, r_c[:, :], xn_c[:, :], scr)
        emit_transpose_cols(p, xn_c[:, :], DC, P4, ident,
                            xnT_c[:, :].rearrange("p (k t) -> p k t", k=DC))
        for k in range(DC):
            p.t(lambda e, k=k: e.matmul(P5[0:16, :], wr_sb[:, k * 16:(k + 1) * 16],
                                        xnT_c[:, k * 128:(k + 1) * 128],
                                        start=(k == 0), stop=(k == DC - 1)))
        p.v(lambda e: e.reduce_sum(rsc[:, :], P5[0:16, :], AX.X))
        p.v(lambda e: e.tensor_tensor(rsum[:, :], rsum[:, :], rsc[:, :], ALU.add))
    p.v(lambda e: e.tensor_scalar(rsum[:, :], rsum[:, :], 1.0 / SEQ, None, ALU.mult))
    p.v(lambda e: e.tensor_tensor(rsum[:, :], rsum[:, :], brc[:, :], ALU.add))
    p.v(lambda e: e.tensor_copy(rmb[:, :], rsum[:, :]))
    p.t(lambda e: e.transpose(P6[0:1, :], rmb[:, :], ident[0:16, 0:16]))
    p.v(lambda e: e.tensor_copy(drow[0:1, :], P6[0:1, 0:16]))
    p.v(lambda e: e.reduce_max(dmax[:, :], drow[0:1, :], AX.X, negate=True))
    p.s(lambda e: e.activation(drow[0:1, :], drow[0:1, :], AF.Exp, bias=dmax[0:1, :],
                               accum_out=dsum[0:1, :]))
    p.v(lambda e: e.reciprocal(dsum[:, :], dsum[:, :]))
    p.v(lambda e: e.tensor_scalar(drow[0:1, :], drow[0:1, :], dsum[0:1, 0:1], None, ALU.mult))
    p.v(lambda e: e.tensor_copy(drow2[0:1, :], drow[0:1, :]))
    for kk in range(4):
        p.v(lambda e: e.reduce_max(dmax[:, :], drow2[0:1, :], AX.X))
        p.v(lambda e, kk=kk: e.tensor_scalar(ohs[0:1, kk * 16:(kk + 1) * 16], drow2[0:1, :],
                                             dmax[0:1, 0:1], None, ALU.is_equal))
        p.v(lambda e, kk=kk: e.scalar_tensor_tensor(
            drow2[0:1, :], ohs[0:1, kk * 16:(kk + 1) * 16], -1e30, drow2[0:1, :],
            ALU.mult, ALU.add))
    p.v(lambda e: e.tensor_tensor(ohsum[0:1, :], ohs[0:1, 0:16], ohs[0:1, 16:32], ALU.add))
    p.v(lambda e: e.tensor_tensor(ohsum[0:1, :], ohsum[0:1, :], ohs[0:1, 32:48], ALU.add))
    p.v(lambda e: e.tensor_tensor(ohsum[0:1, :], ohsum[0:1, :], ohs[0:1, 48:64], ALU.add))
    p.v(lambda e: e.tensor_tensor(drow2[0:1, :], drow[0:1, :], ohsum[0:1, :], ALU.mult))
    p.v(lambda e: e.reduce_max(dmax[:, :], drow2[0:1, :], AX.X, negate=True))
    p.s(lambda e: e.activation(drow2[0:1, :], drow2[0:1, :], AF.Exp, bias=dmax[0:1, :],
                               accum_out=dsum[0:1, :]))
    p.v(lambda e: e.reciprocal(dsum[:, :], dsum[:, :]))
    p.v(lambda e: e.tensor_scalar(drow2[0:1, :], drow2[0:1, :], dsum[0:1, 0:1], None, ALU.mult))
    for kk in range(4):
        p.v(lambda e, kk=kk: e.tensor_tensor(grows[0:1, kk * 16:(kk + 1) * 16],
                                             drow2[0:1, :], ohs[0:1, kk * 16:(kk + 1) * 16],
                                             ALU.mult))
    for kk in range(4):
        p.t(lambda e, kk=kk: e.transpose(P6[0:16, 0:1], grows[0:1, kk * 16:(kk + 1) * 16],
                                         ident[0:1, 0:1]))
        p.v(lambda e: e.tensor_copy(growT[:, :], P6[0:16, 0:1]))
        p.v(lambda e: e.tensor_tensor(gcm[:, :], mh[:, :],
                                      growT[:, :].broadcast_to((16, 8)), ALU.mult))
        p.t(lambda e: e.matmul(P5[:, 0:8], ehh[:, :], gcm[:, :], start=True, stop=True))
        p.v(lambda e: e.tensor_copy(colv[:, :], P5[:, 0:8]))
        for s in range(DC):
            p.v(lambda e, kk=kk, s=s: e.tensor_scalar(
                SelM[:, s * 256 + kk * 64:s * 256 + (kk + 1) * 64], t64[:, :],
                colv[:, s:s + 1], None, ALU.mult))
    for (wsrc, dstW, bcol, bselrow) in [(wqT_in, SelWq, bqc, bsel_q),
                                        (wkT_in, SelWk, bkc, bsel_k),
                                        (wvT_in, SelWv, bvc, bsel_v)]:
        p.gdma(lambda e, wsrc=wsrc: e.dma_start(
            out=wT[:, :].rearrange("p (k n) -> p k n", k=DC),
            in_=wsrc.rearrange("(k p) n -> p k n", p=128)))
        for m in range(DC):
            for k in range(DC):
                p.t(lambda e, m=m, k=k: e.matmul(
                    P1[:, 0:256], wT[:, k * D + m * 128:k * D + (m + 1) * 128],
                    SelM[:, k * 256:(k + 1) * 256], start=(k == 0), stop=(k == DC - 1)))
            p.v(lambda e, m=m, dstW=dstW: e.tensor_copy(
                dstW[:, m * 256:(m + 1) * 256], P1[:, 0:256]))
        for k in range(DC):
            p.t(lambda e, k=k, bcol=bcol: e.matmul(
                P1[0:1, 0:256], bcol[:, k:k + 1], SelM[:, k * 256:(k + 1) * 256],
                start=(k == 0), stop=(k == DC - 1)))
        p.v(lambda e, bselrow=bselrow: e.tensor_copy(bselrow[0:1, :], P1[0:1, 0:256]))
    for mc in range(2):
        p.t(lambda e, mc=mc: e.matmul(P1[:, mc:mc + 1],
                                      bsel_q[0:1, mc * 128:(mc + 1) * 128],
                                      ones_r[0:1, 0:1], start=True, stop=True))
    p.v(lambda e: e.tensor_copy(bselqc[:, :], P1[:, 0:2]))
    for mc in range(2):
        p.t(lambda e, mc=mc: e.matmul(P1[:, mc:mc + 1],
                                      bsel_k[0:1, mc * 128:(mc + 1) * 128],
                                      ones_r[0:1, 0:1], start=True, stop=True))
    p.v(lambda e: e.tensor_copy(bselkc[:, :], P1[:, 0:2]))
    p.t(lambda e: e.matmul(P1[:, 0:256], ones_r[:, :], bsel_v[0:1, :], start=True, stop=True))
    p.v(lambda e: e.tensor_copy(bvselB[:, :], P1[:, 0:256]))
    p.gdma(lambda e: e.dma_start(
        out=woS[:, :].rearrange("p (k n) -> p k n", k=2),
        in_=wo_in.rearrange("(k p) n -> p k n", p=128)))
    for half in range(2):
        p.t(lambda e, half=half: e.matmul(P1[:, :], ones_r[:, :],
                                          bo_bf[0:1, half * 512:(half + 1) * 512],
                                          start=True, stop=True))
        p.v(lambda e, half=half: e.tensor_copy(boB[:, half * 512:(half + 1) * 512], P1[:, :]))
    p.v(lambda e: e.memset(Vsel[:, :], 1.0))

    for c in range(SCH):
        p.dma(lambda e, c=c: e.dma_start(out=r_c[:, :], in_=xq[c * 128:(c + 1) * 128, :]))
        emit_ln(p, r_c[:, :], xn_c[:, :], scr)
        emit_transpose_cols(p, xn_c[:, :], DC, P4, ident,
                            xnT_c[:, :].rearrange("p (k t) -> p k t", k=DC))
        for mc in range(2):
            for k in range(DC):
                p.t(lambda e, mc=mc, k=k: e.matmul(
                    P1[:, 0:128], SelWk[:, k * 256 + mc * 128:k * 256 + (mc + 1) * 128],
                    xnT_c[:, k * 128:(k + 1) * 128], start=(k == 0), stop=(k == DC - 1)))
            p.v(lambda e, mc=mc, c=c: e.tensor_scalar(
                KselT[:, mc * SEQ + c * 128:mc * SEQ + (c + 1) * 128],
                P1[:, 0:128], bselkc[:, mc:mc + 1], None, ALU.add))
            p.s(lambda e, mc=mc, c=c: e.activation(
                sqs[:, :], KselT[:, mc * SEQ + c * 128:mc * SEQ + (c + 1) * 128], AF.Square))
            p.t(lambda e, mc=mc: e.matmul(P5[0:4, :], e4[:, mc * 4:(mc + 1) * 4],
                                          sqs[:, :], start=(mc == 0), stop=(mc == 1)))
        p.v(lambda e: e.reduce_max(kn2c[:, :], P5[0:4, :], AX.X))
        p.v(lambda e: e.tensor_tensor(kn2m[:, :], kn2m[:, :], kn2c[:, :], ALU.max))
        for k in range(DC):
            p.t(lambda e, k=k: e.matmul(
                P1[:, 0:256], xnT_c[:, k * 128:(k + 1) * 128],
                SelWv[:, k * 256:(k + 1) * 256], start=(k == 0), stop=(k == DC - 1)))
        p.v(lambda e, c=c: e.tensor_tensor(
            Vsel[:, :].rearrange("p (ch t) -> p ch t", ch=64)[:, c, :]
                .rearrange("p (h x) -> p h x", h=4)[:, :, 0:64],
            P1[:, 0:256].rearrange("p (h d) -> p h d", h=4),
            bvselB[:, :].rearrange("p (h d) -> p h d", h=4), ALU.add))

    for oc in range(NCH):
        p.dma(lambda e, oc=oc: e.dma_start(out=r_c[:, :], in_=xs[oc * 128:(oc + 1) * 128, :]))
        emit_ln(p, r_c[:, :], xn_c[:, :], scr)
        emit_transpose_cols(p, xn_c[:, :], DC, P4, ident,
                            xnT_c[:, :].rearrange("p (k t) -> p k t", k=DC))
        for mc in range(2):
            for k in range(DC):
                p.t(lambda e, mc=mc, k=k: e.matmul(
                    P1[:, 0:128], SelWq[:, k * 256 + mc * 128:k * 256 + (mc + 1) * 128],
                    xnT_c[:, k * 128:(k + 1) * 128], start=(k == 0), stop=(k == DC - 1)))
            p.v(lambda e, mc=mc, oc=oc: e.tensor_scalar(
                QselT[:, mc * TOK + oc * 128:mc * TOK + (oc + 1) * 128],
                P1[:, 0:128], bselqc[:, mc:mc + 1], 0.125, ALU.add, ALU.mult))
            p.s(lambda e, mc=mc, oc=oc: e.activation(
                sqs[:, :], QselT[:, mc * TOK + oc * 128:mc * TOK + (oc + 1) * 128], AF.Square))
            p.t(lambda e, mc=mc: e.matmul(P5[0:4, :], e4[:, mc * 4:(mc + 1) * 4],
                                          sqs[:, :], start=(mc == 0), stop=(mc == 1)))
        p.v(lambda e, oc=oc: e.tensor_copy(qn2[:, oc * 128:(oc + 1) * 128], P5[0:4, :]))
    for oc in range(NCH):
        p.v(lambda e, oc=oc: e.tensor_scalar(
            qn2[:, oc * 128:(oc + 1) * 128], qn2[:, oc * 128:(oc + 1) * 128],
            kn2m[0:4, 0:1], 1.0, ALU.mult, ALU.mult))
        p.s(lambda e, oc=oc: e.activation(qn2[:, oc * 128:(oc + 1) * 128],
                                          qn2[:, oc * 128:(oc + 1) * 128], AF.Sqrt))
        p.v(lambda e, oc=oc: e.tensor_scalar(
            qbias[:, oc * 128:(oc + 1) * 128], qn2[:, oc * 128:(oc + 1) * 128],
            -1.0, 50.0, ALU.mult, ALU.add))

    for qb in range(4):
        for h in range(4):
            mc = h // 2
            po = (h % 2) * 64
            for kc in range(SCH):
                p.t(lambda e, kc=kc, mc=mc, po=po, qb=qb: e.matmul(
                    PS[:, :],
                    KselT[po:po + 64, mc * SEQ + kc * 128:mc * SEQ + (kc + 1) * 128],
                    QselT[po:po + 64, mc * TOK + qb * 512:mc * TOK + (qb + 1) * 512],
                    start=True, stop=False))
                p.t(lambda e, h=h, qb=qb: e.matmul(
                    PS[:, :], sel4[:, h * 128:(h + 1) * 128],
                    qbias[:, qb * 512:(qb + 1) * 512], start=False, stop=True))
                p.s(lambda e: e.activation(PT[:, :], PS[:, :], AF.Exp))
                p.t(lambda e, kc=kc, h=h: e.matmul(
                    PC65[:, :], Vsel[:, kc * 260 + h * 65:kc * 260 + (h + 1) * 65],
                    PT[:, :], start=(kc == 0), stop=(kc == SCH - 1)))
            p.v(lambda e: e.tensor_copy(rrow[0:1, :], PC65[64:65, :]))
            p.v(lambda e: e.reciprocal(rrow[0:1, :], rrow[0:1, :]))
            p.v(lambda e: e.tensor_copy(rrowb[0:1, :], rrow[0:1, :]))
            p.t(lambda e: e.matmul(PB[:, :], ones_r[0:1, 0:64], rrowb[0:1, :],
                                   start=True, stop=True))
            p.v(lambda e: e.tensor_copy(rBf[:, :], PB[:, :]))
            p.v(lambda e, mc=mc, po=po: e.tensor_tensor(
                ctxall[po:po + 64, mc * 512:(mc + 1) * 512], PC65[0:64, :],
                rBf[:, :], ALU.mult))
        for tc in range(4):
            oc = qb * 4 + tc
            p.dma(lambda e, oc=oc: e.dma_start(out=r_c[:, :], in_=xs[oc * 128:(oc + 1) * 128, :]))
            for half in range(2):
                for k2 in range(2):
                    p.t(lambda e, half=half, k2=k2, tc=tc: e.matmul(
                        P1[:, :], ctxall[:, k2 * 512 + tc * 128:k2 * 512 + tc * 128 + 128],
                        woS[:, k2 * D + half * 512:k2 * D + (half + 1) * 512],
                        start=(k2 == 0), stop=(k2 == 1)))
                p.v(lambda e, half=half: e.tensor_tensor(
                    outs[:, half * 512:(half + 1) * 512], P1[:, :],
                    boB[:, half * 512:(half + 1) * 512], ALU.add))
                p.v(lambda e, half=half: e.tensor_tensor(
                    outs[:, half * 512:(half + 1) * 512], outs[:, half * 512:(half + 1) * 512],
                    r_c[:, half * 512:(half + 1) * 512], ALU.add))
            p.dma(lambda e, oc=oc: e.dma_start(out=ys[oc * 128:(oc + 1) * 128, :], in_=outs[:, :]))

    return p.end()


def _l3f_inmaps(x_full, inputs):
    ehh = np.zeros((16, 128), dtype=np.float32)
    for r in range(16):
        if r % 2 == 0:
            ehh[r, 0:64] = 1.0
        else:
            ehh[r, 64:128] = 1.0
    mhost = np.zeros((16, 8), dtype=np.float32)
    for r in range(16):
        mhost[r, r // 2] = 1.0
    t64 = np.concatenate([np.eye(64, dtype=np.float32)] * 2, axis=0)
    e4 = np.zeros((128, 2 * 4), dtype=np.float32)
    for mc in range(2):
        e4[0:64, mc * 4 + 2 * mc] = 1.0
        e4[64:128, mc * 4 + 2 * mc + 1] = 1.0
    sel4 = np.zeros((4, 4 * 128), dtype=np.float32)
    for h in range(4):
        sel4[h, h * 128:(h + 1) * 128] = 1.0
    def cols(v):
        return _f32(v).reshape(DC, 128).T.copy()
    maps = []
    for cid in range(NCORE):
        b = cid // 4
        maps.append(dict(
            xs=_f32(x_full[cid * TOK:(cid + 1) * TOK]),
            xq=_f32(x_full[b * SEQ:(b + 1) * SEQ]),
            wqT=_f32(np.asarray(inputs["L3_ffn_Wq"]).T), wkT=_f32(np.asarray(inputs["L3_ffn_Wk"]).T),
            wvT=_f32(np.asarray(inputs["L3_ffn_Wv"]).T), wr=_f32(inputs["L3_ffn_Wr"]),
            wo=_f32(inputs["L3_ffn_Wo"]),
            bqc=cols(inputs["L3_ffn_bq"]), bkc=cols(inputs["L3_ffn_bk"]),
            bvc=cols(inputs["L3_ffn_bv"]),
            brc=_f32(inputs["L3_ffn_br"]).reshape(16, 1),
            bo=_f32(inputs["L3_ffn_bo"]).reshape(1, D),
            ident=IDENT, ehh=ehh, mh=mhost, t64=t64, e4=e4, sel4=sel4))
    return maps


def run_l3f(x_full, inputs, nc_cache={}):
    if "l3f" not in nc_cache:
        nc_cache["l3f"] = build_l3f()
    res = run_bass_kernel_spmd(nc_cache["l3f"], _l3f_inmaps(x_full, inputs),
                               core_ids=list(range(NCORE)))
    return np.concatenate([res.results[i]["ys"] for i in range(NCORE)], axis=0)



def kernel(**inputs):
    x = np.ascontiguousarray(np.asarray(inputs["L0_attn_x"], dtype=np.float32)
                             .reshape(16384, D))
    x = run_l0a(x, inputs)
    x = run_ffn(x, inputs["L0_ffn_w_gate"], inputs["L0_ffn_w_up"], inputs["L0_ffn_w_down"])
    x = run_l1a(x, inputs)
    x = run_ffn(x, inputs["L1_ffn_w_gate"], inputs["L1_ffn_w_up"], inputs["L1_ffn_w_down"])
    x = run_l2a(x, inputs)
    x = run_l2f(x, inputs)
    x = run_l3a(x, inputs)
    x = run_l3f(x, inputs)
    return np.ascontiguousarray(x.reshape(2, 8192, D).astype(np.float32))


# version 02
# speedup vs baseline: 1.5631x; 1.5631x over previous
import contextlib
import numpy as np
import ml_dtypes
import concourse.bass as bass
import concourse.mybir as mybir
from concourse.bass_utils import run_bass_kernel_spmd

F32 = mybir.dt.float32
BF16 = mybir.dt.bfloat16
I32 = mybir.dt.int32
AF = mybir.ActivationFunctionType
ALU = mybir.AluOpType
AX = mybir.AxisListType
BFT = ml_dtypes.bfloat16

NCORE = 8
TOK = 2048
NCH = TOK // 128
D = 1024
DC = D // 128
LN_EPS = 1e-5

IDENT = np.eye(128, dtype=np.float32)


def _f32(a):
    return np.ascontiguousarray(a, dtype=np.float32)


def _to_host_bf16(a):
    return np.ascontiguousarray(np.asarray(a).astype(BFT))


class Prog:

    def __init__(self, name="prog"):
        self.nc = bass.Bass("TRN2", target_bir_lowering=False)
        self.es = contextlib.ExitStack()
        self.name = name
        self._started = False
        self._coll_n = 0

    def inp(self, name, shape, dt=F32):
        return self.nc.dram_tensor(name, list(shape), dt, kind="ExternalInput").ap()

    def outp(self, name, shape, dt=F32):
        return self.nc.dram_tensor(name, list(shape), dt, kind="ExternalOutput").ap()

    def dram(self, name, shape, dt=F32, **kw):
        return self.nc.dram_tensor(name, list(shape), dt, kind="Internal", **kw)

    def sb(self, name, shape, dt=BF16, scope=None):
        return (scope or self.es).enter_context(self.nc.sbuf_tensor(name, list(shape), dt))

    def ps(self, name, shape, dt=F32, scope=None):
        return (scope or self.es).enter_context(self.nc.psum_tensor(name, list(shape), dt))

    def scope(self):
        return self.es.enter_context(contextlib.ExitStack()) if False else contextlib.ExitStack()

    def collective(self, kind, cc_in_t, cc_out_t, groups):
        self._coll_n += 1
        sem = self.es.enter_context(self.nc.semaphore(f"CC{self._coll_n}"))
        dummy = self._cdummy

        def emit(e):
            e.collective_compute(
                kind, mybir.AluOpType.bypass, replica_groups=groups,
                ins=[cc_in_t.ap().opt()], outs=[cc_out_t.ap().opt()]).then_inc(sem)
            e.wait_ge(sem, 1)
            return e.memset(dummy[:, :], 0.0)

        self.op("gpsimd", emit, 1)

    def begin(self):
        self._cdummy = self.es.enter_context(self.nc.sbuf_tensor("cc_dummy", [1, 1], F32))
        self.S = self.es.enter_context(self.nc.semaphore("S"))
        self.block = self.es.enter_context(self.nc.Block())
        self.c = 0
        self._pend_eng = None
        self._pend = []
        self._started = True

    def _flush(self):
        if not self._pend:
            return
        items = list(self._pend)
        self._pend = []
        S = self.S

        def f(eng):
            for emit, inc, wait_at in items:
                eng.wait_ge(S, wait_at)
                inst = emit(eng)
                if inst is not None:
                    inst.then_inc(S, inc)

        getattr(self.block, self._pend_eng)(f)

    def op(self, engname, emit, inc=1):
        if self._pend_eng != engname:
            self._flush()
            self._pend_eng = engname
        self._pend.append((emit, inc, self.c))
        self.c += inc

    def t(self, emit):
        self.op("tensor", emit, 1)

    def v(self, emit):
        self.op("vector", emit, 1)

    def s(self, emit):
        self.op("scalar", emit, 1)

    def g(self, emit):
        self.op("gpsimd", emit, 1)

    def dma(self, emit, eng="sync"):
        self.op(eng, emit, 16)

    def gdma(self, emit):
        self.op("gpsimd", emit, 16)

    def end(self):
        self._flush()
        total = self.c
        S = self.S

        def f(eng):
            eng.wait_ge(S, total)

        self.block.sync(f)
        self.es.close()
        return self.nc



def emit_ln(p, r_ap, xn_ap, scr):
    sq, s1, s2, m, rstd, negmr = (scr["sq"], scr["s1"], scr["s2"], scr["m"],
                                  scr["rstd"], scr["negmr"])
    n = r_ap.shape[-1]
    p.v(lambda e: e.reduce_sum(s1[:, :], r_ap, AX.X))
    p.s(lambda e: e.activation(sq[:, :], r_ap, AF.Square, accum_out=s2[:, :]))
    p.v(lambda e: e.tensor_scalar(m[:, :], s1[:, :], 1.0 / n, None, ALU.mult))
    p.v(lambda e: e.tensor_scalar(s2[:, :], s2[:, :], 1.0 / n, None, ALU.mult))
    p.v(lambda e: e.scalar_tensor_tensor(s2[:, :], m[:, :], m[:, :], s2[:, :], ALU.mult, ALU.subtract))
    eps = scr["eps"]
    p.s(lambda e: e.activation(rstd[:, :], s2[:, :], AF.Sqrt, bias=eps[:, :], scale=-1.0))
    p.v(lambda e: e.reciprocal(rstd[:, :], rstd[:, :]))
    p.v(lambda e: e.tensor_scalar(negmr[:, :], m[:, :], rstd[:, :], -1.0, ALU.mult, ALU.mult))
    p.v(lambda e: e.tensor_scalar(xn_ap, r_ap, rstd[:, :], negmr[:, :], ALU.mult, ALU.add))


def ln_scratch(p, tag=""):
    scr = {
        "sq": p.sb("ln_sq" + tag, [128, D], F32),
        "s1": p.sb("ln_s1" + tag, [128, 1], F32),
        "s2": p.sb("ln_s2" + tag, [128, 1], F32),
        "m": p.sb("ln_m" + tag, [128, 1], F32),
        "rstd": p.sb("ln_rstd" + tag, [128, 1], F32),
        "negmr": p.sb("ln_negmr" + tag, [128, 1], F32),
        "eps": p.sb("ln_eps" + tag, [128, 1], F32),
    }
    eps = scr["eps"]
    p.v(lambda e: e.memset(eps[:, :], LN_EPS))
    return scr


def emit_transpose_cols(p, src_bf_ap, ncol_chunks, ptr, ident, out_writes):
    for j in range(ncol_chunks):
        jj = j
        p.t(lambda e, jj=jj: e.transpose(ptr[:, jj * 128:(jj + 1) * 128],
                                         src_bf_ap[:, jj * 128:(jj + 1) * 128],
                                         ident[:, :]))
    if callable(out_writes):
        for j in range(ncol_chunks):
            jj = j
            p.v(lambda e, jj=jj: e.tensor_copy(out_writes(jj), ptr[:, jj * 128:(jj + 1) * 128]))
    else:
        p.v(lambda e: e.tensor_copy(out_writes, ptr[:, 0:ncol_chunks * 128]))



def build_ffn_int8():
    p = Prog("ffn")
    xs = p.inp("xs", [TOK, D], F32)
    wgT = p.inp("wgT", [D, 3072], I32)
    wuT = p.inp("wuT", [D, 3072], I32)
    wdT = p.inp("wdT", [3072, D], I32)
    ident_in = p.inp("ident", [128, 128], F32)
    ys = p.outp("ys", [TOK, D], F32)

    H = 3072
    HC = H // 128
    G = 256
    NG = TOK // G

    wg_sb = p.sb("wg", [128, DC * H], BF16)
    wu_sb = p.sb("wu", [128, DC * H], BF16)
    wd_sb = p.sb("wd", [128, HC * D], BF16)
    ident = p.sb("identb", [128, 128], BF16)
    r_g = p.sb("r_g", [128, 2 * D], F32)
    xn_c = p.sb("xn_c", [128, D], BF16)
    xnT_g = p.sb("xnT_g", [128, DC * G], BF16)
    gt = p.sb("gt", [128, G], BF16)
    ut = p.sb("ut", [128, G], BF16)
    hT_g = p.sb("hT_g", [128, HC * G], BF16)
    outs = p.sb("outs", [128, D], F32)
    scr = None

    ptr = p.ps("ptr", [128, 1024], BF16)
    pmm = p.ps("pmm", [128, G], F32)
    pdn = p.ps("pdn", [128, 512], F32)

    p.begin()
    scr = ln_scratch(p)
    p.gdma(lambda e: e.dma_start(out=ident[:, :], in_=ident_in))
    p.gdma(lambda e: e.dma_start(
        out=wg_sb[:, :].rearrange("p (k h) -> p k h", k=DC),
        in_=wgT.rearrange("(k p) h -> p k h", p=128)))
    p.gdma(lambda e: e.dma_start(
        out=wu_sb[:, :].rearrange("p (k h) -> p k h", k=DC),
        in_=wuT.rearrange("(k p) h -> p k h", p=128)))
    p.gdma(lambda e: e.dma_start(
        out=wd_sb[:, :].rearrange("p (m n) -> p m n", m=HC),
        in_=wdT.rearrange("(m p) n -> p m n", p=128)))

    for grp in range(NG):
        for cc in range(2):
            c = grp * 2 + cc
            rsl = r_g[:, cc * D:(cc + 1) * D]
            p.dma(lambda e, c=c, rsl=rsl: e.dma_start(out=rsl, in_=xs[c * 128:(c + 1) * 128, :]))
            emit_ln(p, rsl, xn_c[:, :], scr)
            dst = xnT_g[:, :].rearrange("p (k t) -> p k t", k=DC)[:, :, cc * 128:cc * 128 + 128]
            emit_transpose_cols(p, xn_c[:, :], DC, ptr, ident, dst)
        for m in range(HC):
            for k in range(DC):
                p.t(lambda e, m=m, k=k: e.matmul(
                    pmm[:, :], wg_sb[:, k * H + m * 128:k * H + (m + 1) * 128],
                    xnT_g[:, k * G:(k + 1) * G], start=(k == 0), stop=(k == DC - 1)))
            p.s(lambda e: e.activation(gt[:, :], pmm[:, :], AF.Silu, scale=1.0 / 127.0))
            for k in range(DC):
                p.t(lambda e, m=m, k=k: e.matmul(
                    pmm[:, :], wu_sb[:, k * H + m * 128:k * H + (m + 1) * 128],
                    xnT_g[:, k * G:(k + 1) * G], start=(k == 0), stop=(k == DC - 1)))
            p.s(lambda e: e.activation(ut[:, :], pmm[:, :], AF.Copy, scale=1.0 / 127.0))
            p.v(lambda e, m=m: e.tensor_tensor(hT_g[:, m * G:(m + 1) * G], gt[:, :], ut[:, :], ALU.mult))
        for cc in range(2):
            c = grp * 2 + cc
            for half in range(2):
                for m in range(HC):
                    p.t(lambda e, m=m, cc=cc, half=half: e.matmul(
                        pdn[:, :], hT_g[:, m * G + cc * 128:m * G + cc * 128 + 128],
                        wd_sb[:, m * D + half * 512:m * D + (half + 1) * 512],
                        start=(m == 0), stop=(m == HC - 1)))
                p.v(lambda e, cc=cc, half=half: e.scalar_tensor_tensor(
                    outs[:, half * 512:(half + 1) * 512], pdn[:, :], 1.0 / 127.0,
                    r_g[:, cc * D + half * 512:cc * D + (half + 1) * 512], ALU.mult, ALU.add))
            p.dma(lambda e, c=c: e.dma_start(out=ys[c * 128:(c + 1) * 128, :], in_=outs[:, :]))
    return p.end()


def _ffn_inmaps(x_full, wg, wu, wd):
    wgT = np.ascontiguousarray(wg.T).astype(np.int32)
    wuT = np.ascontiguousarray(wu.T).astype(np.int32)
    wdT = np.ascontiguousarray(wd.T).astype(np.int32)
    maps = []
    for cid in range(NCORE):
        maps.append({
            "xs": _f32(x_full[cid * TOK:(cid + 1) * TOK]),
            "wgT": wgT, "wuT": wuT, "wdT": wdT, "ident": IDENT,
        })
    return maps


def run_ffn(x_full, wg, wu, wd, nc_cache={}):
    if "ffn" not in nc_cache:
        nc_cache["ffn"] = build_ffn_int8()
    nc = nc_cache["ffn"]
    res = run_bass_kernel_spmd(nc, _ffn_inmaps(x_full, wg, wu, wd), core_ids=list(range(NCORE)))
    return np.concatenate([res.results[i]["ys"] for i in range(NCORE)], axis=0)



HALO_CH = 2
XCH = NCH + 2 * HALO_CH


def _l0_rope_tables():
    h = 16
    d = 64
    inv = 1.0 / 10000.0 ** (np.arange(0, d, 2, dtype=np.float64) / d)
    freqs = np.arange(h, dtype=np.float64)[:, None] * inv[None, :]
    emb = np.concatenate([freqs, freqs], axis=-1)
    C = np.cos(emb).reshape(-1).astype(np.float32)
    S = np.sin(emb).reshape(-1).astype(np.float32)
    return C, S


def build_l0a():
    p = Prog("l0a")
    xh = p.inp("xh", [XCH * 128, D], F32)
    wqT = p.inp("wqT", [D, D], F32)
    wkT = p.inp("wkT", [D, D], F32)
    wvT = p.inp("wvT", [D, D], F32)
    woT = p.inp("woT", [D, D], F32)
    bq_r = p.inp("bq", [1, D], F32)
    bk_r = p.inp("bk", [1, D], F32)
    bv_r = p.inp("bv", [1, D], F32)
    bo_r = p.inp("bo", [1, D], F32)
    ident_in = p.inp("ident", [128, 128], F32)
    ropeq_in = p.inp("ropeq", [128, 4 * 512], F32)
    ropek_in = p.inp("ropek", [128, 4 * 512], F32)
    tri_in = p.inp("tri", [128, 2 * 128], F32)
    e16_in = p.inp("e16", [128, 8 * 16], F32)
    sel16_in = p.inp("sel16", [16, 16 * 128], F32)
    sel64_in = p.inp("sel64", [16, 16 * 64], F32)
    kill01_in = p.inp("kill01", [1, NCH * 5], F32)
    ys = p.outp("ys", [TOK, D], F32)

    NH = 16
    KW = XCH * 128

    wA = p.sb("wA", [128, DC * D], BF16)
    wB = p.sb("wB", [128, DC * D], BF16)
    wC = p.sb("wC", [128, DC * D], BF16)
    ropeq = p.sb("ropeq_sb", [128, 4 * 512], BF16)
    ropek = p.sb("ropek_sb", [128, 4 * 512], BF16)
    rs1 = p.sb("rs1", [128, 512], BF16)
    rs2 = p.sb("rs2", [128, 512], BF16)
    ident = p.sb("identb", [128, 128], BF16)
    tri = p.sb("tri_sb", [128, 2 * 128], F32)
    e16 = p.sb("e16_sb", [128, 8 * 16], BF16)
    sel16 = p.sb("sel16_sb", [16, 16 * 128], BF16)
    sel64 = p.sb("sel64_sb", [16, 16 * 64], BF16)
    kill01 = p.sb("kill01_sb", [1, NCH * 5], F32)
    killrow = p.sb("killrow", [1, 5 * 128], BF16)
    neg_row = p.sb("neg_row", [1, 128], F32)
    ones_c = p.sb("ones_c", [128, 1], BF16)
    ones_r = p.sb("ones_r", [1, 128], BF16)
    bqc = p.sb("bqc", [128, DC], F32)
    bkc = p.sb("bkc", [128, DC], F32)
    brow_b = p.sb("brow_b", [1, D], BF16)
    bx_bf = p.sb("bx_bf", [1, D], BF16)
    bvB = p.sb("bvB", [128, D], F32)
    boB = p.sb("boB", [128, D], F32)

    KT = p.sb("KT", [128, DC * KW], BF16)
    V = p.sb("V", [128, XCH * D], BF16)
    QTc = p.sb("QTc", [128, DC * 128], BF16)
    r_c = p.sb("r_c", [128, D], F32)
    xn_c = p.sb("xn_c", [128, D], BF16)
    xnT_c = p.sb("xnT_c", [128, DC * 128], BF16)
    sqs = p.sb("sqs", [128, 128], BF16)
    qn2c = p.sb("qn2c", [16, 128], F32)
    kn2m = p.sb("kn2m", [16, 1], F32)
    kn2c = p.sb("kn2c", [16, 1], F32)
    qbias_c = p.sb("qbias_c", [16, 128], BF16)
    PT = p.sb("PT", [128, 5 * 128], BF16)
    rS = p.sb("rS", [128, 16], F32)
    rSb = p.sb("rSb", [128, 16], BF16)
    rT = p.sb("rT", [16, 128], BF16)
    rB = p.sb("rB", [64, 128], F32)
    ctxT = p.sb("ctxT", [128, D], BF16)
    outs = p.sb("outs", [128, D], F32)

    P1 = p.ps("P1", [128, 512], F32)
    P2 = p.ps("P2", [128, 640], F32)
    P3 = p.ps("P3", [128, 1024], F32)
    P4 = p.ps("P4", [128, 1024], BF16)
    P5 = p.ps("P5", [128, 128], F32)
    P6 = p.ps("P6", [16, 128], BF16)

    p.begin()
    scr = ln_scratch(p)
    p.gdma(lambda e: e.dma_start(out=ident[:, :], in_=ident_in))
    p.gdma(lambda e: e.dma_start(out=ropeq[:, :], in_=ropeq_in))
    p.gdma(lambda e: e.dma_start(out=ropek[:, :], in_=ropek_in))
    p.dma(lambda e: e.dma_start(out=tri[:, :], in_=tri_in))
    p.gdma(lambda e: e.dma_start(out=e16[:, :], in_=e16_in))
    p.gdma(lambda e: e.dma_start(out=sel16[:, :], in_=sel16_in))
    p.gdma(lambda e: e.dma_start(out=sel64[:, :], in_=sel64_in))
    p.dma(lambda e: e.dma_start(out=kill01[:, :], in_=kill01_in))
    p.v(lambda e: e.memset(ones_c[:, :], 1.0))
    p.v(lambda e: e.memset(ones_r[:, :], 1.0))
    p.v(lambda e: e.memset(neg_row[:, :], -1e30))
    p.v(lambda e: e.memset(kn2m[:, :], 0.0))

    def rope_fold(dst, rope_sb, src_in):
        p.gdma(lambda e: e.dma_start(
            out=wB[:, :].rearrange("p (k n) -> p k n", k=DC),
            in_=src_in.rearrange("(k p) n -> p k n", p=128)))
        for k in range(DC):
            b0 = k * D
            ev = wB[:, b0:b0 + D:2]
            od = wB[:, b0 + 1:b0 + D:2]
            dev = dst[:, b0:b0 + D:2]
            dod = dst[:, b0 + 1:b0 + D:2]
            ce = rope_sb[:, 0:512]
            se = rope_sb[:, 512:1024]
            co = rope_sb[:, 1024:1536]
            so = rope_sb[:, 1536:2048]
            p.v(lambda e, od=od, se=se: e.tensor_tensor(rs1[:, :], od, se, ALU.mult))
            p.v(lambda e, ev=ev, so=so: e.tensor_tensor(rs2[:, :], ev, so, ALU.mult))
            p.v(lambda e, ev=ev, ce=ce, dev=dev: e.tensor_tensor(dev, ev, ce, ALU.mult))
            p.v(lambda e, dev=dev: e.tensor_tensor(dev, dev, rs1[:, :], ALU.subtract))
            p.v(lambda e, od=od, co=co, dod=dod: e.tensor_tensor(dod, od, co, ALU.mult))
            p.v(lambda e, dod=dod: e.tensor_tensor(dod, dod, rs2[:, :], ALU.add))

    rope_fold(wA, ropeq, wqT)
    def rope_fold_k():
        p.gdma(lambda e: e.dma_start(
            out=wC[:, :].rearrange("p (k n) -> p k n", k=DC),
            in_=wkT.rearrange("(k p) n -> p k n", p=128)))
        for k in range(DC):
            b0 = k * D
            ev = wC[:, b0:b0 + D:2]
            od = wC[:, b0 + 1:b0 + D:2]
            dev = wB[:, b0:b0 + D:2]
            dod = wB[:, b0 + 1:b0 + D:2]
            ce = ropek[:, 0:512]
            se = ropek[:, 512:1024]
            co = ropek[:, 1024:1536]
            so = ropek[:, 1536:2048]
            p.v(lambda e, od=od, se=se: e.tensor_tensor(rs1[:, :], od, se, ALU.mult))
            p.v(lambda e, ev=ev, so=so: e.tensor_tensor(rs2[:, :], ev, so, ALU.mult))
            p.v(lambda e, ev=ev, ce=ce, dev=dev: e.tensor_tensor(dev, ev, ce, ALU.mult))
            p.v(lambda e, dev=dev: e.tensor_tensor(dev, dev, rs1[:, :], ALU.subtract))
            p.v(lambda e, od=od, co=co, dod=dod: e.tensor_tensor(dod, od, co, ALU.mult))
            p.v(lambda e, dod=dod: e.tensor_tensor(dod, dod, rs2[:, :], ALU.add))

    rope_fold_k()
    p.gdma(lambda e: e.dma_start(
        out=wC[:, :].rearrange("p (k n) -> p k n", k=DC),
        in_=wvT.rearrange("(k p) n -> p k n", p=128)))

    def rope_bias_row(rope_sb, b_in):
        p.gdma(lambda e: e.dma_start(out=bx_bf[:, :], in_=b_in))
        ev = bx_bf[:, 0:D:2]
        od = bx_bf[:, 1:D:2]
        p.v(lambda e: e.tensor_tensor(rs1[0:1, 0:512], od, rope_sb[0:1, 512:1024], ALU.mult))
        p.v(lambda e: e.tensor_tensor(rs2[0:1, 0:512], ev, rope_sb[0:1, 1536:2048], ALU.mult))
        p.v(lambda e: e.tensor_tensor(brow_b[0:1, 0:D:2], ev, rope_sb[0:1, 0:512], ALU.mult))
        p.v(lambda e: e.tensor_tensor(brow_b[0:1, 0:D:2], brow_b[0:1, 0:D:2], rs1[0:1, 0:512], ALU.subtract))
        p.v(lambda e: e.tensor_tensor(brow_b[0:1, 1:D:2], od, rope_sb[0:1, 1024:1536], ALU.mult))
        p.v(lambda e: e.tensor_tensor(brow_b[0:1, 1:D:2], brow_b[0:1, 1:D:2], rs2[0:1, 0:512], ALU.add))

    def row_to_cols(row_bf, cols_f32):
        for m in range(DC):
            p.t(lambda e, m=m: e.matmul(P1[:, m:m + 1], row_bf[0:1, m * 128:(m + 1) * 128],
                                        ones_r[0:1, 0:1], start=True, stop=True))
        p.v(lambda e: e.tensor_copy(cols_f32[:, :], P1[:, 0:DC]))

    rope_bias_row(ropeq, bq_r)
    row_to_cols(brow_b, bqc)
    rope_bias_row(ropek, bk_r)
    row_to_cols(brow_b, bkc)
    p.gdma(lambda e: e.dma_start(out=bx_bf[:, :], in_=bv_r))
    for half in range(2):
        p.t(lambda e, half=half: e.matmul(P1[:, :], ones_r[:, :],
                                          bx_bf[0:1, half * 512:(half + 1) * 512],
                                          start=True, stop=True))
        p.v(lambda e, half=half: e.tensor_copy(bvB[:, half * 512:(half + 1) * 512], P1[:, :]))
    p.gdma(lambda e: e.dma_start(out=bx_bf[:, :], in_=bo_r))
    for half in range(2):
        p.t(lambda e, half=half: e.matmul(P1[:, :], ones_r[:, :],
                                          bx_bf[0:1, half * 512:(half + 1) * 512],
                                          start=True, stop=True))
        p.v(lambda e, half=half: e.tensor_copy(boB[:, half * 512:(half + 1) * 512], P1[:, :]))

    for c in range(XCH):
        p.dma(lambda e, c=c: e.dma_start(out=r_c[:, :], in_=xh[c * 128:(c + 1) * 128, :]))
        emit_ln(p, r_c[:, :], xn_c[:, :], scr)
        emit_transpose_cols(p, xn_c[:, :], DC, P4, ident,
                            xnT_c[:, :].rearrange("p (k t) -> p k t", k=DC))
        for m in range(DC):
            for k in range(DC):
                p.t(lambda e, m=m, k=k: e.matmul(
                    P1[:, 0:128], wB[:, k * D + m * 128:k * D + (m + 1) * 128],
                    xnT_c[:, k * 128:(k + 1) * 128], start=(k == 0), stop=(k == DC - 1)))
            p.v(lambda e, m=m, c=c: e.tensor_scalar(
                KT[:, m * KW + c * 128:m * KW + (c + 1) * 128],
                P1[:, 0:128], bkc[:, m:m + 1], None, ALU.add))
            p.s(lambda e, m=m, c=c: e.activation(
                sqs[:, :], KT[:, m * KW + c * 128:m * KW + (c + 1) * 128], AF.Square))
            p.t(lambda e, m=m: e.matmul(P5[0:16, :], e16[:, m * 16:(m + 1) * 16],
                                        sqs[:, :], start=(m == 0), stop=(m == DC - 1)))
        p.v(lambda e: e.reduce_max(kn2c[:, :], P5[0:16, :], AX.X))
        p.v(lambda e: e.tensor_tensor(kn2m[:, :], kn2m[:, :], kn2c[:, :], ALU.max))
        for half in range(2):
            for k in range(DC):
                p.t(lambda e, half=half, k=k: e.matmul(
                    P1[:, :], xnT_c[:, k * 128:(k + 1) * 128],
                    wC[:, k * D + half * 512:k * D + (half + 1) * 512],
                    start=(k == 0), stop=(k == DC - 1)))
            p.v(lambda e, half=half, c=c: e.tensor_tensor(
                V[:, c * D + half * 512:c * D + (half + 1) * 512], P1[:, :],
                bvB[:, half * 512:(half + 1) * 512], ALU.add))

    p.gdma(lambda e: e.dma_start(
        out=wB[:, :].rearrange("p (k n) -> p k n", k=DC),
        in_=woT.rearrange("(k p) n -> p k n", p=128)))

    for oc in range(NCH):
        hc = oc + HALO_CH
        p.dma(lambda e, hc=hc: e.dma_start(out=r_c[:, :], in_=xh[hc * 128:(hc + 1) * 128, :]))
        emit_ln(p, r_c[:, :], xn_c[:, :], scr)
        emit_transpose_cols(p, xn_c[:, :], DC, P4, ident,
                            xnT_c[:, :].rearrange("p (k t) -> p k t", k=DC))
        for m in range(DC):
            for k in range(DC):
                p.t(lambda e, m=m, k=k: e.matmul(
                    P1[:, 0:128], wA[:, k * D + m * 128:k * D + (m + 1) * 128],
                    xnT_c[:, k * 128:(k + 1) * 128], start=(k == 0), stop=(k == DC - 1)))
            p.v(lambda e, m=m: e.tensor_scalar(
                QTc[:, m * 128:(m + 1) * 128], P1[:, 0:128], bqc[:, m:m + 1], None, ALU.add))
            p.s(lambda e, m=m: e.activation(sqs[:, :], QTc[:, m * 128:(m + 1) * 128], AF.Square))
            p.t(lambda e, m=m: e.matmul(P5[0:16, :], e16[:, m * 16:(m + 1) * 16],
                                        sqs[:, :], start=(m == 0), stop=(m == DC - 1)))
        p.v(lambda e: e.tensor_copy(qn2c[:, :], P5[0:16, :]))
        p.v(lambda e: e.tensor_scalar(qn2c[:, :], qn2c[:, :], kn2m[0:16, 0:1],
                                      1.0, ALU.mult, ALU.mult))
        p.s(lambda e: e.activation(qn2c[:, :], qn2c[:, :], AF.Sqrt))
        p.v(lambda e: e.tensor_scalar(qbias_c[:, :], qn2c[:, :], -1.0, 50.0, ALU.mult, ALU.add))
        for di in range(5):
            p.v(lambda e, di=di, oc=oc: e.tensor_scalar(
                killrow[0:1, di * 128:(di + 1) * 128], neg_row[0:1, :],
                kill01[0:1, oc * 5 + di:oc * 5 + di + 1], None, ALU.mult))
        for h in range(NH):
            sl = h // 2
            po = (h % 2) * 64
            for di in range(5):
                kc = hc + di - 2
                p.t(lambda e, di=di, kc=kc, sl=sl, po=po: e.matmul(
                    P2[:, di * 128:(di + 1) * 128],
                    KT[po:po + 64, sl * KW + kc * 128:sl * KW + (kc + 1) * 128],
                    QTc[po:po + 64, sl * 128:(sl + 1) * 128],
                    start=True, stop=False))
                p.t(lambda e, di=di, h=h: e.matmul(
                    P2[:, di * 128:(di + 1) * 128], sel16[:, h * 128:(h + 1) * 128],
                    qbias_c[:, :], start=False, stop=False))
                p.t(lambda e, di=di: e.matmul(
                    P2[:, di * 128:(di + 1) * 128],
                    killrow[0:1, di * 128:(di + 1) * 128], ones_r[:, :],
                    start=False, stop=True))
            p.v(lambda e: e.tensor_tensor(P2[:, 0:128], P2[:, 0:128], tri[:, 0:128], ALU.add))
            p.v(lambda e: e.tensor_tensor(P2[:, 512:640], P2[:, 512:640], tri[:, 128:256], ALU.add))
            p.s(lambda e: e.activation(PT[:, :], P2[:, :], AF.Exp))
            for di in range(5):
                p.t(lambda e, di=di: e.matmul(P5[:, 0:1], PT[:, di * 128:(di + 1) * 128],
                                              ones_c[:, 0:1], start=(di == 0), stop=(di == 4)))
            p.v(lambda e, h=h: e.tensor_copy(rS[:, h:h + 1], P5[:, 0:1]))
            for di in range(5):
                kc = hc + di - 2
                p.t(lambda e, di=di, kc=kc, h=h, sl=sl, po=po: e.matmul(
                    P3[po:po + 64, sl * 128:(sl + 1) * 128],
                    V[:, kc * D + h * 64:kc * D + (h + 1) * 64],
                    PT[:, di * 128:(di + 1) * 128], start=(di == 0), stop=(di == 4)))
        p.v(lambda e: e.reciprocal(rS[:, :], rS[:, :]))
        p.v(lambda e: e.tensor_copy(rSb[:, :], rS[:, :]))
        p.t(lambda e: e.transpose(P6[:, :], rSb[:, :], ident[:, :]))
        p.v(lambda e: e.tensor_copy(rT[:, :], P6[0:16, :]))
        for h in range(NH):
            sl = h // 2
            po = (h % 2) * 64
            p.t(lambda e, h=h: e.matmul(P5[0:64, :], sel64[:, h * 64:(h + 1) * 64],
                                        rT[:, :], start=True, stop=True))
            p.v(lambda e: e.tensor_copy(rB[:, :], P5[0:64, :]))
            p.v(lambda e, sl=sl, po=po: e.tensor_tensor(
                ctxT[po:po + 64, sl * 128:(sl + 1) * 128],
                P3[po:po + 64, sl * 128:(sl + 1) * 128], rB[:, :], ALU.mult))
        for half in range(2):
            for k in range(DC):
                p.t(lambda e, half=half, k=k: e.matmul(
                    P1[:, :], ctxT[:, k * 128:(k + 1) * 128],
                    wB[:, k * D + half * 512:k * D + (half + 1) * 512],
                    start=(k == 0), stop=(k == DC - 1)))
            p.v(lambda e, half=half: e.tensor_tensor(
                outs[:, half * 512:(half + 1) * 512], P1[:, :],
                boB[:, half * 512:(half + 1) * 512], ALU.add))
            p.v(lambda e, half=half: e.tensor_tensor(
                outs[:, half * 512:(half + 1) * 512], outs[:, half * 512:(half + 1) * 512],
                r_c[:, half * 512:(half + 1) * 512], ALU.add))
        p.dma(lambda e, oc=oc: e.dma_start(out=ys[oc * 128:(oc + 1) * 128, :], in_=outs[:, :]))

    return p.end()


def _l0a_host():
    C, S = _l0_rope_tables()
    def rep(v):
        return np.ascontiguousarray(np.broadcast_to(v[None, :], (128, 512)), dtype=np.float32)
    CE, SE = C[0::2], S[0::2]
    CO, SO = C[1::2], S[1::2]
    ropeq = np.concatenate([rep(CE / 8), rep(SE / 8), rep(CO / 8), rep(SO / 8)], axis=1)
    ropek = np.concatenate([rep(CE), rep(SE), rep(CO), rep(SO)], axis=1)
    pp = np.arange(128)[:, None]
    ff = np.arange(128)[None, :]
    tri_l = np.where(pp >= ff, 0.0, -1e30).astype(np.float32)
    tri_r = np.where(pp <= ff, 0.0, -1e30).astype(np.float32)
    tri = np.concatenate([tri_l, tri_r], axis=1)
    e16 = np.zeros((128, 8 * 16), dtype=np.float32)
    for m in range(8):
        e16[0:64, m * 16 + 2 * m] = 1.0
        e16[64:128, m * 16 + 2 * m + 1] = 1.0
    sel16 = np.zeros((16, 16 * 128), dtype=np.float32)
    for h in range(16):
        sel16[h, h * 128:(h + 1) * 128] = 1.0
    sel64 = np.zeros((16, 16 * 64), dtype=np.float32)
    for h in range(16):
        sel64[h, h * 64:(h + 1) * 64] = 1.0
    return dict(ropeq=ropeq, ropek=ropek, tri=tri, e16=e16, sel16=sel16,
                sel64=sel64, ident=IDENT)


def _l0a_inmaps(x_full, inputs):
    hostc = _l0a_host()
    wqT = _f32(np.asarray(inputs["L0_attn_Wq"]).T)
    wkT = _f32(np.asarray(inputs["L0_attn_Wk"]).T)
    wvT = _f32(np.asarray(inputs["L0_attn_Wv"]).T)
    woT = _f32(np.asarray(inputs["L0_attn_Wo"]).T)
    bq = _f32(inputs["L0_attn_bq"]).reshape(1, D)
    bk = _f32(inputs["L0_attn_bk"]).reshape(1, D)
    bv = _f32(inputs["L0_attn_bv"]).reshape(1, D)
    bo = _f32(inputs["L0_attn_bo"]).reshape(1, D)
    maps = []
    HALO = HALO_CH * 128
    for cid in range(NCORE):
        t0 = cid * TOK
        seq0 = (cid // 4) * 8192
        seq1 = seq0 + 8192
        lo = t0 - HALO
        hi = t0 + TOK + HALO
        xhp = np.zeros((XCH * 128, D), dtype=np.float32)
        vlo = max(lo, seq0)
        vhi = min(hi, seq1)
        xhp[vlo - lo:vhi - lo] = x_full[vlo:vhi]
        kill = np.zeros((NCH, 5), dtype=np.float32)
        for oc in range(NCH):
            for di in range(5):
                kc_lo = t0 + (oc + di - 2) * 128
                if kc_lo < seq0 or kc_lo + 128 > seq1:
                    kill[oc, di] = 1.0
        maps.append(dict(xh=xhp, wqT=wqT, wkT=wkT, wvT=wvT, woT=woT,
                         bq=bq, bk=bk, bv=bv, bo=bo,
                         kill01=kill.reshape(1, -1), **hostc))
    return maps


def run_l0a(x_full, inputs, nc_cache={}):
    if "l0a" not in nc_cache:
        nc_cache["l0a"] = build_l0a()
    res = run_bass_kernel_spmd(nc_cache["l0a"], _l0a_inmaps(x_full, inputs),
                               core_ids=list(range(NCORE)))
    return np.concatenate([res.results[i]["ys"] for i in range(NCORE)], axis=0)



SEQ = 8192
SCH = SEQ // 128
GN_EPS = 1e-5


def _l1_gammas():
    lo = float(np.log(1.0 - 0.999))
    hi = float(np.log(1.0 - 0.9))
    return 1.0 - np.exp(np.linspace(lo, hi, 16, dtype=np.float64))


def build_l1a():
    p = Prog("l1a")
    xq = p.inp("xq", [SEQ, D], F32)
    wqT = p.inp("wqT", [D, D], F32)
    wkT = p.inp("wkT", [D, D], F32)
    wvT = p.inp("wvT", [D, D], F32)
    woT = p.inp("woT", [D, D], F32)
    gnw_in = p.inp("gnw", [128, DC], F32)
    gnb_in = p.inp("gnb", [128, DC], F32)
    ident_in = p.inp("ident", [128, 128], F32)
    dt_in = p.inp("dtm", [128, 16 * 128], F32)
    lamq_in = p.inp("lamq", [128, DC * 128], F32)
    lamk_in = p.inp("lamk", [128, 16], F32)
    e2_in = p.inp("e2", [128, 2], F32)
    eb_in = p.inp("eb", [2, 128], F32)
    ys = p.outp("ys", [SEQ, D], F32)

    NH = 16
    gam = _l1_gammas()
    lam_c = [float(g ** 128) for g in gam]

    wA = p.sb("wA", [128, DC * D], BF16)
    wB = p.sb("wB", [128, DC * D], BF16)
    wC = p.sb("wC", [128, DC * D], BF16)
    ident = p.sb("identb", [128, 128], BF16)
    dtm = p.sb("dtm_sb", [128, 16 * 128], F32)
    lamq = p.sb("lamq_sb", [128, DC * 128], BF16)
    lamk = p.sb("lamk_sb", [128, 16], F32)
    e2 = p.sb("e2_sb", [128, 2], F32)
    eb = p.sb("eb_sb", [2, 128], F32)
    gnw = p.sb("gnw_sb", [128, DC], F32)
    gnb = p.sb("gnb_sb", [128, DC], F32)
    r_c = p.sb("r_c", [128, D], F32)
    xn_c = p.sb("xn_c", [128, D], BF16)
    xnT_c = p.sb("xnT_c", [128, DC * 128], BF16)
    QTc = p.sb("QTc", [128, DC * 128], BF16)
    QLc = p.sb("QLc", [128, DC * 128], BF16)
    KTc = p.sb("KTc", [128, DC * 128], BF16)
    Ktm = p.sb("Ktm", [128, D], BF16)
    Klam = p.sb("Klam", [128, 64], BF16)
    Vc = p.sb("Vc", [128, D], BF16)
    St = p.sb("St", [128, 128], BF16)
    state = p.sb("state", [128, 512], F32)
    stateb = p.sb("stateb", [128, 512], BF16)
    ysl = p.sb("ysl", [128, 128], BF16)
    ysq = p.sb("ysq", [128, 128], BF16)
    acc1 = p.sb("acc1", [128, 1], F32)
    acc2 = p.sb("acc2", [128, 1], F32)
    ysum = p.sb("ysum", [128, DC], F32)
    yssq = p.sb("yssq", [128, DC], F32)
    sst = p.sb("sst", [2, DC], F32)
    ssq2 = p.sb("ssq2", [2, DC], F32)
    smean = p.sb("smean", [2, DC], F32)
    srstd = p.sb("srstd", [2, DC], F32)
    scomb = p.sb("scomb", [2, 2 * DC], F32)
    statB = p.sb("statB", [128, 2 * DC], F32)
    scale_col = p.sb("scale_col", [128, DC], F32)
    bias_col = p.sb("bias_col", [128, DC], F32)
    geps = p.sb("geps", [128, 1], F32)
    ygn = p.sb("ygn", [128, DC * 128], BF16)
    outs = p.sb("outs", [128, D], F32)

    P1 = p.ps("P1", [128, 512], F32)
    P3 = p.ps("P3", [128, 1024], F32)
    P4 = p.ps("P4", [128, 1024], BF16)
    P5 = p.ps("P5", [128, 128], F32)
    PS2 = p.ps("PS2", [2, DC], F32)

    ybuf = p.dram("ybuf", [D, SEQ], BF16)

    p.begin()
    scr = ln_scratch(p)
    p.gdma(lambda e: e.dma_start(out=ident[:, :], in_=ident_in))
    p.dma(lambda e: e.dma_start(out=dtm[:, :], in_=dt_in))
    p.gdma(lambda e: e.dma_start(out=lamq[:, :], in_=lamq_in))
    p.dma(lambda e: e.dma_start(out=lamk[:, :], in_=lamk_in))
    p.dma(lambda e: e.dma_start(out=e2[:, :], in_=e2_in))
    p.dma(lambda e: e.dma_start(out=eb[:, :], in_=eb_in))
    p.dma(lambda e: e.dma_start(out=gnw[:, :], in_=gnw_in))
    p.dma(lambda e: e.dma_start(out=gnb[:, :], in_=gnb_in))
    p.gdma(lambda e: e.dma_start(
        out=wA[:, :].rearrange("p (k n) -> p k n", k=DC),
        in_=wqT.rearrange("(k p) n -> p k n", p=128)))
    p.gdma(lambda e: e.dma_start(
        out=wB[:, :].rearrange("p (k n) -> p k n", k=DC),
        in_=wkT.rearrange("(k p) n -> p k n", p=128)))
    p.gdma(lambda e: e.dma_start(
        out=wC[:, :].rearrange("p (k n) -> p k n", k=DC),
        in_=wvT.rearrange("(k p) n -> p k n", p=128)))
    p.v(lambda e: e.memset(state[:, :], 0.0))
    p.v(lambda e: e.memset(stateb[:, :], 0.0))
    p.v(lambda e: e.memset(ysum[:, :], 0.0))
    p.v(lambda e: e.memset(yssq[:, :], 0.0))
    p.v(lambda e: e.memset(geps[:, :], GN_EPS))

    for c in range(SCH):
        p.dma(lambda e, c=c: e.dma_start(out=r_c[:, :], in_=xq[c * 128:(c + 1) * 128, :]))
        emit_ln(p, r_c[:, :], xn_c[:, :], scr)
        emit_transpose_cols(p, xn_c[:, :], DC, P4, ident,
                            xnT_c[:, :].rearrange("p (k t) -> p k t", k=DC))
        for m in range(DC):
            for k in range(DC):
                p.t(lambda e, m=m, k=k: e.matmul(
                    P1[:, 0:128], wA[:, k * D + m * 128:k * D + (m + 1) * 128],
                    xnT_c[:, k * 128:(k + 1) * 128], start=(k == 0), stop=(k == DC - 1)))
            p.v(lambda e, m=m: e.tensor_copy(QTc[:, m * 128:(m + 1) * 128], P1[:, 0:128]))
            p.v(lambda e, m=m: e.tensor_tensor(QLc[:, m * 128:(m + 1) * 128],
                                               QTc[:, m * 128:(m + 1) * 128],
                                               lamq[:, m * 128:(m + 1) * 128], ALU.mult))
        for m in range(DC):
            for k in range(DC):
                p.t(lambda e, m=m, k=k: e.matmul(
                    P1[:, 0:128], wB[:, k * D + m * 128:k * D + (m + 1) * 128],
                    xnT_c[:, k * 128:(k + 1) * 128], start=(k == 0), stop=(k == DC - 1)))
            p.v(lambda e, m=m: e.tensor_copy(KTc[:, m * 128:(m + 1) * 128], P1[:, 0:128]))
        for half in range(2):
            for k in range(DC):
                p.t(lambda e, half=half, k=k: e.matmul(
                    P1[:, :], xnT_c[:, k * 128:(k + 1) * 128],
                    wC[:, k * D + half * 512:k * D + (half + 1) * 512],
                    start=(k == 0), stop=(k == DC - 1)))
            p.v(lambda e, half=half: e.tensor_copy(Vc[:, half * 512:(half + 1) * 512], P1[:, :]))
        emit_transpose_cols(p, KTc[:, :].rearrange("p (a b) -> p (b a)", a=DC) if False else KTc[:, :], DC, P4, ident, None) if False else None
        for j in range(DC):
            p.t(lambda e, j=j: e.transpose(P4[:, j * 128:(j + 1) * 128],
                                           KTc[:, j * 128:(j + 1) * 128], ident[:, :]))
        p.v(lambda e: e.tensor_copy(Ktm[:, :].rearrange("p (a b) -> p a b", a=DC),
                                    P4[:, :].rearrange("p (a b) -> p a b", a=DC)))
        for h in range(NH):
            sl = h // 2
            po = (h % 2) * 64
            p.t(lambda e, sl=sl, po=po: e.matmul(
                P5[:, :], KTc[po:po + 64, sl * 128:(sl + 1) * 128],
                QTc[po:po + 64, sl * 128:(sl + 1) * 128], start=True, stop=True))
            p.v(lambda e, h=h: e.tensor_tensor(St[:, :], P5[:, :],
                                               dtm[:, h * 128:(h + 1) * 128], ALU.mult))
            p.t(lambda e, h=h, sl=sl, po=po: e.matmul(
                P3[po:po + 64, sl * 128:(sl + 1) * 128],
                Vc[:, h * 64:(h + 1) * 64], St[:, :], start=True, stop=False))
            p.t(lambda e, h=h, sl=sl, po=po: e.matmul(
                P3[po:po + 64, sl * 128:(sl + 1) * 128],
                stateb[po:po + 64, sl * 64:(sl + 1) * 64],
                QLc[po:po + 64, sl * 128:(sl + 1) * 128], start=False, stop=True))
            p.v(lambda e, h=h: e.tensor_scalar(Klam[:, :], Ktm[:, h * 64:(h + 1) * 64],
                                               lamk[:, h:h + 1], None, ALU.mult))
            p.t(lambda e, h=h: e.matmul(P5[0:64, 0:64], Klam[:, :],
                                        Vc[:, h * 64:(h + 1) * 64], start=True, stop=True))
            p.v(lambda e, h=h, sl=sl, po=po, lc=lam_c[h]: e.tensor_scalar(
                state[po:po + 64, sl * 64:(sl + 1) * 64],
                state[po:po + 64, sl * 64:(sl + 1) * 64], lc, None, ALU.mult))
            p.v(lambda e, sl=sl, po=po: e.tensor_tensor(
                state[po:po + 64, sl * 64:(sl + 1) * 64],
                state[po:po + 64, sl * 64:(sl + 1) * 64], P5[0:64, 0:64], ALU.add))
            p.v(lambda e, sl=sl, po=po: e.tensor_copy(
                stateb[po:po + 64, sl * 64:(sl + 1) * 64],
                state[po:po + 64, sl * 64:(sl + 1) * 64]))
        for s in range(DC):
            p.s(lambda e, s=s: e.activation(ysl[:, :], P3[:, s * 128:(s + 1) * 128],
                                            AF.Copy, accum_out=acc1[:, :]))
            p.s(lambda e, s=s: e.activation(ysq[:, :], P3[:, s * 128:(s + 1) * 128],
                                            AF.Square, accum_out=acc2[:, :]))
            p.v(lambda e, s=s: e.tensor_tensor(ysum[:, s:s + 1], ysum[:, s:s + 1],
                                               acc1[:, :], ALU.add))
            p.v(lambda e, s=s: e.tensor_tensor(yssq[:, s:s + 1], yssq[:, s:s + 1],
                                               acc2[:, :], ALU.add))
            p.dma(lambda e, s=s, c=c: e.dma_start(
                out=ybuf.ap()[s * 128:(s + 1) * 128, c * 128:(c + 1) * 128], in_=ysl[:, :]))

    NTOT = float(SEQ * 64)
    p.t(lambda e: e.matmul(PS2[:, :], e2[:, 0:2], ysum[:, :], start=True, stop=True))
    p.v(lambda e: e.tensor_copy(sst[:, :], PS2[:, :]))
    p.t(lambda e: e.matmul(PS2[:, :], e2[:, 0:2], yssq[:, :], start=True, stop=True))
    p.v(lambda e: e.tensor_copy(ssq2[:, :], PS2[:, :]))
    p.v(lambda e: e.tensor_scalar(smean[:, :], sst[:, :], 1.0 / NTOT, None, ALU.mult))
    p.v(lambda e: e.tensor_scalar(ssq2[:, :], ssq2[:, :], 1.0 / NTOT, None, ALU.mult))
    p.v(lambda e: e.tensor_tensor(sst[:, :], smean[:, :], smean[:, :], ALU.mult))
    p.v(lambda e: e.tensor_tensor(ssq2[:, :], sst[:, :], ssq2[:, :], ALU.subtract))
    p.s(lambda e: e.activation(srstd[:, :], ssq2[:, :], AF.Sqrt, bias=geps[0:2, :], scale=-1.0))
    p.v(lambda e: e.reciprocal(srstd[:, :], srstd[:, :]))
    p.v(lambda e: e.tensor_copy(scomb[:, 0:DC], srstd[:, :]))
    p.v(lambda e: e.scalar_tensor_tensor(scomb[:, DC:2 * DC], smean[:, :], -1.0,
                                         srstd[:, :], ALU.mult, ALU.mult))
    p.t(lambda e: e.matmul(P1[:, 0:2 * DC], eb[:, :], scomb[:, :], start=True, stop=True))
    p.v(lambda e: e.tensor_copy(statB[:, :], P1[:, 0:2 * DC]))
    p.v(lambda e: e.tensor_tensor(scale_col[:, :], statB[:, 0:DC], gnw[:, :], ALU.mult))
    p.v(lambda e: e.tensor_tensor(bias_col[:, :], statB[:, DC:2 * DC], gnw[:, :], ALU.mult))
    p.v(lambda e: e.tensor_tensor(bias_col[:, :], bias_col[:, :], gnb[:, :], ALU.add))
    p.gdma(lambda e: e.dma_start(
        out=wA[:, :].rearrange("p (k n) -> p k n", k=DC),
        in_=woT.rearrange("(k p) n -> p k n", p=128)))

    for c in range(SCH):
        p.gdma(lambda e, c=c: e.dma_start(
            out=ygn[:, :].rearrange("p (s t) -> p s t", s=DC),
            in_=ybuf.ap().rearrange("(s p) t -> p s t", p=128)[:, :, c * 128:(c + 1) * 128]))
        for s in range(DC):
            p.v(lambda e, s=s: e.tensor_scalar(
                ygn[:, s * 128:(s + 1) * 128], ygn[:, s * 128:(s + 1) * 128],
                scale_col[:, s:s + 1], bias_col[:, s:s + 1], ALU.mult, ALU.add))
        p.dma(lambda e, c=c: e.dma_start(out=r_c[:, :], in_=xq[c * 128:(c + 1) * 128, :]))
        for half in range(2):
            for k in range(DC):
                p.t(lambda e, half=half, k=k: e.matmul(
                    P1[:, :], ygn[:, k * 128:(k + 1) * 128],
                    wA[:, k * D + half * 512:k * D + (half + 1) * 512],
                    start=(k == 0), stop=(k == DC - 1)))
            p.v(lambda e, half=half: e.tensor_tensor(
                outs[:, half * 512:(half + 1) * 512], P1[:, :],
                r_c[:, half * 512:(half + 1) * 512], ALU.add))
        p.dma(lambda e, c=c: e.dma_start(out=ys[c * 128:(c + 1) * 128, :], in_=outs[:, :]))

    return p.end()


def _l1a_inmaps(x_full, inputs):
    gam = _l1_gammas()
    dtm = np.zeros((128, 16 * 128), dtype=np.float32)
    ii = np.arange(128)
    for h in range(16):
        diff = ii[None, :] - ii[:, None]
        dtm[:, h * 128:(h + 1) * 128] = np.where(diff >= 0, gam[h] ** np.maximum(diff, 0), 0.0)
    lamq = np.zeros((128, DC * 128), dtype=np.float32)
    for s in range(DC):
        for half in range(2):
            h = 2 * s + half
            lamq[half * 64:(half + 1) * 64, s * 128:(s + 1) * 128] = \
                (gam[h] ** (ii + 1.0))[None, :]
    lamk = np.zeros((128, 16), dtype=np.float32)
    for h in range(16):
        lamk[:, h] = gam[h] ** (127.0 - ii)
    e2 = np.zeros((128, 2), dtype=np.float32)
    e2[0:64, 0] = 1.0
    e2[64:128, 1] = 1.0
    eb = np.zeros((2, 128), dtype=np.float32)
    eb[0, 0:64] = 1.0
    eb[1, 64:128] = 1.0
    gnw = _f32(inputs["L1_attn_gn_w"]).reshape(DC, 128).T.copy()
    gnb = _f32(inputs["L1_attn_gn_b"]).reshape(DC, 128).T.copy()
    wqT = _f32(np.asarray(inputs["L1_attn_wq"]).T)
    wkT = _f32(np.asarray(inputs["L1_attn_wk"]).T)
    wvT = _f32(np.asarray(inputs["L1_attn_wv"]).T)
    woT = _f32(np.asarray(inputs["L1_attn_wo"]).T)
    maps = []
    for cid in range(NCORE):
        b = cid // 4
        xq = _f32(x_full[b * SEQ:(b + 1) * SEQ])
        maps.append(dict(xq=xq, wqT=wqT, wkT=wkT, wvT=wvT, woT=woT,
                         gnw=gnw, gnb=gnb, ident=IDENT, dtm=dtm, lamq=lamq,
                         lamk=lamk, e2=e2, eb=eb))
    return maps


def run_l1a(x_full, inputs, nc_cache={}):
    if "l1a" not in nc_cache:
        nc_cache["l1a"] = build_l1a()
    res = run_bass_kernel_spmd(nc_cache["l1a"], _l1a_inmaps(x_full, inputs),
                               core_ids=list(range(NCORE)))
    return np.concatenate([res.results[0]["ys"], res.results[4]["ys"]], axis=0)



def _l2_slopes():
    def po2(m):
        start = 2.0 ** (-2.0 ** (-(np.log2(m) - 3)))
        return [start * start ** i for i in range(m)]
    s = np.array(po2(16), dtype=np.float64)
    s = s * (1.0 - 0.0 / 11.0 + 1e-5)
    return s


def build_l2a():
    p = Prog("l2a")
    xq = p.inp("xq", [SEQ, D], F32)
    wq_in = p.inp("wq", [D, D], F32)
    wk_in = p.inp("wk", [D, D], F32)
    wv_in = p.inp("wv", [D, D], F32)
    wo_in = p.inp("wo", [D, D], F32)
    wg1_in = p.inp("wg1", [D, 64], F32)
    wg2_in = p.inp("wg2", [64, D], F32)
    gamma_in = p.inp("gamma", [1, D], F32)
    beta_in = p.inp("beta", [1, D], F32)
    ident_in = p.inp("ident", [128, 128], F32)
    dt_in = p.inp("dtm", [128, 16 * 128], F32)
    lamq_in = p.inp("lamq", [128, DC * 128], F32)
    lamk_in = p.inp("lamk", [128, 16], F32)
    ys = p.outp("ys", [SEQ, D], F32)

    NH = 16
    s_h = _l2_slopes()
    lam_c = [float(np.exp(-s * 128.0)) for s in s_h]

    wA = p.sb("wA", [128, DC * D], BF16)
    wB = p.sb("wB", [128, DC * D], BF16)
    wC = p.sb("wC", [128, DC * D], BF16)
    wO = p.sb("wO", [128, DC * D], BF16)
    wg1 = p.sb("wg1_sb", [128, DC * 64], BF16)
    wg2 = p.sb("wg2_sb", [64, D], BF16)
    ident = p.sb("identb", [128, 128], BF16)
    identf = p.sb("identf", [128, 128], F32)
    dtm = p.sb("dtm_sb", [128, 16 * 128], F32)
    lamq = p.sb("lamq_sb", [128, DC * 128], BF16)
    lamk = p.sb("lamk_sb", [128, 16], F32)
    gamB = p.sb("gamB", [128, D], F32)
    betB = p.sb("betB", [128, D], F32)
    grow = p.sb("grow", [1, D], BF16)
    ones_r = p.sb("ones_r", [1, 128], BF16)
    r_c = p.sb("r_c", [128, D], F32)
    xn_c = p.sb("xn_c", [128, D], BF16)
    xnT_c = p.sb("xnT_c", [128, DC * 128], BF16)
    QTc = p.sb("QTc", [128, DC * 128], BF16)
    QLc = p.sb("QLc", [128, DC * 128], BF16)
    KTc = p.sb("KTc", [128, DC * 128], BF16)
    Ktm = p.sb("Ktm", [128, D], BF16)
    Klam = p.sb("Klam", [128, 64], BF16)
    Vc = p.sb("Vc", [128, D], BF16)
    St = p.sb("St", [128, 128], BF16)
    state = p.sb("state", [128, 512], F32)
    stateb = p.sb("stateb", [128, 512], BF16)
    oT = p.sb("oT", [128, D], F32)
    otm = p.sb("otm", [128, D], F32)
    g1T = p.sb("g1T", [64, 128], BF16)
    gate = p.sb("gate", [128, D], F32)
    og = p.sb("og", [128, D], BF16)
    ogT = p.sb("ogT", [128, DC * 128], BF16)
    outs = p.sb("outs", [128, D], F32)

    P1 = p.ps("P1", [128, 512], F32)
    P3 = p.ps("P3", [128, 1024], F32)
    P4 = p.ps("P4", [128, 1024], BF16)
    P4f = p.ps("P4f", [128, 512], F32)
    P5 = p.ps("P5", [128, 128], F32)

    p.begin()
    scr = ln_scratch(p)
    scr2 = ln_scratch(p, "_o")
    p.gdma(lambda e: e.dma_start(out=ident[:, :], in_=ident_in))
    p.dma(lambda e: e.dma_start(out=identf[:, :], in_=ident_in))
    p.dma(lambda e: e.dma_start(out=dtm[:, :], in_=dt_in))
    p.gdma(lambda e: e.dma_start(out=lamq[:, :], in_=lamq_in))
    p.dma(lambda e: e.dma_start(out=lamk[:, :], in_=lamk_in))
    for nm, dst, src in [("wA", wA, wq_in), ("wB", wB, wk_in), ("wC", wC, wv_in), ("wO", wO, wo_in)]:
        p.gdma(lambda e, dst=dst, src=src: e.dma_start(
            out=dst[:, :].rearrange("p (k n) -> p k n", k=DC),
            in_=src.rearrange("(k p) n -> p k n", p=128)))
    p.gdma(lambda e: e.dma_start(
        out=wg1[:, :].rearrange("p (k n) -> p k n", k=DC),
        in_=wg1_in.rearrange("(k p) n -> p k n", p=128)))
    p.gdma(lambda e: e.dma_start(out=wg2[:, :], in_=wg2_in))
    p.v(lambda e: e.memset(ones_r[:, :], 1.0))
    p.v(lambda e: e.memset(state[:, :], 0.0))
    p.v(lambda e: e.memset(stateb[:, :], 0.0))
    p.gdma(lambda e: e.dma_start(out=grow[:, :], in_=gamma_in))
    for half in range(2):
        p.t(lambda e, half=half: e.matmul(P1[:, :], ones_r[:, :],
                                          grow[0:1, half * 512:(half + 1) * 512],
                                          start=True, stop=True))
        p.v(lambda e, half=half: e.tensor_copy(gamB[:, half * 512:(half + 1) * 512], P1[:, :]))
    p.gdma(lambda e: e.dma_start(out=grow[:, :], in_=beta_in))
    for half in range(2):
        p.t(lambda e, half=half: e.matmul(P1[:, :], ones_r[:, :],
                                          grow[0:1, half * 512:(half + 1) * 512],
                                          start=True, stop=True))
        p.v(lambda e, half=half: e.tensor_copy(betB[:, half * 512:(half + 1) * 512], P1[:, :]))

    for c in range(SCH):
        p.dma(lambda e, c=c: e.dma_start(out=r_c[:, :], in_=xq[c * 128:(c + 1) * 128, :]))
        emit_ln(p, r_c[:, :], xn_c[:, :], scr)
        emit_transpose_cols(p, xn_c[:, :], DC, P4, ident,
                            xnT_c[:, :].rearrange("p (k t) -> p k t", k=DC))
        for m in range(DC):
            for k in range(DC):
                p.t(lambda e, m=m, k=k: e.matmul(
                    P1[:, 0:128], wA[:, k * D + m * 128:k * D + (m + 1) * 128],
                    xnT_c[:, k * 128:(k + 1) * 128], start=(k == 0), stop=(k == DC - 1)))
            p.s(lambda e, m=m: e.activation(QTc[:, m * 128:(m + 1) * 128], P1[:, 0:128], AF.Silu))
            p.v(lambda e, m=m: e.tensor_tensor(QLc[:, m * 128:(m + 1) * 128],
                                               QTc[:, m * 128:(m + 1) * 128],
                                               lamq[:, m * 128:(m + 1) * 128], ALU.mult))
        for m in range(DC):
            for k in range(DC):
                p.t(lambda e, m=m, k=k: e.matmul(
                    P1[:, 0:128], wB[:, k * D + m * 128:k * D + (m + 1) * 128],
                    xnT_c[:, k * 128:(k + 1) * 128], start=(k == 0), stop=(k == DC - 1)))
            p.s(lambda e, m=m: e.activation(KTc[:, m * 128:(m + 1) * 128], P1[:, 0:128], AF.Silu))
        for half in range(2):
            for k in range(DC):
                p.t(lambda e, half=half, k=k: e.matmul(
                    P1[:, :], xnT_c[:, k * 128:(k + 1) * 128],
                    wC[:, k * D + half * 512:k * D + (half + 1) * 512],
                    start=(k == 0), stop=(k == DC - 1)))
            p.s(lambda e, half=half: e.activation(Vc[:, half * 512:(half + 1) * 512],
                                                  P1[:, :], AF.Silu))
        for j in range(DC):
            p.t(lambda e, j=j: e.transpose(P4[:, j * 128:(j + 1) * 128],
                                           KTc[:, j * 128:(j + 1) * 128], ident[:, :]))
        p.v(lambda e: e.tensor_copy(Ktm[:, :], P4[:, :]))
        for h in range(NH):
            sl = h // 2
            po = (h % 2) * 64
            p.t(lambda e, sl=sl, po=po: e.matmul(
                P5[:, :], KTc[po:po + 64, sl * 128:(sl + 1) * 128],
                QTc[po:po + 64, sl * 128:(sl + 1) * 128], start=True, stop=True))
            p.v(lambda e, h=h: e.tensor_tensor(St[:, :], P5[:, :],
                                               dtm[:, h * 128:(h + 1) * 128], ALU.mult))
            p.t(lambda e, h=h, sl=sl, po=po: e.matmul(
                P3[po:po + 64, sl * 128:(sl + 1) * 128],
                Vc[:, h * 64:(h + 1) * 64], St[:, :], start=True, stop=False))
            p.t(lambda e, h=h, sl=sl, po=po: e.matmul(
                P3[po:po + 64, sl * 128:(sl + 1) * 128],
                stateb[po:po + 64, sl * 64:(sl + 1) * 64],
                QLc[po:po + 64, sl * 128:(sl + 1) * 128], start=False, stop=True))
            p.v(lambda e, h=h: e.tensor_scalar(Klam[:, :], Ktm[:, h * 64:(h + 1) * 64],
                                               lamk[:, h:h + 1], None, ALU.mult))
            p.t(lambda e, h=h: e.matmul(P5[0:64, 0:64], Klam[:, :],
                                        Vc[:, h * 64:(h + 1) * 64], start=True, stop=True))
            p.v(lambda e, h=h, sl=sl, po=po, lc=lam_c[h]: e.tensor_scalar(
                state[po:po + 64, sl * 64:(sl + 1) * 64],
                state[po:po + 64, sl * 64:(sl + 1) * 64], lc, None, ALU.mult))
            p.v(lambda e, sl=sl, po=po: e.tensor_tensor(
                state[po:po + 64, sl * 64:(sl + 1) * 64],
                state[po:po + 64, sl * 64:(sl + 1) * 64], P5[0:64, 0:64], ALU.add))
            p.v(lambda e, sl=sl, po=po: e.tensor_copy(
                stateb[po:po + 64, sl * 64:(sl + 1) * 64],
                state[po:po + 64, sl * 64:(sl + 1) * 64]))
        p.v(lambda e: e.tensor_copy(oT[:, :], P3[:, :]))
        for j in range(DC):
            half = j // 4
            jj = j % 4
            p.t(lambda e, j=j, half=half, jj=jj: e.transpose(
                P4f[:, jj * 128:(jj + 1) * 128], oT[:, j * 128:(j + 1) * 128], identf[:, :]))
            p.v(lambda e, j=j, half=half, jj=jj: e.tensor_copy(
                otm[:, :].rearrange("p (a b) -> p a b", a=DC)[:, j, :],
                P4f[:, jj * 128:(jj + 1) * 128]))
        emit_ln(p, otm[:, :], outs[:, :], scr2)
        for k in range(DC):
            p.t(lambda e, k=k: e.matmul(P5[0:64, :], wg1[:, k * 64:(k + 1) * 64],
                                        xnT_c[:, k * 128:(k + 1) * 128],
                                        start=(k == 0), stop=(k == DC - 1)))
        p.v(lambda e: e.tensor_copy(g1T[:, :], P5[0:64, :]))
        for half in range(2):
            p.t(lambda e, half=half: e.matmul(
                P1[:, :], g1T[:, :], wg2[:, half * 512:(half + 1) * 512],
                start=True, stop=True))
            p.s(lambda e, half=half: e.activation(gate[:, half * 512:(half + 1) * 512],
                                                  P1[:, :], AF.Sigmoid))
        p.v(lambda e: e.tensor_tensor(otm[:, :], outs[:, :], gamB[:, :], ALU.mult))
        p.v(lambda e: e.tensor_tensor(otm[:, :], otm[:, :], betB[:, :], ALU.add))
        p.v(lambda e: e.tensor_tensor(og[:, :], otm[:, :], gate[:, :], ALU.mult))
        for j in range(DC):
            p.t(lambda e, j=j: e.transpose(P4[:, j * 128:(j + 1) * 128],
                                           og[:, j * 128:(j + 1) * 128], ident[:, :]))
        p.v(lambda e: e.tensor_copy(ogT[:, :], P4[:, :]))
        p.dma(lambda e, c=c: e.dma_start(out=r_c[:, :], in_=xq[c * 128:(c + 1) * 128, :]))
        for half in range(2):
            for k in range(DC):
                p.t(lambda e, half=half, k=k: e.matmul(
                    P1[:, :], ogT[:, k * 128:(k + 1) * 128],
                    wO[:, k * D + half * 512:k * D + (half + 1) * 512],
                    start=(k == 0), stop=(k == DC - 1)))
            p.v(lambda e, half=half: e.tensor_tensor(
                outs[:, half * 512:(half + 1) * 512], P1[:, :],
                r_c[:, half * 512:(half + 1) * 512], ALU.add))
        p.dma(lambda e, c=c: e.dma_start(out=ys[c * 128:(c + 1) * 128, :], in_=outs[:, :]))

    return p.end()


def _l2a_inmaps(x_full, inputs):
    s_h = _l2_slopes()
    ii = np.arange(128)
    dtm = np.zeros((128, 16 * 128), dtype=np.float32)
    for h in range(16):
        diff = ii[None, :] - ii[:, None]
        dtm[:, h * 128:(h + 1) * 128] = np.where(
            diff >= 0, np.exp(-s_h[h] * np.maximum(diff, 0)), 0.0)
    lamq = np.zeros((128, DC * 128), dtype=np.float32)
    for s in range(DC):
        for half in range(2):
            h = 2 * s + half
            lamq[half * 64:(half + 1) * 64, s * 128:(s + 1) * 128] = \
                np.exp(-s_h[h] * (ii + 1.0))[None, :]
    lamk = np.zeros((128, 16), dtype=np.float32)
    for h in range(16):
        lamk[:, h] = np.exp(-s_h[h] * (127.0 - ii))
    maps = []
    for cid in range(NCORE):
        b = cid // 4
        maps.append(dict(
            xq=_f32(x_full[b * SEQ:(b + 1) * SEQ]),
            wq=_f32(inputs["L2_attn_Wq"]), wk=_f32(inputs["L2_attn_Wk"]),
            wv=_f32(inputs["L2_attn_Wv"]), wo=_f32(inputs["L2_attn_Wo"]),
            wg1=_f32(inputs["L2_attn_Wg1"]), wg2=_f32(inputs["L2_attn_Wg2"]),
            gamma=_f32(inputs["L2_attn_gamma"]).reshape(1, D),
            beta=_f32(inputs["L2_attn_beta"]).reshape(1, D),
            ident=IDENT, dtm=dtm, lamq=lamq, lamk=lamk))
    return maps


def run_l2a(x_full, inputs, nc_cache={}):
    if "l2a" not in nc_cache:
        nc_cache["l2a"] = build_l2a()
    res = run_bass_kernel_spmd(nc_cache["l2a"], _l2a_inmaps(x_full, inputs),
                               core_ids=list(range(NCORE)))
    return np.concatenate([res.results[0]["ys"], res.results[4]["ys"]], axis=0)



DH = 4096
DHC = DH // 128


def build_l2f():
    p = Prog("l2f")
    xs = p.inp("xs", [TOK, D], F32)
    waugT = p.inp("waugT", [DH, D + 1], F32)
    wcon = p.inp("wcon", [DH, D], F32)
    upclo = p.inp("upclo", [128, 7 * 4 * 2048], F32)
    upchi = p.inp("upchi", [128, 5 * 4 * 2048], F32)
    dnclo = p.inp("dnclo", [128, 7 * 4 * 2048], F32)
    dnhcols = p.inp("dnhcols", [128, 5 * 16 * 4], F32)
    upbias = p.inp("upbias", [1, DH], F32)
    bcon = p.inp("bcon", [1, D], F32)
    dbias = p.inp("dbias", [1, D], F32)
    eyeb_in = p.inp("eyeb", [128, DH], F32)
    ys = p.outp("ys", [TOK, D], F32)

    WU = p.sb("WU", [128, DC * DH], BF16)
    WC = p.sb("WC", [128, DHC * D], BF16)
    Ea = p.sb("Ea", [128, DH], BF16)
    Eb = p.sb("Eb", [128, DH], BF16)
    coefB = p.sb("coefB", [128, 4 * 2048], BF16)
    scrA = p.sb("scrA", [128, 2048], BF16)
    scrB = p.sb("scrB", [128, 2048], BF16)
    dnh = p.sb("dnh", [128, 5 * 16 * 4], F32)
    blo = p.sb("blo", [1, DH], BF16)
    becols = p.sb("becols", [128, DHC], F32)
    bcoB = p.sb("bcoB", [128, D], F32)
    ones_r = p.sb("ones_r", [1, 128], BF16)
    onef = p.sb("onef", [1, 1], BF16)
    r_c = p.sb("r_c", [128, D], F32)
    xn_c = p.sb("xn_c", [128, D], BF16)
    xnT_c = p.sb("xnT_c", [128, DC * 128], BF16)
    ident = p.sb("identb", [128, 128], BF16)
    hT = p.sb("hT", [128, DHC * 128], BF16)
    outs = p.sb("outs", [128, D], F32)
    wslab = hT[:, 1024:1024 + D + 1]
    chscr = hT[:, 0:1024]

    P1 = p.ps("P1", [128, 512], F32)
    P2 = p.ps("P2", [128, 128], F32)
    P4 = p.ps("P4", [128, 1024], BF16)
    PB = p.ps("PB", [1, 128], F32)
    PC = p.ps("PC", [128, 32], F32)

    p.begin()
    scr = ln_scratch(p)
    p.gdma(lambda e: e.dma_start(out=ident[:, :], in_=eyeb_in.ap()[0:128, 0:128] if False else eyeb_in[0:128, 0:128]))
    p.dma(lambda e: e.dma_start(out=dnh[:, :], in_=dnhcols))
    p.v(lambda e: e.memset(ones_r[:, :], 1.0))
    p.v(lambda e: e.memset(onef[:, :], 1.0))

    def apply_stages_tile(dst_pair, coefs_in, nstages, stage_sizes):
        src, dst = dst_pair
        for st in range(nstages):
            s = stage_sizes[st]
            nb = DH // (2 * s)
            base = st * 4 * 2048
            p.gdma(lambda e, base=base: e.dma_start(
                out=coefB[:, 0:4096], in_=coefs_in[:, base:base + 2 * 2048]))
            av = src[:, :].rearrange("p (nb two s) -> p nb two s", two=2, s=s)[:, :, 0, :]
            bv = src[:, :].rearrange("p (nb two s) -> p nb two s", two=2, s=s)[:, :, 1, :]
            dav = dst[:, :].rearrange("p (nb two s) -> p nb two s", two=2, s=s)[:, :, 0, :]
            dbv = dst[:, :].rearrange("p (nb two s) -> p nb two s", two=2, s=s)[:, :, 1, :]
            w00 = coefB[:, 0:2048].rearrange("p (nb s) -> p nb s", s=s)
            w01 = coefB[:, 2048:4096].rearrange("p (nb s) -> p nb s", s=s)
            scv = scrA[:, :].rearrange("p (nb s) -> p nb s", s=s)
            p.v(lambda e, av=av, w00=w00, dav=dav: e.tensor_tensor(dav, av, w00, ALU.mult))
            p.v(lambda e, bv=bv, w01=w01, scv=scv: e.tensor_tensor(scv, bv, w01, ALU.mult))
            p.v(lambda e, dav=dav, scv=scv: e.tensor_tensor(dav, dav, scv, ALU.add))
            p.gdma(lambda e, base=base: e.dma_start(
                out=coefB[:, 0:4096], in_=coefs_in[:, base + 2 * 2048:base + 4 * 2048]))
            w10 = coefB[:, 0:2048].rearrange("p (nb s) -> p nb s", s=s)
            w11 = coefB[:, 2048:4096].rearrange("p (nb s) -> p nb s", s=s)
            p.v(lambda e, av=av, w10=w10, dbv=dbv: e.tensor_tensor(dbv, av, w10, ALU.mult))
            p.v(lambda e, bv=bv, w11=w11, scv=scv: e.tensor_tensor(scv, bv, w11, ALU.mult))
            p.v(lambda e, dbv=dbv, scv=scv: e.tensor_tensor(dbv, dbv, scv, ALU.add))
            src, dst = dst, src
        return src

    p.gdma(lambda e: e.dma_start(
        out=WC[:, :].rearrange("p (c n) -> p c n", c=DHC),
        in_=wcon.rearrange("(c p) n -> p c n", p=128)))
    for si, st in enumerate(range(11, 6, -1)):
        S = 2 ** st
        dch = S // 128
        pairs = [c for c in range(DHC) if (c // dch) % 2 == 0]
        for pi, c in enumerate(pairs):
            cb = c + dch
            colbase = (si * 16 + pi) * 4
            a = WC[:, c * D:(c + 1) * D]
            b = WC[:, cb * D:(cb + 1) * D]
            w00 = dnh[:, colbase + 0:colbase + 1]
            w01 = dnh[:, colbase + 1:colbase + 2]
            w10 = dnh[:, colbase + 2:colbase + 3]
            w11 = dnh[:, colbase + 3:colbase + 4]
            p.v(lambda e, a=a: e.tensor_copy(chscr[:, :], a))
            p.v(lambda e, a=a, w00=w00: e.tensor_scalar(a, a, w00, None, ALU.mult))
            p.v(lambda e, a=a, b=b, w10=w10: e.scalar_tensor_tensor(a, b, w10, a, ALU.mult, ALU.add))
            p.v(lambda e, b=b, w11=w11: e.tensor_scalar(b, b, w11, None, ALU.mult))
            p.v(lambda e, b=b, w01=w01: e.scalar_tensor_tensor(b, chscr[:, :], w01, b, ALU.mult, ALU.add))
    p.gdma(lambda e: e.dma_start(out=Ea[:, :], in_=eyeb_in))
    e2t = apply_stages_tile((Ea, Eb), dnclo, 7, [64, 32, 16, 8, 4, 2, 1])
    for g in range(DHC):
        for half in range(2):
            p.t(lambda e, g=g, half=half: e.matmul(
                P1[:, :], e2t[:, g * 128:(g + 1) * 128],
                WC[:, g * D + half * 512:g * D + (half + 1) * 512],
                start=True, stop=True))
            p.v(lambda e, g=g, half=half: e.tensor_copy(
                WC[:, g * D + half * 512:g * D + (half + 1) * 512], P1[:, :]))

    p.gdma(lambda e: e.dma_start(out=Ea[:, :], in_=eyeb_in))
    eut = apply_stages_tile((Ea, Eb), upclo, 7, [1, 2, 4, 8, 16, 32, 64])
    for g in range(DHC):
        p.gdma(lambda e, g=g: e.dma_start(out=wslab[:, :], in_=waugT[g * 128:(g + 1) * 128, :]))
        for rc in range(DC):
            p.t(lambda e, g=g, rc=rc: e.matmul(
                P2[:, :], wslab[:, rc * 128:(rc + 1) * 128],
                eut[:, g * 128:(g + 1) * 128], start=True, stop=True))
            p.v(lambda e, g=g, rc=rc: e.tensor_copy(
                WU[:, rc * DH + g * 128:rc * DH + (g + 1) * 128], P2[:, :]))
        p.t(lambda e, g=g: e.matmul(PB[:, :], wslab[:, D:D + 1],
                                    eut[:, g * 128:(g + 1) * 128], start=True, stop=True))
        p.v(lambda e, g=g: e.tensor_copy(blo[0:1, g * 128:(g + 1) * 128], PB[0:1, :]))
    for si, st in enumerate(range(7, 12)):
        s = 2 ** st
        nb = DH // (2 * s)
        base = si * 4 * 2048
        p.gdma(lambda e, base=base: e.dma_start(
            out=coefB[:, :], in_=upchi[:, base:base + 4 * 2048]))
        w00 = coefB[:, 0:2048].rearrange("p (nb s) -> p nb s", s=s)
        w01 = coefB[:, 2048:4096].rearrange("p (nb s) -> p nb s", s=s)
        w10 = coefB[:, 4096:6144].rearrange("p (nb s) -> p nb s", s=s)
        w11 = coefB[:, 6144:8192].rearrange("p (nb s) -> p nb s", s=s)
        scv = scrA[:, :].rearrange("p (nb s) -> p nb s", s=s)
        sc2 = scrB[:, :].rearrange("p (nb s) -> p nb s", s=s)
        for rc in range(DC):
            sl = WU[:, rc * DH:(rc + 1) * DH]
            av = sl.rearrange("p (nb two s) -> p nb two s", two=2, s=s)[:, :, 0, :]
            bv = sl.rearrange("p (nb two s) -> p nb two s", two=2, s=s)[:, :, 1, :]
            p.v(lambda e, av=av, scv=scv: e.tensor_copy(scv, av))
            p.v(lambda e, av=av, w00=w00: e.tensor_tensor(av, av, w00, ALU.mult))
            p.v(lambda e, bv=bv, w01=w01, sc2=sc2: e.tensor_tensor(sc2, bv, w01, ALU.mult))
            p.v(lambda e, av=av, sc2=sc2: e.tensor_tensor(av, av, sc2, ALU.add))
            p.v(lambda e, bv=bv, w11=w11: e.tensor_tensor(bv, bv, w11, ALU.mult))
            p.v(lambda e, scv=scv, w10=w10, sc2=sc2: e.tensor_tensor(sc2, scv, w10, ALU.mult))
            p.v(lambda e, bv=bv, sc2=sc2: e.tensor_tensor(bv, bv, sc2, ALU.add))
        rav = blo[0:1, :].rearrange("p (nb two s) -> p nb two s", two=2, s=s)[:, :, 0, :]
        rbv = blo[0:1, :].rearrange("p (nb two s) -> p nb two s", two=2, s=s)[:, :, 1, :]
        rw00 = coefB[0:1, 0:2048].rearrange("p (nb s) -> p nb s", s=s)
        rw01 = coefB[0:1, 2048:4096].rearrange("p (nb s) -> p nb s", s=s)
        rw10 = coefB[0:1, 4096:6144].rearrange("p (nb s) -> p nb s", s=s)
        rw11 = coefB[0:1, 6144:8192].rearrange("p (nb s) -> p nb s", s=s)
        r1 = scrA[0:1, :].rearrange("p (nb s) -> p nb s", s=s)
        r2 = scrB[0:1, :].rearrange("p (nb s) -> p nb s", s=s)
        p.v(lambda e, rav=rav, r1=r1: e.tensor_copy(r1, rav))
        p.v(lambda e, rav=rav, rw00=rw00: e.tensor_tensor(rav, rav, rw00, ALU.mult))
        p.v(lambda e, rbv=rbv, rw01=rw01, r2=r2: e.tensor_tensor(r2, rbv, rw01, ALU.mult))
        p.v(lambda e, rav=rav, r2=r2: e.tensor_tensor(rav, rav, r2, ALU.add))
        p.v(lambda e, rbv=rbv, rw11=rw11: e.tensor_tensor(rbv, rbv, rw11, ALU.mult))
        p.v(lambda e, r1=r1, rw10=rw10, r2=r2: e.tensor_tensor(r2, r1, rw10, ALU.mult))
        p.v(lambda e, rbv=rbv, r2=r2: e.tensor_tensor(rbv, rbv, r2, ALU.add))
    upb = Ea[0:1, 0:DH]
    p.gdma(lambda e: e.dma_start(out=upb, in_=upbias))
    p.v(lambda e: e.tensor_tensor(blo[0:1, :], blo[0:1, :], upb, ALU.add))
    for g in range(DHC):
        p.t(lambda e, g=g: e.matmul(PC[:, g:g + 1], blo[0:1, g * 128:(g + 1) * 128],
                                    onef[0:1, 0:1], start=True, stop=True))
    p.v(lambda e: e.tensor_copy(becols[:, :], PC[:, :]))
    brow_d = Ea[32:33, 0:D]
    brow_d2 = Ea[32:33, D:2 * D]
    p.gdma(lambda e: e.dma_start(out=brow_d, in_=bcon))
    p.gdma(lambda e: e.dma_start(out=brow_d2, in_=dbias))
    p.v(lambda e: e.tensor_tensor(brow_d, brow_d, brow_d2, ALU.add))
    p.v(lambda e: e.tensor_copy(blo[0:1, 0:D], brow_d))
    for half in range(2):
        p.t(lambda e, half=half: e.matmul(P1[:, :], ones_r[:, :],
                                          blo[0:1, half * 512:(half + 1) * 512],
                                          start=True, stop=True))
        p.v(lambda e, half=half: e.tensor_copy(bcoB[:, half * 512:(half + 1) * 512], P1[:, :]))

    for c in range(NCH):
        p.dma(lambda e, c=c: e.dma_start(out=r_c[:, :], in_=xs[c * 128:(c + 1) * 128, :]))
        emit_ln(p, r_c[:, :], xn_c[:, :], scr)
        emit_transpose_cols(p, xn_c[:, :], DC, P4, ident,
                            xnT_c[:, :].rearrange("p (k t) -> p k t", k=DC))
        for oc in range(DHC):
            for k in range(DC):
                p.t(lambda e, oc=oc, k=k: e.matmul(
                    P2[:, :], WU[:, k * DH + oc * 128:k * DH + (oc + 1) * 128],
                    xnT_c[:, k * 128:(k + 1) * 128], start=(k == 0), stop=(k == DC - 1)))
            p.s(lambda e, oc=oc: e.activation(hT[:, oc * 128:(oc + 1) * 128], P2[:, :],
                                              AF.Gelu, bias=becols[:, oc:oc + 1]))
        for half in range(2):
            for k in range(DHC):
                p.t(lambda e, half=half, k=k: e.matmul(
                    P1[:, :], hT[:, k * 128:(k + 1) * 128],
                    WC[:, k * D + half * 512:k * D + (half + 1) * 512],
                    start=(k == 0), stop=(k == DHC - 1)))
            p.v(lambda e, half=half: e.tensor_tensor(
                outs[:, half * 512:(half + 1) * 512], P1[:, :],
                bcoB[:, half * 512:(half + 1) * 512], ALU.add))
            p.v(lambda e, half=half: e.tensor_tensor(
                outs[:, half * 512:(half + 1) * 512], outs[:, half * 512:(half + 1) * 512],
                r_c[:, half * 512:(half + 1) * 512], ALU.add))
        p.dma(lambda e, c=c: e.dma_start(out=ys[c * 128:(c + 1) * 128, :], in_=outs[:, :]))

    return p.end()


def _l2f_hostprep(inputs):
    W_exp = _f32(inputs["L2_ffn_W_exp"])
    b_exp = _f32(inputs["L2_ffn_b_exp"])
    up_w = _f32(inputs["L2_ffn_up_weights"])
    dn_w = _f32(inputs["L2_ffn_down_weights"])
    waug = np.concatenate([W_exp, b_exp.reshape(1, DH)], axis=0)
    waugT = np.ascontiguousarray(waug.T)

    def coef_tiles(w, stage, swap):
        s = 2 ** stage
        nb = DH // (2 * s)
        wr = w[stage].reshape(nb, s, 2, 2)
        w00 = wr[:, :, 0, 0].reshape(-1)
        w01 = wr[:, :, 0, 1].reshape(-1)
        w10 = wr[:, :, 1, 0].reshape(-1)
        w11 = wr[:, :, 1, 1].reshape(-1)
        if swap:
            w01, w10 = w10, w01
        out = np.concatenate([w00, w01, w10, w11])
        return np.broadcast_to(out[None, :], (128, 4 * 2048)).astype(np.float32)

    upclo = np.concatenate([coef_tiles(up_w, st, False) for st in range(7)], axis=1)
    upchi = np.concatenate([coef_tiles(up_w, st, False) for st in range(7, 12)], axis=1)
    dnclo = np.concatenate([coef_tiles(dn_w, st, True) for st in range(6, -1, -1)], axis=1)
    dnh = np.zeros((128, 5 * 16 * 4), dtype=np.float32)
    pp = np.arange(128)
    for si, st in enumerate(range(11, 6, -1)):
        S = 2 ** st
        dch = S // 128
        pairs = [cc for cc in range(DHC) if (cc // dch) % 2 == 0]
        wr = dn_w[st].reshape(DH // (2 * S), S, 2, 2)
        for pi, cc in enumerate(pairs):
            i = cc * 128 + pp
            kidx = (i // (2 * S)) * S + (i % S)
            cb = (si * 16 + pi) * 4
            q, r = kidx // S, kidx % S
            dnh[:, cb + 0] = wr[q, r, 0, 0]
            dnh[:, cb + 1] = wr[q, r, 0, 1]
            dnh[:, cb + 2] = wr[q, r, 1, 0]
            dnh[:, cb + 3] = wr[q, r, 1, 1]
        del wr
    eyeb = np.zeros((128, DH), dtype=np.float32)
    for g in range(DHC):
        eyeb[:, g * 128:(g + 1) * 128] = IDENT
    return dict(waugT=waugT, wcon=_f32(inputs["L2_ffn_W_con"]), upclo=upclo,
                upchi=upchi, dnclo=dnclo, dnhcols=dnh,
                upbias=_f32(inputs["L2_ffn_up_bias"]).reshape(1, DH),
                bcon=_f32(inputs["L2_ffn_b_con"]).reshape(1, D),
                dbias=_f32(inputs["L2_ffn_down_bias"]).reshape(1, D),
                eyeb=eyeb)


def run_l2f(x_full, inputs, nc_cache={}):
    if "l2f" not in nc_cache:
        nc_cache["l2f"] = build_l2f()
    hp = _l2f_hostprep(inputs)
    maps = [dict(xs=_f32(x_full[cid * TOK:(cid + 1) * TOK]), **hp) for cid in range(NCORE)]
    res = run_bass_kernel_spmd(nc_cache["l2f"], maps, core_ids=list(range(NCORE)))
    return np.concatenate([res.results[i]["ys"] for i in range(NCORE)], axis=0)



X3CH = NCH + 2
LNH = 8
LHD = 64


def build_l3a():
    p = Prog("l3a")
    xh = p.inp("xh", [X3CH * 128, D], F32)
    xq = p.inp("xq", [SEQ, D], F32)
    wq_in = p.inp("wq", [D, D], F32)
    wk_in = p.inp("wk", [D, D], F32)
    wv_in = p.inp("wv", [D, D], F32)
    wgq_in = p.inp("wgq", [D, 512], F32)
    wgk_in = p.inp("wgk", [D, 512], F32)
    wgv_in = p.inp("wgv", [D, 512], F32)
    rout_in = p.inp("rout", [512, D], F32)
    wout_in = p.inp("wout", [D, D], F32)
    glbd_in = p.inp("glbd", [D, 16], F32)
    grbd_in = p.inp("grbd", [D, 16], F32)
    ident_in = p.inp("ident", [128, 128], F32)
    tri3_in = p.inp("tri3", [128, 2 * 128], F32)
    e16_in = p.inp("e16", [128, 8 * 16], F32)
    e8_in = p.inp("e8", [128, 4 * 8], F32)
    sel16_in = p.inp("sel16", [16, 16 * 128], F32)
    sel64_in = p.inp("sel64", [16, 16 * 64], F32)
    sel8_in = p.inp("sel8", [8, 8 * 128], F32)
    sel8b_in = p.inp("sel8b", [8, 8 * 64], F32)
    kill01_in = p.inp("kill01", [1, NCH * 3], F32)
    lmask_in = p.inp("lmask", [64, NCH], F32)
    ys = p.outp("ys", [TOK, D], F32)

    NH = 16
    KW3 = X3CH * 128

    wBig1 = p.sb("wBig1", [128, DC * D], BF16)
    wBig2 = p.sb("wBig2", [128, DC * D], BF16)
    wSm1 = p.sb("wSm1", [128, DC * 512], BF16)
    wSm2 = p.sb("wSm2", [128, 4 * D], BF16)
    glbd = p.sb("glbd_sb", [128, DC * 16], BF16)
    grbd = p.sb("grbd_sb", [128, DC * 16], BF16)
    ident = p.sb("identb", [128, 128], BF16)
    tri3 = p.sb("tri3_sb", [128, 2 * 128], F32)
    e16 = p.sb("e16_sb", [128, 8 * 16], BF16)
    e8 = p.sb("e8_sb", [128, 4 * 8], BF16)
    sel16 = p.sb("sel16_sb", [16, 16 * 128], BF16)
    sel64 = p.sb("sel64_sb", [16, 16 * 64], BF16)
    sel8 = p.sb("sel8_sb", [8, 8 * 128], BF16)
    sel8b = p.sb("sel8b_sb", [8, 8 * 64], BF16)
    kill01 = p.sb("kill01_sb", [1, NCH * 3], F32)
    lmask = p.sb("lmask_sb", [64, NCH], F32)
    killrow = p.sb("killrow", [1, 3 * 128], BF16)
    neg_row = p.sb("neg_row", [1, 128], F32)
    ones_c = p.sb("ones_c", [128, 1], BF16)
    ones_r = p.sb("ones_r", [1, 128], BF16)

    KT = p.sb("KT", [128, DC * KW3], BF16)
    V = p.sb("V", [128, X3CH * D], BF16)
    QTc = p.sb("QTc", [128, DC * 128], BF16)
    GQTc = p.sb("GQTc", [128, 4 * 128], BF16)
    pooledT = p.sb("pooledT", [128, DC * 64], BF16)
    GKT = p.sb("GKT", [128, 4 * 64], BF16)
    GV = p.sb("GV", [64, 512], BF16)
    r_c = p.sb("r_c", [128, D], F32)
    xn_c = p.sb("xn_c", [128, D], BF16)
    xnT_c = p.sb("xnT_c", [128, DC * 128], BF16)
    sqs = p.sb("sqs", [128, 128], BF16)
    qn2c = p.sb("qn2c", [16, 128], F32)
    gqn2c = p.sb("gqn2c", [8, 128], F32)
    kn2m = p.sb("kn2m", [16, 1], F32)
    kn2c = p.sb("kn2c", [16, 1], F32)
    gkn2m = p.sb("gkn2m", [8, 1], F32)
    qbias_c = p.sb("qbias_c", [16, 128], BF16)
    gqbias_c = p.sb("gqbias_c", [8, 128], BF16)
    PT = p.sb("PT", [128, 3 * 128], BF16)
    PTl = p.sb("PTl", [64, 128], BF16)
    rS = p.sb("rS", [128, 16], F32)
    rSl = p.sb("rSl", [128, 8], F32)
    rSb = p.sb("rSb", [128, 16], BF16)
    rT = p.sb("rT", [16, 128], BF16)
    rB = p.sb("rB", [128, 128], F32)
    glT = p.sb("glT", [16, 128], BF16)
    grT = p.sb("grT", [16, 128], BF16)
    locT = p.sb("locT", [128, D], F32)
    remT = p.sb("remT", [128, 512], BF16)
    cmbT = p.sb("cmbT", [128, D], BF16)
    rem2s = p.sb("rem2s", [128, 128], F32)
    cmbF = p.sb("cmbF", [128, 128], F32)
    outs = p.sb("outs", [128, D], F32)

    P1 = p.ps("P1", [128, 512], F32)
    P2 = p.ps("P2", [128, 384], F32)
    P3 = p.ps("P3", [128, 1024], F32)
    P4 = p.ps("P4", [128, 1024], BF16)
    P5 = p.ps("P5", [128, 128], F32)
    P6 = p.ps("P6", [16, 128], BF16)

    p.begin()
    scr = ln_scratch(p)
    for dst, src in [(ident, ident_in), (e16, e16_in), (e8, e8_in), (sel16, sel16_in),
                     (sel64, sel64_in), (sel8, sel8_in), (sel8b, sel8b_in),
                     (glbd, glbd_in), (grbd, grbd_in)]:
        p.gdma(lambda e, dst=dst, src=src: e.dma_start(out=dst[:, :], in_=src))
    p.dma(lambda e: e.dma_start(out=tri3[:, :], in_=tri3_in))
    p.dma(lambda e: e.dma_start(out=kill01[:, :], in_=kill01_in))
    p.dma(lambda e: e.dma_start(out=lmask[:, :], in_=lmask_in))
    p.v(lambda e: e.memset(ones_c[:, :], 1.0))
    p.v(lambda e: e.memset(ones_r[:, :], 1.0))
    p.v(lambda e: e.memset(neg_row[:, :], -1e30))
    p.v(lambda e: e.memset(kn2m[:, :], 0.0))
    p.v(lambda e: e.memset(gkn2m[:, :], 0.0))

    p.gdma(lambda e: e.dma_start(
        out=wSm1[:, :].rearrange("p (k n) -> p k n", k=DC),
        in_=wgk_in.rearrange("(k p) n -> p k n", p=128)))
    p.gdma(lambda e: e.dma_start(
        out=wSm2[:, 0:DC * 512].rearrange("p (k n) -> p k n", k=DC),
        in_=wgv_in.rearrange("(k p) n -> p k n", p=128)))
    for c in range(SCH):
        p.dma(lambda e, c=c: e.dma_start(out=r_c[:, :], in_=xq[c * 128:(c + 1) * 128, :]))
        emit_ln(p, r_c[:, :], xn_c[:, :], scr)
        for s in range(DC):
            p.t(lambda e, s=s: e.matmul(P5[:, s:s + 1], xn_c[:, s * 128:(s + 1) * 128],
                                        ones_c[:, :], start=True, stop=True))
        p.s(lambda e, c=c: e.activation(
            pooledT[:, :].rearrange("p (s l) -> p s l", s=DC)[:, :, c % 64:c % 64 + 1],
            P5[:, 0:DC], AF.Copy, scale=1.0 / 128.0))
    for m in range(4):
        for k in range(DC):
            p.t(lambda e, m=m, k=k: e.matmul(
                P5[:, 0:64], wSm1[:, k * 512 + m * 128:k * 512 + (m + 1) * 128],
                pooledT[:, k * 64:(k + 1) * 64], start=(k == 0), stop=(k == DC - 1)))
        p.v(lambda e, m=m: e.tensor_copy(GKT[:, m * 64:(m + 1) * 64], P5[:, 0:64]))
        p.s(lambda e, m=m: e.activation(sqs[:, 0:64], GKT[:, m * 64:(m + 1) * 64], AF.Square))
        p.t(lambda e, m=m: e.matmul(P5[0:8, 64:128], e8[:, m * 8:(m + 1) * 8],
                                    sqs[:, 0:64], start=(m == 0), stop=(m == 3)))
    p.v(lambda e: e.reduce_max(gkn2m[:, :], P5[0:8, 64:128], AX.X))
    for k in range(DC):
        p.t(lambda e, k=k: e.matmul(
            P1[0:64, :], pooledT[:, k * 64:(k + 1) * 64],
            wSm2[:, k * 512:(k + 1) * 512], start=(k == 0), stop=(k == DC - 1)))
    p.v(lambda e: e.tensor_copy(GV[:, :], P1[0:64, :]))

    p.gdma(lambda e: e.dma_start(
        out=wBig1[:, :].rearrange("p (k n) -> p k n", k=DC),
        in_=wk_in.rearrange("(k p) n -> p k n", p=128)))
    p.gdma(lambda e: e.dma_start(
        out=wBig2[:, :].rearrange("p (k n) -> p k n", k=DC),
        in_=wv_in.rearrange("(k p) n -> p k n", p=128)))
    for c in range(X3CH):
        p.dma(lambda e, c=c: e.dma_start(out=r_c[:, :], in_=xh[c * 128:(c + 1) * 128, :]))
        emit_ln(p, r_c[:, :], xn_c[:, :], scr)
        emit_transpose_cols(p, xn_c[:, :], DC, P4, ident,
                            xnT_c[:, :].rearrange("p (k t) -> p k t", k=DC))
        for m in range(DC):
            for k in range(DC):
                p.t(lambda e, m=m, k=k: e.matmul(
                    P1[:, 0:128], wBig1[:, k * D + m * 128:k * D + (m + 1) * 128],
                    xnT_c[:, k * 128:(k + 1) * 128], start=(k == 0), stop=(k == DC - 1)))
            p.v(lambda e, m=m, c=c: e.tensor_copy(
                KT[:, m * KW3 + c * 128:m * KW3 + (c + 1) * 128], P1[:, 0:128]))
            p.s(lambda e, m=m, c=c: e.activation(
                sqs[:, :], KT[:, m * KW3 + c * 128:m * KW3 + (c + 1) * 128], AF.Square))
            p.t(lambda e, m=m: e.matmul(P5[0:16, :], e16[:, m * 16:(m + 1) * 16],
                                        sqs[:, :], start=(m == 0), stop=(m == DC - 1)))
        p.v(lambda e: e.reduce_max(kn2c[:, :], P5[0:16, :], AX.X))
        p.v(lambda e: e.tensor_tensor(kn2m[:, :], kn2m[:, :], kn2c[:, :], ALU.max))
        for half in range(2):
            for k in range(DC):
                p.t(lambda e, half=half, k=k: e.matmul(
                    P1[:, :], xnT_c[:, k * 128:(k + 1) * 128],
                    wBig2[:, k * D + half * 512:k * D + (half + 1) * 512],
                    start=(k == 0), stop=(k == DC - 1)))
            p.v(lambda e, half=half, c=c: e.tensor_copy(
                V[:, c * D + half * 512:c * D + (half + 1) * 512], P1[:, :]))

    p.gdma(lambda e: e.dma_start(
        out=wBig1[:, :].rearrange("p (k n) -> p k n", k=DC),
        in_=wq_in.rearrange("(k p) n -> p k n", p=128)))
    p.gdma(lambda e: e.dma_start(
        out=wBig2[:, :].rearrange("p (k n) -> p k n", k=DC),
        in_=wout_in.rearrange("(k p) n -> p k n", p=128)))
    p.gdma(lambda e: e.dma_start(
        out=wSm1[:, :].rearrange("p (k n) -> p k n", k=DC),
        in_=wgq_in.rearrange("(k p) n -> p k n", p=128)))
    p.gdma(lambda e: e.dma_start(
        out=wSm2[:, 0:4 * D].rearrange("p (k n) -> p k n", k=4),
        in_=rout_in.rearrange("(k p) n -> p k n", p=128)))

    for oc in range(NCH):
        hc = oc + 2
        p.dma(lambda e, hc=hc: e.dma_start(out=r_c[:, :], in_=xh[hc * 128:(hc + 1) * 128, :]))
        emit_ln(p, r_c[:, :], xn_c[:, :], scr)
        emit_transpose_cols(p, xn_c[:, :], DC, P4, ident,
                            xnT_c[:, :].rearrange("p (k t) -> p k t", k=DC))
        for m in range(DC):
            for k in range(DC):
                p.t(lambda e, m=m, k=k: e.matmul(
                    P1[:, 0:128], wBig1[:, k * D + m * 128:k * D + (m + 1) * 128],
                    xnT_c[:, k * 128:(k + 1) * 128], start=(k == 0), stop=(k == DC - 1)))
            p.s(lambda e, m=m: e.activation(QTc[:, m * 128:(m + 1) * 128], P1[:, 0:128],
                                            AF.Copy, scale=1.0 / 8.0))
            p.s(lambda e, m=m: e.activation(sqs[:, :], QTc[:, m * 128:(m + 1) * 128], AF.Square))
            p.t(lambda e, m=m: e.matmul(P5[0:16, :], e16[:, m * 16:(m + 1) * 16],
                                        sqs[:, :], start=(m == 0), stop=(m == DC - 1)))
        p.v(lambda e: e.tensor_copy(qn2c[:, :], P5[0:16, :]))
        p.v(lambda e: e.tensor_scalar(qn2c[:, :], qn2c[:, :], kn2m[0:16, 0:1],
                                      1.0, ALU.mult, ALU.mult))
        p.s(lambda e: e.activation(qn2c[:, :], qn2c[:, :], AF.Sqrt))
        p.v(lambda e: e.tensor_scalar(qbias_c[:, :], qn2c[:, :], -1.0, 50.0, ALU.mult, ALU.add))
        for k in range(DC):
            p.t(lambda e, k=k: e.matmul(P5[0:16, :], glbd[:, k * 16:(k + 1) * 16],
                                        QTc[:, k * 128:(k + 1) * 128],
                                        start=(k == 0), stop=(k == DC - 1)))
        p.s(lambda e: e.activation(glT[:, :], P5[0:16, :], AF.Sigmoid, scale=8.0))
        for k in range(DC):
            p.t(lambda e, k=k: e.matmul(P5[0:16, :], grbd[:, k * 16:(k + 1) * 16],
                                        QTc[:, k * 128:(k + 1) * 128],
                                        start=(k == 0), stop=(k == DC - 1)))
        p.s(lambda e: e.activation(grT[:, :], P5[0:16, :], AF.Sigmoid, scale=8.0))
        for m in range(4):
            for k in range(DC):
                p.t(lambda e, m=m, k=k: e.matmul(
                    P1[:, 0:128], wSm1[:, k * 512 + m * 128:k * 512 + (m + 1) * 128],
                    xnT_c[:, k * 128:(k + 1) * 128], start=(k == 0), stop=(k == DC - 1)))
            p.s(lambda e, m=m: e.activation(GQTc[:, m * 128:(m + 1) * 128], P1[:, 0:128],
                                            AF.Copy, scale=1.0 / 8.0))
            p.s(lambda e, m=m: e.activation(sqs[:, :], GQTc[:, m * 128:(m + 1) * 128], AF.Square))
            p.t(lambda e, m=m: e.matmul(P5[0:8, :], e8[:, m * 8:(m + 1) * 8],
                                        sqs[:, :], start=(m == 0), stop=(m == 3)))
        p.v(lambda e: e.tensor_copy(gqn2c[:, :], P5[0:8, :]))
        p.v(lambda e: e.tensor_scalar(gqn2c[:, :], gqn2c[:, :], gkn2m[0:8, 0:1],
                                      1.0, ALU.mult, ALU.mult))
        p.s(lambda e: e.activation(gqn2c[:, :], gqn2c[:, :], AF.Sqrt))
        p.v(lambda e: e.tensor_scalar(gqbias_c[:, :], gqn2c[:, :], -1.0, 50.0, ALU.mult, ALU.add))
        for di in range(3):
            p.v(lambda e, di=di, oc=oc: e.tensor_scalar(
                killrow[0:1, di * 128:(di + 1) * 128], neg_row[0:1, :],
                kill01[0:1, oc * 3 + di:oc * 3 + di + 1], None, ALU.mult))
        for h in range(NH):
            sl = h // 2
            po = (h % 2) * 64
            for di in range(3):
                kc = hc + di - 2
                p.t(lambda e, di=di, kc=kc, sl=sl, po=po: e.matmul(
                    P2[:, di * 128:(di + 1) * 128],
                    KT[po:po + 64, sl * KW3 + kc * 128:sl * KW3 + (kc + 1) * 128],
                    QTc[po:po + 64, sl * 128:(sl + 1) * 128], start=True, stop=False))
                p.t(lambda e, di=di, h=h: e.matmul(
                    P2[:, di * 128:(di + 1) * 128], sel16[:, h * 128:(h + 1) * 128],
                    qbias_c[:, :], start=False, stop=False))
                p.t(lambda e, di=di: e.matmul(
                    P2[:, di * 128:(di + 1) * 128],
                    killrow[0:1, di * 128:(di + 1) * 128], ones_r[:, :],
                    start=False, stop=True))
            p.v(lambda e: e.tensor_tensor(P2[:, 0:128], P2[:, 0:128], tri3[:, 0:128], ALU.add))
            p.v(lambda e: e.tensor_tensor(P2[:, 256:384], P2[:, 256:384], tri3[:, 128:256], ALU.add))
            p.s(lambda e: e.activation(PT[:, :], P2[:, :], AF.Exp))
            for di in range(3):
                p.t(lambda e, di=di: e.matmul(P5[:, 0:1], PT[:, di * 128:(di + 1) * 128],
                                              ones_c[:, 0:1], start=(di == 0), stop=(di == 2)))
            p.v(lambda e, h=h: e.tensor_copy(rS[:, h:h + 1], P5[:, 0:1]))
            for di in range(3):
                kc = hc + di - 2
                p.t(lambda e, di=di, kc=kc, h=h, sl=sl, po=po: e.matmul(
                    P3[po:po + 64, sl * 128:(sl + 1) * 128],
                    V[:, kc * D + h * 64:kc * D + (h + 1) * 64],
                    PT[:, di * 128:(di + 1) * 128], start=(di == 0), stop=(di == 2)))
        p.v(lambda e: e.reciprocal(rS[:, :], rS[:, :]))
        p.v(lambda e: e.tensor_copy(rSb[:, :], rS[:, :]))
        p.t(lambda e: e.transpose(P6[:, :], rSb[:, :], ident[:, :]))
        p.v(lambda e: e.tensor_copy(rT[:, :], P6[0:16, :]))
        for h in range(NH):
            sl = h // 2
            po = (h % 2) * 64
            p.t(lambda e, h=h: e.matmul(P5[0:64, :], sel64[:, h * 64:(h + 1) * 64],
                                        rT[:, :], start=True, stop=True))
            p.v(lambda e: e.tensor_copy(rB[0:64, :], P5[0:64, :]))
            p.v(lambda e, sl=sl, po=po: e.tensor_tensor(
                locT[po:po + 64, sl * 128:(sl + 1) * 128],
                P3[po:po + 64, sl * 128:(sl + 1) * 128], rB[0:64, :], ALU.mult))
        for lh in range(LNH):
            sl = lh // 2
            po = (lh % 2) * 64
            p.t(lambda e, lh=lh, sl=sl, po=po: e.matmul(
                P5[0:64, :], GKT[po:po + 64, sl * 64:(sl + 1) * 64],
                GQTc[po:po + 64, sl * 128:(sl + 1) * 128], start=True, stop=False))
            p.t(lambda e, lh=lh: e.matmul(
                P5[0:64, :], sel8b[:, lh * 64:(lh + 1) * 64],
                gqbias_c[:, :], start=False, stop=True))
            p.v(lambda e, oc=oc: e.tensor_scalar(P5[0:64, :], P5[0:64, :],
                                                 lmask[0:64, oc:oc + 1], None, ALU.add))
            p.s(lambda e: e.activation(PTl[:, :], P5[0:64, :], AF.Exp))
            p.t(lambda e: e.matmul(P5[:, 64:65], PTl[:, :], ones_c[0:64, 0:1],
                                   start=True, stop=True))
            p.v(lambda e, lh=lh: e.tensor_copy(rSl[:, lh:lh + 1], P5[:, 64:65]))
            p.t(lambda e, lh=lh, sl=sl, po=po: e.matmul(
                P3[po:po + 64, sl * 128:(sl + 1) * 128],
                GV[:, lh * 64:(lh + 1) * 64], PTl[:, :], start=True, stop=True))
        p.v(lambda e: e.reciprocal(rSl[:, :], rSl[:, :]))
        p.v(lambda e: e.tensor_copy(rSb[:, 0:8], rSl[:, :]))
        p.t(lambda e: e.transpose(P6[0:8, :], rSb[:, 0:8], ident[:, :]))
        p.v(lambda e: e.tensor_copy(rT[0:8, :], P6[0:8, :]))
        for lh in range(LNH):
            sl = lh // 2
            po = (lh % 2) * 64
            p.t(lambda e, lh=lh: e.matmul(P5[0:64, :], sel8b[:, lh * 64:(lh + 1) * 64],
                                          rT[0:8, :], start=True, stop=True))
            p.v(lambda e: e.tensor_copy(rB[0:64, :], P5[0:64, :]))
            p.v(lambda e, sl=sl, po=po: e.tensor_tensor(
                remT[po:po + 64, sl * 128:(sl + 1) * 128],
                P3[po:po + 64, sl * 128:(sl + 1) * 128], rB[0:64, :], ALU.mult))
        for m in range(DC):
            for k in range(4):
                p.t(lambda e, m=m, k=k: e.matmul(
                    P1[:, 0:128], wSm2[:, k * D + m * 128:k * D + (m + 1) * 128],
                    remT[:, k * 128:(k + 1) * 128], start=(k == 0), stop=(k == 3)))
            p.v(lambda e, m=m: e.tensor_copy(rem2s[:, :], P1[:, 0:128]))
            for half in range(2):
                h = 2 * m + half
                po = half * 64
                p.t(lambda e, h=h: e.matmul(P5[0:64, :], sel64[:, h * 64:(h + 1) * 64],
                                            glT[:, :], start=True, stop=True))
                p.v(lambda e, po=po: e.tensor_copy(rB[po:po + 64, :], P5[0:64, :]))
                p.v(lambda e, m=m, po=po: e.tensor_tensor(
                    cmbF[po:po + 64, :], locT[po:po + 64, m * 128:(m + 1) * 128],
                    rB[po:po + 64, :], ALU.mult))
                p.t(lambda e, h=h: e.matmul(P5[0:64, :], sel64[:, h * 64:(h + 1) * 64],
                                            grT[:, :], start=True, stop=True))
                p.v(lambda e, po=po: e.tensor_copy(rB[po:po + 64, :], P5[0:64, :]))
                p.v(lambda e, po=po: e.tensor_tensor(
                    rB[po:po + 64, :], rem2s[po:po + 64, :], rB[po:po + 64, :], ALU.mult))
                p.v(lambda e, m=m, po=po: e.tensor_tensor(
                    cmbT[po:po + 64, m * 128:(m + 1) * 128], cmbF[po:po + 64, :],
                    rB[po:po + 64, :], ALU.add))
        for half in range(2):
            for k in range(DC):
                p.t(lambda e, half=half, k=k: e.matmul(
                    P1[:, :], cmbT[:, k * 128:(k + 1) * 128],
                    wBig2[:, k * D + half * 512:k * D + (half + 1) * 512],
                    start=(k == 0), stop=(k == DC - 1)))
            p.v(lambda e, half=half: e.tensor_tensor(
                outs[:, half * 512:(half + 1) * 512], P1[:, :],
                r_c[:, half * 512:(half + 1) * 512], ALU.add))
        p.dma(lambda e, oc=oc: e.dma_start(out=ys[oc * 128:(oc + 1) * 128, :], in_=outs[:, :]))

    return p.end()


def _l3a_inmaps(x_full, inputs):
    pp = np.arange(128)[:, None]
    ff = np.arange(128)[None, :]
    tri_a = np.where(pp > ff, 0.0, -1e30).astype(np.float32)
    tri_b = np.where(pp <= ff, 0.0, -1e30).astype(np.float32)
    tri3 = np.concatenate([tri_a, tri_b], axis=1)
    e16 = np.zeros((128, 8 * 16), dtype=np.float32)
    for m in range(8):
        e16[0:64, m * 16 + 2 * m] = 1.0
        e16[64:128, m * 16 + 2 * m + 1] = 1.0
    e8 = np.zeros((128, 4 * 8), dtype=np.float32)
    for m in range(4):
        e8[0:64, m * 8 + 2 * m] = 1.0
        e8[64:128, m * 8 + 2 * m + 1] = 1.0
    sel16 = np.zeros((16, 16 * 128), dtype=np.float32)
    for h in range(16):
        sel16[h, h * 128:(h + 1) * 128] = 1.0
    sel64 = np.zeros((16, 16 * 64), dtype=np.float32)
    for h in range(16):
        sel64[h, h * 64:(h + 1) * 64] = 1.0
    sel8 = np.zeros((8, 8 * 128), dtype=np.float32)
    for h in range(8):
        sel8[h, h * 128:(h + 1) * 128] = 1.0
    sel8b = np.zeros((8, 8 * 64), dtype=np.float32)
    for h in range(8):
        sel8b[h, h * 64:(h + 1) * 64] = 1.0
    glw = _f32(inputs["L3_attn_gate_local_w"])
    grw = _f32(inputs["L3_attn_gate_remote_w"])
    glbd = np.zeros((D, 16), dtype=np.float32)
    grbd = np.zeros((D, 16), dtype=np.float32)
    for h in range(16):
        glbd[h * 64:(h + 1) * 64, h] = glw[h]
        grbd[h * 64:(h + 1) * 64, h] = grw[h]
    maps = []
    for cid in range(NCORE):
        b = cid // 4
        sc0 = (cid % 4) * NCH
        t0 = cid * TOK
        seq0 = b * SEQ
        lo = t0 - 256
        xhp = np.zeros((X3CH * 128, D), dtype=np.float32)
        vlo = max(lo, seq0)
        xhp[vlo - lo:] = x_full[vlo:t0 + TOK]
        kill = np.zeros((NCH, 3), dtype=np.float32)
        for oc in range(NCH):
            for di in range(3):
                if sc0 + oc + di - 2 < 0:
                    kill[oc, di] = 1.0
        lm = np.zeros((64, NCH), dtype=np.float32)
        for oc in range(NCH):
            cg = sc0 + oc
            lm[cg + 1:, oc] = -1e30
        maps.append(dict(
            xh=xhp, xq=_f32(x_full[b * SEQ:(b + 1) * SEQ]),
            wq=_f32(inputs["L3_attn_wq"]), wk=_f32(inputs["L3_attn_wk"]),
            wv=_f32(inputs["L3_attn_wv"]), wgq=_f32(inputs["L3_attn_wgq"]),
            wgk=_f32(inputs["L3_attn_wgk"]), wgv=_f32(inputs["L3_attn_wgv"]),
            rout=_f32(inputs["L3_attn_r_out_w"]), wout=_f32(inputs["L3_attn_w_out"]),
            glbd=glbd, grbd=grbd, ident=IDENT, tri3=tri3, e16=e16, e8=e8,
            sel16=sel16, sel64=sel64, sel8=sel8, sel8b=sel8b,
            kill01=kill.reshape(1, -1), lmask=lm))
    return maps


def run_l3a(x_full, inputs, nc_cache={}):
    if "l3a" not in nc_cache:
        nc_cache["l3a"] = build_l3a()
    res = run_bass_kernel_spmd(nc_cache["l3a"], _l3a_inmaps(x_full, inputs),
                               core_ids=list(range(NCORE)))
    return np.concatenate([res.results[i]["ys"] for i in range(NCORE)], axis=0)



def build_l3f():
    p = Prog("l3f")
    xs = p.inp("xs", [TOK, D], F32)
    xq = p.inp("xq", [SEQ, D], F32)
    wqT_in = p.inp("wqT", [D, D], F32)
    wkT_in = p.inp("wkT", [D, D], F32)
    wvT_in = p.inp("wvT", [D, D], F32)
    wr_in = p.inp("wr", [D, 16], F32)
    wo_in = p.inp("wo", [256, D], F32)
    bqc_in = p.inp("bqc", [128, DC], F32)
    bkc_in = p.inp("bkc", [128, DC], F32)
    bvc_in = p.inp("bvc", [128, DC], F32)
    brc_in = p.inp("brc", [16, 1], F32)
    bo_in = p.inp("bo", [1, D], F32)
    ident_in = p.inp("ident", [128, 128], F32)
    ehh_in = p.inp("ehh", [16, 128], F32)
    mh_in = p.inp("mh", [16, 8], F32)
    t64_in = p.inp("t64", [128, 64], F32)
    e4_in = p.inp("e4", [128, 2 * 4], F32)
    sel4_in = p.inp("sel4", [4, 4 * 128], F32)
    ys = p.outp("ys", [TOK, D], F32)

    KW4 = SEQ

    wT = p.sb("wT", [128, DC * D], BF16)
    SelM = p.sb("SelM", [128, DC * 256], BF16)
    SelWq = p.sb("SelWq", [128, DC * 256], BF16)
    SelWk = p.sb("SelWk", [128, DC * 256], BF16)
    SelWv = p.sb("SelWv", [128, DC * 256], BF16)
    woS = p.sb("woS", [128, 2 * D], BF16)
    KselT = p.sb("KselT", [128, 2 * SEQ], BF16)
    Vsel = p.sb("Vsel", [128, 64 * 260], BF16)
    QselT = p.sb("QselT", [128, 2 * TOK], BF16)
    ident = p.sb("identb", [128, 128], BF16)
    ehh = p.sb("ehh_sb", [16, 128], BF16)
    mh = p.sb("mh_sb", [16, 8], BF16)
    t64 = p.sb("t64_sb", [128, 64], BF16)
    e4 = p.sb("e4_sb", [128, 2 * 4], BF16)
    sel4 = p.sb("sel4_sb", [4, 4 * 128], BF16)
    bqc = p.sb("bqc_sb", [128, DC], BF16)
    bkc = p.sb("bkc_sb", [128, DC], BF16)
    bvc = p.sb("bvc_sb", [128, DC], BF16)
    brc = p.sb("brc_sb", [16, 1], F32)
    ones_r = p.sb("ones_r", [1, 128], BF16)
    ones_c = p.sb("ones_c", [128, 1], BF16)
    wr_sb = p.sb("wr_sb", [128, DC * 16], BF16)
    rsum = p.sb("rsum", [16, 1], F32)
    rsc = p.sb("rsc", [16, 1], F32)
    rmb = p.sb("rmb", [16, 1], BF16)
    drow = p.sb("drow", [1, 16], F32)
    drow2 = p.sb("drow2", [1, 16], F32)
    dmax = p.sb("dmax", [1, 1], F32)
    dsum = p.sb("dsum", [1, 1], F32)
    ohs = p.sb("ohs", [1, 4 * 16], F32)
    ohsum = p.sb("ohsum", [1, 16], F32)
    grows = p.sb("grows", [1, 4 * 16], BF16)
    growT = p.sb("growT", [16, 1], BF16)
    gcm = p.sb("gcm", [16, 8], BF16)
    colv = p.sb("colv", [128, 8], F32)
    bsel_q = p.sb("bsel_q", [1, 256], BF16)
    bsel_k = p.sb("bsel_k", [1, 256], BF16)
    bsel_v = p.sb("bsel_v", [1, 256], BF16)
    bselqc = p.sb("bselqc", [128, 2], F32)
    bselkc = p.sb("bselkc", [128, 2], F32)
    bvselB = p.sb("bvselB", [128, 256], F32)
    boB = p.sb("boB", [128, D], F32)
    bo_bf = p.sb("bo_bf", [1, D], BF16)
    r_c = p.sb("r_c", [128, D], F32)
    xn_c = p.sb("xn_c", [128, D], BF16)
    xnT_c = p.sb("xnT_c", [128, DC * 128], BF16)
    sqs = p.sb("sqs", [128, 128], BF16)
    kn2m = p.sb("kn2m", [4, 1], F32)
    kn2c = p.sb("kn2c", [4, 1], F32)
    qn2 = p.sb("qn2", [4, TOK], F32)
    qbias = p.sb("qbias", [4, TOK], BF16)
    PT = p.sb("PT", [128, 512], BF16)
    rrow = p.sb("rrow", [1, 512], F32)
    rrowb = p.sb("rrowb", [1, 512], BF16)
    rBf = p.sb("rBf", [64, 512], F32)
    ctxall = p.sb("ctxall", [128, 2 * 512], BF16)
    outs = p.sb("outs", [128, D], F32)

    P1 = p.ps("P1", [128, 512], F32)
    PS = p.ps("PS", [128, 512], F32)
    PC65 = p.ps("PC65", [65, 512], F32)
    PB = p.ps("PB", [64, 512], F32)
    P4 = p.ps("P4", [128, 1024], BF16)
    P5 = p.ps("P5", [128, 128], F32)
    P6 = p.ps("P6", [16, 16], BF16)

    p.begin()
    scr = ln_scratch(p)
    for dst, src in [(ident, ident_in), (ehh, ehh_in), (mh, mh_in), (t64, t64_in),
                     (e4, e4_in), (sel4, sel4_in), (bqc, bqc_in), (bkc, bkc_in),
                     (bvc, bvc_in), (bo_bf, bo_in)]:
        p.gdma(lambda e, dst=dst, src=src: e.dma_start(out=dst[:, :], in_=src))
    p.dma(lambda e: e.dma_start(out=brc[:, :], in_=brc_in))
    p.gdma(lambda e: e.dma_start(
        out=wr_sb[:, :].rearrange("p (k n) -> p k n", k=DC),
        in_=wr_in.rearrange("(k p) n -> p k n", p=128)))
    p.v(lambda e: e.memset(ones_r[:, :], 1.0))
    p.v(lambda e: e.memset(ones_c[:, :], 1.0))
    p.v(lambda e: e.memset(rsum[:, :], 0.0))
    p.v(lambda e: e.memset(kn2m[:, :], 0.0))

    for c in range(SCH):
        p.dma(lambda e, c=c: e.dma_start(out=r_c[:, :], in_=xq[c * 128:(c + 1) * 128, :]))
        emit_ln(p, r_c[:, :], xn_c[:, :], scr)
        emit_transpose_cols(p, xn_c[:, :], DC, P4, ident,
                            xnT_c[:, :].rearrange("p (k t) -> p k t", k=DC))
        for k in range(DC):
            p.t(lambda e, k=k: e.matmul(P5[0:16, :], wr_sb[:, k * 16:(k + 1) * 16],
                                        xnT_c[:, k * 128:(k + 1) * 128],
                                        start=(k == 0), stop=(k == DC - 1)))
        p.v(lambda e: e.reduce_sum(rsc[:, :], P5[0:16, :], AX.X))
        p.v(lambda e: e.tensor_tensor(rsum[:, :], rsum[:, :], rsc[:, :], ALU.add))
    p.v(lambda e: e.tensor_scalar(rsum[:, :], rsum[:, :], 1.0 / SEQ, None, ALU.mult))
    p.v(lambda e: e.tensor_tensor(rsum[:, :], rsum[:, :], brc[:, :], ALU.add))
    p.v(lambda e: e.tensor_copy(rmb[:, :], rsum[:, :]))
    p.t(lambda e: e.transpose(P6[0:1, :], rmb[:, :], ident[0:16, 0:16]))
    p.v(lambda e: e.tensor_copy(drow[0:1, :], P6[0:1, 0:16]))
    p.v(lambda e: e.reduce_max(dmax[:, :], drow[0:1, :], AX.X, negate=True))
    p.s(lambda e: e.activation(drow[0:1, :], drow[0:1, :], AF.Exp, bias=dmax[0:1, :],
                               accum_out=dsum[0:1, :]))
    p.v(lambda e: e.reciprocal(dsum[:, :], dsum[:, :]))
    p.v(lambda e: e.tensor_scalar(drow[0:1, :], drow[0:1, :], dsum[0:1, 0:1], None, ALU.mult))
    p.v(lambda e: e.tensor_copy(drow2[0:1, :], drow[0:1, :]))
    for kk in range(4):
        p.v(lambda e: e.reduce_max(dmax[:, :], drow2[0:1, :], AX.X))
        p.v(lambda e, kk=kk: e.tensor_scalar(ohs[0:1, kk * 16:(kk + 1) * 16], drow2[0:1, :],
                                             dmax[0:1, 0:1], None, ALU.is_equal))
        p.v(lambda e, kk=kk: e.scalar_tensor_tensor(
            drow2[0:1, :], ohs[0:1, kk * 16:(kk + 1) * 16], -1e30, drow2[0:1, :],
            ALU.mult, ALU.add))
    p.v(lambda e: e.tensor_tensor(ohsum[0:1, :], ohs[0:1, 0:16], ohs[0:1, 16:32], ALU.add))
    p.v(lambda e: e.tensor_tensor(ohsum[0:1, :], ohsum[0:1, :], ohs[0:1, 32:48], ALU.add))
    p.v(lambda e: e.tensor_tensor(ohsum[0:1, :], ohsum[0:1, :], ohs[0:1, 48:64], ALU.add))
    p.v(lambda e: e.tensor_tensor(drow2[0:1, :], drow[0:1, :], ohsum[0:1, :], ALU.mult))
    p.v(lambda e: e.reduce_max(dmax[:, :], drow2[0:1, :], AX.X, negate=True))
    p.s(lambda e: e.activation(drow2[0:1, :], drow2[0:1, :], AF.Exp, bias=dmax[0:1, :],
                               accum_out=dsum[0:1, :]))
    p.v(lambda e: e.reciprocal(dsum[:, :], dsum[:, :]))
    p.v(lambda e: e.tensor_scalar(drow2[0:1, :], drow2[0:1, :], dsum[0:1, 0:1], None, ALU.mult))
    for kk in range(4):
        p.v(lambda e, kk=kk: e.tensor_tensor(grows[0:1, kk * 16:(kk + 1) * 16],
                                             drow2[0:1, :], ohs[0:1, kk * 16:(kk + 1) * 16],
                                             ALU.mult))
    for kk in range(4):
        p.t(lambda e, kk=kk: e.transpose(P6[0:16, 0:1], grows[0:1, kk * 16:(kk + 1) * 16],
                                         ident[0:1, 0:1]))
        p.v(lambda e: e.tensor_copy(growT[:, :], P6[0:16, 0:1]))
        p.v(lambda e: e.tensor_tensor(gcm[:, :], mh[:, :],
                                      growT[:, :].broadcast_to((16, 8)), ALU.mult))
        p.t(lambda e: e.matmul(P5[:, 0:8], ehh[:, :], gcm[:, :], start=True, stop=True))
        p.v(lambda e: e.tensor_copy(colv[:, :], P5[:, 0:8]))
        for s in range(DC):
            p.v(lambda e, kk=kk, s=s: e.tensor_scalar(
                SelM[:, s * 256 + kk * 64:s * 256 + (kk + 1) * 64], t64[:, :],
                colv[:, s:s + 1], None, ALU.mult))
    for (wsrc, dstW, bcol, bselrow) in [(wqT_in, SelWq, bqc, bsel_q),
                                        (wkT_in, SelWk, bkc, bsel_k),
                                        (wvT_in, SelWv, bvc, bsel_v)]:
        p.gdma(lambda e, wsrc=wsrc: e.dma_start(
            out=wT[:, :].rearrange("p (k n) -> p k n", k=DC),
            in_=wsrc.rearrange("(k p) n -> p k n", p=128)))
        for m in range(DC):
            for k in range(DC):
                p.t(lambda e, m=m, k=k: e.matmul(
                    P1[:, 0:256], wT[:, k * D + m * 128:k * D + (m + 1) * 128],
                    SelM[:, k * 256:(k + 1) * 256], start=(k == 0), stop=(k == DC - 1)))
            p.v(lambda e, m=m, dstW=dstW: e.tensor_copy(
                dstW[:, m * 256:(m + 1) * 256], P1[:, 0:256]))
        for k in range(DC):
            p.t(lambda e, k=k, bcol=bcol: e.matmul(
                P1[0:1, 0:256], bcol[:, k:k + 1], SelM[:, k * 256:(k + 1) * 256],
                start=(k == 0), stop=(k == DC - 1)))
        p.v(lambda e, bselrow=bselrow: e.tensor_copy(bselrow[0:1, :], P1[0:1, 0:256]))
    for mc in range(2):
        p.t(lambda e, mc=mc: e.matmul(P1[:, mc:mc + 1],
                                      bsel_q[0:1, mc * 128:(mc + 1) * 128],
                                      ones_r[0:1, 0:1], start=True, stop=True))
    p.v(lambda e: e.tensor_copy(bselqc[:, :], P1[:, 0:2]))
    for mc in range(2):
        p.t(lambda e, mc=mc: e.matmul(P1[:, mc:mc + 1],
                                      bsel_k[0:1, mc * 128:(mc + 1) * 128],
                                      ones_r[0:1, 0:1], start=True, stop=True))
    p.v(lambda e: e.tensor_copy(bselkc[:, :], P1[:, 0:2]))
    p.t(lambda e: e.matmul(P1[:, 0:256], ones_r[:, :], bsel_v[0:1, :], start=True, stop=True))
    p.v(lambda e: e.tensor_copy(bvselB[:, :], P1[:, 0:256]))
    p.gdma(lambda e: e.dma_start(
        out=woS[:, :].rearrange("p (k n) -> p k n", k=2),
        in_=wo_in.rearrange("(k p) n -> p k n", p=128)))
    for half in range(2):
        p.t(lambda e, half=half: e.matmul(P1[:, :], ones_r[:, :],
                                          bo_bf[0:1, half * 512:(half + 1) * 512],
                                          start=True, stop=True))
        p.v(lambda e, half=half: e.tensor_copy(boB[:, half * 512:(half + 1) * 512], P1[:, :]))
    p.v(lambda e: e.memset(Vsel[:, :], 1.0))

    for c in range(SCH):
        p.dma(lambda e, c=c: e.dma_start(out=r_c[:, :], in_=xq[c * 128:(c + 1) * 128, :]))
        emit_ln(p, r_c[:, :], xn_c[:, :], scr)
        emit_transpose_cols(p, xn_c[:, :], DC, P4, ident,
                            xnT_c[:, :].rearrange("p (k t) -> p k t", k=DC))
        for mc in range(2):
            for k in range(DC):
                p.t(lambda e, mc=mc, k=k: e.matmul(
                    P1[:, 0:128], SelWk[:, k * 256 + mc * 128:k * 256 + (mc + 1) * 128],
                    xnT_c[:, k * 128:(k + 1) * 128], start=(k == 0), stop=(k == DC - 1)))
            p.v(lambda e, mc=mc, c=c: e.tensor_scalar(
                KselT[:, mc * SEQ + c * 128:mc * SEQ + (c + 1) * 128],
                P1[:, 0:128], bselkc[:, mc:mc + 1], None, ALU.add))
            p.s(lambda e, mc=mc, c=c: e.activation(
                sqs[:, :], KselT[:, mc * SEQ + c * 128:mc * SEQ + (c + 1) * 128], AF.Square))
            p.t(lambda e, mc=mc: e.matmul(P5[0:4, :], e4[:, mc * 4:(mc + 1) * 4],
                                          sqs[:, :], start=(mc == 0), stop=(mc == 1)))
        p.v(lambda e: e.reduce_max(kn2c[:, :], P5[0:4, :], AX.X))
        p.v(lambda e: e.tensor_tensor(kn2m[:, :], kn2m[:, :], kn2c[:, :], ALU.max))
        for k in range(DC):
            p.t(lambda e, k=k: e.matmul(
                P1[:, 0:256], xnT_c[:, k * 128:(k + 1) * 128],
                SelWv[:, k * 256:(k + 1) * 256], start=(k == 0), stop=(k == DC - 1)))
        p.v(lambda e, c=c: e.tensor_tensor(
            Vsel[:, :].rearrange("p (ch t) -> p ch t", ch=64)[:, c, :]
                .rearrange("p (h x) -> p h x", h=4)[:, :, 0:64],
            P1[:, 0:256].rearrange("p (h d) -> p h d", h=4),
            bvselB[:, :].rearrange("p (h d) -> p h d", h=4), ALU.add))

    for oc in range(NCH):
        p.dma(lambda e, oc=oc: e.dma_start(out=r_c[:, :], in_=xs[oc * 128:(oc + 1) * 128, :]))
        emit_ln(p, r_c[:, :], xn_c[:, :], scr)
        emit_transpose_cols(p, xn_c[:, :], DC, P4, ident,
                            xnT_c[:, :].rearrange("p (k t) -> p k t", k=DC))
        for mc in range(2):
            for k in range(DC):
                p.t(lambda e, mc=mc, k=k: e.matmul(
                    P1[:, 0:128], SelWq[:, k * 256 + mc * 128:k * 256 + (mc + 1) * 128],
                    xnT_c[:, k * 128:(k + 1) * 128], start=(k == 0), stop=(k == DC - 1)))
            p.v(lambda e, mc=mc, oc=oc: e.tensor_scalar(
                QselT[:, mc * TOK + oc * 128:mc * TOK + (oc + 1) * 128],
                P1[:, 0:128], bselqc[:, mc:mc + 1], 0.125, ALU.add, ALU.mult))
            p.s(lambda e, mc=mc, oc=oc: e.activation(
                sqs[:, :], QselT[:, mc * TOK + oc * 128:mc * TOK + (oc + 1) * 128], AF.Square))
            p.t(lambda e, mc=mc: e.matmul(P5[0:4, :], e4[:, mc * 4:(mc + 1) * 4],
                                          sqs[:, :], start=(mc == 0), stop=(mc == 1)))
        p.v(lambda e, oc=oc: e.tensor_copy(qn2[:, oc * 128:(oc + 1) * 128], P5[0:4, :]))
    for oc in range(NCH):
        p.v(lambda e, oc=oc: e.tensor_scalar(
            qn2[:, oc * 128:(oc + 1) * 128], qn2[:, oc * 128:(oc + 1) * 128],
            kn2m[0:4, 0:1], 1.0, ALU.mult, ALU.mult))
        p.s(lambda e, oc=oc: e.activation(qn2[:, oc * 128:(oc + 1) * 128],
                                          qn2[:, oc * 128:(oc + 1) * 128], AF.Sqrt))
        p.v(lambda e, oc=oc: e.tensor_scalar(
            qbias[:, oc * 128:(oc + 1) * 128], qn2[:, oc * 128:(oc + 1) * 128],
            -1.0, 50.0, ALU.mult, ALU.add))

    for qb in range(4):
        for h in range(4):
            mc = h // 2
            po = (h % 2) * 64
            for kc in range(SCH):
                p.t(lambda e, kc=kc, mc=mc, po=po, qb=qb: e.matmul(
                    PS[:, :],
                    KselT[po:po + 64, mc * SEQ + kc * 128:mc * SEQ + (kc + 1) * 128],
                    QselT[po:po + 64, mc * TOK + qb * 512:mc * TOK + (qb + 1) * 512],
                    start=True, stop=False))
                p.t(lambda e, h=h, qb=qb: e.matmul(
                    PS[:, :], sel4[:, h * 128:(h + 1) * 128],
                    qbias[:, qb * 512:(qb + 1) * 512], start=False, stop=True))
                p.s(lambda e: e.activation(PT[:, :], PS[:, :], AF.Exp))
                p.t(lambda e, kc=kc, h=h: e.matmul(
                    PC65[:, :], Vsel[:, kc * 260 + h * 65:kc * 260 + (h + 1) * 65],
                    PT[:, :], start=(kc == 0), stop=(kc == SCH - 1)))
            p.v(lambda e: e.tensor_copy(rrow[0:1, :], PC65[64:65, :]))
            p.v(lambda e: e.reciprocal(rrow[0:1, :], rrow[0:1, :]))
            p.v(lambda e: e.tensor_copy(rrowb[0:1, :], rrow[0:1, :]))
            p.t(lambda e: e.matmul(PB[:, :], ones_r[0:1, 0:64], rrowb[0:1, :],
                                   start=True, stop=True))
            p.v(lambda e: e.tensor_copy(rBf[:, :], PB[:, :]))
            p.v(lambda e, mc=mc, po=po: e.tensor_tensor(
                ctxall[po:po + 64, mc * 512:(mc + 1) * 512], PC65[0:64, :],
                rBf[:, :], ALU.mult))
        for tc in range(4):
            oc = qb * 4 + tc
            p.dma(lambda e, oc=oc: e.dma_start(out=r_c[:, :], in_=xs[oc * 128:(oc + 1) * 128, :]))
            for half in range(2):
                for k2 in range(2):
                    p.t(lambda e, half=half, k2=k2, tc=tc: e.matmul(
                        P1[:, :], ctxall[:, k2 * 512 + tc * 128:k2 * 512 + tc * 128 + 128],
                        woS[:, k2 * D + half * 512:k2 * D + (half + 1) * 512],
                        start=(k2 == 0), stop=(k2 == 1)))
                p.v(lambda e, half=half: e.tensor_tensor(
                    outs[:, half * 512:(half + 1) * 512], P1[:, :],
                    boB[:, half * 512:(half + 1) * 512], ALU.add))
                p.v(lambda e, half=half: e.tensor_tensor(
                    outs[:, half * 512:(half + 1) * 512], outs[:, half * 512:(half + 1) * 512],
                    r_c[:, half * 512:(half + 1) * 512], ALU.add))
            p.dma(lambda e, oc=oc: e.dma_start(out=ys[oc * 128:(oc + 1) * 128, :], in_=outs[:, :]))

    return p.end()


def _l3f_inmaps(x_full, inputs):
    ehh = np.zeros((16, 128), dtype=np.float32)
    for r in range(16):
        if r % 2 == 0:
            ehh[r, 0:64] = 1.0
        else:
            ehh[r, 64:128] = 1.0
    mhost = np.zeros((16, 8), dtype=np.float32)
    for r in range(16):
        mhost[r, r // 2] = 1.0
    t64 = np.concatenate([np.eye(64, dtype=np.float32)] * 2, axis=0)
    e4 = np.zeros((128, 2 * 4), dtype=np.float32)
    for mc in range(2):
        e4[0:64, mc * 4 + 2 * mc] = 1.0
        e4[64:128, mc * 4 + 2 * mc + 1] = 1.0
    sel4 = np.zeros((4, 4 * 128), dtype=np.float32)
    for h in range(4):
        sel4[h, h * 128:(h + 1) * 128] = 1.0
    def cols(v):
        return _f32(v).reshape(DC, 128).T.copy()
    maps = []
    for cid in range(NCORE):
        b = cid // 4
        maps.append(dict(
            xs=_f32(x_full[cid * TOK:(cid + 1) * TOK]),
            xq=_f32(x_full[b * SEQ:(b + 1) * SEQ]),
            wqT=_f32(np.asarray(inputs["L3_ffn_Wq"]).T), wkT=_f32(np.asarray(inputs["L3_ffn_Wk"]).T),
            wvT=_f32(np.asarray(inputs["L3_ffn_Wv"]).T), wr=_f32(inputs["L3_ffn_Wr"]),
            wo=_f32(inputs["L3_ffn_Wo"]),
            bqc=cols(inputs["L3_ffn_bq"]), bkc=cols(inputs["L3_ffn_bk"]),
            bvc=cols(inputs["L3_ffn_bv"]),
            brc=_f32(inputs["L3_ffn_br"]).reshape(16, 1),
            bo=_f32(inputs["L3_ffn_bo"]).reshape(1, D),
            ident=IDENT, ehh=ehh, mh=mhost, t64=t64, e4=e4, sel4=sel4))
    return maps


def run_l3f(x_full, inputs, nc_cache={}):
    if "l3f" not in nc_cache:
        nc_cache["l3f"] = build_l3f()
    res = run_bass_kernel_spmd(nc_cache["l3f"], _l3f_inmaps(x_full, inputs),
                               core_ids=list(range(NCORE)))
    return np.concatenate([res.results[i]["ys"] for i in range(NCORE)], axis=0)



_FUSED_CACHE = {}


def kernel(**inputs):
    x = np.ascontiguousarray(np.asarray(inputs["L0_attn_x"], dtype=np.float32)
                             .reshape(16384, D))
    if "nc" not in _FUSED_CACHE:
        _FUSED_CACHE["nc"] = build_fused(nblocks=8, debug_taps=False)
    nc = _FUSED_CACHE["nc"]
    maps = _fused_inmaps(inputs, x)
    res = run_bass_kernel_spmd(nc, maps, core_ids=list(range(NCORE)))
    y = np.concatenate([res.results[i]["ys"] for i in range(NCORE)], axis=0)
    return np.ascontiguousarray(y.reshape(2, 8192, D).astype(np.float32))



GROUPS = [[0, 1, 2, 3], [4, 5, 6, 7]]


def _prog_globals(p):
    g = {}
    g["ident"] = p.sb("g_ident", [128, 128], BF16)
    g["identf"] = p.sb("g_identf", [128, 128], F32)
    g["ones_r"] = p.sb("g_ones_r", [1, 128], BF16)
    g["ones_c"] = p.sb("g_ones_c", [128, 1], BF16)
    g["neg_row"] = p.sb("g_neg_row", [1, 128], F32)
    g["r_c"] = p.sb("g_r_c", [128, D], F32)
    g["xn_c"] = p.sb("g_xn_c", [128, D], BF16)
    g["xnT_c"] = p.sb("g_xnT_c", [128, DC * 128], BF16)
    g["outs"] = p.sb("g_outs", [128, D], F32)
    g["sqs"] = p.sb("g_sqs", [128, 128], BF16)
    g["P1"] = p.ps("g_P1", [128, 512], F32)
    g["P2"] = p.ps("g_P2", [128, 640], F32)
    g["P3"] = p.ps("g_P3", [128, 1024], F32)
    g["P4"] = p.ps("g_P4", [128, 1024], BF16)
    g["P4f"] = p.ps("g_P4f", [128, 512], F32)
    g["P5"] = p.ps("g_P5", [128, 128], F32)
    return g


def _prog_globals_init(p, g, ident_in):
    g["scr"] = ln_scratch(p)
    p.gdma(lambda e: e.dma_start(out=g["ident"][:, :], in_=ident_in))
    p.dma(lambda e: e.dma_start(out=g["identf"][:, :], in_=ident_in))
    p.v(lambda e: e.memset(g["ones_r"][:, :], 1.0))
    p.v(lambda e: e.memset(g["ones_c"][:, :], 1.0))
    p.v(lambda e: e.memset(g["neg_row"][:, :], -1e30))


def emit_l0a_f(p, g, xcur, xdst, taps=None):
    pre = "l0_"
    wqT = p.inp(pre + "wqT", [D, D], F32)
    wkT = p.inp(pre + "wkT", [D, D], F32)
    wvT = p.inp(pre + "wvT", [D, D], F32)
    woT = p.inp(pre + "woT", [D, D], F32)
    bq_r = p.inp(pre + "bq", [1, D], F32)
    bk_r = p.inp(pre + "bk", [1, D], F32)
    bv_r = p.inp(pre + "bv", [1, D], F32)
    bo_r = p.inp(pre + "bo", [1, D], F32)
    ropeq_in = p.inp(pre + "ropeq", [128, 4 * 512], F32)
    ropek_in = p.inp(pre + "ropek", [128, 4 * 512], F32)
    tri_in = p.inp(pre + "tri", [128, 2 * 128], F32)
    e16_in = p.inp(pre + "e16", [128, 8 * 16], F32)
    sel16_in = p.inp(pre + "sel16", [16, 16 * 128], F32)
    sel64_in = p.inp(pre + "sel64", [16, 16 * 64], F32)
    kill01_in = p.inp(pre + "kill01", [1, NCH * 5], F32)
    hind_in = p.inp(pre + "hind", [128, 8], F32)

    cc_in = p.dram(pre + "cc_in", [512, D], F32)
    cc_out = p.dram(pre + "cc_out", [2048, D], F32)

    sc = p.scope()
    with sc:
        NH = 16
        KW = XCH * 128
        ident = g["ident"]
        ones_r = g["ones_r"]
        ones_c = g["ones_c"]
        neg_row = g["neg_row"]
        r_c = g["r_c"]
        xn_c = g["xn_c"]
        xnT_c = g["xnT_c"]
        outs = g["outs"]
        sqs = g["sqs"]
        scr = g["scr"]
        P1, P2, P3, P4, P5 = g["P1"], g["P2"], g["P3"], g["P4"], g["P5"]
        P6 = P4[0:16, 0:128]

        wA = p.sb(pre + "wA", [128, DC * D], BF16, sc)
        wB = p.sb(pre + "wB", [128, DC * D], BF16, sc)
        wC = p.sb(pre + "wC", [128, DC * D], BF16, sc)
        ropeq = p.sb(pre + "ropeq_sb", [128, 4 * 512], BF16, sc)
        ropek = p.sb(pre + "ropek_sb", [128, 4 * 512], BF16, sc)
        rs1 = p.sb(pre + "rs1", [128, 512], BF16, sc)
        rs2 = p.sb(pre + "rs2", [128, 512], BF16, sc)
        tri = p.sb(pre + "tri_sb", [128, 2 * 128], F32, sc)
        e16 = p.sb(pre + "e16_sb", [128, 8 * 16], BF16, sc)
        sel16 = p.sb(pre + "sel16_sb", [16, 16 * 128], BF16, sc)
        sel64 = p.sb(pre + "sel64_sb", [16, 16 * 64], BF16, sc)
        kill01 = p.sb(pre + "kill01_sb", [1, NCH * 5], F32, sc)
        killrow = p.sb(pre + "killrow", [1, 5 * 128], BF16, sc)
        hind = p.sb(pre + "hind_sb", [128, 8], F32, sc)
        halo = p.sb(pre + "halo", [128, 4 * D], F32, sc)
        gtmp = p.sb(pre + "gtmp", [128, D], F32, sc)
        bqc = p.sb(pre + "bqc", [128, DC], F32, sc)
        bkc = p.sb(pre + "bkc", [128, DC], F32, sc)
        brow_b = p.sb(pre + "brow_b", [1, D], BF16, sc)
        bx_bf = p.sb(pre + "bx_bf", [1, D], BF16, sc)
        bvB = p.sb(pre + "bvB", [128, D], F32, sc)
        boB = p.sb(pre + "boB", [128, D], F32, sc)
        KT = p.sb(pre + "KT", [128, DC * KW], BF16, sc)
        V = p.sb(pre + "V", [128, XCH * D], BF16, sc)
        QTc = p.sb(pre + "QTc", [128, DC * 128], BF16, sc)
        qn2c = p.sb(pre + "qn2c", [16, 128], F32, sc)
        kn2m = p.sb(pre + "kn2m", [16, 1], F32, sc)
        kn2c = p.sb(pre + "kn2c", [16, 1], F32, sc)
        qbias_c = p.sb(pre + "qbias_c", [16, 128], BF16, sc)
        PT = p.sb(pre + "PT", [128, 5 * 128], BF16, sc)
        rS = p.sb(pre + "rS", [128, 16], F32, sc)
        rSb = p.sb(pre + "rSb", [128, 16], BF16, sc)
        rT = p.sb(pre + "rT", [16, 128], BF16, sc)
        rB = p.sb(pre + "rB", [64, 128], F32, sc)
        ctxT = p.sb(pre + "ctxT", [128, D], BF16, sc)

        for i, c in enumerate([0, 1, NCH - 2, NCH - 1]):
            p.dma(lambda e, i=i, c=c: e.dma_start(
                out=cc_in.ap()[i * 128:(i + 1) * 128, :], in_=xcur[c * 128:(c + 1) * 128, :]))
        p.collective("AllGather", cc_in, cc_out, GROUPS)

        p.gdma(lambda e: e.dma_start(out=ropeq[:, :], in_=ropeq_in))
        p.gdma(lambda e: e.dma_start(out=ropek[:, :], in_=ropek_in))
        p.dma(lambda e: e.dma_start(out=tri[:, :], in_=tri_in))
        p.gdma(lambda e: e.dma_start(out=e16[:, :], in_=e16_in))
        p.gdma(lambda e: e.dma_start(out=sel16[:, :], in_=sel16_in))
        p.gdma(lambda e: e.dma_start(out=sel64[:, :], in_=sel64_in))
        p.dma(lambda e: e.dma_start(out=kill01[:, :], in_=kill01_in))
        p.dma(lambda e: e.dma_start(out=hind[:, :], in_=hind_in))
        p.v(lambda e: e.memset(kn2m[:, :], 0.0))
        p.v(lambda e: e.memset(halo[:, :], 0.0))

        for j in range(2):
            for r in range(4):
                p.dma(lambda e, r=r, j=j: e.dma_start(
                    out=gtmp[:, :], in_=cc_out.ap()[r * 512 + (2 + j) * 128:r * 512 + (3 + j) * 128, :]))
                p.v(lambda e, r=r, j=j: e.scalar_tensor_tensor(
                    halo[:, j * D:(j + 1) * D], gtmp[:, :], hind[:, r:r + 1],
                    halo[:, j * D:(j + 1) * D], ALU.mult, ALU.add))
        for j in range(2):
            for r in range(4):
                p.dma(lambda e, r=r, j=j: e.dma_start(
                    out=gtmp[:, :], in_=cc_out.ap()[r * 512 + j * 128:r * 512 + (j + 1) * 128, :]))
                p.v(lambda e, r=r, j=j: e.scalar_tensor_tensor(
                    halo[:, (2 + j) * D:(3 + j) * D], gtmp[:, :], hind[:, 4 + r:5 + r],
                    halo[:, (2 + j) * D:(3 + j) * D], ALU.mult, ALU.add))

        def load_xh(c):
            if c < HALO_CH:
                return halo[:, c * D:(c + 1) * D]
            if c >= XCH - HALO_CH:
                return halo[:, (2 + c - (XCH - HALO_CH)) * D:(3 + c - (XCH - HALO_CH)) * D]
            oc = c - HALO_CH
            p.dma(lambda e, oc=oc: e.dma_start(out=r_c[:, :], in_=xcur[oc * 128:(oc + 1) * 128, :]))
            return r_c[:, :]

        def rope_fold(dst, rope_sb, src_in, raw):
            p.gdma(lambda e: e.dma_start(
                out=raw[:, :].rearrange("p (k n) -> p k n", k=DC),
                in_=src_in.rearrange("(k p) n -> p k n", p=128)))
            for k in range(DC):
                b0 = k * D
                ev = raw[:, b0:b0 + D:2]
                od = raw[:, b0 + 1:b0 + D:2]
                dev = dst[:, b0:b0 + D:2]
                dod = dst[:, b0 + 1:b0 + D:2]
                ce = rope_sb[:, 0:512]
                se = rope_sb[:, 512:1024]
                co = rope_sb[:, 1024:1536]
                so = rope_sb[:, 1536:2048]
                p.v(lambda e, od=od, se=se: e.tensor_tensor(rs1[:, :], od, se, ALU.mult))
                p.v(lambda e, ev=ev, so=so: e.tensor_tensor(rs2[:, :], ev, so, ALU.mult))
                p.v(lambda e, ev=ev, ce=ce, dev=dev: e.tensor_tensor(dev, ev, ce, ALU.mult))
                p.v(lambda e, dev=dev: e.tensor_tensor(dev, dev, rs1[:, :], ALU.subtract))
                p.v(lambda e, od=od, co=co, dod=dod: e.tensor_tensor(dod, od, co, ALU.mult))
                p.v(lambda e, dod=dod: e.tensor_tensor(dod, dod, rs2[:, :], ALU.add))

        rope_fold(wA, ropeq, wqT, wB)
        rope_fold(wB, ropek, wkT, wC)
        p.gdma(lambda e: e.dma_start(
            out=wC[:, :].rearrange("p (k n) -> p k n", k=DC),
            in_=wvT.rearrange("(k p) n -> p k n", p=128)))

        def rope_bias_row(rope_sb, b_in):
            p.gdma(lambda e: e.dma_start(out=bx_bf[:, :], in_=b_in))
            ev = bx_bf[:, 0:D:2]
            od = bx_bf[:, 1:D:2]
            p.v(lambda e: e.tensor_tensor(rs1[0:1, 0:512], od, rope_sb[0:1, 512:1024], ALU.mult))
            p.v(lambda e: e.tensor_tensor(rs2[0:1, 0:512], ev, rope_sb[0:1, 1536:2048], ALU.mult))
            p.v(lambda e: e.tensor_tensor(brow_b[0:1, 0:D:2], ev, rope_sb[0:1, 0:512], ALU.mult))
            p.v(lambda e: e.tensor_tensor(brow_b[0:1, 0:D:2], brow_b[0:1, 0:D:2], rs1[0:1, 0:512], ALU.subtract))
            p.v(lambda e: e.tensor_tensor(brow_b[0:1, 1:D:2], od, rope_sb[0:1, 1024:1536], ALU.mult))
            p.v(lambda e: e.tensor_tensor(brow_b[0:1, 1:D:2], brow_b[0:1, 1:D:2], rs2[0:1, 0:512], ALU.add))

        def row_to_cols(row_bf, cols_f32):
            for m in range(DC):
                p.t(lambda e, m=m: e.matmul(P1[:, m:m + 1], row_bf[0:1, m * 128:(m + 1) * 128],
                                            ones_r[0:1, 0:1], start=True, stop=True))
            p.v(lambda e: e.tensor_copy(cols_f32[:, :], P1[:, 0:DC]))

        rope_bias_row(ropeq, bq_r)
        row_to_cols(brow_b, bqc)
        rope_bias_row(ropek, bk_r)
        row_to_cols(brow_b, bkc)
        p.gdma(lambda e: e.dma_start(out=bx_bf[:, :], in_=bv_r))
        for half in range(2):
            p.t(lambda e, half=half: e.matmul(P1[:, :], ones_r[:, :],
                                              bx_bf[0:1, half * 512:(half + 1) * 512],
                                              start=True, stop=True))
            p.v(lambda e, half=half: e.tensor_copy(bvB[:, half * 512:(half + 1) * 512], P1[:, :]))
        p.gdma(lambda e: e.dma_start(out=bx_bf[:, :], in_=bo_r))
        for half in range(2):
            p.t(lambda e, half=half: e.matmul(P1[:, :], ones_r[:, :],
                                              bx_bf[0:1, half * 512:(half + 1) * 512],
                                              start=True, stop=True))
            p.v(lambda e, half=half: e.tensor_copy(boB[:, half * 512:(half + 1) * 512], P1[:, :]))

        for c in range(XCH):
            xsrc = load_xh(c)
            emit_ln(p, xsrc, xn_c[:, :], scr)
            emit_transpose_cols(p, xn_c[:, :], DC, P4, ident,
                                xnT_c[:, :].rearrange("p (k t) -> p k t", k=DC))
            for m in range(DC):
                for k in range(DC):
                    p.t(lambda e, m=m, k=k: e.matmul(
                        P1[:, 0:128], wB[:, k * D + m * 128:k * D + (m + 1) * 128],
                        xnT_c[:, k * 128:(k + 1) * 128], start=(k == 0), stop=(k == DC - 1)))
                p.v(lambda e, m=m, c=c: e.tensor_scalar(
                    KT[:, m * KW + c * 128:m * KW + (c + 1) * 128],
                    P1[:, 0:128], bkc[:, m:m + 1], None, ALU.add))
                p.s(lambda e, m=m, c=c: e.activation(
                    sqs[:, :], KT[:, m * KW + c * 128:m * KW + (c + 1) * 128], AF.Square))
                p.t(lambda e, m=m: e.matmul(P5[0:16, :], e16[:, m * 16:(m + 1) * 16],
                                            sqs[:, :], start=(m == 0), stop=(m == DC - 1)))
            p.v(lambda e: e.reduce_max(kn2c[:, :], P5[0:16, :], AX.X))
            p.v(lambda e: e.tensor_tensor(kn2m[:, :], kn2m[:, :], kn2c[:, :], ALU.max))
            for half in range(2):
                for k in range(DC):
                    p.t(lambda e, half=half, k=k: e.matmul(
                        P1[:, :], xnT_c[:, k * 128:(k + 1) * 128],
                        wC[:, k * D + half * 512:k * D + (half + 1) * 512],
                        start=(k == 0), stop=(k == DC - 1)))
                p.v(lambda e, half=half, c=c: e.tensor_tensor(
                    V[:, c * D + half * 512:c * D + (half + 1) * 512], P1[:, :],
                    bvB[:, half * 512:(half + 1) * 512], ALU.add))

        p.gdma(lambda e: e.dma_start(
            out=wB[:, :].rearrange("p (k n) -> p k n", k=DC),
            in_=woT.rearrange("(k p) n -> p k n", p=128)))

        for oc in range(NCH):
            hc = oc + HALO_CH
            p.dma(lambda e, oc=oc: e.dma_start(out=r_c[:, :], in_=xcur[oc * 128:(oc + 1) * 128, :]))
            emit_ln(p, r_c[:, :], xn_c[:, :], scr)
            emit_transpose_cols(p, xn_c[:, :], DC, P4, ident,
                                xnT_c[:, :].rearrange("p (k t) -> p k t", k=DC))
            for m in range(DC):
                for k in range(DC):
                    p.t(lambda e, m=m, k=k: e.matmul(
                        P1[:, 0:128], wA[:, k * D + m * 128:k * D + (m + 1) * 128],
                        xnT_c[:, k * 128:(k + 1) * 128], start=(k == 0), stop=(k == DC - 1)))
                p.v(lambda e, m=m: e.tensor_scalar(
                    QTc[:, m * 128:(m + 1) * 128], P1[:, 0:128], bqc[:, m:m + 1], None, ALU.add))
                p.s(lambda e, m=m: e.activation(sqs[:, :], QTc[:, m * 128:(m + 1) * 128], AF.Square))
                p.t(lambda e, m=m: e.matmul(P5[0:16, :], e16[:, m * 16:(m + 1) * 16],
                                            sqs[:, :], start=(m == 0), stop=(m == DC - 1)))
            p.v(lambda e: e.tensor_copy(qn2c[:, :], P5[0:16, :]))
            p.v(lambda e: e.tensor_scalar(qn2c[:, :], qn2c[:, :], kn2m[0:16, 0:1],
                                          1.0, ALU.mult, ALU.mult))
            p.s(lambda e: e.activation(qn2c[:, :], qn2c[:, :], AF.Sqrt))
            p.v(lambda e: e.tensor_scalar(qbias_c[:, :], qn2c[:, :], -1.0, 50.0, ALU.mult, ALU.add))
            for di in range(5):
                p.v(lambda e, di=di, oc=oc: e.tensor_scalar(
                    killrow[0:1, di * 128:(di + 1) * 128], neg_row[0:1, :],
                    kill01[0:1, oc * 5 + di:oc * 5 + di + 1], None, ALU.mult))
            for h in range(NH):
                sl = h // 2
                po = (h % 2) * 64
                for di in range(5):
                    kc = hc + di - 2
                    p.t(lambda e, di=di, kc=kc, sl=sl, po=po: e.matmul(
                        P2[:, di * 128:(di + 1) * 128],
                        KT[po:po + 64, sl * KW + kc * 128:sl * KW + (kc + 1) * 128],
                        QTc[po:po + 64, sl * 128:(sl + 1) * 128], start=True, stop=False))
                    p.t(lambda e, di=di, h=h: e.matmul(
                        P2[:, di * 128:(di + 1) * 128], sel16[:, h * 128:(h + 1) * 128],
                        qbias_c[:, :], start=False, stop=False))
                    p.t(lambda e, di=di: e.matmul(
                        P2[:, di * 128:(di + 1) * 128],
                        killrow[0:1, di * 128:(di + 1) * 128], ones_r[:, :],
                        start=False, stop=True))
                p.v(lambda e: e.tensor_tensor(P2[:, 0:128], P2[:, 0:128], tri[:, 0:128], ALU.add))
                p.v(lambda e: e.tensor_tensor(P2[:, 512:640], P2[:, 512:640], tri[:, 128:256], ALU.add))
                p.s(lambda e: e.activation(PT[:, :], P2[:, :], AF.Exp))
                for di in range(5):
                    p.t(lambda e, di=di: e.matmul(P5[:, 0:1], PT[:, di * 128:(di + 1) * 128],
                                                  ones_c[:, 0:1], start=(di == 0), stop=(di == 4)))
                p.v(lambda e, h=h: e.tensor_copy(rS[:, h:h + 1], P5[:, 0:1]))
                for di in range(5):
                    kc = hc + di - 2
                    p.t(lambda e, di=di, kc=kc, h=h, sl=sl, po=po: e.matmul(
                        P3[po:po + 64, sl * 128:(sl + 1) * 128],
                        V[:, kc * D + h * 64:kc * D + (h + 1) * 64],
                        PT[:, di * 128:(di + 1) * 128], start=(di == 0), stop=(di == 4)))
            p.v(lambda e: e.reciprocal(rS[:, :], rS[:, :]))
            p.v(lambda e: e.tensor_copy(rSb[:, :], rS[:, :]))
            p.t(lambda e: e.transpose(P6, rSb[:, :], ident[:, :]))
            p.v(lambda e: e.tensor_copy(rT[:, :], P6))
            for h in range(NH):
                sl = h // 2
                po = (h % 2) * 64
                p.t(lambda e, h=h: e.matmul(P5[0:64, :], sel64[:, h * 64:(h + 1) * 64],
                                            rT[:, :], start=True, stop=True))
                p.v(lambda e: e.tensor_copy(rB[:, :], P5[0:64, :]))
                p.v(lambda e, sl=sl, po=po: e.tensor_tensor(
                    ctxT[po:po + 64, sl * 128:(sl + 1) * 128],
                    P3[po:po + 64, sl * 128:(sl + 1) * 128], rB[:, :], ALU.mult))
            for half in range(2):
                for k in range(DC):
                    p.t(lambda e, half=half, k=k: e.matmul(
                        P1[:, :], ctxT[:, k * 128:(k + 1) * 128],
                        wB[:, k * D + half * 512:k * D + (half + 1) * 512],
                        start=(k == 0), stop=(k == DC - 1)))
                p.v(lambda e, half=half: e.tensor_tensor(
                    outs[:, half * 512:(half + 1) * 512], P1[:, :],
                    boB[:, half * 512:(half + 1) * 512], ALU.add))
                p.v(lambda e, half=half: e.tensor_tensor(
                    outs[:, half * 512:(half + 1) * 512], outs[:, half * 512:(half + 1) * 512],
                    r_c[:, half * 512:(half + 1) * 512], ALU.add))
            p.dma(lambda e, oc=oc: e.dma_start(out=xdst.ap()[oc * 128:(oc + 1) * 128, :], in_=outs[:, :]))
            if taps is not None:
                p.dma(lambda e, oc=oc: e.dma_start(out=taps[oc * 128:(oc + 1) * 128, :], in_=outs[:, :]))


def _l0a_f_host(inputs, maps):
    hostc = _l0a_host()
    wqT = _f32(np.asarray(inputs["L0_attn_Wq"]).T)
    wkT = _f32(np.asarray(inputs["L0_attn_Wk"]).T)
    wvT = _f32(np.asarray(inputs["L0_attn_Wv"]).T)
    woT = _f32(np.asarray(inputs["L0_attn_Wo"]).T)
    for cid in range(NCORE):
        rk = cid % 4
        kill = np.zeros((NCH, 5), dtype=np.float32)
        sc0 = rk * NCH
        for oc in range(NCH):
            for di in range(5):
                gc = sc0 + oc + di - 2
                if gc < 0 or gc >= 64:
                    kill[oc, di] = 1.0
        hind = np.zeros((128, 8), dtype=np.float32)
        if rk > 0:
            hind[:, rk - 1] = 1.0
        if rk < 3:
            hind[:, 4 + rk + 1] = 1.0
        maps[cid].update({
            "l0_wqT": wqT, "l0_wkT": wkT, "l0_wvT": wvT, "l0_woT": woT,
            "l0_bq": _f32(inputs["L0_attn_bq"]).reshape(1, D),
            "l0_bk": _f32(inputs["L0_attn_bk"]).reshape(1, D),
            "l0_bv": _f32(inputs["L0_attn_bv"]).reshape(1, D),
            "l0_bo": _f32(inputs["L0_attn_bo"]).reshape(1, D),
            "l0_ropeq": hostc["ropeq"], "l0_ropek": hostc["ropek"],
            "l0_tri": hostc["tri"], "l0_e16": hostc["e16"],
            "l0_sel16": hostc["sel16"], "l0_sel64": hostc["sel64"],
            "l0_kill01": kill.reshape(1, -1), "l0_hind": hind,
        })


def emit_ffn_f(p, g, pre, xcur, xdst, taps=None):
    wgT = p.inp(pre + "wgT", [D, 3072], I32)
    wuT = p.inp(pre + "wuT", [D, 3072], I32)
    wdT = p.inp(pre + "wdT", [3072, D], I32)
    H = 3072
    HC = H // 128
    G = 256
    NG = TOK // G
    sc = p.scope()
    with sc:
        ident = g["ident"]
        r_c = g["r_c"]
        xn_c = g["xn_c"]
        outs = g["outs"]
        scr = g["scr"]
        P1, P4 = g["P1"], g["P4"]
        pmm = g["P5"]
        pmm = g["P2"][:, 0:G]
        wg_sb = p.sb(pre + "wg", [128, DC * H], BF16, sc)
        wu_sb = p.sb(pre + "wu", [128, DC * H], BF16, sc)
        wd_sb = p.sb(pre + "wd", [128, HC * D], BF16, sc)
        r_g = p.sb(pre + "r_g", [128, 2 * D], F32, sc)
        xnT_g = p.sb(pre + "xnT_g", [128, DC * G], BF16, sc)
        gt = p.sb(pre + "gt", [128, G], BF16, sc)
        ut = p.sb(pre + "ut", [128, G], BF16, sc)
        hT_g = p.sb(pre + "hT_g", [128, HC * G], BF16, sc)

        p.gdma(lambda e: e.dma_start(
            out=wg_sb[:, :].rearrange("p (k h) -> p k h", k=DC),
            in_=wgT.rearrange("(k p) h -> p k h", p=128)))
        p.gdma(lambda e: e.dma_start(
            out=wu_sb[:, :].rearrange("p (k h) -> p k h", k=DC),
            in_=wuT.rearrange("(k p) h -> p k h", p=128)))
        p.gdma(lambda e: e.dma_start(
            out=wd_sb[:, :].rearrange("p (m n) -> p m n", m=HC),
            in_=wdT.rearrange("(m p) n -> p m n", p=128)))

        for grp in range(NG):
            for cc in range(2):
                c = grp * 2 + cc
                rsl = r_g[:, cc * D:(cc + 1) * D]
                p.dma(lambda e, c=c, rsl=rsl: e.dma_start(out=rsl, in_=xcur[c * 128:(c + 1) * 128, :]))
                emit_ln(p, rsl, xn_c[:, :], scr)
                dst = xnT_g[:, :].rearrange("p (k t) -> p k t", k=DC)[:, :, cc * 128:cc * 128 + 128]
                emit_transpose_cols(p, xn_c[:, :], DC, P4, ident, dst)
            for m in range(HC):
                for k in range(DC):
                    p.t(lambda e, m=m, k=k: e.matmul(
                        pmm, wg_sb[:, k * H + m * 128:k * H + (m + 1) * 128],
                        xnT_g[:, k * G:(k + 1) * G], start=(k == 0), stop=(k == DC - 1)))
                p.s(lambda e: e.activation(gt[:, :], pmm, AF.Silu, scale=1.0 / 127.0))
                for k in range(DC):
                    p.t(lambda e, m=m, k=k: e.matmul(
                        pmm, wu_sb[:, k * H + m * 128:k * H + (m + 1) * 128],
                        xnT_g[:, k * G:(k + 1) * G], start=(k == 0), stop=(k == DC - 1)))
                p.s(lambda e: e.activation(ut[:, :], pmm, AF.Copy, scale=1.0 / 127.0))
                p.v(lambda e, m=m: e.tensor_tensor(hT_g[:, m * G:(m + 1) * G], gt[:, :], ut[:, :], ALU.mult))
            for cc in range(2):
                c = grp * 2 + cc
                for half in range(2):
                    for m in range(HC):
                        p.t(lambda e, m=m, cc=cc, half=half: e.matmul(
                            P1[:, :], hT_g[:, m * G + cc * 128:m * G + cc * 128 + 128],
                            wd_sb[:, m * D + half * 512:m * D + (half + 1) * 512],
                            start=(m == 0), stop=(m == HC - 1)))
                    p.v(lambda e, cc=cc, half=half: e.scalar_tensor_tensor(
                        outs[:, half * 512:(half + 1) * 512], P1[:, :], 1.0 / 127.0,
                        r_g[:, cc * D + half * 512:cc * D + (half + 1) * 512], ALU.mult, ALU.add))
                p.dma(lambda e, c=c: e.dma_start(out=xdst.ap()[c * 128:(c + 1) * 128, :], in_=outs[:, :]))
                if taps is not None:
                    p.dma(lambda e, c=c: e.dma_start(out=taps[c * 128:(c + 1) * 128, :], in_=outs[:, :]))


def build_fused(nblocks=8, debug_taps=False):
    p = Prog("fused")
    xs_in = p.inp("x0", [TOK, D], F32)
    ident_in = p.inp("ident", [128, 128], F32)
    ys = p.outp("ys", [TOK, D], F32)
    xA = p.dram("xA", [TOK, D], F32)
    xB = p.dram("xB", [TOK, D], F32)
    taps = []
    if debug_taps:
        for i in range(nblocks):
            taps.append(p.outp(f"tap{i}", [TOK, D], F32))
    else:
        taps = [None] * nblocks

    p.begin()
    g = _prog_globals(p)
    _prog_globals_init(p, g, ident_in)

    p.dma(lambda e: e.dma_start(out=xA.ap()[:, :], in_=xs_in))
    bufs = [xA, xB]
    emitters = []
    emitters.append(lambda src, dst, tp: emit_l0a_f(p, g, src.ap(), dst, taps=tp))
    emitters.append(lambda src, dst, tp: emit_ffn_f(p, g, "l0f_", src.ap(), dst, taps=tp))
    if nblocks > 2:
        emitters.append(lambda src, dst, tp: emit_l1a_f(p, g, src.ap(), dst, taps=tp))
        emitters.append(lambda src, dst, tp: emit_ffn_f(p, g, "l1f_", src.ap(), dst, taps=tp))
    if nblocks > 4:
        emitters.append(lambda src, dst, tp: emit_l2a_f(p, g, src.ap(), dst, taps=tp))
        emitters.append(lambda src, dst, tp: emit_l2f_f(p, g, src.ap(), dst, taps=tp))
    if nblocks > 6:
        emitters.append(lambda src, dst, tp: emit_l3a_f(p, g, src.ap(), dst, taps=tp))
        emitters.append(lambda src, dst, tp: emit_l3f_f(p, g, src.ap(), dst, taps=tp))

    cur = 0
    for i, em in enumerate(emitters[:nblocks]):
        src = bufs[cur]
        dst = bufs[1 - cur]
        em(src, dst, taps[i])
        cur = 1 - cur
    last = bufs[cur]
    p.dma(lambda e: e.dma_start(out=ys, in_=last.ap()[:, :]))
    return p.end()


def _fused_inmaps(inputs, x0):
    maps = [dict(x0=_f32(x0[cid * TOK:(cid + 1) * TOK]), ident=IDENT)
            for cid in range(NCORE)]
    _l0a_f_host(inputs, maps)
    _l1a_f_host(inputs, maps)
    _l2a_f_host(inputs, maps)
    _l2f_f_host(inputs, maps)
    _l3a_f_host(inputs, maps)
    _l3f_f_host(inputs, maps)
    for pre, lw in [("l0f_", "L0_ffn"), ("l1f_", "L1_ffn")]:
        wgT = np.ascontiguousarray(np.asarray(inputs[lw + "_w_gate"]).T).astype(np.int32)
        wuT = np.ascontiguousarray(np.asarray(inputs[lw + "_w_up"]).T).astype(np.int32)
        wdT = np.ascontiguousarray(np.asarray(inputs[lw + "_w_down"]).T).astype(np.int32)
        for cid in range(NCORE):
            maps[cid].update({pre + "wgT": wgT, pre + "wuT": wuT, pre + "wdT": wdT})
    return maps


def emit_l1a_f(p, g, xcur, xdst, taps=None):
    pre = "l1_"
    wqT = p.inp(pre + "wqT", [D, D], F32)
    wkT = p.inp(pre + "wkT", [D, D], F32)
    wvT = p.inp(pre + "wvT", [D, D], F32)
    woT = p.inp(pre + "woT", [D, D], F32)
    gnw_in = p.inp(pre + "gnw", [128, DC], F32)
    gnb_in = p.inp(pre + "gnb", [128, DC], F32)
    dt_in = p.inp(pre + "dtm", [128, 16 * 128], F32)
    lamq_in = p.inp(pre + "lamq", [128, DC * 128], F32)
    lamk_in = p.inp(pre + "lamk", [128, 16], F32)
    e2_in = p.inp(pre + "e2", [128, 2], F32)
    eb_in = p.inp(pre + "eb", [2, 128], F32)
    scoef_in = p.inp(pre + "scoef", [128, 4 * 512], F32)

    ccs_in = p.dram(pre + "ccs_in", [128, 512], F32)
    ccs_out = p.dram(pre + "ccs_out", [512, 512], F32)
    cct_in = p.dram(pre + "cct_in", [2, 16], F32)
    cct_out = p.dram(pre + "cct_out", [8, 16], F32)
    ybuf = p.dram(pre + "ybuf", [D, TOK], BF16)
    import os as _os
    dbg = p.outp("l1dbg", [D, TOK], F32) if _os.environ.get("DBG_L1") else None
    dbg2 = p.outp("l1dbg2", [128, 64], F32) if _os.environ.get("DBG_L1") else None
    dbg3 = p.outp("l1dbg3", [128, 2 * D], F32) if _os.environ.get("DBG_L1") else None
    dbg4 = p.outp("l1dbg4", [128, 2 * D], F32) if _os.environ.get("DBG_L1") else None

    NH = 16
    gam = _l1_gammas()
    lam_c = [float(gg ** 128) for gg in gam]

    sc = p.scope()
    with sc:
        ident, identf = g["ident"], g["identf"]
        r_c, xn_c, xnT_c, outs, sqs, scr = (g["r_c"], g["xn_c"], g["xnT_c"],
                                            g["outs"], g["sqs"], g["scr"])
        P1, P2, P3, P4, P5 = g["P1"], g["P2"], g["P3"], g["P4"], g["P5"]
        PS2 = P5[0:2, 0:DC]

        wA = p.sb(pre + "wA", [128, DC * D], BF16, sc)
        wB = p.sb(pre + "wB", [128, DC * D], BF16, sc)
        wC = p.sb(pre + "wC", [128, DC * D], BF16, sc)
        dtm = p.sb(pre + "dtm_sb", [128, 16 * 128], F32, sc)
        lamq = p.sb(pre + "lamq_sb", [128, DC * 128], BF16, sc)
        lamk = p.sb(pre + "lamk_sb", [128, 16], F32, sc)
        e2 = p.sb(pre + "e2_sb", [128, 2], F32, sc)
        eb = p.sb(pre + "eb_sb", [2, 128], F32, sc)
        gnw = p.sb(pre + "gnw_sb", [128, DC], F32, sc)
        gnb = p.sb(pre + "gnb_sb", [128, DC], F32, sc)
        scoef = p.sb(pre + "scoef_sb", [128, 4 * 512], F32, sc)
        XN = p.sb(pre + "XN", [128, NCH * D], BF16, sc)
        KTa = p.sb(pre + "KTa", [128, NCH * D], BF16, sc)
        Va = p.sb(pre + "Va", [128, NCH * D], BF16, sc)
        QTc = p.sb(pre + "QTc", [128, DC * 128], BF16, sc)
        QLc = p.sb(pre + "QLc", [128, DC * 128], BF16, sc)
        Ktm = p.sb(pre + "Ktm", [128, D], BF16, sc)
        Klam = p.sb(pre + "Klam", [128, 64], BF16, sc)
        St = p.sb(pre + "St", [128, 128], BF16, sc)
        state = p.sb(pre + "state", [128, 512], F32, sc)
        stateb = p.sb(pre + "stateb", [128, 512], BF16, sc)
        gtmp = p.sb(pre + "gtmp", [128, 512], F32, sc)
        ysl = p.sb(pre + "ysl", [128, 128], BF16, sc)
        ysq = p.sb(pre + "ysq", [128, 128], BF16, sc)
        acc1 = p.sb(pre + "acc1", [128, 1], F32, sc)
        acc2 = p.sb(pre + "acc2", [128, 1], F32, sc)
        ysum = p.sb(pre + "ysum", [128, DC], F32, sc)
        yssq = p.sb(pre + "yssq", [128, DC], F32, sc)
        sst = p.sb(pre + "sst", [2, 2 * DC], F32, sc)
        stat8 = p.sb(pre + "stat8", [2, 4 * 16], F32, sc)
        smean = p.sb(pre + "smean", [2, DC], F32, sc)
        srstd = p.sb(pre + "srstd", [2, DC], F32, sc)
        ssq2 = p.sb(pre + "ssq2", [2, DC], F32, sc)
        sscr = p.sb(pre + "sscr", [2, DC], F32, sc)
        scomb = p.sb(pre + "scomb", [2, 2 * DC], F32, sc)
        statB = p.sb(pre + "statB", [128, 2 * DC], F32, sc)
        scale_col = p.sb(pre + "scale_col", [128, DC], F32, sc)
        bias_col = p.sb(pre + "bias_col", [128, DC], F32, sc)
        geps = p.sb(pre + "geps", [128, 1], F32, sc)
        ygn = p.sb(pre + "ygn", [128, DC * 128], BF16, sc)

        for dst, src in [(dtm, dt_in), (lamk, lamk_in), (e2, e2_in), (eb, eb_in),
                         (gnw, gnw_in), (gnb, gnb_in), (scoef, scoef_in)]:
            p.dma(lambda e, dst=dst, src=src: e.dma_start(out=dst[:, :], in_=src))
        p.gdma(lambda e: e.dma_start(out=lamq[:, :], in_=lamq_in))
        for dst, src in [(wA, wqT), (wB, wkT), (wC, wvT)]:
            p.gdma(lambda e, dst=dst, src=src: e.dma_start(
                out=dst[:, :].rearrange("p (k n) -> p k n", k=DC),
                in_=src.rearrange("(k p) n -> p k n", p=128)))
        p.v(lambda e: e.memset(state[:, :], 0.0))
        p.v(lambda e: e.memset(ysum[:, :], 0.0))
        p.v(lambda e: e.memset(yssq[:, :], 0.0))
        p.v(lambda e: e.memset(geps[:, :], GN_EPS))

        for c in range(NCH):
            p.dma(lambda e, c=c: e.dma_start(out=r_c[:, :], in_=xcur[c * 128:(c + 1) * 128, :]))
            emit_ln(p, r_c[:, :], xn_c[:, :], scr)
            emit_transpose_cols(p, xn_c[:, :], DC, P4, ident,
                                XN[:, :].rearrange("p (c k t) -> p (c k) t", c=NCH, k=DC)[:, c * DC:(c + 1) * DC, :]
                                if False else
                                XN[:, c * D:(c + 1) * D].rearrange("p (k t) -> p k t", k=DC))
            for m in range(DC):
                for k in range(DC):
                    p.t(lambda e, m=m, k=k, c=c: e.matmul(
                        P1[:, 0:128], wB[:, k * D + m * 128:k * D + (m + 1) * 128],
                        XN[:, c * D + k * 128:c * D + (k + 1) * 128], start=(k == 0), stop=(k == DC - 1)))
                p.v(lambda e, m=m, c=c: e.tensor_copy(KTa[:, c * D + m * 128:c * D + (m + 1) * 128], P1[:, 0:128]))
            for half in range(2):
                for k in range(DC):
                    p.t(lambda e, half=half, k=k, c=c: e.matmul(
                        P1[:, :], XN[:, c * D + k * 128:c * D + (k + 1) * 128],
                        wC[:, k * D + half * 512:k * D + (half + 1) * 512],
                        start=(k == 0), stop=(k == DC - 1)))
                p.v(lambda e, half=half, c=c: e.tensor_copy(
                    Va[:, c * D + half * 512:c * D + (half + 1) * 512], P1[:, :]))
            for j in range(DC):
                p.t(lambda e, j=j, c=c: e.transpose(P4[:, j * 128:(j + 1) * 128],
                                                    KTa[:, c * D + j * 128:c * D + (j + 1) * 128],
                                                    ident[:, :]))
            p.v(lambda e: e.tensor_copy(Ktm[:, :], P4[:, :]))
            for h in range(NH):
                sl = h // 2
                po = (h % 2) * 64
                p.v(lambda e, h=h: e.tensor_scalar(Klam[:, :], Ktm[:, h * 64:(h + 1) * 64],
                                                   lamk[:, h:h + 1], None, ALU.mult))
                p.t(lambda e, h=h, c=c: e.matmul(P5[0:64, 0:64], Klam[:, :],
                                                 Va[:, c * D + h * 64:c * D + (h + 1) * 64],
                                                 start=True, stop=True))
                p.v(lambda e, h=h, sl=sl, po=po, lc=lam_c[h]: e.tensor_scalar(
                    state[po:po + 64, sl * 64:(sl + 1) * 64],
                    state[po:po + 64, sl * 64:(sl + 1) * 64], lc, None, ALU.mult))
                p.v(lambda e, sl=sl, po=po: e.tensor_tensor(
                    state[po:po + 64, sl * 64:(sl + 1) * 64],
                    state[po:po + 64, sl * 64:(sl + 1) * 64], P5[0:64, 0:64], ALU.add))

        p.dma(lambda e: e.dma_start(out=ccs_in.ap()[:, :], in_=state[:, :]))
        p.collective("AllGather", ccs_in, ccs_out, GROUPS)
        p.v(lambda e: e.memset(state[:, :], 0.0))
        for r in range(4):
            p.dma(lambda e, r=r: e.dma_start(out=gtmp[:, :], in_=ccs_out.ap()[r * 128:(r + 1) * 128, :]))
            p.v(lambda e, r=r: e.tensor_tensor(gtmp[:, :], gtmp[:, :],
                                               scoef[:, r * 512:(r + 1) * 512], ALU.mult))
            p.v(lambda e: e.tensor_tensor(state[:, :], state[:, :], gtmp[:, :], ALU.add))
        p.v(lambda e: e.tensor_copy(stateb[:, :], state[:, :]))

        for c in range(NCH):
            for m in range(DC):
                for k in range(DC):
                    p.t(lambda e, m=m, k=k, c=c: e.matmul(
                        P1[:, 0:128], wA[:, k * D + m * 128:k * D + (m + 1) * 128],
                        XN[:, c * D + k * 128:c * D + (k + 1) * 128], start=(k == 0), stop=(k == DC - 1)))
                p.v(lambda e, m=m: e.tensor_copy(QTc[:, m * 128:(m + 1) * 128], P1[:, 0:128]))
                p.v(lambda e, m=m: e.tensor_tensor(QLc[:, m * 128:(m + 1) * 128],
                                                   QTc[:, m * 128:(m + 1) * 128],
                                                   lamq[:, m * 128:(m + 1) * 128], ALU.mult))
            for j in range(DC):
                p.t(lambda e, j=j, c=c: e.transpose(P4[:, j * 128:(j + 1) * 128],
                                                    KTa[:, c * D + j * 128:c * D + (j + 1) * 128],
                                                    ident[:, :]))
            p.v(lambda e: e.tensor_copy(Ktm[:, :], P4[:, :]))
            for h in range(NH):
                sl = h // 2
                po = (h % 2) * 64
                p.t(lambda e, sl=sl, po=po, c=c: e.matmul(
                    P5[:, :], KTa[po:po + 64, c * D + sl * 128:c * D + (sl + 1) * 128],
                    QTc[po:po + 64, sl * 128:(sl + 1) * 128], start=True, stop=True))
                p.v(lambda e, h=h: e.tensor_tensor(St[:, :], P5[:, :],
                                                   dtm[:, h * 128:(h + 1) * 128], ALU.mult))
                p.t(lambda e, h=h, sl=sl, po=po, c=c: e.matmul(
                    P3[po:po + 64, sl * 128:(sl + 1) * 128],
                    Va[:, c * D + h * 64:c * D + (h + 1) * 64], St[:, :], start=True, stop=False))
                p.t(lambda e, h=h, sl=sl, po=po: e.matmul(
                    P3[po:po + 64, sl * 128:(sl + 1) * 128],
                    stateb[po:po + 64, sl * 64:(sl + 1) * 64],
                    QLc[po:po + 64, sl * 128:(sl + 1) * 128], start=False, stop=True))
                p.v(lambda e, h=h: e.tensor_scalar(Klam[:, :], Ktm[:, h * 64:(h + 1) * 64],
                                                   lamk[:, h:h + 1], None, ALU.mult))
                p.t(lambda e, h=h, c=c: e.matmul(P5[0:64, 0:64], Klam[:, :],
                                                 Va[:, c * D + h * 64:c * D + (h + 1) * 64],
                                                 start=True, stop=True))
                p.v(lambda e, h=h, sl=sl, po=po, lc=lam_c[h]: e.tensor_scalar(
                    state[po:po + 64, sl * 64:(sl + 1) * 64],
                    state[po:po + 64, sl * 64:(sl + 1) * 64], lc, None, ALU.mult))
                p.v(lambda e, sl=sl, po=po: e.tensor_tensor(
                    state[po:po + 64, sl * 64:(sl + 1) * 64],
                    state[po:po + 64, sl * 64:(sl + 1) * 64], P5[0:64, 0:64], ALU.add))
                p.v(lambda e, sl=sl, po=po: e.tensor_copy(
                    stateb[po:po + 64, sl * 64:(sl + 1) * 64],
                    state[po:po + 64, sl * 64:(sl + 1) * 64]))
            for s in range(DC):
                p.s(lambda e, s=s: e.activation(ysl[:, :], P3[:, s * 128:(s + 1) * 128],
                                                AF.Copy, accum_out=acc1[:, :]))
                p.s(lambda e, s=s: e.activation(ysq[:, :], P3[:, s * 128:(s + 1) * 128],
                                                AF.Square, accum_out=acc2[:, :]))
                p.v(lambda e, s=s: e.tensor_tensor(ysum[:, s:s + 1], ysum[:, s:s + 1],
                                                   acc1[:, :], ALU.add))
                p.v(lambda e, s=s: e.tensor_tensor(yssq[:, s:s + 1], yssq[:, s:s + 1],
                                                   acc2[:, :], ALU.add))
                p.dma(lambda e, s=s, c=c: e.dma_start(
                    out=ybuf.ap()[s * 128:(s + 1) * 128, c * 128:(c + 1) * 128], in_=ysl[:, :]))

        if dbg is not None:
            p.gdma(lambda e: e.dma_start(out=dbg, in_=ybuf.ap()[:, :]))
        p.t(lambda e: e.matmul(PS2, e2[:, 0:2], ysum[:, :], start=True, stop=True))
        p.v(lambda e: e.tensor_copy(sst[:, 0:DC], PS2))
        p.t(lambda e: e.matmul(PS2, e2[:, 0:2], yssq[:, :], start=True, stop=True))
        p.v(lambda e: e.tensor_copy(sst[:, DC:2 * DC], PS2))
        p.dma(lambda e: e.dma_start(out=cct_in.ap()[:, :], in_=sst[:, :]))
        p.collective("AllGather", cct_in, cct_out, GROUPS)
        p.gdma(lambda e: e.dma_start(
            out=stat8[:, :].rearrange("p (r f) -> p r f", r=4),
            in_=cct_out.ap().rearrange("(r s) f -> s r f", s=2)))
        p.v(lambda e: e.tensor_tensor(sst[:, :], stat8[:, 0:16], stat8[:, 16:32], ALU.add))
        p.v(lambda e: e.tensor_tensor(sst[:, :], sst[:, :], stat8[:, 32:48], ALU.add))
        p.v(lambda e: e.tensor_tensor(sst[:, :], sst[:, :], stat8[:, 48:64], ALU.add))
        NTOT = float(SEQ * 64)
        p.v(lambda e: e.tensor_scalar(smean[:, :], sst[:, 0:DC], 1.0 / NTOT, None, ALU.mult))
        p.v(lambda e: e.tensor_scalar(ssq2[:, :], sst[:, DC:2 * DC], 1.0 / NTOT, None, ALU.mult))
        p.v(lambda e: e.tensor_tensor(sscr[:, :], smean[:, :], smean[:, :], ALU.mult))
        p.v(lambda e: e.tensor_tensor(ssq2[:, :], sscr[:, :], ssq2[:, :], ALU.subtract))
        p.s(lambda e: e.activation(srstd[:, :], ssq2[:, :], AF.Sqrt, bias=geps[0:2, :], scale=-1.0))
        p.v(lambda e: e.reciprocal(srstd[:, :], srstd[:, :]))
        p.v(lambda e: e.tensor_copy(scomb[:, 0:DC], srstd[:, :]))
        p.v(lambda e: e.scalar_tensor_tensor(scomb[:, DC:2 * DC], smean[:, :], -1.0,
                                             srstd[:, :], ALU.mult, ALU.mult))
        p.t(lambda e: e.matmul(P1[:, 0:2 * DC], eb[:, :], scomb[:, :], start=True, stop=True))
        p.v(lambda e: e.tensor_copy(statB[:, :], P1[:, 0:2 * DC]))
        p.v(lambda e: e.tensor_tensor(scale_col[:, :], statB[:, 0:DC], gnw[:, :], ALU.mult))
        p.v(lambda e: e.tensor_tensor(bias_col[:, :], statB[:, DC:2 * DC], gnw[:, :], ALU.mult))
        p.v(lambda e: e.tensor_tensor(bias_col[:, :], bias_col[:, :], gnb[:, :], ALU.add))
        p.gdma(lambda e: e.dma_start(
            out=wA[:, :].rearrange("p (k n) -> p k n", k=DC),
            in_=woT.rearrange("(k p) n -> p k n", p=128)))

        if dbg2 is not None:
            p.v(lambda e: e.tensor_copy(gtmp[:, 0:DC], scale_col[:, :]))
            p.v(lambda e: e.tensor_copy(gtmp[:, DC:2 * DC], bias_col[:, :]))
            p.v(lambda e: e.tensor_copy(gtmp[:, 2 * DC:3 * DC], ysum[:, :]))
            p.v(lambda e: e.tensor_copy(gtmp[:, 3 * DC:4 * DC], yssq[:, :]))
            p.dma(lambda e: e.dma_start(out=dbg2, in_=gtmp[:, 0:64]))
        for c in range(NCH):
            p.gdma(lambda e, c=c: e.dma_start(
                out=ygn[:, :].rearrange("p (s t) -> p s t", s=DC),
                in_=ybuf.ap().rearrange("(s p) t -> p s t", p=128)[:, :, c * 128:(c + 1) * 128]))
            for s in range(DC):
                p.v(lambda e, s=s: e.tensor_scalar(
                    ygn[:, s * 128:(s + 1) * 128], ygn[:, s * 128:(s + 1) * 128],
                    scale_col[:, s:s + 1], bias_col[:, s:s + 1], ALU.mult, ALU.add))
            if dbg3 is not None and c == 0:
                p.v(lambda e: e.tensor_copy(gtmp[:, :], ygn[:, 0:512]))
                p.dma(lambda e: e.dma_start(out=dbg3[0:128, 0:512], in_=gtmp[:, 0:512]))
            p.dma(lambda e, c=c: e.dma_start(out=r_c[:, :], in_=xcur[c * 128:(c + 1) * 128, :]))
            for half in range(2):
                for k in range(DC):
                    p.t(lambda e, half=half, k=k: e.matmul(
                        P1[:, :], ygn[:, k * 128:(k + 1) * 128],
                        wA[:, k * D + half * 512:k * D + (half + 1) * 512],
                        start=(k == 0), stop=(k == DC - 1)))
                if dbg3 is not None and c == 0:
                    p.v(lambda e, half=half: e.tensor_copy(gtmp[:, 0:512], P1[:, :]))
                    p.dma(lambda e, half=half: e.dma_start(
                        out=dbg3[0:128, D + half * 512:D + (half + 1) * 512], in_=gtmp[:, 0:512]))
                p.v(lambda e, half=half: e.tensor_tensor(
                    outs[:, half * 512:(half + 1) * 512], P1[:, :],
                    r_c[:, half * 512:(half + 1) * 512], ALU.add))
            if dbg4 is not None and c == 0:
                p.dma(lambda e: e.dma_start(out=dbg4[0:128, 0:D], in_=r_c[:, :]))
                p.dma(lambda e: e.dma_start(out=dbg4[0:128, D:2 * D], in_=outs[:, :]))
            p.dma(lambda e, c=c: e.dma_start(out=xdst.ap()[c * 128:(c + 1) * 128, :], in_=outs[:, :]))
            if taps is not None:
                p.dma(lambda e, c=c: e.dma_start(out=taps[c * 128:(c + 1) * 128, :], in_=outs[:, :]))


def _l1a_f_host(inputs, maps):
    base = _l1a_inmaps(np.zeros((16384, D), np.float32), inputs)[0]
    gam = _l1_gammas()
    for cid in range(NCORE):
        rk = cid % 4
        scoef = np.zeros((128, 4 * 512), dtype=np.float32)
        for r in range(4):
            if r >= rk:
                continue
            dpow = 2048.0 * (rk - 1 - r) + 0.0
            tile = np.zeros((128, 512), dtype=np.float32)
            for h in range(16):
                sl = h // 2
                po = (h % 2) * 64
                tile[po:po + 64, sl * 64:(sl + 1) * 64] = float(gam[h] ** (2048.0 * (rk - r)))
            tile2 = np.zeros((128, 512), dtype=np.float32)
            for h in range(16):
                sl = h // 2
                po = (h % 2) * 64
                tile2[po:po + 64, sl * 64:(sl + 1) * 64] = float(gam[h] ** (2048.0 * (rk - 1 - r)))
            scoef[:, r * 512:(r + 1) * 512] = tile2
        maps[cid].update({
            "l1_wqT": base["wqT"], "l1_wkT": base["wkT"], "l1_wvT": base["wvT"],
            "l1_woT": base["woT"], "l1_gnw": base["gnw"], "l1_gnb": base["gnb"],
            "l1_dtm": base["dtm"], "l1_lamq": base["lamq"], "l1_lamk": base["lamk"],
            "l1_e2": base["e2"], "l1_eb": base["eb"], "l1_scoef": scoef,
        })


def emit_l2a_f(p, g, xcur, xdst, taps=None):
    pre = "l2_"
    wq_in = p.inp(pre + "wq", [D, D], F32)
    wk_in = p.inp(pre + "wk", [D, D], F32)
    wv_in = p.inp(pre + "wv", [D, D], F32)
    wo_in = p.inp(pre + "wo", [D, D], F32)
    wg1_in = p.inp(pre + "wg1", [D, 64], F32)
    wg2_in = p.inp(pre + "wg2", [64, D], F32)
    gamma_in = p.inp(pre + "gamma", [1, D], F32)
    beta_in = p.inp(pre + "beta", [1, D], F32)
    dt_in = p.inp(pre + "dtm", [128, 16 * 128], F32)
    lamq_in = p.inp(pre + "lamq", [128, DC * 128], F32)
    lamk_in = p.inp(pre + "lamk", [128, 16], F32)
    scoef_in = p.inp(pre + "scoef", [128, 4 * 512], F32)

    ccs_in = p.dram(pre + "ccs_in", [128, 512], F32)
    ccs_out = p.dram(pre + "ccs_out", [512, 512], F32)

    NH = 16
    s_h = _l2_slopes()
    lam_c = [float(np.exp(-s * 128.0)) for s in s_h]

    sc = p.scope()
    with sc:
        ident, identf = g["ident"], g["identf"]
        r_c, xn_c, xnT_c, outs, scr = g["r_c"], g["xn_c"], g["xnT_c"], g["outs"], g["scr"]
        P1, P2, P3, P4, P5 = g["P1"], g["P2"], g["P3"], g["P4"], g["P5"]
        P4f = g["P4f"]
        scr2 = ln_scratch(p, "_o2") if "scr2_l2" not in g else g["scr2_l2"]
        g["scr2_l2"] = scr2

        wA = p.sb(pre + "wA", [128, DC * D], BF16, sc)
        wB = p.sb(pre + "wB", [128, DC * D], BF16, sc)
        wC = p.sb(pre + "wC", [128, DC * D], BF16, sc)
        wg1 = p.sb(pre + "wg1_sb", [128, DC * 64], BF16, sc)
        wg2 = p.sb(pre + "wg2_sb", [64, D], BF16, sc)
        dtm = p.sb(pre + "dtm_sb", [128, 16 * 128], F32, sc)
        lamq = p.sb(pre + "lamq_sb", [128, DC * 128], BF16, sc)
        lamk = p.sb(pre + "lamk_sb", [128, 16], F32, sc)
        scoef = p.sb(pre + "scoef_sb", [128, 4 * 512], F32, sc)
        gamB = p.sb(pre + "gamB", [128, D], F32, sc)
        betB = p.sb(pre + "betB", [128, D], F32, sc)
        grow = p.sb(pre + "grow", [1, D], BF16, sc)
        KTa = p.sb(pre + "KTa", [128, NCH * D], BF16, sc)
        Va = p.sb(pre + "Va", [128, NCH * D], BF16, sc)
        QTc = p.sb(pre + "QTc", [128, DC * 128], BF16, sc)
        QLc = p.sb(pre + "QLc", [128, DC * 128], BF16, sc)
        Ktm = p.sb(pre + "Ktm", [128, D], BF16, sc)
        Klam = p.sb(pre + "Klam", [128, 64], BF16, sc)
        St = p.sb(pre + "St", [128, 128], BF16, sc)
        state = p.sb(pre + "state", [128, 512], F32, sc)
        stateb = p.sb(pre + "stateb", [128, 512], BF16, sc)
        gtmp = p.sb(pre + "gtmp", [128, 512], F32, sc)
        oT = p.sb(pre + "oT", [128, D], F32, sc)
        otm = p.sb(pre + "otm", [128, D], F32, sc)
        g1T = p.sb(pre + "g1T", [64, 128], BF16, sc)
        gate = p.sb(pre + "gate", [128, D], F32, sc)
        og = p.sb(pre + "og", [128, D], BF16, sc)
        ogT = p.sb(pre + "ogT", [128, DC * 128], BF16, sc)
        ones_r = g["ones_r"]

        p.dma(lambda e: e.dma_start(out=dtm[:, :], in_=dt_in))
        p.gdma(lambda e: e.dma_start(out=lamq[:, :], in_=lamq_in))
        p.dma(lambda e: e.dma_start(out=lamk[:, :], in_=lamk_in))
        p.dma(lambda e: e.dma_start(out=scoef[:, :], in_=scoef_in))
        for dst, src in [(wA, wq_in), (wB, wk_in), (wC, wv_in)]:
            p.gdma(lambda e, dst=dst, src=src: e.dma_start(
                out=dst[:, :].rearrange("p (k n) -> p k n", k=DC),
                in_=src.rearrange("(k p) n -> p k n", p=128)))
        p.gdma(lambda e: e.dma_start(
            out=wg1[:, :].rearrange("p (k n) -> p k n", k=DC),
            in_=wg1_in.rearrange("(k p) n -> p k n", p=128)))
        p.gdma(lambda e: e.dma_start(out=wg2[:, :], in_=wg2_in))
        p.v(lambda e: e.memset(state[:, :], 0.0))
        p.gdma(lambda e: e.dma_start(out=grow[:, :], in_=gamma_in))
        for half in range(2):
            p.t(lambda e, half=half: e.matmul(P1[:, :], ones_r[:, :],
                                              grow[0:1, half * 512:(half + 1) * 512],
                                              start=True, stop=True))
            p.v(lambda e, half=half: e.tensor_copy(gamB[:, half * 512:(half + 1) * 512], P1[:, :]))
        p.gdma(lambda e: e.dma_start(out=grow[:, :], in_=beta_in))
        for half in range(2):
            p.t(lambda e, half=half: e.matmul(P1[:, :], ones_r[:, :],
                                              grow[0:1, half * 512:(half + 1) * 512],
                                              start=True, stop=True))
            p.v(lambda e, half=half: e.tensor_copy(betB[:, half * 512:(half + 1) * 512], P1[:, :]))

        for c in range(NCH):
            p.dma(lambda e, c=c: e.dma_start(out=r_c[:, :], in_=xcur[c * 128:(c + 1) * 128, :]))
            emit_ln(p, r_c[:, :], xn_c[:, :], scr)
            emit_transpose_cols(p, xn_c[:, :], DC, P4, ident,
                                xnT_c[:, :].rearrange("p (k t) -> p k t", k=DC))
            for m in range(DC):
                for k in range(DC):
                    p.t(lambda e, m=m, k=k: e.matmul(
                        P1[:, 0:128], wB[:, k * D + m * 128:k * D + (m + 1) * 128],
                        xnT_c[:, k * 128:(k + 1) * 128], start=(k == 0), stop=(k == DC - 1)))
                p.s(lambda e, m=m, c=c: e.activation(
                    KTa[:, c * D + m * 128:c * D + (m + 1) * 128], P1[:, 0:128], AF.Silu))
            for half in range(2):
                for k in range(DC):
                    p.t(lambda e, half=half, k=k: e.matmul(
                        P1[:, :], xnT_c[:, k * 128:(k + 1) * 128],
                        wC[:, k * D + half * 512:k * D + (half + 1) * 512],
                        start=(k == 0), stop=(k == DC - 1)))
                p.s(lambda e, half=half, c=c: e.activation(
                    Va[:, c * D + half * 512:c * D + (half + 1) * 512], P1[:, :], AF.Silu))
            for j in range(DC):
                p.t(lambda e, j=j, c=c: e.transpose(P4[:, j * 128:(j + 1) * 128],
                                                    KTa[:, c * D + j * 128:c * D + (j + 1) * 128],
                                                    ident[:, :]))
            p.v(lambda e: e.tensor_copy(Ktm[:, :], P4[:, :]))
            for h in range(NH):
                sl = h // 2
                po = (h % 2) * 64
                p.v(lambda e, h=h: e.tensor_scalar(Klam[:, :], Ktm[:, h * 64:(h + 1) * 64],
                                                   lamk[:, h:h + 1], None, ALU.mult))
                p.t(lambda e, h=h, c=c: e.matmul(P5[0:64, 0:64], Klam[:, :],
                                                 Va[:, c * D + h * 64:c * D + (h + 1) * 64],
                                                 start=True, stop=True))
                p.v(lambda e, h=h, sl=sl, po=po, lc=lam_c[h]: e.tensor_scalar(
                    state[po:po + 64, sl * 64:(sl + 1) * 64],
                    state[po:po + 64, sl * 64:(sl + 1) * 64], lc, None, ALU.mult))
                p.v(lambda e, sl=sl, po=po: e.tensor_tensor(
                    state[po:po + 64, sl * 64:(sl + 1) * 64],
                    state[po:po + 64, sl * 64:(sl + 1) * 64], P5[0:64, 0:64], ALU.add))

        p.dma(lambda e: e.dma_start(out=ccs_in.ap()[:, :], in_=state[:, :]))
        p.collective("AllGather", ccs_in, ccs_out, GROUPS)
        p.v(lambda e: e.memset(state[:, :], 0.0))
        for r in range(4):
            p.dma(lambda e, r=r: e.dma_start(out=gtmp[:, :], in_=ccs_out.ap()[r * 128:(r + 1) * 128, :]))
            p.v(lambda e, r=r: e.tensor_tensor(gtmp[:, :], gtmp[:, :],
                                               scoef[:, r * 512:(r + 1) * 512], ALU.mult))
            p.v(lambda e: e.tensor_tensor(state[:, :], state[:, :], gtmp[:, :], ALU.add))
        p.v(lambda e: e.tensor_copy(stateb[:, :], state[:, :]))
        p.gdma(lambda e: e.dma_start(
            out=wB[:, :].rearrange("p (k n) -> p k n", k=DC),
            in_=wo_in.rearrange("(k p) n -> p k n", p=128)))

        for c in range(NCH):
            p.dma(lambda e, c=c: e.dma_start(out=r_c[:, :], in_=xcur[c * 128:(c + 1) * 128, :]))
            emit_ln(p, r_c[:, :], xn_c[:, :], scr)
            emit_transpose_cols(p, xn_c[:, :], DC, P4, ident,
                                xnT_c[:, :].rearrange("p (k t) -> p k t", k=DC))
            for m in range(DC):
                for k in range(DC):
                    p.t(lambda e, m=m, k=k: e.matmul(
                        P1[:, 0:128], wA[:, k * D + m * 128:k * D + (m + 1) * 128],
                        xnT_c[:, k * 128:(k + 1) * 128], start=(k == 0), stop=(k == DC - 1)))
                p.s(lambda e, m=m: e.activation(QTc[:, m * 128:(m + 1) * 128], P1[:, 0:128], AF.Silu))
                p.v(lambda e, m=m: e.tensor_tensor(QLc[:, m * 128:(m + 1) * 128],
                                                   QTc[:, m * 128:(m + 1) * 128],
                                                   lamq[:, m * 128:(m + 1) * 128], ALU.mult))
            for j in range(DC):
                p.t(lambda e, j=j, c=c: e.transpose(P4[:, j * 128:(j + 1) * 128],
                                                    KTa[:, c * D + j * 128:c * D + (j + 1) * 128],
                                                    ident[:, :]))
            p.v(lambda e: e.tensor_copy(Ktm[:, :], P4[:, :]))
            for h in range(NH):
                sl = h // 2
                po = (h % 2) * 64
                p.t(lambda e, sl=sl, po=po, c=c: e.matmul(
                    P5[:, :], KTa[po:po + 64, c * D + sl * 128:c * D + (sl + 1) * 128],
                    QTc[po:po + 64, sl * 128:(sl + 1) * 128], start=True, stop=True))
                p.v(lambda e, h=h: e.tensor_tensor(St[:, :], P5[:, :],
                                                   dtm[:, h * 128:(h + 1) * 128], ALU.mult))
                p.t(lambda e, h=h, sl=sl, po=po, c=c: e.matmul(
                    P3[po:po + 64, sl * 128:(sl + 1) * 128],
                    Va[:, c * D + h * 64:c * D + (h + 1) * 64], St[:, :], start=True, stop=False))
                p.t(lambda e, h=h, sl=sl, po=po: e.matmul(
                    P3[po:po + 64, sl * 128:(sl + 1) * 128],
                    stateb[po:po + 64, sl * 64:(sl + 1) * 64],
                    QLc[po:po + 64, sl * 128:(sl + 1) * 128], start=False, stop=True))
                p.v(lambda e, h=h: e.tensor_scalar(Klam[:, :], Ktm[:, h * 64:(h + 1) * 64],
                                                   lamk[:, h:h + 1], None, ALU.mult))
                p.t(lambda e, h=h, c=c: e.matmul(P5[0:64, 0:64], Klam[:, :],
                                                 Va[:, c * D + h * 64:c * D + (h + 1) * 64],
                                                 start=True, stop=True))
                p.v(lambda e, h=h, sl=sl, po=po, lc=lam_c[h]: e.tensor_scalar(
                    state[po:po + 64, sl * 64:(sl + 1) * 64],
                    state[po:po + 64, sl * 64:(sl + 1) * 64], lc, None, ALU.mult))
                p.v(lambda e, sl=sl, po=po: e.tensor_tensor(
                    state[po:po + 64, sl * 64:(sl + 1) * 64],
                    state[po:po + 64, sl * 64:(sl + 1) * 64], P5[0:64, 0:64], ALU.add))
                p.v(lambda e, sl=sl, po=po: e.tensor_copy(
                    stateb[po:po + 64, sl * 64:(sl + 1) * 64],
                    state[po:po + 64, sl * 64:(sl + 1) * 64]))
            p.v(lambda e: e.tensor_copy(oT[:, :], P3[:, :]))
            for j in range(DC):
                jj = j % 4
                p.t(lambda e, j=j, jj=jj: e.transpose(
                    P4f[:, jj * 128:(jj + 1) * 128], oT[:, j * 128:(j + 1) * 128], identf[:, :]))
                p.v(lambda e, j=j, jj=jj: e.tensor_copy(
                    otm[:, j * 128:(j + 1) * 128], P4f[:, jj * 128:(jj + 1) * 128]))
            emit_ln(p, otm[:, :], outs[:, :], scr2)
            for k in range(DC):
                p.t(lambda e, k=k: e.matmul(P5[0:64, :], wg1[:, k * 64:(k + 1) * 64],
                                            xnT_c[:, k * 128:(k + 1) * 128],
                                            start=(k == 0), stop=(k == DC - 1)))
            p.v(lambda e: e.tensor_copy(g1T[:, :], P5[0:64, :]))
            for half in range(2):
                p.t(lambda e, half=half: e.matmul(
                    P1[:, :], g1T[:, :], wg2[:, half * 512:(half + 1) * 512],
                    start=True, stop=True))
                p.s(lambda e, half=half: e.activation(gate[:, half * 512:(half + 1) * 512],
                                                      P1[:, :], AF.Sigmoid))
            p.v(lambda e: e.tensor_tensor(otm[:, :], outs[:, :], gamB[:, :], ALU.mult))
            p.v(lambda e: e.tensor_tensor(otm[:, :], otm[:, :], betB[:, :], ALU.add))
            p.v(lambda e: e.tensor_tensor(og[:, :], otm[:, :], gate[:, :], ALU.mult))
            for j in range(DC):
                p.t(lambda e, j=j: e.transpose(P4[:, j * 128:(j + 1) * 128],
                                               og[:, j * 128:(j + 1) * 128], ident[:, :]))
            p.v(lambda e: e.tensor_copy(ogT[:, :], P4[:, :]))
            p.dma(lambda e, c=c: e.dma_start(out=r_c[:, :], in_=xcur[c * 128:(c + 1) * 128, :]))
            for half in range(2):
                for k in range(DC):
                    p.t(lambda e, half=half, k=k: e.matmul(
                        P1[:, :], ogT[:, k * 128:(k + 1) * 128],
                        wB[:, k * D + half * 512:k * D + (half + 1) * 512],
                        start=(k == 0), stop=(k == DC - 1)))
                p.v(lambda e, half=half: e.tensor_tensor(
                    outs[:, half * 512:(half + 1) * 512], P1[:, :],
                    r_c[:, half * 512:(half + 1) * 512], ALU.add))
            p.dma(lambda e, c=c: e.dma_start(out=xdst.ap()[c * 128:(c + 1) * 128, :], in_=outs[:, :]))
            if taps is not None:
                p.dma(lambda e, c=c: e.dma_start(out=taps[c * 128:(c + 1) * 128, :], in_=outs[:, :]))


def _l2a_f_host(inputs, maps):
    base = _l2a_inmaps(np.zeros((16384, D), np.float32), inputs)[0]
    s_h = _l2_slopes()
    for cid in range(NCORE):
        rk = cid % 4
        scoef = np.zeros((128, 4 * 512), dtype=np.float32)
        for r in range(rk):
            tile = np.zeros((128, 512), dtype=np.float32)
            for h in range(16):
                sl = h // 2
                po = (h % 2) * 64
                tile[po:po + 64, sl * 64:(sl + 1) * 64] = float(np.exp(-s_h[h] * 2048.0 * (rk - 1 - r)))
            scoef[:, r * 512:(r + 1) * 512] = tile
        maps[cid].update({
            "l2_wq": base["wq"], "l2_wk": base["wk"], "l2_wv": base["wv"],
            "l2_wo": base["wo"], "l2_wg1": base["wg1"], "l2_wg2": base["wg2"],
            "l2_gamma": base["gamma"], "l2_beta": base["beta"],
            "l2_dtm": base["dtm"], "l2_lamq": base["lamq"], "l2_lamk": base["lamk"],
            "l2_scoef": scoef,
        })


def emit_l2f_f(p, g, xcur, xdst, taps=None):
    pre = "l2f_"
    waugT = p.inp(pre + "waugT", [DH, D + 1], F32)
    wcon = p.inp(pre + "wcon", [DH, D], F32)
    upclo = p.inp(pre + "upclo", [128, 7 * 4 * 2048], F32)
    upchi = p.inp(pre + "upchi", [128, 5 * 4 * 2048], F32)
    dnclo = p.inp(pre + "dnclo", [128, 7 * 4 * 2048], F32)
    dnhcols = p.inp(pre + "dnhcols", [128, 5 * 16 * 4], F32)
    upbias = p.inp(pre + "upbias", [1, DH], F32)
    bcon = p.inp(pre + "bcon", [1, D], F32)
    dbias = p.inp(pre + "dbias", [1, D], F32)
    eyeb_in = p.inp(pre + "eyeb", [128, DH], F32)

    sc = p.scope()
    with sc:
        ident = g["ident"]
        r_c, xn_c, xnT_c, outs, scr = g["r_c"], g["xn_c"], g["xnT_c"], g["outs"], g["scr"]
        P1, P4, P5 = g["P1"], g["P4"], g["P5"]
        P2 = g["P2"][:, 0:128]
        PB = g["P5"][0:1, 0:128]
        PC = g["P3"][:, 0:32]
        ones_r = g["ones_r"]
        onef = ones_r[0:1, 0:1]

        WU = p.sb(pre + "WU", [128, DC * DH], BF16, sc)
        WC = p.sb(pre + "WC", [128, DHC * D], BF16, sc)
        Ea = p.sb(pre + "Ea", [128, DH], BF16, sc)
        Eb = p.sb(pre + "Eb", [128, DH], BF16, sc)
        coefB = p.sb(pre + "coefB", [128, 4 * 2048], BF16, sc)
        scrA = p.sb(pre + "scrA", [128, 2048], BF16, sc)
        scrB = p.sb(pre + "scrB", [128, 2048], BF16, sc)
        dnh = p.sb(pre + "dnh", [128, 5 * 16 * 4], F32, sc)
        becols = p.sb(pre + "becols", [128, DHC], F32, sc)
        bcoB = p.sb(pre + "bcoB", [128, D], F32, sc)
        hT = p.sb(pre + "hT", [128, DHC * 128], BF16, sc)
        wslab = hT[:, 1024:1024 + D + 1]
        chscr = hT[:, 0:1024]
        blo = Ea[0:1, 0:DH]

        p.dma(lambda e: e.dma_start(out=dnh[:, :], in_=dnhcols))

        def apply_stages_tile(dst_pair, coefs_in, nstages, stage_sizes):
            src, dst = dst_pair
            for st in range(nstages):
                s = stage_sizes[st]
                base = st * 4 * 2048
                p.gdma(lambda e, base=base: e.dma_start(
                    out=coefB[:, 0:4096], in_=coefs_in[:, base:base + 2 * 2048]))
                av = src[:, :].rearrange("p (nb two s) -> p nb two s", two=2, s=s)[:, :, 0, :]
                bv = src[:, :].rearrange("p (nb two s) -> p nb two s", two=2, s=s)[:, :, 1, :]
                dav = dst[:, :].rearrange("p (nb two s) -> p nb two s", two=2, s=s)[:, :, 0, :]
                dbv = dst[:, :].rearrange("p (nb two s) -> p nb two s", two=2, s=s)[:, :, 1, :]
                w00 = coefB[:, 0:2048].rearrange("p (nb s) -> p nb s", s=s)
                w01 = coefB[:, 2048:4096].rearrange("p (nb s) -> p nb s", s=s)
                scv = scrA[:, :].rearrange("p (nb s) -> p nb s", s=s)
                p.v(lambda e, av=av, w00=w00, dav=dav: e.tensor_tensor(dav, av, w00, ALU.mult))
                p.v(lambda e, bv=bv, w01=w01, scv=scv: e.tensor_tensor(scv, bv, w01, ALU.mult))
                p.v(lambda e, dav=dav, scv=scv: e.tensor_tensor(dav, dav, scv, ALU.add))
                p.gdma(lambda e, base=base: e.dma_start(
                    out=coefB[:, 0:4096], in_=coefs_in[:, base + 2 * 2048:base + 4 * 2048]))
                w10 = coefB[:, 0:2048].rearrange("p (nb s) -> p nb s", s=s)
                w11 = coefB[:, 2048:4096].rearrange("p (nb s) -> p nb s", s=s)
                p.v(lambda e, av=av, w10=w10, dbv=dbv: e.tensor_tensor(dbv, av, w10, ALU.mult))
                p.v(lambda e, bv=bv, w11=w11, scv=scv: e.tensor_tensor(scv, bv, w11, ALU.mult))
                p.v(lambda e, dbv=dbv, scv=scv: e.tensor_tensor(dbv, dbv, scv, ALU.add))
                src, dst = dst, src
            return src

        p.gdma(lambda e: e.dma_start(
            out=WC[:, :].rearrange("p (c n) -> p c n", c=DHC),
            in_=wcon.rearrange("(c p) n -> p c n", p=128)))
        for si, st in enumerate(range(11, 6, -1)):
            S = 2 ** st
            dch = S // 128
            pairs = [cc for cc in range(DHC) if (cc // dch) % 2 == 0]
            for pi, cc in enumerate(pairs):
                cb = cc + dch
                colbase = (si * 16 + pi) * 4
                a = WC[:, cc * D:(cc + 1) * D]
                b = WC[:, cb * D:(cb + 1) * D]
                w00 = dnh[:, colbase + 0:colbase + 1]
                w01 = dnh[:, colbase + 1:colbase + 2]
                w10 = dnh[:, colbase + 2:colbase + 3]
                w11 = dnh[:, colbase + 3:colbase + 4]
                p.v(lambda e, a=a: e.tensor_copy(chscr[:, :], a))
                p.v(lambda e, a=a, w00=w00: e.tensor_scalar(a, a, w00, None, ALU.mult))
                p.v(lambda e, a=a, b=b, w10=w10: e.scalar_tensor_tensor(a, b, w10, a, ALU.mult, ALU.add))
                p.v(lambda e, b=b, w11=w11: e.tensor_scalar(b, b, w11, None, ALU.mult))
                p.v(lambda e, b=b, w01=w01: e.scalar_tensor_tensor(b, chscr[:, :], w01, b, ALU.mult, ALU.add))
        p.gdma(lambda e: e.dma_start(out=Ea[:, :], in_=eyeb_in))
        e2t = apply_stages_tile((Ea, Eb), dnclo, 7, [64, 32, 16, 8, 4, 2, 1])
        for gg in range(DHC):
            for half in range(2):
                p.t(lambda e, gg=gg, half=half: e.matmul(
                    P1[:, :], e2t[:, gg * 128:(gg + 1) * 128],
                    WC[:, gg * D + half * 512:gg * D + (half + 1) * 512],
                    start=True, stop=True))
                p.v(lambda e, gg=gg, half=half: e.tensor_copy(
                    WC[:, gg * D + half * 512:gg * D + (half + 1) * 512], P1[:, :]))

        p.gdma(lambda e: e.dma_start(out=Ea[:, :], in_=eyeb_in))
        eut = apply_stages_tile((Ea, Eb), upclo, 7, [1, 2, 4, 8, 16, 32, 64])
        for gg in range(DHC):
            p.gdma(lambda e, gg=gg: e.dma_start(out=wslab, in_=waugT[gg * 128:(gg + 1) * 128, :]))
            for rc in range(DC):
                p.t(lambda e, gg=gg, rc=rc: e.matmul(
                    P2, wslab[:, rc * 128:(rc + 1) * 128],
                    eut[:, gg * 128:(gg + 1) * 128], start=True, stop=True))
                p.v(lambda e, gg=gg, rc=rc: e.tensor_copy(
                    WU[:, rc * DH + gg * 128:rc * DH + (gg + 1) * 128], P2))
            p.t(lambda e, gg=gg: e.matmul(PB, wslab[:, D:D + 1],
                                          eut[:, gg * 128:(gg + 1) * 128], start=True, stop=True))
            p.v(lambda e, gg=gg: e.tensor_copy(blo[0:1, gg * 128:(gg + 1) * 128], PB[0:1, :]))
        for si, st in enumerate(range(7, 12)):
            s = 2 ** st
            base = si * 4 * 2048
            p.gdma(lambda e, base=base: e.dma_start(
                out=coefB[:, :], in_=upchi[:, base:base + 4 * 2048]))
            w00 = coefB[:, 0:2048].rearrange("p (nb s) -> p nb s", s=s)
            w01 = coefB[:, 2048:4096].rearrange("p (nb s) -> p nb s", s=s)
            w10 = coefB[:, 4096:6144].rearrange("p (nb s) -> p nb s", s=s)
            w11 = coefB[:, 6144:8192].rearrange("p (nb s) -> p nb s", s=s)
            scv = scrA[:, :].rearrange("p (nb s) -> p nb s", s=s)
            sc2 = scrB[:, :].rearrange("p (nb s) -> p nb s", s=s)
            for rc in range(DC):
                sl = WU[:, rc * DH:(rc + 1) * DH]
                av = sl.rearrange("p (nb two s) -> p nb two s", two=2, s=s)[:, :, 0, :]
                bv = sl.rearrange("p (nb two s) -> p nb two s", two=2, s=s)[:, :, 1, :]
                p.v(lambda e, av=av, scv=scv: e.tensor_copy(scv, av))
                p.v(lambda e, av=av, w00=w00: e.tensor_tensor(av, av, w00, ALU.mult))
                p.v(lambda e, bv=bv, w01=w01, sc2=sc2: e.tensor_tensor(sc2, bv, w01, ALU.mult))
                p.v(lambda e, av=av, sc2=sc2: e.tensor_tensor(av, av, sc2, ALU.add))
                p.v(lambda e, bv=bv, w11=w11: e.tensor_tensor(bv, bv, w11, ALU.mult))
                p.v(lambda e, scv=scv, w10=w10, sc2=sc2: e.tensor_tensor(sc2, scv, w10, ALU.mult))
                p.v(lambda e, bv=bv, sc2=sc2: e.tensor_tensor(bv, bv, sc2, ALU.add))
            rav = blo[0:1, :].rearrange("p (nb two s) -> p nb two s", two=2, s=s)[:, :, 0, :]
            rbv = blo[0:1, :].rearrange("p (nb two s) -> p nb two s", two=2, s=s)[:, :, 1, :]
            rw00 = coefB[0:1, 0:2048].rearrange("p (nb s) -> p nb s", s=s)
            rw01 = coefB[0:1, 2048:4096].rearrange("p (nb s) -> p nb s", s=s)
            rw10 = coefB[0:1, 4096:6144].rearrange("p (nb s) -> p nb s", s=s)
            rw11 = coefB[0:1, 6144:8192].rearrange("p (nb s) -> p nb s", s=s)
            r1 = scrA[0:1, :].rearrange("p (nb s) -> p nb s", s=s)
            r2 = scrB[0:1, :].rearrange("p (nb s) -> p nb s", s=s)
            p.v(lambda e, rav=rav, r1=r1: e.tensor_copy(r1, rav))
            p.v(lambda e, rav=rav, rw00=rw00: e.tensor_tensor(rav, rav, rw00, ALU.mult))
            p.v(lambda e, rbv=rbv, rw01=rw01, r2=r2: e.tensor_tensor(r2, rbv, rw01, ALU.mult))
            p.v(lambda e, rav=rav, r2=r2: e.tensor_tensor(rav, rav, r2, ALU.add))
            p.v(lambda e, rbv=rbv, rw11=rw11: e.tensor_tensor(rbv, rbv, rw11, ALU.mult))
            p.v(lambda e, r1=r1, rw10=rw10, r2=r2: e.tensor_tensor(r2, r1, rw10, ALU.mult))
            p.v(lambda e, rbv=rbv, r2=r2: e.tensor_tensor(rbv, rbv, r2, ALU.add))
        upb = Eb[0:1, 0:DH]
        p.gdma(lambda e: e.dma_start(out=upb, in_=upbias))
        p.v(lambda e: e.tensor_tensor(blo[0:1, :], blo[0:1, :], upb, ALU.add))
        for gg in range(DHC):
            p.t(lambda e, gg=gg: e.matmul(PC[:, gg:gg + 1], blo[0:1, gg * 128:(gg + 1) * 128],
                                          onef, start=True, stop=True))
        p.v(lambda e: e.tensor_copy(becols[:, :], PC[:, :]))
        brow_d = Eb[32:33, 0:D]
        brow_d2 = Eb[32:33, D:2 * D]
        p.gdma(lambda e: e.dma_start(out=brow_d, in_=bcon))
        p.gdma(lambda e: e.dma_start(out=brow_d2, in_=dbias))
        p.v(lambda e: e.tensor_tensor(brow_d, brow_d, brow_d2, ALU.add))
        p.v(lambda e: e.tensor_copy(blo[0:1, 0:D], brow_d))
        for half in range(2):
            p.t(lambda e, half=half: e.matmul(P1[:, :], ones_r[:, :],
                                              blo[0:1, half * 512:(half + 1) * 512],
                                              start=True, stop=True))
            p.v(lambda e, half=half: e.tensor_copy(bcoB[:, half * 512:(half + 1) * 512], P1[:, :]))

        for c in range(NCH):
            p.dma(lambda e, c=c: e.dma_start(out=r_c[:, :], in_=xcur[c * 128:(c + 1) * 128, :]))
            emit_ln(p, r_c[:, :], xn_c[:, :], scr)
            emit_transpose_cols(p, xn_c[:, :], DC, P4, ident,
                                xnT_c[:, :].rearrange("p (k t) -> p k t", k=DC))
            for oc in range(DHC):
                for k in range(DC):
                    p.t(lambda e, oc=oc, k=k: e.matmul(
                        P2, WU[:, k * DH + oc * 128:k * DH + (oc + 1) * 128],
                        xnT_c[:, k * 128:(k + 1) * 128], start=(k == 0), stop=(k == DC - 1)))
                p.s(lambda e, oc=oc: e.activation(hT[:, oc * 128:(oc + 1) * 128], P2,
                                                  AF.Gelu, bias=becols[:, oc:oc + 1]))
            for half in range(2):
                for k in range(DHC):
                    p.t(lambda e, half=half, k=k: e.matmul(
                        P1[:, :], hT[:, k * 128:(k + 1) * 128],
                        WC[:, k * D + half * 512:k * D + (half + 1) * 512],
                        start=(k == 0), stop=(k == DHC - 1)))
                p.v(lambda e, half=half: e.tensor_tensor(
                    outs[:, half * 512:(half + 1) * 512], P1[:, :],
                    bcoB[:, half * 512:(half + 1) * 512], ALU.add))
                p.v(lambda e, half=half: e.tensor_tensor(
                    outs[:, half * 512:(half + 1) * 512], outs[:, half * 512:(half + 1) * 512],
                    r_c[:, half * 512:(half + 1) * 512], ALU.add))
            p.dma(lambda e, c=c: e.dma_start(out=xdst.ap()[c * 128:(c + 1) * 128, :], in_=outs[:, :]))
            if taps is not None:
                p.dma(lambda e, c=c: e.dma_start(out=taps[c * 128:(c + 1) * 128, :], in_=outs[:, :]))


def _l2f_f_host(inputs, maps):
    hp = _l2f_hostprep(inputs)
    for cid in range(NCORE):
        maps[cid].update({"l2f_" + k: v for k, v in hp.items()})


def emit_l3a_f(p, g, xcur, xdst, taps=None):
    pre = "l3_"
    wq_in = p.inp(pre + "wq", [D, D], F32)
    wk_in = p.inp(pre + "wk", [D, D], F32)
    wv_in = p.inp(pre + "wv", [D, D], F32)
    wgq_in = p.inp(pre + "wgq", [D, 512], F32)
    wgk_in = p.inp(pre + "wgk", [D, 512], F32)
    wgv_in = p.inp(pre + "wgv", [D, 512], F32)
    rout_in = p.inp(pre + "rout", [512, D], F32)
    wout_in = p.inp(pre + "wout", [D, D], F32)
    glbd_in = p.inp(pre + "glbd", [D, 16], F32)
    grbd_in = p.inp(pre + "grbd", [D, 16], F32)
    tri3_in = p.inp(pre + "tri3", [128, 2 * 128], F32)
    e16_in = p.inp(pre + "e16", [128, 8 * 16], F32)
    e8_in = p.inp(pre + "e8", [128, 4 * 8], F32)
    sel16_in = p.inp(pre + "sel16", [16, 16 * 128], F32)
    sel64_in = p.inp(pre + "sel64", [16, 16 * 64], F32)
    sel8b_in = p.inp(pre + "sel8b", [8, 8 * 64], F32)
    kill01_in = p.inp(pre + "kill01", [1, NCH * 3], F32)
    lmask_in = p.inp(pre + "lmask", [64, NCH], F32)
    hind_in = p.inp(pre + "hind", [128, 4], F32)

    cch_in = p.dram(pre + "cch_in", [256, D], F32)
    cch_out = p.dram(pre + "cch_out", [1024, D], F32)
    ccp_in = p.dram(pre + "ccp_in", [128, 128], BF16)
    ccp_out = p.dram(pre + "ccp_out", [512, 128], BF16)

    NH = 16
    KW3 = X3CH * 128

    sc = p.scope()
    with sc:
        ident = g["ident"]
        r_c, xn_c, xnT_c, outs, sqs, scr = (g["r_c"], g["xn_c"], g["xnT_c"],
                                            g["outs"], g["sqs"], g["scr"])
        P1, P2, P3, P4, P5 = g["P1"], g["P2"], g["P3"], g["P4"], g["P5"]
        P6 = P4[0:16, 0:128]
        ones_r, ones_c, neg_row = g["ones_r"], g["ones_c"], g["neg_row"]

        wBig1 = p.sb(pre + "wBig1", [128, DC * D], BF16, sc)
        wBig2 = p.sb(pre + "wBig2", [128, DC * D], BF16, sc)
        wSm1 = p.sb(pre + "wSm1", [128, DC * 512], BF16, sc)
        wSm2 = p.sb(pre + "wSm2", [128, 4 * D], BF16, sc)
        glbd = p.sb(pre + "glbd_sb", [128, DC * 16], BF16, sc)
        grbd = p.sb(pre + "grbd_sb", [128, DC * 16], BF16, sc)
        tri3 = p.sb(pre + "tri3_sb", [128, 2 * 128], F32, sc)
        e16 = p.sb(pre + "e16_sb", [128, 8 * 16], BF16, sc)
        e8 = p.sb(pre + "e8_sb", [128, 4 * 8], BF16, sc)
        sel16 = p.sb(pre + "sel16_sb", [16, 16 * 128], BF16, sc)
        sel64 = p.sb(pre + "sel64_sb", [16, 16 * 64], BF16, sc)
        sel8b = p.sb(pre + "sel8b_sb", [8, 8 * 64], BF16, sc)
        kill01 = p.sb(pre + "kill01_sb", [1, NCH * 3], F32, sc)
        lmask = p.sb(pre + "lmask_sb", [64, NCH], F32, sc)
        hind = p.sb(pre + "hind_sb", [128, 4], F32, sc)
        killrow = p.sb(pre + "killrow", [1, 3 * 128], BF16, sc)
        halo = p.sb(pre + "halo", [128, 2 * D], F32, sc)
        gtmp = p.sb(pre + "gtmp", [128, D], F32, sc)
        KT = p.sb(pre + "KT", [128, DC * KW3], BF16, sc)
        V = p.sb(pre + "V", [128, X3CH * D], BF16, sc)
        QTc = p.sb(pre + "QTc", [128, DC * 128], BF16, sc)
        GQTc = p.sb(pre + "GQTc", [128, 4 * 128], BF16, sc)
        pooledT = p.sb(pre + "pooledT", [128, DC * 64], BF16, sc)
        pooledO = p.sb(pre + "pooledO", [128, DC * 16], BF16, sc)
        GKT = p.sb(pre + "GKT", [128, 4 * 64], BF16, sc)
        GV = p.sb(pre + "GV", [64, 512], BF16, sc)
        qn2c = p.sb(pre + "qn2c", [16, 128], F32, sc)
        gqn2c = p.sb(pre + "gqn2c", [8, 128], F32, sc)
        kn2m = p.sb(pre + "kn2m", [16, 1], F32, sc)
        kn2c = p.sb(pre + "kn2c", [16, 1], F32, sc)
        gkn2m = p.sb(pre + "gkn2m", [8, 1], F32, sc)
        qbias_c = p.sb(pre + "qbias_c", [16, 128], BF16, sc)
        gqbias_c = p.sb(pre + "gqbias_c", [8, 128], BF16, sc)
        PT = p.sb(pre + "PT", [128, 3 * 128], BF16, sc)
        PTl = p.sb(pre + "PTl", [64, 128], BF16, sc)
        rS = p.sb(pre + "rS", [128, 16], F32, sc)
        rSl = p.sb(pre + "rSl", [128, 8], F32, sc)
        rSb = p.sb(pre + "rSb", [128, 16], BF16, sc)
        rT = p.sb(pre + "rT", [16, 128], BF16, sc)
        rB = p.sb(pre + "rB", [128, 128], F32, sc)
        glT = p.sb(pre + "glT", [16, 128], BF16, sc)
        grT = p.sb(pre + "grT", [16, 128], BF16, sc)
        locT = p.sb(pre + "locT", [128, D], F32, sc)
        remT = p.sb(pre + "remT", [128, 512], BF16, sc)
        cmbT = p.sb(pre + "cmbT", [128, D], BF16, sc)
        rem2s = p.sb(pre + "rem2s", [128, 128], F32, sc)
        cmbF = p.sb(pre + "cmbF", [128, 128], F32, sc)

        for i, c in enumerate([NCH - 2, NCH - 1]):
            p.dma(lambda e, i=i, c=c: e.dma_start(
                out=cch_in.ap()[i * 128:(i + 1) * 128, :], in_=xcur[c * 128:(c + 1) * 128, :]))
        p.collective("AllGather", cch_in, cch_out, GROUPS)

        for dst, src in [(e16, e16_in), (e8, e8_in), (sel16, sel16_in),
                         (sel64, sel64_in), (sel8b, sel8b_in),
                         (glbd, glbd_in), (grbd, grbd_in)]:
            p.gdma(lambda e, dst=dst, src=src: e.dma_start(out=dst[:, :], in_=src))
        p.dma(lambda e: e.dma_start(out=tri3[:, :], in_=tri3_in))
        p.dma(lambda e: e.dma_start(out=kill01[:, :], in_=kill01_in))
        p.dma(lambda e: e.dma_start(out=lmask[:, :], in_=lmask_in))
        p.dma(lambda e: e.dma_start(out=hind[:, :], in_=hind_in))
        p.v(lambda e: e.memset(kn2m[:, :], 0.0))
        p.v(lambda e: e.memset(gkn2m[:, :], 0.0))
        p.v(lambda e: e.memset(halo[:, :], 0.0))

        for j in range(2):
            for r in range(4):
                p.dma(lambda e, r=r, j=j: e.dma_start(
                    out=gtmp[:, :], in_=cch_out.ap()[r * 256 + j * 128:r * 256 + (j + 1) * 128, :]))
                p.v(lambda e, r=r, j=j: e.scalar_tensor_tensor(
                    halo[:, j * D:(j + 1) * D], gtmp[:, :], hind[:, r:r + 1],
                    halo[:, j * D:(j + 1) * D], ALU.mult, ALU.add))

        for c in range(NCH):
            p.dma(lambda e, c=c: e.dma_start(out=r_c[:, :], in_=xcur[c * 128:(c + 1) * 128, :]))
            emit_ln(p, r_c[:, :], xn_c[:, :], scr)
            for s in range(DC):
                p.t(lambda e, s=s: e.matmul(P5[:, s:s + 1], xn_c[:, s * 128:(s + 1) * 128],
                                            ones_c[:, :], start=True, stop=True))
            p.s(lambda e, c=c: e.activation(
                pooledO[:, :].rearrange("p (s l) -> p s l", s=DC)[:, :, c % 16:c % 16 + 1],
                P5[:, 0:DC], AF.Copy, scale=1.0 / 128.0))
        p.dma(lambda e: e.dma_start(out=ccp_in.ap()[:, :], in_=pooledO[:, :]))
        p.collective("AllGather", ccp_in, ccp_out, GROUPS)
        p.gdma(lambda e: e.dma_start(
            out=pooledT[:, :].rearrange("p (s r l) -> p s r l", s=DC, r=4),
            in_=ccp_out.ap().rearrange("(r p) (s l) -> p s r l", p=128, s=DC)))

        p.gdma(lambda e: e.dma_start(
            out=wSm1[:, :].rearrange("p (k n) -> p k n", k=DC),
            in_=wgk_in.rearrange("(k p) n -> p k n", p=128)))
        p.gdma(lambda e: e.dma_start(
            out=wSm2[:, 0:DC * 512].rearrange("p (k n) -> p k n", k=DC),
            in_=wgv_in.rearrange("(k p) n -> p k n", p=128)))
        for m in range(4):
            for k in range(DC):
                p.t(lambda e, m=m, k=k: e.matmul(
                    P5[:, 0:64], wSm1[:, k * 512 + m * 128:k * 512 + (m + 1) * 128],
                    pooledT[:, k * 64:(k + 1) * 64], start=(k == 0), stop=(k == DC - 1)))
            p.v(lambda e, m=m: e.tensor_copy(GKT[:, m * 64:(m + 1) * 64], P5[:, 0:64]))
            p.s(lambda e, m=m: e.activation(sqs[:, 0:64], GKT[:, m * 64:(m + 1) * 64], AF.Square))
            p.t(lambda e, m=m: e.matmul(P5[0:8, 64:128], e8[:, m * 8:(m + 1) * 8],
                                        sqs[:, 0:64], start=(m == 0), stop=(m == 3)))
        p.v(lambda e: e.reduce_max(gkn2m[:, :], P5[0:8, 64:128], AX.X))
        for k in range(DC):
            p.t(lambda e, k=k: e.matmul(
                P1[0:64, :], pooledT[:, k * 64:(k + 1) * 64],
                wSm2[:, k * 512:(k + 1) * 512], start=(k == 0), stop=(k == DC - 1)))
        p.v(lambda e: e.tensor_copy(GV[:, :], P1[0:64, :]))

        p.gdma(lambda e: e.dma_start(
            out=wBig1[:, :].rearrange("p (k n) -> p k n", k=DC),
            in_=wk_in.rearrange("(k p) n -> p k n", p=128)))
        p.gdma(lambda e: e.dma_start(
            out=wBig2[:, :].rearrange("p (k n) -> p k n", k=DC),
            in_=wv_in.rearrange("(k p) n -> p k n", p=128)))
        for c in range(X3CH):
            if c < 2:
                xsrc = halo[:, c * D:(c + 1) * D]
            else:
                oc = c - 2
                p.dma(lambda e, oc=oc: e.dma_start(out=r_c[:, :], in_=xcur[oc * 128:(oc + 1) * 128, :]))
                xsrc = r_c[:, :]
            emit_ln(p, xsrc, xn_c[:, :], scr)
            emit_transpose_cols(p, xn_c[:, :], DC, P4, ident,
                                xnT_c[:, :].rearrange("p (k t) -> p k t", k=DC))
            for m in range(DC):
                for k in range(DC):
                    p.t(lambda e, m=m, k=k: e.matmul(
                        P1[:, 0:128], wBig1[:, k * D + m * 128:k * D + (m + 1) * 128],
                        xnT_c[:, k * 128:(k + 1) * 128], start=(k == 0), stop=(k == DC - 1)))
                p.v(lambda e, m=m, c=c: e.tensor_copy(
                    KT[:, m * KW3 + c * 128:m * KW3 + (c + 1) * 128], P1[:, 0:128]))
                p.s(lambda e, m=m, c=c: e.activation(
                    sqs[:, :], KT[:, m * KW3 + c * 128:m * KW3 + (c + 1) * 128], AF.Square))
                p.t(lambda e, m=m: e.matmul(P5[0:16, :], e16[:, m * 16:(m + 1) * 16],
                                            sqs[:, :], start=(m == 0), stop=(m == DC - 1)))
            p.v(lambda e: e.reduce_max(kn2c[:, :], P5[0:16, :], AX.X))
            p.v(lambda e: e.tensor_tensor(kn2m[:, :], kn2m[:, :], kn2c[:, :], ALU.max))
            for half in range(2):
                for k in range(DC):
                    p.t(lambda e, half=half, k=k: e.matmul(
                        P1[:, :], xnT_c[:, k * 128:(k + 1) * 128],
                        wBig2[:, k * D + half * 512:k * D + (half + 1) * 512],
                        start=(k == 0), stop=(k == DC - 1)))
                p.v(lambda e, half=half, c=c: e.tensor_copy(
                    V[:, c * D + half * 512:c * D + (half + 1) * 512], P1[:, :]))

        p.gdma(lambda e: e.dma_start(
            out=wBig1[:, :].rearrange("p (k n) -> p k n", k=DC),
            in_=wq_in.rearrange("(k p) n -> p k n", p=128)))
        p.gdma(lambda e: e.dma_start(
            out=wBig2[:, :].rearrange("p (k n) -> p k n", k=DC),
            in_=wout_in.rearrange("(k p) n -> p k n", p=128)))
        p.gdma(lambda e: e.dma_start(
            out=wSm1[:, :].rearrange("p (k n) -> p k n", k=DC),
            in_=wgq_in.rearrange("(k p) n -> p k n", p=128)))
        p.gdma(lambda e: e.dma_start(
            out=wSm2[:, 0:4 * D].rearrange("p (k n) -> p k n", k=4),
            in_=rout_in.rearrange("(k p) n -> p k n", p=128)))

        for oc in range(NCH):
            hc = oc + 2
            p.dma(lambda e, oc=oc: e.dma_start(out=r_c[:, :], in_=xcur[oc * 128:(oc + 1) * 128, :]))
            emit_ln(p, r_c[:, :], xn_c[:, :], scr)
            emit_transpose_cols(p, xn_c[:, :], DC, P4, ident,
                                xnT_c[:, :].rearrange("p (k t) -> p k t", k=DC))
            for m in range(DC):
                for k in range(DC):
                    p.t(lambda e, m=m, k=k: e.matmul(
                        P1[:, 0:128], wBig1[:, k * D + m * 128:k * D + (m + 1) * 128],
                        xnT_c[:, k * 128:(k + 1) * 128], start=(k == 0), stop=(k == DC - 1)))
                p.s(lambda e, m=m: e.activation(QTc[:, m * 128:(m + 1) * 128], P1[:, 0:128],
                                                AF.Copy, scale=1.0 / 8.0))
                p.s(lambda e, m=m: e.activation(sqs[:, :], QTc[:, m * 128:(m + 1) * 128], AF.Square))
                p.t(lambda e, m=m: e.matmul(P5[0:16, :], e16[:, m * 16:(m + 1) * 16],
                                            sqs[:, :], start=(m == 0), stop=(m == DC - 1)))
            p.v(lambda e: e.tensor_copy(qn2c[:, :], P5[0:16, :]))
            p.v(lambda e: e.tensor_scalar(qn2c[:, :], qn2c[:, :], kn2m[0:16, 0:1],
                                          1.0, ALU.mult, ALU.mult))
            p.s(lambda e: e.activation(qn2c[:, :], qn2c[:, :], AF.Sqrt))
            p.v(lambda e: e.tensor_scalar(qbias_c[:, :], qn2c[:, :], -1.0, 50.0, ALU.mult, ALU.add))
            for k in range(DC):
                p.t(lambda e, k=k: e.matmul(P5[0:16, :], glbd[:, k * 16:(k + 1) * 16],
                                            QTc[:, k * 128:(k + 1) * 128],
                                            start=(k == 0), stop=(k == DC - 1)))
            p.s(lambda e: e.activation(glT[:, :], P5[0:16, :], AF.Sigmoid, scale=8.0))
            for k in range(DC):
                p.t(lambda e, k=k: e.matmul(P5[0:16, :], grbd[:, k * 16:(k + 1) * 16],
                                            QTc[:, k * 128:(k + 1) * 128],
                                            start=(k == 0), stop=(k == DC - 1)))
            p.s(lambda e: e.activation(grT[:, :], P5[0:16, :], AF.Sigmoid, scale=8.0))
            for m in range(4):
                for k in range(DC):
                    p.t(lambda e, m=m, k=k: e.matmul(
                        P1[:, 0:128], wSm1[:, k * 512 + m * 128:k * 512 + (m + 1) * 128],
                        xnT_c[:, k * 128:(k + 1) * 128], start=(k == 0), stop=(k == DC - 1)))
                p.s(lambda e, m=m: e.activation(GQTc[:, m * 128:(m + 1) * 128], P1[:, 0:128],
                                                AF.Copy, scale=1.0 / 8.0))
                p.s(lambda e, m=m: e.activation(sqs[:, :], GQTc[:, m * 128:(m + 1) * 128], AF.Square))
                p.t(lambda e, m=m: e.matmul(P5[0:8, :], e8[:, m * 8:(m + 1) * 8],
                                            sqs[:, :], start=(m == 0), stop=(m == 3)))
            p.v(lambda e: e.tensor_copy(gqn2c[:, :], P5[0:8, :]))
            p.v(lambda e: e.tensor_scalar(gqn2c[:, :], gqn2c[:, :], gkn2m[0:8, 0:1],
                                          1.0, ALU.mult, ALU.mult))
            p.s(lambda e: e.activation(gqn2c[:, :], gqn2c[:, :], AF.Sqrt))
            p.v(lambda e: e.tensor_scalar(gqbias_c[:, :], gqn2c[:, :], -1.0, 50.0, ALU.mult, ALU.add))
            for di in range(3):
                p.v(lambda e, di=di, oc=oc: e.tensor_scalar(
                    killrow[0:1, di * 128:(di + 1) * 128], neg_row[0:1, :],
                    kill01[0:1, oc * 3 + di:oc * 3 + di + 1], None, ALU.mult))
            for h in range(NH):
                sl = h // 2
                po = (h % 2) * 64
                for di in range(3):
                    kc = hc + di - 2
                    p.t(lambda e, di=di, kc=kc, sl=sl, po=po: e.matmul(
                        P2[:, di * 128:(di + 1) * 128],
                        KT[po:po + 64, sl * KW3 + kc * 128:sl * KW3 + (kc + 1) * 128],
                        QTc[po:po + 64, sl * 128:(sl + 1) * 128], start=True, stop=False))
                    p.t(lambda e, di=di, h=h: e.matmul(
                        P2[:, di * 128:(di + 1) * 128], sel16[:, h * 128:(h + 1) * 128],
                        qbias_c[:, :], start=False, stop=False))
                    p.t(lambda e, di=di: e.matmul(
                        P2[:, di * 128:(di + 1) * 128],
                        killrow[0:1, di * 128:(di + 1) * 128], ones_r[:, :],
                        start=False, stop=True))
                p.v(lambda e: e.tensor_tensor(P2[:, 0:128], P2[:, 0:128], tri3[:, 0:128], ALU.add))
                p.v(lambda e: e.tensor_tensor(P2[:, 256:384], P2[:, 256:384], tri3[:, 128:256], ALU.add))
                p.s(lambda e: e.activation(PT[:, :], P2[:, 0:384], AF.Exp))
                for di in range(3):
                    p.t(lambda e, di=di: e.matmul(P5[:, 0:1], PT[:, di * 128:(di + 1) * 128],
                                                  ones_c[:, 0:1], start=(di == 0), stop=(di == 2)))
                p.v(lambda e, h=h: e.tensor_copy(rS[:, h:h + 1], P5[:, 0:1]))
                for di in range(3):
                    kc = hc + di - 2
                    p.t(lambda e, di=di, kc=kc, h=h, sl=sl, po=po: e.matmul(
                        P3[po:po + 64, sl * 128:(sl + 1) * 128],
                        V[:, kc * D + h * 64:kc * D + (h + 1) * 64],
                        PT[:, di * 128:(di + 1) * 128], start=(di == 0), stop=(di == 2)))
            p.v(lambda e: e.reciprocal(rS[:, :], rS[:, :]))
            p.v(lambda e: e.tensor_copy(rSb[:, :], rS[:, :]))
            p.t(lambda e: e.transpose(P6, rSb[:, :], ident[:, :]))
            p.v(lambda e: e.tensor_copy(rT[:, :], P6))
            for h in range(NH):
                sl = h // 2
                po = (h % 2) * 64
                p.t(lambda e, h=h: e.matmul(P5[0:64, :], sel64[:, h * 64:(h + 1) * 64],
                                            rT[:, :], start=True, stop=True))
                p.v(lambda e: e.tensor_copy(rB[0:64, :], P5[0:64, :]))
                p.v(lambda e, sl=sl, po=po: e.tensor_tensor(
                    locT[po:po + 64, sl * 128:(sl + 1) * 128],
                    P3[po:po + 64, sl * 128:(sl + 1) * 128], rB[0:64, :], ALU.mult))
            for lh in range(LNH):
                sl = lh // 2
                po = (lh % 2) * 64
                p.t(lambda e, lh=lh, sl=sl, po=po: e.matmul(
                    P5[0:64, :], GKT[po:po + 64, sl * 64:(sl + 1) * 64],
                    GQTc[po:po + 64, sl * 128:(sl + 1) * 128], start=True, stop=False))
                p.t(lambda e, lh=lh: e.matmul(
                    P5[0:64, :], sel8b[:, lh * 64:(lh + 1) * 64],
                    gqbias_c[:, :], start=False, stop=True))
                p.v(lambda e, oc=oc: e.tensor_scalar(P5[0:64, :], P5[0:64, :],
                                                     lmask[0:64, oc:oc + 1], None, ALU.add))
                p.s(lambda e: e.activation(PTl[:, :], P5[0:64, :], AF.Exp))
                p.t(lambda e: e.matmul(P5[:, 64:65], PTl[:, :], ones_c[0:64, 0:1],
                                       start=True, stop=True))
                p.v(lambda e, lh=lh: e.tensor_copy(rSl[:, lh:lh + 1], P5[:, 64:65]))
                p.t(lambda e, lh=lh, sl=sl, po=po: e.matmul(
                    P3[po:po + 64, sl * 128:(sl + 1) * 128],
                    GV[:, lh * 64:(lh + 1) * 64], PTl[:, :], start=True, stop=True))
            p.v(lambda e: e.reciprocal(rSl[:, :], rSl[:, :]))
            p.v(lambda e: e.tensor_copy(rSb[:, 0:8], rSl[:, :]))
            p.t(lambda e: e.transpose(P6[0:8, :], rSb[:, 0:8], ident[:, :]))
            p.v(lambda e: e.tensor_copy(rT[0:8, :], P6[0:8, :]))
            for lh in range(LNH):
                sl = lh // 2
                po = (lh % 2) * 64
                p.t(lambda e, lh=lh: e.matmul(P5[0:64, :], sel8b[:, lh * 64:(lh + 1) * 64],
                                              rT[0:8, :], start=True, stop=True))
                p.v(lambda e: e.tensor_copy(rB[0:64, :], P5[0:64, :]))
                p.v(lambda e, sl=sl, po=po: e.tensor_tensor(
                    remT[po:po + 64, sl * 128:(sl + 1) * 128],
                    P3[po:po + 64, sl * 128:(sl + 1) * 128], rB[0:64, :], ALU.mult))
            for m in range(DC):
                for k in range(4):
                    p.t(lambda e, m=m, k=k: e.matmul(
                        P1[:, 0:128], wSm2[:, k * D + m * 128:k * D + (m + 1) * 128],
                        remT[:, k * 128:(k + 1) * 128], start=(k == 0), stop=(k == 3)))
                p.v(lambda e, m=m: e.tensor_copy(rem2s[:, :], P1[:, 0:128]))
                for half in range(2):
                    h = 2 * m + half
                    po = half * 64
                    p.t(lambda e, h=h: e.matmul(P5[0:64, :], sel64[:, h * 64:(h + 1) * 64],
                                                glT[:, :], start=True, stop=True))
                    p.v(lambda e, po=po: e.tensor_copy(rB[po:po + 64, :], P5[0:64, :]))
                    p.v(lambda e, m=m, po=po: e.tensor_tensor(
                        cmbF[po:po + 64, :], locT[po:po + 64, m * 128:(m + 1) * 128],
                        rB[po:po + 64, :], ALU.mult))
                    p.t(lambda e, h=h: e.matmul(P5[0:64, :], sel64[:, h * 64:(h + 1) * 64],
                                                grT[:, :], start=True, stop=True))
                    p.v(lambda e, po=po: e.tensor_copy(rB[po:po + 64, :], P5[0:64, :]))
                    p.v(lambda e, po=po: e.tensor_tensor(
                        rB[po:po + 64, :], rem2s[po:po + 64, :], rB[po:po + 64, :], ALU.mult))
                    p.v(lambda e, m=m, po=po: e.tensor_tensor(
                        cmbT[po:po + 64, m * 128:(m + 1) * 128], cmbF[po:po + 64, :],
                        rB[po:po + 64, :], ALU.add))
            for half in range(2):
                for k in range(DC):
                    p.t(lambda e, half=half, k=k: e.matmul(
                        P1[:, :], cmbT[:, k * 128:(k + 1) * 128],
                        wBig2[:, k * D + half * 512:k * D + (half + 1) * 512],
                        start=(k == 0), stop=(k == DC - 1)))
                p.v(lambda e, half=half: e.tensor_tensor(
                    outs[:, half * 512:(half + 1) * 512], P1[:, :],
                    r_c[:, half * 512:(half + 1) * 512], ALU.add))
            p.dma(lambda e, oc=oc: e.dma_start(out=xdst.ap()[oc * 128:(oc + 1) * 128, :], in_=outs[:, :]))
            if taps is not None:
                p.dma(lambda e, oc=oc: e.dma_start(out=taps[oc * 128:(oc + 1) * 128, :], in_=outs[:, :]))


def _l3a_f_host(inputs, maps):
    base = _l3a_inmaps(np.zeros((16384, D), np.float32), inputs)
    for cid in range(NCORE):
        b = base[cid]
        rk = cid % 4
        hind = np.zeros((128, 4), dtype=np.float32)
        if rk > 0:
            hind[:, rk - 1] = 1.0
        maps[cid].update({
            "l3_wq": b["wq"], "l3_wk": b["wk"], "l3_wv": b["wv"],
            "l3_wgq": b["wgq"], "l3_wgk": b["wgk"], "l3_wgv": b["wgv"],
            "l3_rout": b["rout"], "l3_wout": b["wout"], "l3_glbd": b["glbd"],
            "l3_grbd": b["grbd"], "l3_tri3": b["tri3"], "l3_e16": b["e16"],
            "l3_e8": b["e8"], "l3_sel16": b["sel16"], "l3_sel64": b["sel64"],
            "l3_sel8b": b["sel8b"], "l3_kill01": b["kill01"], "l3_lmask": b["lmask"],
            "l3_hind": hind,
        })


def emit_l3f_f(p, g, xcur, xdst, taps=None):
    pre = "l3f_"
    wqT_in = p.inp(pre + "wqT", [D, D], F32)
    wkT_in = p.inp(pre + "wkT", [D, D], F32)
    wvT_in = p.inp(pre + "wvT", [D, D], F32)
    wr_in = p.inp(pre + "wr", [D, 16], F32)
    wo_in = p.inp(pre + "wo", [256, D], F32)
    bqc_in = p.inp(pre + "bqc", [128, DC], F32)
    bkc_in = p.inp(pre + "bkc", [128, DC], F32)
    bvc_in = p.inp(pre + "bvc", [128, DC], F32)
    brc_in = p.inp(pre + "brc", [16, 1], F32)
    bo_in = p.inp(pre + "bo", [1, D], F32)
    ehh_in = p.inp(pre + "ehh", [16, 128], F32)
    mh_in = p.inp(pre + "mh", [16, 8], F32)
    t64_in = p.inp(pre + "t64", [128, 64], F32)
    e4_in = p.inp(pre + "e4", [128, 2 * 4], F32)
    sel4_in = p.inp(pre + "sel4", [4, 4 * 128], F32)

    ccr_in = p.dram(pre + "ccr_in", [16, 1], F32)
    ccr_out = p.dram(pre + "ccr_out", [64, 1], F32)
    cckv_in = p.dram(pre + "cckv_in", [128, 8257], BF16)
    cckv_out = p.dram(pre + "cckv_out", [512, 8257], BF16)

    sc = p.scope()
    with sc:
        ident = g["ident"]
        r_c, xn_c, xnT_c, outs, sqs, scr = (g["r_c"], g["xn_c"], g["xnT_c"],
                                            g["outs"], g["sqs"], g["scr"])
        P1, P2, P3, P4, P5 = g["P1"], g["P2"], g["P3"], g["P4"], g["P5"]
        PS = P1
        PC65 = g["P2"][0:65, 0:512]
        PB = g["P3"][0:64, 0:512]
        P6 = P4[0:16, 0:16]
        ones_r, ones_c = g["ones_r"], g["ones_c"]

        SelM = p.sb(pre + "SelM", [128, DC * 256], BF16, sc)
        SelWq = p.sb(pre + "SelWq", [128, DC * 256], BF16, sc)
        SelWk = p.sb(pre + "SelWk", [128, DC * 256], BF16, sc)
        SelWv = p.sb(pre + "SelWv", [128, DC * 256], BF16, sc)
        woS = p.sb(pre + "woS", [128, 2 * D], BF16, sc)
        XN = p.sb(pre + "XN", [128, NCH * D], BF16, sc)
        KselT = p.sb(pre + "KselT", [128, 2 * SEQ], BF16, sc)
        Vsel = p.sb(pre + "Vsel", [128, 64 * 260], BF16, sc)
        QselT = p.sb(pre + "QselT", [128, 2 * TOK], BF16, sc)
        KVown = p.sb(pre + "KVown", [128, 8257], BF16, sc)
        wT = KVown[:, 0:8192]
        ehh = p.sb(pre + "ehh_sb", [16, 128], BF16, sc)
        mh = p.sb(pre + "mh_sb", [16, 8], BF16, sc)
        t64 = p.sb(pre + "t64_sb", [128, 64], BF16, sc)
        e4 = p.sb(pre + "e4_sb", [128, 2 * 4], BF16, sc)
        sel4 = p.sb(pre + "sel4_sb", [4, 4 * 128], BF16, sc)
        bqc = p.sb(pre + "bqc_sb", [128, DC], BF16, sc)
        bkc = p.sb(pre + "bkc_sb", [128, DC], BF16, sc)
        bvc = p.sb(pre + "bvc_sb", [128, DC], BF16, sc)
        brc = p.sb(pre + "brc_sb", [16, 1], F32, sc)
        wr_sb = p.sb(pre + "wr_sb", [128, DC * 16], BF16, sc)
        rsum = p.sb(pre + "rsum", [16, 1], F32, sc)
        rsc = p.sb(pre + "rsc", [16, 1], F32, sc)
        rst4 = p.sb(pre + "rst4", [16, 4], F32, sc)
        rmb = p.sb(pre + "rmb", [16, 1], BF16, sc)
        drow = p.sb(pre + "drow", [1, 16], F32, sc)
        drow2 = p.sb(pre + "drow2", [1, 16], F32, sc)
        dmax = p.sb(pre + "dmax", [1, 1], F32, sc)
        dsum = p.sb(pre + "dsum", [1, 1], F32, sc)
        ohs = p.sb(pre + "ohs", [1, 4 * 16], F32, sc)
        ohsum = p.sb(pre + "ohsum", [1, 16], F32, sc)
        grows = p.sb(pre + "grows", [1, 4 * 16], BF16, sc)
        growT = p.sb(pre + "growT", [16, 1], BF16, sc)
        gcm = p.sb(pre + "gcm", [16, 8], BF16, sc)
        colv = p.sb(pre + "colv", [128, 8], F32, sc)
        bsel_q = p.sb(pre + "bsel_q", [1, 256], BF16, sc)
        bsel_k = p.sb(pre + "bsel_k", [1, 256], BF16, sc)
        bsel_v = p.sb(pre + "bsel_v", [1, 256], BF16, sc)
        bselqc = p.sb(pre + "bselqc", [128, 2], F32, sc)
        bselkc = p.sb(pre + "bselkc", [128, 2], F32, sc)
        bvselB = p.sb(pre + "bvselB", [128, 256], F32, sc)
        boB = p.sb(pre + "boB", [128, D], F32, sc)
        bo_bf = p.sb(pre + "bo_bf", [1, D], BF16, sc)
        kn2m = p.sb(pre + "kn2m", [4, 1], F32, sc)
        kmax4 = p.sb(pre + "kmax4", [4, 4], F32, sc)
        qn2 = p.sb(pre + "qn2", [4, TOK], F32, sc)
        qbias = p.sb(pre + "qbias", [4, TOK], BF16, sc)
        PT = p.sb(pre + "PT", [128, 512], BF16, sc)
        rrow = p.sb(pre + "rrow", [1, 512], F32, sc)
        rrowb = p.sb(pre + "rrowb", [1, 512], BF16, sc)
        rBf = p.sb(pre + "rBf", [64, 512], F32, sc)
        ctxall = p.sb(pre + "ctxall", [128, 2 * 512], BF16, sc)

        for dst, src in [(ehh, ehh_in), (mh, mh_in), (t64, t64_in),
                         (e4, e4_in), (sel4, sel4_in), (bqc, bqc_in), (bkc, bkc_in),
                         (bvc, bvc_in), (bo_bf, bo_in)]:
            p.gdma(lambda e, dst=dst, src=src: e.dma_start(out=dst[:, :], in_=src))
        p.dma(lambda e: e.dma_start(out=brc[:, :], in_=brc_in))
        p.gdma(lambda e: e.dma_start(
            out=wr_sb[:, :].rearrange("p (k n) -> p k n", k=DC),
            in_=wr_in.rearrange("(k p) n -> p k n", p=128)))
        p.v(lambda e: e.memset(rsum[:, :], 0.0))
        p.v(lambda e: e.memset(kn2m[:, :], 0.0))

        for c in range(NCH):
            p.dma(lambda e, c=c: e.dma_start(out=r_c[:, :], in_=xcur[c * 128:(c + 1) * 128, :]))
            emit_ln(p, r_c[:, :], xn_c[:, :], scr)
            emit_transpose_cols(p, xn_c[:, :], DC, P4, ident,
                                XN[:, c * D:(c + 1) * D].rearrange("p (k t) -> p k t", k=DC))
            for k in range(DC):
                p.t(lambda e, k=k, c=c: e.matmul(P5[0:16, :], wr_sb[:, k * 16:(k + 1) * 16],
                                                 XN[:, c * D + k * 128:c * D + (k + 1) * 128],
                                                 start=(k == 0), stop=(k == DC - 1)))
            p.v(lambda e: e.reduce_sum(rsc[:, :], P5[0:16, :], AX.X))
            p.v(lambda e: e.tensor_tensor(rsum[:, :], rsum[:, :], rsc[:, :], ALU.add))
        p.dma(lambda e: e.dma_start(out=ccr_in.ap()[:, :], in_=rsum[:, :]))
        p.collective("AllGather", ccr_in, ccr_out, GROUPS)
        p.dma(lambda e: e.dma_start(
            out=rst4[:, :], in_=ccr_out.ap().rearrange("(r s) o -> s r o", s=16),
            allow_slow_non_contiguous=True))
        p.v(lambda e: e.reduce_sum(rsum[:, :], rst4[:, :], AX.X))
        p.v(lambda e: e.tensor_scalar(rsum[:, :], rsum[:, :], 1.0 / SEQ, None, ALU.mult))
        p.v(lambda e: e.tensor_tensor(rsum[:, :], rsum[:, :], brc[:, :], ALU.add))
        p.v(lambda e: e.tensor_copy(rmb[:, :], rsum[:, :]))
        p.t(lambda e: e.transpose(P6[0:1, :], rmb[:, :], ident[0:16, 0:16]))
        p.v(lambda e: e.tensor_copy(drow[0:1, :], P6[0:1, 0:16]))
        p.v(lambda e: e.reduce_max(dmax[:, :], drow[0:1, :], AX.X, negate=True))
        p.s(lambda e: e.activation(drow[0:1, :], drow[0:1, :], AF.Exp, bias=dmax[0:1, :],
                                   accum_out=dsum[0:1, :]))
        p.v(lambda e: e.reciprocal(dsum[:, :], dsum[:, :]))
        p.v(lambda e: e.tensor_scalar(drow[0:1, :], drow[0:1, :], dsum[0:1, 0:1], None, ALU.mult))
        p.v(lambda e: e.tensor_copy(drow2[0:1, :], drow[0:1, :]))
        for kk in range(4):
            p.v(lambda e: e.reduce_max(dmax[:, :], drow2[0:1, :], AX.X))
            p.v(lambda e, kk=kk: e.tensor_scalar(ohs[0:1, kk * 16:(kk + 1) * 16], drow2[0:1, :],
                                                 dmax[0:1, 0:1], None, ALU.is_equal))
            p.v(lambda e, kk=kk: e.scalar_tensor_tensor(
                drow2[0:1, :], ohs[0:1, kk * 16:(kk + 1) * 16], -1e30, drow2[0:1, :],
                ALU.mult, ALU.add))
        p.v(lambda e: e.tensor_tensor(ohsum[0:1, :], ohs[0:1, 0:16], ohs[0:1, 16:32], ALU.add))
        p.v(lambda e: e.tensor_tensor(ohsum[0:1, :], ohsum[0:1, :], ohs[0:1, 32:48], ALU.add))
        p.v(lambda e: e.tensor_tensor(ohsum[0:1, :], ohsum[0:1, :], ohs[0:1, 48:64], ALU.add))
        p.v(lambda e: e.tensor_tensor(drow2[0:1, :], drow[0:1, :], ohsum[0:1, :], ALU.mult))
        p.v(lambda e: e.reduce_max(dmax[:, :], drow2[0:1, :], AX.X, negate=True))
        p.s(lambda e: e.activation(drow2[0:1, :], drow2[0:1, :], AF.Exp, bias=dmax[0:1, :],
                                   accum_out=dsum[0:1, :]))
        p.v(lambda e: e.reciprocal(dsum[:, :], dsum[:, :]))
        p.v(lambda e: e.tensor_scalar(drow2[0:1, :], drow2[0:1, :], dsum[0:1, 0:1], None, ALU.mult))
        for kk in range(4):
            p.v(lambda e, kk=kk: e.tensor_tensor(grows[0:1, kk * 16:(kk + 1) * 16],
                                                 drow2[0:1, :], ohs[0:1, kk * 16:(kk + 1) * 16],
                                                 ALU.mult))
        for kk in range(4):
            p.t(lambda e, kk=kk: e.transpose(P6[0:16, 0:1], grows[0:1, kk * 16:(kk + 1) * 16],
                                             ident[0:1, 0:1]))
            p.v(lambda e: e.tensor_copy(growT[:, :], P6[0:16, 0:1]))
            p.v(lambda e: e.tensor_tensor(gcm[:, :], mh[:, :],
                                          growT[:, :].broadcast_to((16, 8)), ALU.mult))
            p.t(lambda e: e.matmul(P5[:, 0:8], ehh[:, :], gcm[:, :], start=True, stop=True))
            p.v(lambda e: e.tensor_copy(colv[:, :], P5[:, 0:8]))
            for s in range(DC):
                p.v(lambda e, kk=kk, s=s: e.tensor_scalar(
                    SelM[:, s * 256 + kk * 64:s * 256 + (kk + 1) * 64], t64[:, :],
                    colv[:, s:s + 1], None, ALU.mult))
        for (wsrc, dstW, bcol, bselrow) in [(wqT_in, SelWq, bqc, bsel_q),
                                            (wkT_in, SelWk, bkc, bsel_k),
                                            (wvT_in, SelWv, bvc, bsel_v)]:
            p.gdma(lambda e, wsrc=wsrc: e.dma_start(
                out=wT[:, :].rearrange("p (k n) -> p k n", k=DC),
                in_=wsrc.rearrange("(k p) n -> p k n", p=128)))
            for m in range(DC):
                for k in range(DC):
                    p.t(lambda e, m=m, k=k: e.matmul(
                        P1[:, 0:256], wT[:, k * D + m * 128:k * D + (m + 1) * 128],
                        SelM[:, k * 256:(k + 1) * 256], start=(k == 0), stop=(k == DC - 1)))
                p.v(lambda e, m=m, dstW=dstW: e.tensor_copy(
                    dstW[:, m * 256:(m + 1) * 256], P1[:, 0:256]))
            for k in range(DC):
                p.t(lambda e, k=k, bcol=bcol: e.matmul(
                    P1[0:1, 0:256], bcol[:, k:k + 1], SelM[:, k * 256:(k + 1) * 256],
                    start=(k == 0), stop=(k == DC - 1)))
            p.v(lambda e, bselrow=bselrow: e.tensor_copy(bselrow[0:1, :], P1[0:1, 0:256]))
        for mc in range(2):
            p.t(lambda e, mc=mc: e.matmul(P1[:, mc:mc + 1],
                                          bsel_q[0:1, mc * 128:(mc + 1) * 128],
                                          ones_r[0:1, 0:1], start=True, stop=True))
        p.v(lambda e: e.tensor_copy(bselqc[:, :], P1[:, 0:2]))
        for mc in range(2):
            p.t(lambda e, mc=mc: e.matmul(P1[:, mc:mc + 1],
                                          bsel_k[0:1, mc * 128:(mc + 1) * 128],
                                          ones_r[0:1, 0:1], start=True, stop=True))
        p.v(lambda e: e.tensor_copy(bselkc[:, :], P1[:, 0:2]))
        p.t(lambda e: e.matmul(P1[:, 0:256], ones_r[:, :], bsel_v[0:1, :], start=True, stop=True))
        p.v(lambda e: e.tensor_copy(bvselB[:, :], P1[:, 0:256]))
        p.gdma(lambda e: e.dma_start(
            out=woS[:, :].rearrange("p (k n) -> p k n", k=2),
            in_=wo_in.rearrange("(k p) n -> p k n", p=128)))
        for half in range(2):
            p.t(lambda e, half=half: e.matmul(P1[:, :], ones_r[:, :],
                                              bo_bf[0:1, half * 512:(half + 1) * 512],
                                              start=True, stop=True))
            p.v(lambda e, half=half: e.tensor_copy(boB[:, half * 512:(half + 1) * 512], P1[:, :]))
        p.v(lambda e: e.memset(KVown[:, :], 1.0))

        for c in range(NCH):
            for mc in range(2):
                for k in range(DC):
                    p.t(lambda e, mc=mc, k=k, c=c: e.matmul(
                        P1[:, 0:128], SelWk[:, k * 256 + mc * 128:k * 256 + (mc + 1) * 128],
                        XN[:, c * D + k * 128:c * D + (k + 1) * 128], start=(k == 0), stop=(k == DC - 1)))
                p.v(lambda e, mc=mc, c=c: e.tensor_scalar(
                    KVown[:, mc * TOK + c * 128:mc * TOK + (c + 1) * 128],
                    P1[:, 0:128], bselkc[:, mc:mc + 1], None, ALU.add))
                p.s(lambda e, mc=mc, c=c: e.activation(
                    sqs[:, :], KVown[:, mc * TOK + c * 128:mc * TOK + (c + 1) * 128], AF.Square))
                p.t(lambda e, mc=mc: e.matmul(P5[0:4, :], e4[:, mc * 4:(mc + 1) * 4],
                                              sqs[:, :], start=(mc == 0), stop=(mc == 1)))
            p.v(lambda e: e.reduce_max(rsc[0:4, :], P5[0:4, :], AX.X))
            p.v(lambda e: e.tensor_tensor(kn2m[:, :], kn2m[:, :], rsc[0:4, 0:1], ALU.max))
            for k in range(DC):
                p.t(lambda e, k=k, c=c: e.matmul(
                    P1[:, 0:256], XN[:, c * D + k * 128:c * D + (k + 1) * 128],
                    SelWv[:, k * 256:(k + 1) * 256], start=(k == 0), stop=(k == DC - 1)))
            p.v(lambda e, c=c: e.tensor_tensor(
                KVown[:, 2 * TOK + c * 260:2 * TOK + c * 260 + 260]
                    .rearrange("p (h x) -> p h x", h=4)[:, :, 0:64],
                P1[:, 0:256].rearrange("p (h d) -> p h d", h=4),
                bvselB[:, :].rearrange("p (h d) -> p h d", h=4), ALU.add))
            for mc in range(2):
                for k in range(DC):
                    p.t(lambda e, mc=mc, k=k, c=c: e.matmul(
                        P1[:, 0:128], SelWq[:, k * 256 + mc * 128:k * 256 + (mc + 1) * 128],
                        XN[:, c * D + k * 128:c * D + (k + 1) * 128], start=(k == 0), stop=(k == DC - 1)))
                p.v(lambda e, mc=mc, c=c: e.tensor_scalar(
                    QselT[:, mc * TOK + c * 128:mc * TOK + (c + 1) * 128],
                    P1[:, 0:128], bselqc[:, mc:mc + 1], 0.125, ALU.add, ALU.mult))
                p.s(lambda e, mc=mc, c=c: e.activation(
                    sqs[:, :], QselT[:, mc * TOK + c * 128:mc * TOK + (c + 1) * 128], AF.Square))
                p.t(lambda e, mc=mc: e.matmul(P5[0:4, :], e4[:, mc * 4:(mc + 1) * 4],
                                              sqs[:, :], start=(mc == 0), stop=(mc == 1)))
            p.v(lambda e, c=c: e.tensor_copy(qn2[:, c * 128:(c + 1) * 128], P5[0:4, :]))
        p.v(lambda e: e.tensor_copy(KVown[0:4, 8256:8257], kn2m[:, :]))
        p.dma(lambda e: e.dma_start(out=cckv_in.ap()[:, :], in_=KVown[:, :]))
        p.collective("AllGather", cckv_in, cckv_out, GROUPS)
        for mc in range(2):
            for r in range(4):
                p.gdma(lambda e, mc=mc, r=r: e.dma_start(
                    out=KselT[:, mc * SEQ + r * TOK:mc * SEQ + (r + 1) * TOK],
                    in_=cckv_out.ap()[r * 128:(r + 1) * 128, mc * TOK:(mc + 1) * TOK]))
        for r in range(4):
            p.gdma(lambda e, r=r: e.dma_start(
                out=Vsel[:, r * NCH * 260:(r + 1) * NCH * 260],
                in_=cckv_out.ap()[r * 128:(r + 1) * 128, 2 * TOK:2 * TOK + NCH * 260]))
        p.gdma(lambda e: e.dma_start(
            out=kmax4[:, :],
            in_=cckv_out.ap()[:, 8256:8257].rearrange("(r p) o -> p r o", p=128)[0:4, :, 0],
            allow_slow_non_contiguous=True))
        p.v(lambda e: e.reduce_max(kn2m[:, :], kmax4[:, :], AX.X))
        for c in range(NCH):
            p.v(lambda e, c=c: e.tensor_scalar(
                qn2[:, c * 128:(c + 1) * 128], qn2[:, c * 128:(c + 1) * 128],
                kn2m[0:4, 0:1], 1.0, ALU.mult, ALU.mult))
            p.s(lambda e, c=c: e.activation(qn2[:, c * 128:(c + 1) * 128],
                                            qn2[:, c * 128:(c + 1) * 128], AF.Sqrt))
            p.v(lambda e, c=c: e.tensor_scalar(
                qbias[:, c * 128:(c + 1) * 128], qn2[:, c * 128:(c + 1) * 128],
                -1.0, 50.0, ALU.mult, ALU.add))

        for qb in range(4):
            for h in range(4):
                mc = h // 2
                po = (h % 2) * 64
                for kc in range(SCH):
                    p.t(lambda e, kc=kc, mc=mc, po=po, qb=qb: e.matmul(
                        PS[:, :],
                        KselT[po:po + 64, mc * SEQ + kc * 128:mc * SEQ + (kc + 1) * 128],
                        QselT[po:po + 64, mc * TOK + qb * 512:mc * TOK + (qb + 1) * 512],
                        start=True, stop=False))
                    p.t(lambda e, h=h, qb=qb: e.matmul(
                        PS[:, :], sel4[:, h * 128:(h + 1) * 128],
                        qbias[:, qb * 512:(qb + 1) * 512], start=False, stop=True))
                    p.s(lambda e: e.activation(PT[:, :], PS[:, :], AF.Exp))
                    p.t(lambda e, kc=kc, h=h: e.matmul(
                        PC65, Vsel[:, kc * 260 + h * 65:kc * 260 + (h + 1) * 65],
                        PT[:, :], start=(kc == 0), stop=(kc == SCH - 1)))
                p.v(lambda e: e.tensor_copy(rrow[0:1, :], PC65[64:65, :]))
                p.v(lambda e: e.reciprocal(rrow[0:1, :], rrow[0:1, :]))
                p.v(lambda e: e.tensor_copy(rrowb[0:1, :], rrow[0:1, :]))
                p.t(lambda e: e.matmul(PB, ones_r[0:1, 0:64], rrowb[0:1, :],
                                       start=True, stop=True))
                p.v(lambda e: e.tensor_copy(rBf[:, :], PB))
                p.v(lambda e, mc=mc, po=po: e.tensor_tensor(
                    ctxall[po:po + 64, mc * 512:(mc + 1) * 512], PC65[0:64, :],
                    rBf[:, :], ALU.mult))
            for tc in range(4):
                oc = qb * 4 + tc
                p.dma(lambda e, oc=oc: e.dma_start(out=r_c[:, :], in_=xcur[oc * 128:(oc + 1) * 128, :]))
                for half in range(2):
                    for k2 in range(2):
                        p.t(lambda e, half=half, k2=k2, tc=tc: e.matmul(
                            P1[:, :], ctxall[:, k2 * 512 + tc * 128:k2 * 512 + tc * 128 + 128],
                            woS[:, k2 * D + half * 512:k2 * D + (half + 1) * 512],
                            start=(k2 == 0), stop=(k2 == 1)))
                    p.v(lambda e, half=half: e.tensor_tensor(
                        outs[:, half * 512:(half + 1) * 512], P1[:, :],
                        boB[:, half * 512:(half + 1) * 512], ALU.add))
                    p.v(lambda e, half=half: e.tensor_tensor(
                        outs[:, half * 512:(half + 1) * 512], outs[:, half * 512:(half + 1) * 512],
                        r_c[:, half * 512:(half + 1) * 512], ALU.add))
                p.dma(lambda e, oc=oc: e.dma_start(out=xdst.ap()[oc * 128:(oc + 1) * 128, :], in_=outs[:, :]))
                if taps is not None:
                    p.dma(lambda e, oc=oc: e.dma_start(out=taps[oc * 128:(oc + 1) * 128, :], in_=outs[:, :]))


def _l3f_f_host(inputs, maps):
    base = _l3f_inmaps(np.zeros((16384, D), np.float32), inputs)
    for cid in range(NCORE):
        b = base[cid]
        maps[cid].update({
            "l3f_wqT": b["wqT"], "l3f_wkT": b["wkT"], "l3f_wvT": b["wvT"],
            "l3f_wr": b["wr"], "l3f_wo": b["wo"], "l3f_bqc": b["bqc"],
            "l3f_bkc": b["bkc"], "l3f_bvc": b["bvc"], "l3f_brc": b["brc"],
            "l3f_bo": b["bo"], "l3f_ehh": b["ehh"], "l3f_mh": b["mh"],
            "l3f_t64": b["t64"], "l3f_e4": b["e4"], "l3f_sel4": b["sel4"],
        })
